# version 07
# baseline (speedup 1.0000x reference)
import numpy as np
import ml_dtypes
import concourse.bass as bass
import concourse.mybir as mybir
from concourse.bass_utils import run_bass_kernel_spmd

F32 = mybir.dt.float32
BF16 = mybir.dt.bfloat16
AF = mybir.ActivationFunctionType
ALU = mybir.AluOpType
AX = mybir.AxisListType

ENGS = ("pe", "act", "dve", "pool", "sp")
QUEUES = ("sp", "pool")
NQSLOT = 4

D = 1024
DC = 8
NT = 2048
S_FULL = 4096
NSEG = 8
LSEG_LOC = 256
EPS = 1e-5


class Sched:

    def __init__(self, nc):
        self.nc = nc
        self.ops = []
        self.eng_seq = {e: 0 for e in ENGS}
        self.eng_ops = {e: [] for e in ENGS}
        self.qslot_count = {}
        self.qslot_last = {}
        self.dma_rr = {q: 0 for q in QUEUES}
        self.last_writer = {}
        self.readers = {}
        self.eng_last = {e: None for e in ENGS}

    def _event(self, op):
        if op["dma"]:
            return (op["qchan"], op["qidx"])
        return (op["eng"], op["seq"] + 1)

    def add(self, eng, fn, reads=(), writes=(), sreads=(), dma=False, queue=None, tag=""):
        idx = len(self.ops)
        op = {
            "idx": idx, "eng": eng, "fn": fn, "dma": dma, "tag": tag,
            "reads": tuple(reads), "writes": tuple(writes), "sreads": tuple(sreads),
            "seq": self.eng_seq[eng], "inc": False, "waits": {}, "vc": {},
        }
        if dma:
            q = queue or eng
            slot = self.dma_rr[q] % NQSLOT
            self.dma_rr[q] += 1
            op["qchan"] = ("q", q, slot)
            op["qidx"] = self.qslot_count.get((q, slot), 0) + 1
            self.qslot_count[(q, slot)] = op["qidx"]
            op["qprev"] = self.qslot_last.get((q, slot))
            self.qslot_last[(q, slot)] = idx
        deps = {}

        def need2(widx, force=False):
            if widx is None:
                return
            w = self.ops[widx]
            ev = self._event(w)
            same = (not w["dma"]) and w["eng"] == eng
            if same and eng == "pe" and not force:
                return
            old = deps.get(ev[0])
            if old is None or ev[1] > old[0]:
                deps[ev[0]] = (ev[1], widx, force)

        if dma and op.get("qprev") is not None:
            need2(op["qprev"])
        for r in op["reads"]:
            need2(self.last_writer.get(r))
        for r in op["sreads"]:
            need2(self.last_writer.get(r), force=True)
        for wkey in op["writes"]:
            need2(self.last_writer.get(wkey))
            for ridx in self.readers.get(wkey, ()):
                if ridx != idx:
                    need2(ridx)

        prev = self.eng_last[eng]
        vc = dict(self.ops[prev]["vc"]) if prev is not None else {}
        waits = {}
        for chan, (sq, widx, force) in deps.items():
            known = vc.get(chan, 0)
            if sq > known:
                waits[chan] = sq
                self.ops[widx]["inc"] = True
                w = self.ops[widx]
                for c2, v2 in w["vc"].items():
                    if v2 > vc.get(c2, 0):
                        vc[c2] = v2
                vc[chan] = sq
        op["waits"] = waits
        op["vc"] = vc
        self.ops.append(op)
        self.eng_seq[eng] += 1
        self.eng_ops[eng].append(idx)
        self.eng_last[eng] = idx
        for wkey in op["writes"]:
            self.last_writer[wkey] = idx
            self.readers[wkey] = []
        for r in op["reads"] + op["sreads"]:
            self.readers.setdefault(r, []).append(idx)
        return idx

    def dma(self, queue, fn, reads=(), writes=(), tag=""):
        return self.add(queue, fn, reads=reads, writes=writes, dma=True, queue=queue, tag=tag)

    def barrier(self, tag="bar"):
        gather_deps = [self.eng_last[e] for e in ENGS if self.eng_last[e] is not None]
        key = f"__bar{len(self.ops)}"
        for widx in gather_deps:
            self.ops[widx]["inc"] = True

        def mk_nop(eng):
            return lambda e: e.nop()

        gidx = self.add("dve", mk_nop("dve"), tag=tag + "_gather")
        gop = self.ops[gidx]
        for widx in gather_deps:
            w = self.ops[widx]
            ev = self._event(w)
            if (not w["dma"]) and w["eng"] == "dve":
                if gop["vc"].get(ev[0], 0) < ev[1]:
                    gop["waits"][ev[0]] = max(gop["waits"].get(ev[0], 0), ev[1])
                    gop["vc"][ev[0]] = ev[1]
            else:
                if gop["vc"].get(ev[0], 0) < ev[1]:
                    gop["waits"][ev[0]] = max(gop["waits"].get(ev[0], 0), ev[1])
                    gop["vc"][ev[0]] = ev[1]
        gop["writes"] = (key,)
        self.last_writer[key] = gidx
        self.readers[key] = []
        for e in ENGS:
            if e == "dve":
                continue
            self.add(e, mk_nop(e), reads=(key,), tag=tag + "_join")

    def assign_semvals(self):
        for e in ENGS:
            cnt = 0
            for oi in self.eng_ops[e]:
                o = self.ops[oi]
                if o["inc"] and not o["dma"]:
                    cnt += 1
                o["semval"] = cnt

    def verify(self):
        self.assign_semvals()
        pos = {e: 0 for e in ENGS}
        sem = {e: 0 for e in ENGS}
        qsem = {}
        progress = True
        while progress:
            progress = False
            for e in ENGS:
                while pos[e] < len(self.eng_ops[e]):
                    oi = self.eng_ops[e][pos[e]]
                    op = self.ops[oi]
                    ok = True
                    for chan, sq in op["waits"].items():
                        if isinstance(chan, tuple):
                            if qsem.get(chan, 0) < sq:
                                ok = False
                                break
                        else:
                            val = self.ops[self.eng_ops[chan][sq - 1]]["semval"]
                            if sem[chan] < val:
                                ok = False
                                break
                    if not ok:
                        break
                    if op["dma"]:
                        qsem[op["qchan"]] = qsem.get(op["qchan"], 0) + 1
                    elif op["inc"]:
                        sem[e] += 1
                    pos[e] += 1
                    progress = True
        stuck = {e: pos[e] for e in ENGS if pos[e] < len(self.eng_ops[e])}
        if stuck:
            det = []
            for e in stuck:
                op = self.ops[self.eng_ops[e][pos[e]]]
                det.append(f"{e}@{op['idx']}:{op['tag']} waits {op['waits']}")
            raise RuntimeError("Sched.verify deadlock: " + "; ".join(det))

    def emit(self, sems, qsems, engmap):
        self.assign_semvals()

        def run_engine(ename):
            def body(eng):
                for oi in self.eng_ops[ename]:
                    op = self.ops[oi]
                    for chan, sq in sorted(op["waits"].items(), key=lambda kv: str(kv[0])):
                        if isinstance(chan, tuple):
                            _, q, slot = chan
                            eng.wait_ge(qsems[(q, slot)], 16 * sq)
                        else:
                            pi = self.eng_ops[chan]
                            val = self.ops[pi[sq - 1]]["semval"]
                            if val > 0:
                                eng.wait_ge(sems[chan], val)
                    inst = op["fn"](eng)
                    if op["dma"]:
                        inst.then_inc(qsems[(op["qchan"][1], op["qchan"][2])], 16)
                    elif op["inc"]:
                        inst.then_inc(sems[op["eng"]], 1)
            return body

        blk = self.nc.Block()
        with blk:
            blk.tensor(run_engine("pe"))
            blk.scalar(run_engine("act"))
            blk.vector(run_engine("dve"))
            blk.gpsimd(run_engine("pool"))
            blk.sync(run_engine("sp"))

    def stats(self):
        from collections import Counter
        c = Counter(o["eng"] for o in self.ops)
        w = sum(len(o["waits"]) for o in self.ops)
        i = sum(1 for o in self.ops if o["inc"])
        return f"ops={len(self.ops)} per-eng={dict(c)} waits={w} incs={i}"


def engobjs(nc):
    return {"pe": nc.tensor, "act": nc.scalar, "dve": nc.vector, "pool": nc.gpsimd, "sp": nc.sync}



def _host_consts():
    bf = ml_dtypes.bfloat16
    cid16 = np.eye(128, dtype=bf)
    cid32 = np.eye(128, dtype=np.float32)
    ones128 = np.ones((128, 128), dtype=bf)
    onesrow = np.ones((1, 512), dtype=bf)
    sel4 = np.zeros((4, 4 * 64), dtype=bf)
    for r in range(4):
        sel4[r, r * 64:(r + 1) * 64] = 1
    selz = np.zeros((128, 2 * 64), dtype=bf)
    selz[0, 0:64] = 1
    selz[64, 64:128] = 1
    masks = np.zeros((128, 4 * 512), dtype=bf)
    c = np.arange(512)
    lq = 16 * (c % 32) + c // 32
    for kt in range(4):
        p = np.arange(128)
        lk = 16 * (p % 32) + 4 * kt + p // 32
        masks[:, kt * 512:(kt + 1) * 512] = (lk[:, None] <= lq[None, :]).astype(bf)
    iota8 = np.broadcast_to(np.arange(8, dtype=np.float32), (128, 8)).copy()
    return {"cid16": cid16, "cid32": cid32, "ones128": ones128, "onesrow": onesrow,
            "sel4": sel4, "selz": selz, "masks": masks, "iota8": iota8}


def _core_rows(c):
    half = c % 2
    own = np.concatenate([np.arange(s * 512 + half * 256, s * 512 + half * 256 + 256) for s in range(8)])
    oth = np.concatenate([np.arange(s * 512 + (1 - half) * 256, s * 512 + (1 - half) * 256 + 256) for s in range(8)])
    return own, oth



def _small(nc, name, n, cols=1, dtype=F32, parts=128):
    return [(nc.alloc_sbuf_tensor(f"{name}{i}", [parts, cols], dtype), f"{name}{i}") for i in range(n)]


class Bufs:
    pass


def alloc_consts(nc, s):
    B = Bufs()
    B.nc = nc
    B.cid16 = nc.alloc_sbuf_tensor("cid16", [128, 128], BF16)
    B.cid32 = nc.alloc_sbuf_tensor("cid32", [128, 128], F32)
    B.ones128 = nc.alloc_sbuf_tensor("ones128", [128, 128], BF16)
    B.onesrow = nc.alloc_sbuf_tensor("onesrow", [1, 512], BF16)
    B.selz = nc.alloc_sbuf_tensor("selz", [128, 128], BF16)
    for nm in ("cid16", "cid32", "ones128", "onesrow", "selz"):
        ap = nc.dram_tensor(f"c_{nm}", list(getattr(B, nm).ap().shape), getattr(B, nm).ap().dtype,
                            kind="ExternalInput").ap()
        t = getattr(B, nm)
        s.dma("sp", lambda q, t=t, ap=ap: q.dma_start(t[:], ap), writes=[nm], tag=f"ld_{nm}")
    return B


def emit_ln_tok(s, B, xt, xtk, st, ncols=1024, eps=EPS):
    inv = 1.0 / ncols
    (nm, nmk), (e2, e2k), (var, vark), (rstd, rstdk), (nb, nbk) = st
    s.add("dve", lambda v, a=nm, x=xt: v.reduce_sum(a[:], x[:], axis=AX.X, negate=True),
          reads=[xtk], writes=[nmk])
    s.add("dve", lambda v, a=nm: v.tensor_scalar(a[:], a[:], inv, None, op0=ALU.mult),
          reads=[nmk], writes=[nmk])
    sq, sqk = B.sq_rot[B.sq_i % len(B.sq_rot)]
    B.sq_i += 1
    s.add("act", lambda sc, o=sq, x=xt: sc.activation(o[:, 0:x.ap().shape[1]], x[:], AF.Square),
          reads=[xtk], writes=[sqk])
    s.add("dve", lambda v, a=e2, x=sq, n=ncols: v.reduce_sum(a[:], x[:, 0:n], axis=AX.X),
          reads=[sqk], writes=[e2k])
    s.add("dve", lambda v, a=e2: v.tensor_scalar(a[:], a[:], inv, None, op0=ALU.mult),
          reads=[e2k], writes=[e2k])
    s.add("dve", lambda v, o=var, a=nm, b=nm: v.tensor_tensor(o[:], a[:], b[:], op=ALU.mult),
          reads=[nmk], writes=[vark])
    s.add("dve", lambda v, o=var, a=e2, b=var, e=float(eps):
          v.scalar_tensor_tensor(o[:], a[:], e, b[:], op0=ALU.add, op1=ALU.subtract),
          reads=[e2k, vark], writes=[vark])
    s.add("act", lambda sc, o=var, a=var: sc.activation(o[:], a[:], AF.Ln),
          reads=[vark], writes=[vark])
    s.add("act", lambda sc, o=rstd, a=var: sc.activation(o[:], a[:], AF.Exp, scale=-0.5),
          reads=[vark], writes=[rstdk])
    s.add("dve", lambda v, o=nb, a=nm, b=rstd: v.tensor_tensor(o[:], a[:], b[:], op=ALU.mult),
          reads=[nmk, rstdk], writes=[nbk])
    return (rstd, rstdk), (nb, nbk)


def emit_load_ln_fm(s, B, src_ap, xn, xnk, colfn, ntiles=16, ln=True):
    nc = B.nc
    for tt in range(ntiles):
        xt, xtk = B.xt_rot[B.xt_i % len(B.xt_rot)]
        B.xt_i += 1
        s.dma("sp", lambda q, t=xt, a=src_ap, tt=tt: q.dma_start(t[:], a[tt * 128:(tt + 1) * 128, :]),
              reads=[xtk], writes=[xtk], tag="ldx")
        if ln:
            st = B.lnst[B.lnst_i % len(B.lnst)]
            B.lnst_i += 1
            (rstd, rstdk), (nb, nbk) = emit_ln_tok(s, B, xt, xtk, st)
            xnt, xntk = B.xnt_rot[B.xnt_i % len(B.xnt_rot)]
            B.xnt_i += 1
            s.add("act", lambda sc, o=xnt, x=xt, r=rstd, b=nb:
                  sc.activation(o[:], x[:], AF.Identity, bias=b[:], scale=r[:]),
                  reads=[xtk], sreads=[rstdk, nbk], writes=[xntk], tag="xnt")
        else:
            xnt, xntk = B.xnt_rot[B.xnt_i % len(B.xnt_rot)]
            B.xnt_i += 1
            s.add("dve", lambda v, o=xnt, x=xt: v.tensor_copy(o[:], x[:]),
                  reads=[xtk], writes=[xntk], tag="xcast")
        for dc in range(DC):
            pt, ptk = B.tps[(tt * DC + dc) % len(B.tps)]
            s.add("pe", lambda t, p=pt, x=xnt, dc=dc, c=B.cid16:
                  t.matmul(p[:], x[:, dc * 128:(dc + 1) * 128], c[:], start=True, stop=True),
                  reads=[xntk, "cid16"], writes=[ptk], tag="xT")
            eng = "dve" if dc % 2 == 0 else "act"
            if eng == "dve":
                s.add("dve", lambda v, p=pt, o=xn, col=colfn(dc, tt): v.tensor_copy(o[:, col:col + 128], p[:]),
                      reads=[ptk], writes=[xnk], tag="xTev")
            else:
                s.add("act", lambda sc, p=pt, o=xn, col=colfn(dc, tt): sc.copy(o[:, col:col + 128], p[:]),
                      reads=[ptk], writes=[xnk], tag="xTev")


def emit_exit_delta(s, B, delta, deltak, xprev_ap, out_ap, outkey):
    nc = B.nc
    for tt in range(16):
        xt, xtk = B.xt_rot[B.xt_i % len(B.xt_rot)]
        B.xt_i += 1
        s.dma("sp", lambda q, t=xt, a=xprev_ap, tt=tt: q.dma_start(t[:], a[tt * 128:(tt + 1) * 128, :]),
              reads=[xtk], writes=[xtk], tag="ldxe")
        ob, obk = B.out_rot[B.out_i % len(B.out_rot)]
        B.out_i += 1
        for dc in range(DC):
            pt, ptk = B.tps[(tt * DC + dc) % len(B.tps)]
            s.add("pe", lambda t, p=pt, d=delta, dc=dc, tt=tt, c=B.cid32:
                  t.matmul(p[:], d[:, dc * 2048 + tt * 128: dc * 2048 + (tt + 1) * 128], c[:], start=True, stop=True),
                  reads=[deltak, "cid32"], writes=[ptk], tag="dT")
            s.add("dve", lambda v, p=pt, o=ob, x=xt, dc=dc:
                  v.tensor_tensor(o[:, dc * 128:(dc + 1) * 128], p[:], x[:, dc * 128:(dc + 1) * 128], op=ALU.add),
                  reads=[ptk, xtk], writes=[obk], tag="dadd")
        s.dma("sp", lambda q, a=out_ap, t=ob, tt=tt: q.dma_start(a[tt * 128:(tt + 1) * 128, :], t[:]),
              reads=[obk], writes=[f"{outkey}_{tt}"], tag="stx")


def alloc_common(nc, s, B):
    B.xt_rot = [(nc.alloc_sbuf_tensor(f"xt{i}", [128, 1024], F32), f"xt{i}") for i in range(2)]
    B.xt_i = 0
    B.sq_rot = [(nc.alloc_sbuf_tensor(f"sqs{i}", [128, 1024], BF16), f"sqs{i}") for i in range(2)]
    B.sq_i = 0
    B.xnt_rot = [(nc.alloc_sbuf_tensor(f"xnt{i}", [128, 1024], BF16), f"xnt{i}") for i in range(2)]
    B.xnt_i = 0
    B.out_rot = [(nc.alloc_sbuf_tensor(f"outb{i}", [128, 1024], F32), f"outb{i}") for i in range(2)]
    B.out_i = 0
    B.lnst = []
    for i in range(2):
        B.lnst.append([( nc.alloc_sbuf_tensor(f"ln{nm}{i}", [128, 1], F32), f"ln{nm}{i}")
                       for nm in ("nm", "e2", "var", "rstd", "nb")])
    B.lnst_i = 0
    B.tps = None



def build_l0_attn(nc, s, B, xout):
    din = {}
    for nm, shp, dt in [("x_own", [NT, D], F32), ("x_oth", [NT, D], F32),
                        ("wq", [16, D, 128], F32), ("bq", [16, 128], F32),
                        ("wk", [16, D, 128], F32), ("bk", [16, 128], F32),
                        ("wv", [16, D, 64], F32), ("bv", [16, 64], F32),
                        ("wo", [D, D], F32), ("bo", [D], F32),
                        ("gam_t", [128, D], F32), ("bet_t", [128, D], F32),
                        ("lam64", [64, 1], F32)]:
        din[nm] = nc.dram_tensor(nm, shp, dt, kind="ExternalInput").ap()

    gblk, gph = [], []

    def sb(name, shape, dtype, scope=None):
        g = nc.sbuf_tensor(name, shape, dtype)
        (gph if scope is None else scope).append(g)
        return g.__enter__()

    def ps(name, shape, dtype=F32):
        g = nc.psum_tensor(name, shape, dtype)
        gph.append(g)
        return g.__enter__()

    def close(lst=None):
        lst = gph if lst is None else lst
        for g in reversed(lst):
            g.__exit__(None, None, None)
        lst.clear()

    Xn = sb("Xn", [128, DC * S_FULL], BF16, scope=gblk)
    OcatT = sb("OcatT", [128, DC * NT], BF16, scope=gblk)
    selz_sb = B.selz
    lam_sb = sb("lam_sb", [64, 1], F32, scope=gblk)
    s.dma("sp", lambda q, t=lam_sb, a=din["lam64"]: q.dma_start(t[:], a), writes=["lam"], tag="ldlam")
    s.add("dve", lambda v, t=lam_sb: v.tensor_scalar(t[:], t[:], -1.0, None, op0=ALU.mult),
          reads=["lam"], writes=["lam"], tag="neglam")

    B.tps = [(ps(f"tpA{i}", [128, 128]), f"tpA{i}") for i in range(4)]
    emit_load_ln_fm(s, B, din["x_own"], Xn, "Xn", lambda dc, tt: dc * S_FULL + tt * 128)
    emit_load_ln_fm(s, B, din["x_oth"], Xn, "Xn", lambda dc, tt: dc * S_FULL + NT + tt * 128)
    s.barrier("endA")
    close()

    pA = ps("pA", [128, 1024])
    pB = ps("pB", [128, 1024])
    pbank = [(pA, 0, "pA0"), (pA, 512, "pA1"), (pB, 0, "pB0"), (pB, 512, "pB1")]
    pO1 = ps("pO1", [65, 512])
    pO2 = ps("pO2", [65, 512])
    pu = [ps(f"pu{i}", [128, 512]) for i in range(2)]
    pubank = [(pu[0], 0, "pu0"), (pu[1], 0, "pu1")]

    Kg = [sb(f"Kg{i}", [128, S_FULL], BF16) for i in range(2)]
    Qg = [sb(f"Qg{i}", [128, NT], BF16) for i in range(2)]
    Vg = [sb(f"Vgp{i}", [128, 2 * 65 * 32], BF16) for i in range(2)]
    wkb = [sb(f"wkb{i}", [128, 1024], BF16) for i in range(2)]
    wqb = [sb(f"wqb{i}", [128, 1024], BF16) for i in range(2)]
    wvg = [sb(f"wvgp{i}", [128, 1024], BF16) for i in range(2)]
    bqs = sb("bqs", [128, 16], F32)
    bks = sb("bks", [128, 16], F32)
    bvrow = sb("bvrow", [1, 1024], BF16)
    us1 = [sb(f"us1_{i}", [64, 2048], BF16) for i in range(2)]
    us2 = [sb(f"us2_{i}", [64, 2048], BF16) for i in range(2)]
    zb = [sb(f"zbb{i}", [128, 2048], BF16) for i in range(2)]
    e12 = [sb(f"e12_{i}", [128, 1024], BF16) for i in range(2)]
    ct1 = sb("ct1", [64, 512], BF16)
    ct2 = sb("ct2", [64, 512], BF16)
    for i in range(2):
        s.add("dve", lambda v, t=zb[i], i=i: v.memset(t[:], 1.0), writes=[f"zbk{i}"], tag="zinit")

    bqnat = sb("bqnat", [16, 128], F32)
    s.dma("sp", lambda q, t=bqnat, a=din["bq"]: q.dma_start(t[:], a), reads=["bqnat"], writes=["bqnat"], tag="ldbq")
    s.add("pe", lambda t: t.matmul(pA[:, 0:16], bqnat[:], B.cid32[0:16, 0:16], start=True, stop=True),
          reads=["bqnat", "cid32"], writes=["pA0"], tag="bqT")
    s.add("dve", lambda v: v.tensor_copy(bqs[:], pA[:, 0:16]), reads=["pA0", "bqnat"], writes=["bqs"], tag="bqTe")
    s.dma("sp", lambda q, t=bqnat, a=din["bk"]: q.dma_start(t[:], a), reads=["bqnat"], writes=["bqnat"], tag="ldbk")
    s.add("pe", lambda t: t.matmul(pA[:, 0:16], bqnat[:], B.cid32[0:16, 0:16], start=True, stop=True),
          reads=["bqnat", "cid32"], writes=["pA0"], tag="bkT")
    s.add("dve", lambda v: v.tensor_copy(bks[:], pA[:, 0:16]), reads=["pA0"], writes=["bks"], tag="bkTe")
    s.dma("pool", lambda q, t=bvrow, a=din["bv"].rearrange("h d -> (h d)"): q.dma_start(t[0:1, :], a[None, :]),
          writes=["bvrow"], tag="ldbv")
    for i in range(2):
        s.add("dve", lambda v, t=Vg[i], i=i:
              v.memset(t[:].rearrange("p (h tt c) -> p h tt c", h=2, tt=32)[:, :, :, 64:65], 1.0),
              writes=[f"Vg{i}"], tag="vones")

    units = []
    ready = {}
    for pr in range(8):
        vslot = pr % 2

        def wv_load(pr=pr, vslot=vslot):
            for hh in range(2):
                s.dma("pool", lambda q, a=din["wv"][pr * 2 + hh].rearrange("(ej p) d -> p ej d", p=128),
                      hh=hh, vslot=vslot:
                      q.dma_start(wvg[vslot][:].rearrange("p (ej h d) -> p ej h d", ej=8, h=2)[:, :, hh, :], a),
                      reads=[f"wvg{vslot}"], writes=[f"wvg{vslot}"], tag="ldwv")
        units.append(wv_load)
        for tt in range(32):
            def v_unit(tt=tt, pr=pr, vslot=vslot):
                pt_, off, pk = pubank[tt % 2]
                for ej in range(DC):
                    s.add("pe", lambda t, p=pt_, o=off, w=wvg[vslot], ej=ej, tt=tt:
                          t.matmul(p[:, o:o + 128], Xn[:, ej * S_FULL + tt * 128: ej * S_FULL + (tt + 1) * 128],
                                   w[:, ej * 128:(ej + 1) * 128], start=(ej == 0), stop=False),
                          reads=["Xn", f"wvg{vslot}"], writes=[pk], tag="vmm")
                s.add("pe", lambda t, p=pt_, o=off, pr=pr:
                      t.matmul(p[:, o:o + 128], B.onesrow[0:1, 0:128], bvrow[0:1, pr * 128:(pr + 1) * 128],
                               start=False, stop=True),
                      reads=["onesrow", "bvrow"], writes=[pk], tag="vbias")

                def vev(e, p=pt_, o=off, tt=tt, vslot=vslot):
                    src = p[:, o:o + 128].rearrange("p (h d) -> p h d", h=2)
                    dst = Vg[vslot][:].rearrange("p (h t) -> p h t", h=2)[:, :, tt * 65: tt * 65 + 64]
                    return e.tensor_copy(dst, src)
                s.add("dve", vev, reads=[pk], writes=[f"Vg{vslot}"], tag="vev")
            units.append(v_unit)
        for hh in range(2):
            h = pr * 2 + hh
            hs = h % 2

            def w_load(h=h, hs=hs):
                s.dma("pool", lambda q, t=wkb[hs], a=din["wk"][h].rearrange("(ej p) c -> p ej c", p=128):
                      q.dma_start(t[:].rearrange("p (ej c) -> p ej c", ej=8), a),
                      reads=[f"wkb{hs}"], writes=[f"wkb{hs}"], tag="ldwk")
                s.dma("pool", lambda q, t=wqb[hs], a=din["wq"][h].rearrange("(ej p) c -> p ej c", p=128):
                      q.dma_start(t[:].rearrange("p (ej c) -> p ej c", ej=8), a),
                      reads=[f"wqb{hs}"], writes=[f"wqb{hs}"], tag="ldwq")
            units.append(w_load)
            for n8 in range(8):
                def k_unit(n8=n8, h=h, hs=hs):
                    pt_, off, pk = pubank[n8 % 2]
                    for ej in range(DC):
                        s.add("pe", lambda t, p=pt_, o=off, w=wkb[hs], ej=ej, n8=n8:
                              t.matmul(p[:, o:o + 512], w[:, ej * 128:(ej + 1) * 128],
                                       Xn[:, ej * S_FULL + n8 * 512: ej * S_FULL + (n8 + 1) * 512],
                                       start=(ej == 0), stop=(ej == DC - 1)),
                              reads=[f"wkb{hs}", "Xn"], writes=[pk], tag="kmm")
                    s.add("dve", lambda v, p=pt_, o=off, d=Kg[hs], n8=n8, h=h:
                          v.tensor_scalar(d[:, n8 * 512:(n8 + 1) * 512], p[:, o:o + 512],
                                          bks[:, h:h + 1], None, op0=ALU.add),
                          reads=[pk], sreads=["bks"], writes=[f"Kgk{hs}"], tag="kev")
                units.append(k_unit)
            for n4 in range(4):
                def q_unit(n4=n4, h=h, hs=hs):
                    pt_, off, pk = pubank[n4 % 2]
                    for ej in range(DC):
                        s.add("pe", lambda t, p=pt_, o=off, w=wqb[hs], ej=ej, n4=n4:
                              t.matmul(p[:, o:o + 512], w[:, ej * 128:(ej + 1) * 128],
                                       Xn[:, ej * S_FULL + n4 * 512: ej * S_FULL + (n4 + 1) * 512],
                                       start=(ej == 0), stop=(ej == DC - 1)),
                              reads=[f"wqb{hs}", "Xn"], writes=[pk], tag="qmm")
                    s.add("dve", lambda v, p=pt_, o=off, d=Qg[hs], n4=n4, h=h:
                          v.tensor_scalar(d[:, n4 * 512:(n4 + 1) * 512], p[:, o:o + 512],
                                          bqs[:, h:h + 1], None, op0=ALU.add),
                          reads=[pk], sreads=["bqs"], writes=[f"Qgk{hs}"], tag="qev")
                units.append(q_unit)
            ready[h] = len(units)

    drained = [0]

    def drain(upto):
        upto = min(upto, len(units))
        while drained[0] < upto:
            units[drained[0]]()
            drained[0] += 1

    pending_combine = [None]
    for h in range(16):
        hs = h % 2
        vslot = (h // 2) % 2
        drain(ready[h])
        if pending_combine[0] is not None:
            pending_combine[0]()
            pending_combine[0] = None
        nxt = ready[h + 1] if h + 1 < 16 else len(units)
        need = nxt - ready[h]
        slot = [0]

        def pace():
            slot[0] += 1
            drain(ready[h] + (need * slot[0] + 127) // 128)
        for qt in range(4):
            def emit_s(kc, qt=qt, hs=hs):
                pt_ = pA if kc % 2 == 0 else pB
                k0 = "pA0" if kc % 2 == 0 else "pB0"
                k1 = "pA1" if kc % 2 == 0 else "pB1"
                s.add("pe", lambda t, p=pt_, K=Kg[hs], Q=Qg[hs], kc=kc, qt=qt:
                      t.matmul(p[:, 0:512], K[0:64, kc * 128:(kc + 1) * 128],
                               Q[0:64, qt * 512:(qt + 1) * 512], start=True, stop=True,
                               tile_position=(0, 0)),
                      reads=[f"Kgk{hs}", f"Qgk{hs}"], writes=[k0], tag="s1")
                s.add("pe", lambda t, p=pt_, K=Kg[hs], Q=Qg[hs], kc=kc, qt=qt:
                      t.matmul(p[:, 512:1024], K[64:128, kc * 128:(kc + 1) * 128],
                               Q[64:128, qt * 512:(qt + 1) * 512], start=True, stop=True,
                               tile_position=(64, 0)),
                      reads=[f"Kgk{hs}", f"Qgk{hs}"], writes=[k1], tag="s2")

            def emit_epv(kc, vslot=vslot, hh=(h % 2)):
                pt_ = pA if kc % 2 == 0 else pB
                k0 = "pA0" if kc % 2 == 0 else "pB0"
                k1 = "pA1" if kc % 2 == 0 else "pB1"
                eb, ebk = e12[kc % 2], f"e12_{kc % 2}"
                s.add("act", lambda sc, o=eb, p=pt_: sc.activation(o[:], p[:], AF.Exp, scale=0.125),
                      reads=[k0, k1], writes=[ebk], tag="exp")
                hh2 = h % 2
                s.add("pe", lambda t, e=eb, kc=kc, hh2=hh2, vslot=vslot:
                      t.matmul(pO1[:], Vg[vslot][:, hh2 * 2080 + kc * 65: hh2 * 2080 + (kc + 1) * 65],
                               e[:, 0:512], start=(kc == 0), stop=(kc == 31)),
                      reads=[f"Vg{vslot}", ebk], writes=["pO1"], tag="pv1")
                s.add("pe", lambda t, e=eb, kc=kc, hh2=hh2, vslot=vslot:
                      t.matmul(pO2[:], Vg[vslot][:, hh2 * 2080 + kc * 65: hh2 * 2080 + (kc + 1) * 65],
                               e[:, 512:1024], start=(kc == 0), stop=(kc == 31)),
                      reads=[f"Vg{vslot}", ebk], writes=["pO2"], tag="pv2")
            emit_s(0)
            for kc in range(1, 32):
                emit_s(kc)
                emit_epv(kc - 1)
                pace()
            emit_epv(31)
            pace()
            s.add("act", lambda sc, qt=qt, u=us1[hs]: sc.copy(u[:, qt * 512:(qt + 1) * 512], pO1[0:64, :]),
                  reads=["pO1"], writes=[f"us1k{hs}"], tag="uev1")
            s.add("dve", lambda v, qt=qt, z=zb[hs]: v.tensor_copy(z[0:1, qt * 512:(qt + 1) * 512], pO1[64:65, :]),
                  reads=["pO1"], writes=[f"zbk{hs}"], tag="zev1")
            s.add("dve", lambda v, qt=qt, u=us2[hs]: v.tensor_copy(u[:, qt * 512:(qt + 1) * 512], pO2[0:64, :]),
                  reads=["pO2"], writes=[f"us2k{hs}"], tag="uev2")
            s.add("act", lambda sc, qt=qt, z=zb[hs]: sc.copy(z[64:65, qt * 512:(qt + 1) * 512], pO2[64:65, :]),
                  reads=["pO2"], writes=[f"zbk{hs}"], tag="zev2")

        def mk_combine(h=h, hs=hs):
            def emit_combine():
                z = zb[hs]
                s.add("act", lambda sc, z=z: sc.activation(z[:], z[:], AF.Ln),
                      reads=[f"zbk{hs}"], writes=[f"zbk{hs}"], tag="lnz")
                s.add("act", lambda sc, z=z: sc.activation(z[:], z[:], AF.Exp, scale=-1.0),
                      reads=[f"zbk{hs}"], writes=[f"zbk{hs}"], tag="expz")
                for qt in range(4):
                    s.add("pe", lambda t, qt=qt, z=z:
                          t.matmul(pu[0][0:64, :], selz_sb[:, 0:64], z[:, qt * 512:(qt + 1) * 512],
                                   start=True, stop=True),
                          reads=["selz", f"zbk{hs}"], writes=["pu0"], tag="bz1")
                    s.add("pe", lambda t, qt=qt, z=z:
                          t.matmul(pu[1][0:64, :], selz_sb[:, 64:128], z[:, qt * 512:(qt + 1) * 512],
                                   start=True, stop=True),
                          reads=["selz", f"zbk{hs}"], writes=["pu1"], tag="bz2")
                    s.add("dve", lambda v, qt=qt, u=us1[hs]:
                          v.tensor_tensor(ct1[:], u[:, qt * 512:(qt + 1) * 512], pu[0][0:64, :], op=ALU.mult),
                          reads=[f"us1k{hs}", "pu0"], writes=["ct1"], tag="c1")
                    s.add("dve", lambda v, qt=qt, u=us2[hs]:
                          v.tensor_tensor(ct2[:], u[:, qt * 512:(qt + 1) * 512], pu[1][0:64, :], op=ALU.mult),
                          reads=[f"us2k{hs}", "pu1"], writes=["ct2"], tag="c2")
                    s.add("dve", lambda v, h=h, qt=qt:
                          v.scalar_tensor_tensor(
                              OcatT[64 * (h % 2):64 * (h % 2) + 64,
                                    (h // 2) * NT + qt * 512:(h // 2) * NT + (qt + 1) * 512],
                              ct2[:], lam_sb[:], ct1[:], op0=ALU.mult, op1=ALU.add),
                          reads=["ct1", "ct2"], sreads=["lam"], writes=["OcatT"], tag="comb")
            return emit_combine
        pending_combine[0] = mk_combine()
    pending_combine[0]()
    s.barrier("endB")
    close()

    wo_sb = sb("wo_sb", [128, DC * 1024], BF16)
    bo_row = sb("bo_row", [1, 1024], BF16)
    gam = sb("gam", [128, 1024], F32)
    bet = sb("bet", [128, 1024], F32)
    ytok = [sb(f"ytok{i}", [128, 1024], F32) for i in range(2)]
    ywork = [sb(f"ywork{i}", [128, 1024], F32) for i in range(2)]
    pY = [ps(f"pY{i}", [128, 512]) for i in range(2)]
    s.dma("pool", lambda q, t=wo_sb, a=din["wo"].rearrange("(cc p) d -> p cc d", p=128):
          q.dma_start(t[:].rearrange("p (cc d) -> p cc d", cc=8), a), writes=["wo_sb"], tag="ldwo")
    s.dma("pool", lambda q, t=bo_row, a=din["bo"]: q.dma_start(t[0:1, :], a[None, :]), writes=["bo_row"], tag="ldbo")
    s.dma("sp", lambda q, t=gam, a=din["gam_t"]: q.dma_start(t[:], a), writes=["gam"], tag="ldgam")
    s.dma("sp", lambda q, t=bet, a=din["bet_t"]: q.dma_start(t[:], a), writes=["bet"], tag="ldbet")
    s.add("dve", lambda v, t=gam: v.tensor_scalar(t[:], t[:], 0.2, None, op0=ALU.mult),
          reads=["gam"], writes=["gam"], tag="g02")
    s.add("dve", lambda v, t=bet: v.tensor_scalar(t[:], t[:], 0.2, None, op0=ALU.mult),
          reads=["bet"], writes=["bet"], tag="b02")
    for tt in range(16):
        yb, ybk = ytok[tt % 2], f"ytok{tt % 2}"
        for nh in range(2):
            p = pY[nh]
            pk = f"pY{nh}"
            for cc in range(8):
                s.add("pe", lambda t, p=p, cc=cc, tt=tt, nh=nh:
                      t.matmul(p[:], OcatT[:, cc * NT + tt * 128: cc * NT + (tt + 1) * 128],
                               wo_sb[:, cc * 1024 + nh * 512: cc * 1024 + (nh + 1) * 512],
                               start=(cc == 0), stop=False),
                      reads=["OcatT", "wo_sb"], writes=[pk], tag="ymm")
            s.add("pe", lambda t, p=p, nh=nh:
                  t.matmul(p[:], B.onesrow[0:1, 0:128], bo_row[0:1, nh * 512:(nh + 1) * 512],
                           start=False, stop=True),
                  reads=["onesrow", "bo_row"], writes=[pk], tag="ybias")
            s.add("dve", lambda v, p=p, o=yb, nh=nh: v.tensor_copy(o[:, nh * 512:(nh + 1) * 512], p[:]),
                  reads=[pk], writes=[ybk], tag="yev")
        st = B.lnst[B.lnst_i % len(B.lnst)]
        B.lnst_i += 1
        (rstd, rstdk), (nb, nbk) = emit_ln_tok(s, B, yb, ybk, st)
        yw, ywk = ywork[tt % 2], f"ywork{tt % 2}"
        s.add("act", lambda sc, o=yw, x=yb, r=rstd, b=nb:
              sc.activation(o[:], x[:], AF.Identity, bias=b[:], scale=r[:]),
              reads=[ybk], sreads=[rstdk, nbk], writes=[ywk], tag="ynorm")
        s.add("dve", lambda v, t=yw: v.tensor_tensor(t[:], t[:], gam[:], op=ALU.mult),
              reads=[ywk, "gam"], writes=[ywk], tag="ygam")
        s.add("pool", lambda g_, t=yw: g_.tensor_tensor(t[:], t[:], bet[:], op=ALU.add),
              reads=[ywk, "bet"], writes=[ywk], tag="ybet")
        xt, xtk = B.xt_rot[B.xt_i % len(B.xt_rot)]
        B.xt_i += 1
        s.dma("sp", lambda q, t=xt, a=din["x_own"], tt=tt: q.dma_start(t[:], a[tt * 128:(tt + 1) * 128, :]),
              reads=[xtk], writes=[xtk], tag="ldxc")
        ob, obk = B.out_rot[B.out_i % len(B.out_rot)]
        B.out_i += 1
        s.add("dve", lambda v, o=ob, a=yw, b=xt: v.tensor_tensor(o[:], a[:], b[:], op=ALU.add),
              reads=[ywk, xtk], writes=[obk], tag="oadd")
        s.dma("sp", lambda q, t=ob, tt=tt: q.dma_start(xout[tt * 128:(tt + 1) * 128, :], t[:]),
              reads=[obk], writes=[f"out_x{tt}"], tag="stx")
    s.add("sp", lambda q: q.nop(), reads=[f"out_x{t}" for t in range(16)], tag="fin")
    close()
    close(gblk)
    return din



def _mk_sched():
    nc = bass.Bass("TRN2", target_bir_lowering=False)
    s = Sched(nc)
    B = alloc_consts(nc, s)
    alloc_common(nc, s, B)
    return nc, s, B


def _finish(nc, s):
    s.verify()
    sems = {eng: nc.alloc_semaphore(f"s_{eng}") for eng in ENGS}
    qsems = {(q, i): nc.alloc_semaphore(f"q_{q}{i}") for q in QUEUES for i in range(NQSLOT)}
    s.emit(sems, qsems, None)
    return nc


_B0_CACHE = {}


def build_b0_program():
    if "nc" not in _B0_CACHE:
        nc, s, B = _mk_sched()
        xout = nc.dram_tensor("xb0", [NT, D], F32, kind="ExternalOutput").ap()
        build_l0_attn(nc, s, B, xout)
        _B0_CACHE["nc"] = _finish(nc, s)
        _B0_CACHE["stats"] = s.stats()
    return _B0_CACHE["nc"]


def run_block0(inputs, x_cur=None):
    C = _host_consts()
    nc = build_b0_program()
    X = np.asarray(inputs["L0_attn_X"], dtype=np.float32)
    gam_t = np.broadcast_to(np.asarray(inputs["L0_attn_gamma"], np.float32), (128, D)).copy()
    bet_t = np.broadcast_to(np.asarray(inputs["L0_attn_beta"], np.float32), (128, D)).copy()
    lam64 = np.broadcast_to(np.asarray(inputs["L0_attn_lam"], np.float32).reshape(1, 1), (64, 1)).copy()
    shared = {
        "wq": np.asarray(inputs["L0_attn_Wq"], np.float32), "bq": np.asarray(inputs["L0_attn_bq"], np.float32),
        "wk": np.asarray(inputs["L0_attn_Wk"], np.float32), "bk": np.asarray(inputs["L0_attn_bk"], np.float32),
        "wv": np.asarray(inputs["L0_attn_Wv"], np.float32), "bv": np.asarray(inputs["L0_attn_bv"], np.float32),
        "wo": np.asarray(inputs["L0_attn_Wo"], np.float32), "bo": np.asarray(inputs["L0_attn_bo"], np.float32),
        "gam_t": gam_t, "bet_t": bet_t, "lam64": lam64, "c_selz": C["selz"],
        "c_cid16": C["cid16"], "c_cid32": C["cid32"], "c_ones128": C["ones128"], "c_onesrow": C["onesrow"],
    }
    in_maps = []
    for c in range(8):
        b = c // 2
        own, oth = _core_rows(c)
        m = dict(shared)
        m["x_own"] = np.ascontiguousarray(X[b][own])
        m["x_oth"] = np.ascontiguousarray(X[b][oth])
        in_maps.append(m)
    res = run_bass_kernel_spmd(nc, in_maps, core_ids=list(range(8)))
    return [r["xb0"] for r in res.results]


def kernel(**inputs):
    xb = run_block0(inputs)
    xb = run_block1(inputs, xb)
    xb = run_block2(inputs, xb)
    xb = run_block3(inputs, xb)
    out = np.zeros((4, S_FULL, D), np.float32)
    for c in range(8):
        b = c // 2
        own, _ = _core_rows(c)
        out[b][own] = xb[c]
    return out



def build_l0_ffn(nc, s, B, xprev, xout):
    din = {"xprev": xprev}
    for nm, shp, dt in [("w_fc", [4096, D], F32),
                        ("ln_w", [4096], F32), ("ln_b", [4096], F32),
                        ("w_proj", [D, 4096], F32)]:
        din[nm] = nc.dram_tensor(nm, shp, dt, kind="ExternalInput").ap()
    wqfcT = nc.dram_tensor("wqfcT", [256, 16384], BF16, kind="Internal").ap()
    wpT = nc.dram_tensor("wpT", [256, 16384], BF16, kind="Internal").ap()

    gblk, gph = [], []

    def sb(name, shape, dtype, scope=None):
        g = nc.sbuf_tensor(name, shape, dtype)
        (gph if scope is None else scope).append(g)
        return g.__enter__()

    def ps(name, shape, dtype=F32):
        g = nc.psum_tensor(name, shape, dtype)
        gph.append(g)
        return g.__enter__()

    def close(lst=None):
        lst = gph if lst is None else lst
        for g in reversed(lst):
            g.__exit__(None, None, None)
        lst.clear()

    xn = sb("xn1", [128, DC * NT], BF16, scope=gblk)
    d1dram = nc.dram_tensor("d1dram", [D, NT], F32, kind="Internal").ap()

    B.tps = [(ps(f"tpF{i}", [128, 128]), f"tpF{i}") for i in range(4)]
    pq = ps("pq", [128, 512])
    emit_load_ln_fm(s, B, din["xprev"], xn, "xn1", lambda dc, tt: dc * NT + tt * 128)
    onescol = sb("onescol", [128, 1], F32)
    ones1f = sb("ones1f", [1, 128], F32)
    s.add("dve", lambda v: v.memset(onescol[:], 1.0), writes=["onescol"], tag="oc")
    s.add("dve", lambda v: v.memset(ones1f[:], 1.0), writes=["ones1f"], tag="o1f")

    for wi, (wnm, nrow_t, ncol) in enumerate([("w_fc", 32, 1024), ("w_proj", 8, 4096)]):
        gwi = []
        wt = [sb(f"wt{wi}_{i}", [128, ncol], F32, scope=gwi) for i in range(2)]
        g1t = [sb(f"g1t{wi}_{i}", [128, ncol], BF16, scope=gwi) for i in range(2)]
        g2t = [sb(f"g2t{wi}_{i}", [128, ncol], BF16, scope=gwi) for i in range(2)]
        tb = [sb(f"tb{wi}_{i}", [128, 512], BF16, scope=gwi) for i in range(2)]
        cols = sb(f"cols{wi}", [128, nrow_t], F32, scope=gwi)
        rowt = sb(f"rowt{wi}", [128, 1], F32, scope=gwi)
        nthr = sb(f"nthr{wi}", [128, 1], F32, scope=gwi)
        pthr = sb(f"pthr{wi}", [128, 1], F32, scope=gwi)
        for i in range(nrow_t):
            w_, wk_ = wt[i % 2], f"wt{wi}_{i % 2}"
            s.dma("sp", lambda q, t=w_, i=i, a=din[wnm]: q.dma_start(t[:], a[i * 128:(i + 1) * 128, :]),
                  reads=[wk_], writes=[wk_], tag="qld")
            s.add("dve", lambda v, t=w_, i=i, c=cols: v.reduce_sum(c[:, i:i + 1], t[:], axis=AX.X,
                                                                   apply_absolute_value=True),
                  reads=[wk_], writes=[f"cols{wi}"], tag="qabs")
        s.add("dve", lambda v, c=cols, r=rowt: v.reduce_sum(r[:], c[:], axis=AX.X),
              reads=[f"cols{wi}"], writes=[f"rowt{wi}"], tag="qrow")
        s.add("pe", lambda t, r=rowt: t.matmul(pq[0:1, 0:1], r[:], onescol[:], start=True, stop=True),
              reads=[f"rowt{wi}", "onescol"], writes=["pq"], tag="qtot")
        nel = float(nrow_t * 128 * ncol)
        nthr0 = sb(f"nthr0_{wi}", [1, 1], F32, scope=gwi)
        s.add("act", lambda sc, o=nthr0, sc_=-0.5 / nel: sc.activation(o[:], pq[0:1, 0:1], AF.Identity, scale=sc_),
              reads=["pq"], writes=[f"nthr0k{wi}"], tag="qthr")
        s.add("pe", lambda t, r=nthr0: t.matmul(pq[:, 32:33], ones1f[:], r[:], start=True, stop=True),
              reads=[f"nthr0k{wi}", "ones1f"], writes=["pq3"], tag="qbc")
        s.add("dve", lambda v, n=nthr: v.tensor_scalar(n[:], pq[:, 32:33], -0.5e-6, None, op0=ALU.add),
              reads=["pq3"], writes=[f"nthr{wi}"], tag="qnth")
        s.add("dve", lambda v, n=nthr, p=pthr: v.tensor_scalar(p[:], n[:], -1.0, None, op0=ALU.mult),
              reads=[f"nthr{wi}"], writes=[f"pthr{wi}"], tag="qpth")
        for i in range(nrow_t):
            w_, wk_ = wt[i % 2], f"wt{wi}_{i % 2}"
            g1_, g1k = g1t[i % 2], f"g1t{wi}_{i % 2}"
            g2_, g2k = g2t[i % 2], f"g2t{wi}_{i % 2}"
            q_, qk_ = g1_, g1k
            s.dma("sp", lambda q, t=w_, i=i, a=din[wnm]: q.dma_start(t[:], a[i * 128:(i + 1) * 128, :]),
                  reads=[wk_], writes=[wk_], tag="qld2")
            s.add("dve", lambda v, t=w_, o=g1_, th=pthr: v.tensor_scalar(o[:], t[:], th[:], None, op0=ALU.is_gt),
                  reads=[wk_], sreads=[f"pthr{wi}"], writes=[g1k], tag="qgt")
            s.add("dve", lambda v, t=w_, o=g2_, th=nthr: v.tensor_scalar(o[:], t[:], th[:], None, op0=ALU.is_lt),
                  reads=[wk_], sreads=[f"nthr{wi}"], writes=[g2k], tag="qlt")
            s.add("dve", lambda v, a=g1_, b=g2_: v.tensor_tensor(a[:], a[:], b[:], op=ALU.subtract),
                  reads=[g1k, g2k], writes=[g1k], tag="qsub")
            ngrp = ncol // 512
            for grp in range(ngrp):
                t_, tk_ = tb[grp % 2], f"tb{wi}_{grp % 2}"
                for e4 in range(4):
                    pt_, ptk = B.tps[e4 % 4]
                    ec = grp * 4 + e4
                    s.add("pe", lambda t, p=pt_, q2=q_, ec=ec:
                          t.matmul(p[:], q2[:, ec * 128:(ec + 1) * 128], B.cid16[:], start=True, stop=True),
                          reads=[qk_, "cid16"], writes=[ptk], tag="qT")
                    eng = "dve" if e4 % 2 == 0 else "act"
                    if eng == "dve":
                        s.add("dve", lambda v, p=pt_, o=t_, e4=e4: v.tensor_copy(o[:, e4 * 128:(e4 + 1) * 128], p[:]),
                              reads=[ptk], writes=[tk_], tag="qTe")
                    else:
                        s.add("act", lambda sc, p=pt_, o=t_, e4=e4: sc.copy(o[:, e4 * 128:(e4 + 1) * 128], p[:]),
                              reads=[ptk], writes=[tk_], tag="qTe")
                if wi == 0:
                    dst = wqfcT.rearrange("(ec hc) n -> ec hc n", hc=32)[grp * 4:(grp + 1) * 4, i, :]
                else:
                    dst = wpT.rearrange("(hc dc) n -> hc dc n", dc=8)[grp * 4:(grp + 1) * 4, i, :]
                s.dma("pool", lambda q, t=t_, d=dst.rearrange("e (p c) -> p e c", p=128):
                      q.dma_start(d, t[:].rearrange("p (e c) -> p e c", e=4)),
                      reads=[tk_], writes=[f"wdram{wi}"], tag="qst")
        s.barrier(f"endQ{wi}")
        close(gwi)
    s.barrier("endQ")
    close()

    ph = [ps(f"ph{i}", [128, 512]) for i in range(2)]
    ps1 = ps("ps1", [128, 512])
    ps2 = ps("ps2", [128, 512])
    pm = [ps(f"pm{i}", [128, 512]) for i in range(2)]
    hsb2 = [sb(f"hsb{i}", [128, 16384], BF16) for i in range(2)]
    dt1 = [sb(f"dt1_{i}", [128, 512], F32) for i in range(2)]

    lw = [sb(f"lw{i}", [128, 1024], BF16) for i in range(3)]
    lw2 = [sb(f"lw2_{i}", [128, 4096], BF16) for i in range(2)]
    sqs = [sb(f"hsq{i}", [128, 512], BF16) for i in range(2)]
    mu = sb("mu", [128, 512], F32)
    mu2 = sb("mu2", [128, 512], F32)
    var = sb("var", [128, 512], F32)
    rstd = sb("rstd", [128, 512], F32)
    mr = sb("mr", [128, 512], F32)
    tmv = [sb(f"tmv{i}", [128, 512], F32) for i in range(2)]
    lnwn = sb("lnwn", [32, 128], F32)
    lnbn = sb("lnbn", [32, 128], F32)
    lnw = sb("lnw", [128, 32], F32)
    lnb = sb("lnb", [128, 32], F32)
    s.dma("sp", lambda q: q.dma_start(lnwn[:], din["ln_w"].rearrange("(a b) -> a b", a=32)),
          writes=["lnwn"], tag="ldlnw")
    s.dma("sp", lambda q: q.dma_start(lnbn[:], din["ln_b"].rearrange("(a b) -> a b", a=32)),
          writes=["lnbn"], tag="ldlnb")
    s.add("pe", lambda t: t.matmul(ph[0][:, 0:32], lnwn[:], B.cid32[0:32, 0:32], start=True, stop=True),
          reads=["lnwn", "cid32"], writes=["ph0"], tag="lnwT")
    s.add("dve", lambda v: v.tensor_copy(lnw[:], ph[0][:, 0:32]), reads=["ph0"], writes=["lnwk"], tag="lnwTe")
    s.add("pe", lambda t: t.matmul(ph[0][:, 0:32], lnbn[:], B.cid32[0:32, 0:32], start=True, stop=True),
          reads=["lnbn", "cid32"], writes=["ph0"], tag="lnbT")
    s.add("dve", lambda v: v.tensor_copy(lnb[:], ph[0][:, 0:32]), reads=["ph0"], writes=["lnbk"], tag="lnbTe")

    for nt in range(4):
        hsb, hsbk = hsb2[nt % 2], f"hsb{nt % 2}"
        for hc in range(32):
            w_, wk_ = lw[hc % 3], f"lw{hc % 3}"
            src = wqfcT.rearrange("(ec hc) n -> ec hc n", hc=32)[:, hc, :]
            s.dma("sp", lambda q, t=w_, sa=src.rearrange("e (p c) -> p e c", p=128):
                  q.dma_start(t[:].rearrange("p (e c) -> p e c", e=8), sa),
                  reads=[wk_], writes=[wk_], tag="m1w")
            p_, pk_ = ph[hc % 2], f"ph{hc % 2}"
            for ec in range(8):
                s.add("pe", lambda t, p=p_, w=w_, ec=ec, nt=nt:
                      t.matmul(p[:], w[:, ec * 128:(ec + 1) * 128],
                               xn[:, ec * NT + nt * 512:ec * NT + nt * 512 + 512],
                               start=(ec == 0), stop=(ec == 7)),
                      reads=[wk_, "xn1"], writes=[pk_], tag="m1")
            s.add("act", lambda sc, p=p_, hc=hc, hsb=hsb: sc.activation(hsb[:, hc * 512:(hc + 1) * 512], p[:], AF.Gelu),
                  reads=[pk_], writes=[f"{hsbk}:{hc}"], tag="gelu")
            sq_, sqk_ = sqs[hc % 2], f"hsq{hc % 2}"
            s.add("act", lambda sc, o=sq_, hc=hc, hsb=hsb: sc.activation(o[:], hsb[:, hc * 512:(hc + 1) * 512], AF.Square),
                  reads=[f"{hsbk}:{hc}"], writes=[sqk_], tag="hsqr")
            s.add("pe", lambda t, hc=hc, hsb=hsb: t.matmul(ps1[:], B.ones128[:], hsb[:, hc * 512:(hc + 1) * 512],
                                                           start=(hc == 0), stop=(hc == 31)),
                  reads=[f"{hsbk}:{hc}", "ones128"], writes=["ps1"], tag="st1")
            s.add("pe", lambda t, q2=sq_, hc=hc: t.matmul(ps2[:], B.ones128[:], q2[:],
                                                          start=(hc == 0), stop=(hc == 31)),
                  reads=[sqk_, "ones128"], writes=["ps2"], tag="st2")
        s.add("act", lambda sc: sc.activation(mu[:], ps1[:], AF.Identity, scale=1.0 / 4096.0),
              reads=["ps1"], writes=["muk"], tag="lmu")
        s.add("act", lambda sc: sc.activation(mu2[:], mu[:], AF.Square),
              reads=["muk"], writes=["mu2k"], tag="lmu2")
        s.add("dve", lambda v: v.scalar_tensor_tensor(var[:], ps2[:], 1.0 / 4096.0, mu2[:],
                                                      op0=ALU.mult, op1=ALU.subtract),
              reads=["ps2", "mu2k"], writes=["vark"], tag="lvar")
        s.add("dve", lambda v: v.tensor_scalar(var[:], var[:], float(EPS), None, op0=ALU.add),
              reads=["vark"], writes=["vark"], tag="lveps")
        s.add("act", lambda sc: sc.activation(var[:], var[:], AF.Ln),
              reads=["vark"], writes=["vark"], tag="lln")
        s.add("act", lambda sc: sc.activation(rstd[:], var[:], AF.Exp, scale=-0.5),
              reads=["vark"], writes=["rstdk"], tag="lexp")
        s.add("dve", lambda v: v.tensor_tensor(mr[:], mu[:], rstd[:], op=ALU.mult),
              reads=["muk", "rstdk"], writes=["mrk"], tag="lmr")
        for hc in range(32):
            t_, tk_ = tmv[hc % 2], f"tmv{hc % 2}"
            s.add("dve", lambda v, o=t_, hc=hc, hsb=hsb: v.tensor_tensor(o[:], hsb[:, hc * 512:(hc + 1) * 512], rstd[:],
                                                                         op=ALU.mult),
                  reads=[f"{hsbk}:{hc}", "rstdk"], writes=[tk_], tag="ln1")
            s.add("pool", lambda g, o=t_: g.tensor_tensor(o[:], o[:], mr[:], op=ALU.subtract),
                  reads=[tk_, "mrk"], writes=[tk_], tag="ln2")
            s.add("act", lambda sc, t2=t_, hc=hc, hsb=hsb:
                  sc.activation(hsb[:, hc * 512:(hc + 1) * 512], t2[:], AF.Identity,
                                scale=lnw[:, hc:hc + 1], bias=lnb[:, hc:hc + 1]),
                  reads=[tk_], sreads=["lnwk", "lnbk"], writes=[f"{hsbk}:{hc}"], tag="ln3")
        for dc in range(8):
            w_, wk_ = lw2[dc % 2], f"lw2_{dc % 2}"
            src = wpT.rearrange("(hc dc) n -> hc dc n", dc=8)[:, dc, :]
            s.dma("sp", lambda q, t=w_, sa=src.rearrange("h (p c) -> p h c", p=128):
                  q.dma_start(t[:].rearrange("p (h c) -> p h c", h=32), sa),
                  reads=[wk_], writes=[wk_], tag="m2w")
            p_, pk_ = pm[dc % 2], f"pm{dc % 2}"
            for hc in range(32):
                s.add("pe", lambda t, p=p_, w=w_, hc=hc, hsb=hsb:
                      t.matmul(p[:], w[:, hc * 128:(hc + 1) * 128], hsb[:, hc * 512:(hc + 1) * 512],
                               start=(hc == 0), stop=(hc == 31)),
                      reads=[wk_, f"{hsbk}:{hc}"], writes=[pk_], tag="m2")
            d_, dk_ = dt1[dc % 2], f"dt1_{dc % 2}"
            s.add("dve", lambda v, p=p_, o=d_: v.tensor_copy(o[:], p[:]),
                  reads=[pk_], writes=[dk_], tag="m2e")
            s.dma("sp", lambda q, t=d_, dc=dc, nt=nt:
                  q.dma_start(d1dram.rearrange("(dc p) t -> dc p t", p=128)[dc, :, nt * 512:(nt + 1) * 512], t[:]),
                  reads=[dk_], writes=["d1dram"], tag="m2st")
    s.barrier("endM")
    close()

    B.tps = [(ps(f"tpE{i}", [128, 128]), f"tpE{i}") for i in range(4)]
    dl_rot1 = [sb(f"dl1_{i}", [128, 1024], F32) for i in range(2)]
    for tt in range(16):
        dlb, dlk = dl_rot1[tt % 2], f"dl1_{tt % 2}"
        s.dma("sp", lambda q, t=dlb, tt=tt:
              q.dma_start(t[:].rearrange("p (dc c) -> p dc c", dc=8),
                          d1dram.rearrange("(dc p) t -> p dc t", p=128)[:, :, tt * 128:(tt + 1) * 128]),
              reads=[dlk, "d1dram"], writes=[dlk], tag="dlld1")
        xt, xtk = B.xt_rot[B.xt_i % len(B.xt_rot)]
        B.xt_i += 1
        s.dma("sp", lambda q, t=xt, a=din["xprev"], tt=tt: q.dma_start(t[:], a[tt * 128:(tt + 1) * 128, :]),
              reads=[xtk], writes=[xtk], tag="ldxe1")
        ob, obk = B.out_rot[B.out_i % len(B.out_rot)]
        B.out_i += 1
        for dc in range(DC):
            pt_, ptk = B.tps[dc % 4]
            s.add("pe", lambda t, p=pt_, d=dlb, dc=dc:
                  t.matmul(p[:], d[:, dc * 128:(dc + 1) * 128], B.cid32[:], start=True, stop=True),
                  reads=[dlk, "cid32"], writes=[ptk], tag="dT1")
            s.add("dve", lambda v, p=pt_, o=ob, x=xt, dc=dc:
                  v.tensor_tensor(o[:, dc * 128:(dc + 1) * 128], p[:], x[:, dc * 128:(dc + 1) * 128], op=ALU.add),
                  reads=[ptk, xtk], writes=[obk], tag="dadd1")
        s.dma("sp", lambda q, t=ob, tt=tt: q.dma_start(xout[tt * 128:(tt + 1) * 128, :], t[:]),
              reads=[obk], writes=[f"outx1_{tt}"], tag="stx1")
    s.add("sp", lambda q: q.nop(), reads=[f"outx1_{t}" for t in range(16)], tag="fin")
    close()
    close(gblk)
    return din


_B1_CACHE = {}


def build_b1_program():
    if "nc" not in _B1_CACHE:
        nc, s, B = _mk_sched()
        xin = nc.dram_tensor("xprev", [NT, D], F32, kind="ExternalInput").ap()
        xout = nc.dram_tensor("xb1", [NT, D], F32, kind="ExternalOutput").ap()
        build_l0_ffn(nc, s, B, xin, xout)
        _B1_CACHE["nc"] = _finish(nc, s)
        _B1_CACHE["stats"] = s.stats()
    return _B1_CACHE["nc"]


def run_block1(inputs, x_cur):
    C = _host_consts()
    nc = build_b1_program()
    shared = {
        "w_fc": np.asarray(inputs["L0_ffn_W_fc"], np.float32),
        "ln_w": np.asarray(inputs["L0_ffn_ln_w"], np.float32),
        "ln_b": np.asarray(inputs["L0_ffn_ln_b"], np.float32),
        "w_proj": np.asarray(inputs["L0_ffn_W_proj"], np.float32),
        "c_cid16": C["cid16"], "c_cid32": C["cid32"], "c_ones128": C["ones128"], "c_onesrow": C["onesrow"],
        "c_selz": C["selz"],
    }
    in_maps = []
    for c in range(8):
        m = dict(shared)
        m["xprev"] = np.ascontiguousarray(x_cur[c])
        in_maps.append(m)
    res = run_bass_kernel_spmd(nc, in_maps, core_ids=list(range(8)))
    return [r["xb1"] for r in res.results]



def build_l1_attn(nc, s, B, xprev, xout):
    din = {"xprev": xprev}
    for nm, shp, dt in [("wq1", [D, D], F32), ("wk1", [D, D], F32),
                        ("wv1", [D, D], F32), ("wo1", [D, D], F32),
                        ("betas8", [8, 64], F32), ("m08", [8, 64, 64], F32), ("z08", [8, 64], F32),
                        ("c_masks", [128, 2048], BF16)]:
        din[nm] = nc.dram_tensor(nm, shp, dt, kind="ExternalInput").ap()
    ddram = nc.dram_tensor("d2dram", [D, NT], F32, kind="Internal").ap()

    gblk, gph = [], []

    def sb(name, shape, dtype, scope=None):
        g = nc.sbuf_tensor(name, shape, dtype)
        (gph if scope is None else scope).append(g)
        return g.__enter__()

    def ps(name, shape, dtype=F32):
        g = nc.psum_tensor(name, shape, dtype)
        gph.append(g)
        return g.__enter__()

    def close(lst=None):
        lst = gph if lst is None else lst
        for g in reversed(lst):
            g.__exit__(None, None, None)
        lst.clear()

    xn = sb("xn2", [128, DC * NT], BF16, scope=gblk)

    B.tps = [(ps(f"tpL{i}", [128, 128]), f"tpL{i}") for i in range(4)]
    emit_load_ln_fm(s, B, din["xprev"], xn, "xn2", lambda dc, tt: dc * NT + tt * 128)
    s.barrier("endA2")
    close()

    pg = [ps(f"pg{i}", [128, 512]) for i in range(2)]
    tp2 = [(ps(f"tpT{i}", [128, 128]), f"tpT{i}") for i in range(2)]
    pO = ps("pO", [65, 512])
    pAm = ps("pAm", [65, 512])
    pD = ps("pD", [64, 128])
    pb = ps("pb", [64, 512])

    wo_sb = sb("wo1_sb", [128, 8192], BF16)
    wsl = {nm: sb(f"wsl_{nm}", [128, 8192], BF16) for nm in ("q", "k", "v")}
    for nm, wdram in (("q", "wq1"), ("k", "wk1"), ("v", "wv1")):
        s.dma("pool", lambda q, t=wsl[nm], a=din[wdram].rearrange("(dc p) c -> p dc c", p=128):
              q.dma_start(t[:].rearrange("p (dc c) -> p dc c", dc=8), a),
              writes=[f"wsl{nm}"], tag="ldwsl")
    qT = sb("qT", [64, 4096], F32)
    qTb = sb("qTb", [64, 4096], BF16)
    kTb = sb("kTb", [64, 4096], BF16)
    vT = sb("vT", [64, 4096], BF16)
    k_h = [sb(f"k_h{i}", [128, 256], BF16) for i in range(2)]
    v_h = [sb(f"v_h{i}", [128, 4 * 65], BF16) for i in range(2)]
    skt = [sb(f"skt{i}", [128, 256], F32) for i in range(2)]
    sk = [sb(f"sk{i}", [128, 256], BF16) for i in range(2)]
    sqt = [sb(f"sqt{i}", [64, 512], F32) for i in range(2)]
    sqf = [sb(f"sqf{i}", [64, 512], F32) for i in range(2)]
    eb = [sb(f"ebL{i}", [128, 512], BF16) for i in range(2)]
    ustA = [sb(f"ustA{i}", [64, 512], BF16) for i in range(2)]
    ustB = [sb(f"ustB{i}", [64, 512], BF16) for i in range(2)]
    zb1 = [sb(f"zb1d{i}", [66, 512], F32) for i in range(2)]
    sgnb = [sb(f"sgnbd{i}", [66, 512], BF16) for i in range(2)]
    izb1 = [sb(f"izb1d{i}", [66, 512], BF16) for i in range(2)]
    ct1 = [sb(f"ct1L{i}", [64, 512], BF16) for i in range(2)]
    ct2 = [sb(f"ct2L{i}", [64, 512], BF16) for i in range(2)]
    attT = [sb(f"attT{i}", [64, 512], BF16) for i in range(2)]
    preT = sb("preT", [128, 2048], BF16)
    dtile = [sb("dtile_s", [128, 256], F32)] * 2
    masks = sb("masks", [128, 2048], BF16)
    MZ = sb("MZ", [64, 8 * 65], F32)
    betn = sb("betn", [8, 64], F32)
    z0n = sb("z0n", [8, 64], F32)
    gat = sb("gat", [64, 8], F32)
    ogat = sb("ogat", [64, 8], F32)

    s.dma("pool", lambda q: q.dma_start(wo_sb[:].rearrange("p (cc d) -> p cc d", cc=8),
                                        din["wo1"].rearrange("(cc p) d -> p cc d", p=128)),
          writes=["wo_sb"], tag="ldwo1")
    s.dma("sp", lambda q: q.dma_start(masks[:], din["c_masks"]), writes=["masks"], tag="ldmask")
    s.dma("sp", lambda q: q.dma_start(betn[:], din["betas8"]), writes=["betn"], tag="ldbet")
    s.dma("sp", lambda q: q.dma_start(z0n[:], din["z08"]), writes=["z0n"], tag="ldz0")
    s.dma("sp", lambda q: q.dma_start(MZ[:].rearrange("d (h c) -> d h c", h=8)[:, :, 0:64],
                                      din["m08"].rearrange("h d v -> d h v")),
          writes=["MZ"], tag="ldm0")
    s.add("pe", lambda t: t.matmul(pg[0][0:64, 0:8], z0n[:], B.cid32[0:8, 0:8], start=True, stop=True),
          reads=["z0n", "cid32"], writes=["pg0"], tag="z0T")
    s.add("dve", lambda v: v.tensor_copy(MZ[:].rearrange("d (h c) -> d h c", h=8)[:, :, 64:65],
                                         pg[0][0:64, 0:8].rearrange("d (h o) -> d h o", o=1)),
          reads=["pg0"], writes=["MZ"], tag="z0Te")
    s.add("pe", lambda t: t.matmul(pg[0][0:64, 64:72], betn[:], B.cid32[0:8, 0:8], start=True, stop=True),
          reads=["betn", "cid32"], writes=["pg0"], tag="betT")
    s.add("act", lambda sc: sc.activation(gat[:], pg[0][0:64, 64:72], AF.Exp, scale=-1.0),
          reads=["pg0"], writes=["gat"], tag="gexp")
    s.add("dve", lambda v: v.tensor_scalar(gat[:], gat[:], 1.0, None, op0=ALU.add),
          reads=["gat"], writes=["gat"], tag="gadd")
    s.add("dve", lambda v: v.reciprocal(gat[:], gat[:]), reads=["gat"], writes=["gat"], tag="ginv")
    s.add("dve", lambda v: v.tensor_scalar(ogat[:], gat[:], -1.0, 1.0, op0=ALU.mult, op1=ALU.add),
          reads=["gat"], writes=["ogat"], tag="gneg")
    for i in range(2):
        s.add("dve", lambda v, i=i: v.memset(zb1[i][:], 1.0), writes=[f"zb1k{i}"], tag="zb1i")
    for i in range(2):
        s.add("dve", lambda v, i=i: v.memset(v_h[i][:].rearrange("p (l c) -> p l c", l=4)[:, :, 64:65], 1.0),
              writes=[f"v_h{i}"], tag="vhones")

    for seg in range(NSEG):
        for jp in range(8):
            for tn, (dst, dstk, eng) in {
                    "q": (qT, "qTk", "dve"),
                    "k": (kTb, "kTbk", "act"),
                    "v": (vT, "vTk", "dve")}.items():
                w_, wk_ = wsl[tn], f"wsl{tn}"
                p_, pk_ = pg[jp % 2], f"pg{jp % 2}"
                for dc in range(DC):
                    s.add("pe", lambda t, p=p_, w=w_, dc=dc, seg=seg, jp=jp:
                          t.matmul(p[:, 0:256], w[:, dc * 1024 + jp * 128: dc * 1024 + (jp + 1) * 128],
                                   xn[:, dc * NT + seg * 256: dc * NT + (seg + 1) * 256],
                                   start=(dc == 0), stop=(dc == 7)),
                          reads=[wk_, "xn2"], writes=[pk_], tag="jmm")
                for jj in range(2):
                    j = 2 * jp + jj

                    def evac(e, p=p_, jj=jj, j=j, dst=dst, eng=eng):
                        src = p[64 * jj:64 * jj + 64, 0:256].rearrange("d (h r) -> d h r", h=8)
                        dd = dst[:].rearrange("d (h l) -> d h l", h=8)[:, :, 32 * j: 32 * j + 32]
                        return e.tensor_copy(dd, src) if eng == "dve" else e.copy(dd, src)
                    s.add(eng, evac, reads=[pk_], writes=[dstk], tag=f"jev{tn}")
        s.add("dve", lambda v: v.tensor_copy(qTb[:], qT[:]), reads=["qTk"], writes=["qTbk"], tag="qTb")
        pending_tail = [None]
        for hp in range(8):
            hb = hp % 2
            kh, khk = k_h[hb], f"k_h{hb}"
            vh, vhk = v_h[hb], f"v_h{hb}"
            skb, skk = sk[hb], f"sk{hb}"
            sqb, sqk = sqf[hb], f"sqf{hb}"
            uA_, uAk = ustA[hb], f"ustA{hb}"
            uB_, uBk = ustB[hb], f"ustB{hb}"
            zb_, zbk = zb1[hb], f"zb1k{hb}"
            sg_, sgk = sgnb[hb], f"sgnbk{hb}"
            iz_, izk = izb1[hb], f"izb1k{hb}"
            c1_, c1k = ct1[hb], f"ct1L{hb}"
            c2_, c2k = ct2[hb], f"ct2L{hb}"
            aT_, aTk = attT[hb], f"attT{hb}"
            sk0, sk0k = skt[0], "skt0"
            sk1, sk1k = skt[1], "skt1"
            sq0, sq0k = sqt[0], "sqt0"
            sq1, sq1k = sqt[1], "sqt1"
            for lt in range(4):
                pt_, ptk = tp2[lt % 2]
                s.add("pe", lambda t, p=pt_, hp=hp, lt=lt:
                      t.matmul(p[:, 0:64], kTb[0:64, hp * 512 + lt * 128: hp * 512 + (lt + 1) * 128],
                               B.cid16[0:64, 0:64], start=True, stop=True),
                      reads=["kTbk", "cid16"], writes=[ptk], tag="kT")
                s.add("dve", lambda e, p=pt_, lt=lt, kh=kh: e.tensor_copy(kh[:, lt * 64:(lt + 1) * 64], p[:, 0:64]),
                      reads=[ptk], writes=[khk], tag="kTe")
            for lt in range(4):
                pt_, ptk = tp2[lt % 2]
                s.add("pe", lambda t, p=pt_, hp=hp, lt=lt:
                      t.matmul(p[:, 64:128], vT[0:64, hp * 512 + lt * 128: hp * 512 + (lt + 1) * 128],
                               B.cid16[0:64, 0:64], start=True, stop=True),
                      reads=["vTk", "cid16"], writes=[ptk], tag="vT")
                s.add("act", lambda e, p=pt_, lt=lt, vh=vh: e.copy(vh[:, lt * 65: lt * 65 + 64], p[:, 64:128]),
                      reads=[ptk], writes=[vhk], tag="vTe")
            s.add("dve", lambda v, o=sk0, kh=kh: v.tensor_scalar(o[:], kh[:], 0.0, None, op0=ALU.min),
                  reads=[khk], writes=[sk0k], tag="sk1")
            s.add("act", lambda sc, o=sk1, i=sk0: sc.activation(o[:], i[:], AF.Exp),
                  reads=[sk0k], writes=[sk1k], tag="sk2")
            s.add("dve", lambda g, o=sk0, kh=kh: g.tensor_relu(o[:], kh[:]),
                  reads=[khk], writes=[sk0k], tag="sk3")
            s.add("dve", lambda v, o=skb, a=sk1, b=sk0: v.tensor_tensor(o[:], a[:], b[:], op=ALU.add),
                  reads=[sk0k, sk1k], writes=[skk], tag="sk4")
            s.add("dve", lambda v, o=sq0, hp=hp: v.tensor_scalar(o[:], qT[:, hp * 512:(hp + 1) * 512], 0.0,
                                                                 None, op0=ALU.min),
                  reads=["qTk"], writes=[sq0k], tag="sq1")
            s.add("act", lambda sc, o=sq1, i=sq0: sc.activation(o[:], i[:], AF.Exp),
                  reads=[sq0k], writes=[sq1k], tag="sq2")
            s.add("dve", lambda g, o=sq0, hp=hp: g.tensor_relu(o[:], qT[:, hp * 512:(hp + 1) * 512]),
                  reads=["qTk"], writes=[sq0k], tag="sq3")
            s.add("dve", lambda v, o=sqb, a=sq1, b=sq0: v.tensor_tensor(o[:], a[:], b[:], op=ALU.add),
                  reads=[sq0k, sq1k], writes=[sqk], tag="sq4")

            def emit_sL(kt, hp=hp):
                p_, pk_ = pg[kt % 2], f"pg{kt % 2}"
                s.add("pe", lambda t, p=p_, hp=hp, kt=kt:
                      t.matmul(p[:], kTb[0:64, hp * 512 + kt * 128: hp * 512 + (kt + 1) * 128],
                               qTb[0:64, hp * 512:(hp + 1) * 512], start=True, stop=True),
                      reads=["kTbk", "qTbk"], writes=[pk_], tag="sL")

            def emit_pvL(kt, vh=vh, vhk=vhk):
                p_, pk_ = pg[kt % 2], f"pg{kt % 2}"
                e_, ek_ = eb[kt % 2], f"ebL{kt % 2}"
                s.add("act", lambda sc, o=e_, p=p_: sc.activation(o[:], p[:], AF.Exp, scale=0.125),
                      reads=[pk_], writes=[ek_], tag="eL")
                s.add("dve", lambda g, e2=e_, kt=kt:
                      g.tensor_tensor(e2[:], e2[:], masks[:, kt * 512:(kt + 1) * 512], op=ALU.mult),
                      reads=[ek_, "masks"], writes=[ek_], tag="mL")
                s.add("pe", lambda t, m2=e_, kt=kt, vh=vh:
                      t.matmul(pO[:], vh[:, kt * 65:(kt + 1) * 65], m2[:],
                               start=(kt == 0), stop=(kt == 3)),
                      reads=[vhk, ek_], writes=["pO"], tag="pvL")
            if pending_tail[0] is not None:
                pending_tail[0]()
                pending_tail[0] = None
            emit_sL(0)
            emit_sL(1)
            emit_pvL(0)
            emit_sL(2)
            emit_pvL(1)
            emit_sL(3)
            emit_pvL(2)
            emit_pvL(3)
            s.add("pe", lambda t, hp=hp, sqb=sqb: t.matmul(pAm[:], MZ[:, hp * 65:(hp + 1) * 65], sqb[:],
                                                           start=True, stop=True),
                  reads=["MZ", sqk], writes=["pAm"], tag="amm")
            for lt in range(4):
                s.add("pe", lambda t, lt=lt, skb=skb, vh=vh: t.matmul(pD[:, 0:65], skb[:, lt * 64:(lt + 1) * 64],
                                                                     vh[:, lt * 65:(lt + 1) * 65],
                                                                     start=(lt == 0), stop=(lt == 3)),
                      reads=[skk, vhk], writes=["pD"], tag="dmz")
            s.add("act", lambda sc, o=uA_: sc.copy(o[:], pO[0:64, :]), reads=["pO"], writes=[uAk], tag="uA")
            s.add("dve", lambda v, o=zb_: v.tensor_copy(o[0:1, :], pO[64:65, :]), reads=["pO"], writes=[zbk], tag="z0r")
            s.add("dve", lambda v, o=uB_: v.tensor_copy(o[:], pAm[0:64, :]), reads=["pAm"], writes=[uBk], tag="uB")
            s.add("act", lambda sc, o=zb_: sc.copy(o[64:65, :], pAm[64:65, :]), reads=["pAm"], writes=[zbk], tag="z64r")
            s.add("dve", lambda v, hp=hp: v.tensor_tensor(MZ[:, hp * 65:(hp + 1) * 65],
                                                          MZ[:, hp * 65:(hp + 1) * 65], pD[:, 0:65], op=ALU.add),
                  reads=["MZ", "pD"], writes=["MZ"], tag="mzup")
            def mk_tail(hp=hp, hb=hb, zb_=zb_, zbk=zbk, sg_=sg_, sgk=sgk, iz_=iz_, izk=izk,
                        uA_=uA_, uAk=uAk, uB_=uB_, uBk=uBk, c1_=c1_, c1k=c1k, c2_=c2_, c2k=c2k,
                        aT_=aT_, aTk=aTk):
                def emit_tail():
                    s.add("act", lambda sc, o=sg_, i=zb_: sc.activation(o[:], i[:], AF.Sign),
                          reads=[zbk], writes=[sgk], tag="sgnL")
                    s.add("dve", lambda v, o=zb_, b=sg_: v.tensor_tensor(o[:], o[:], b[:], op=ALU.mult),
                          reads=[zbk, sgk], writes=[zbk], tag="absL")
                    s.add("act", lambda sc, o=zb_: sc.activation(o[:], o[:], AF.Ln), reads=[zbk], writes=[zbk], tag="lnzL")
                    s.add("act", lambda sc, o=zb_: sc.activation(o[:], o[:], AF.Exp, scale=-1.0),
                          reads=[zbk], writes=[zbk], tag="expzL")
                    s.add("dve", lambda v, o=iz_, a=zb_, b=sg_: v.tensor_tensor(o[:], a[:], b[:], op=ALU.mult),
                          reads=[zbk, sgk], writes=[izk], tag="izbL")
                    s.add("pe", lambda t, i=iz_: t.matmul(pb[:], B.selz[0:66, 0:64], i[:], start=True, stop=True),
                          reads=[izk, "selz"], writes=["pb"], tag="bzL")
                    s.add("dve", lambda v, o=c1_, a=uA_: v.tensor_tensor(o[:], a[:], pb[:], op=ALU.mult),
                          reads=[uAk, "pb"], writes=[c1k], tag="cL1")
                    s.add("pe", lambda t, i=iz_: t.matmul(pb[:], B.selz[0:66, 64:128], i[:], start=True, stop=True),
                          reads=[izk, "selz"], writes=["pb"], tag="bdL")
                    s.add("dve", lambda v, o=c2_, a=uB_: v.tensor_tensor(o[:], a[:], pb[:], op=ALU.mult),
                          reads=[uBk, "pb"], writes=[c2k], tag="cL2")
                    s.add("dve", lambda v, o=c1_, hp=hp: v.tensor_scalar(o[:], o[:], ogat[:, hp:hp + 1], None, op0=ALU.mult),
                          reads=[c1k], sreads=["ogat"], writes=[c1k], tag="cL3")
                    s.add("dve", lambda v, o=aT_, a=c2_, b=c1_, hp=hp:
                          v.scalar_tensor_tensor(o[:], a[:], gat[:, hp:hp + 1], b[:], op0=ALU.mult, op1=ALU.add),
                          reads=[c2k, c1k], sreads=["gat"], writes=[aTk], tag="cL4")
                    s.add("dve", lambda v, hp=hp, a=aT_: v.tensor_copy(
                        preT[0:64, :].rearrange("d (cc t) -> d cc t", cc=8)[:, :, 32 * hp: 32 * hp + 32],
                        a[:].rearrange("d (m c) -> d m c", m=8)[:, :, 0:32]),
                        reads=[aTk], writes=["preT"], tag="scat0")
                    s.add("dve", lambda v, hp=hp, a=aT_: v.tensor_copy(
                        preT[64:128, :].rearrange("d (cc t) -> d cc t", cc=8)[:, :, 32 * hp: 32 * hp + 32],
                        a[:].rearrange("d (m c) -> d m c", m=8)[:, :, 32:64]),
                        reads=[aTk], writes=["preT"], tag="scat1")
                return emit_tail
            pending_tail[0] = mk_tail()
        if pending_tail[0] is not None:
            pending_tail[0]()
            pending_tail[0] = None
        for dc in range(DC):
            p_, pk_ = pg[dc % 2], f"pg{dc % 2}"
            for cc in range(8):
                s.add("pe", lambda t, p=p_, cc=cc, dc=dc:
                      t.matmul(p[:, 0:256], wo_sb[:, cc * 1024 + dc * 128: cc * 1024 + (dc + 1) * 128],
                               preT[:, cc * 256:(cc + 1) * 256], start=(cc == 0), stop=(cc == 7)),
                      reads=["wo_sb", "preT"], writes=[pk_], tag="woL")
            d_, dk_ = dtile[0], "dtilek"
            s.add("dve", lambda v, p=p_, o=d_: v.tensor_copy(o[:], p[:, 0:256]),
                  reads=[pk_], writes=[dk_], tag="woLe")
            s.dma("sp", lambda q, t=d_, dc=dc, seg=seg:
                  q.dma_start(ddram.rearrange("(dc p) t -> dc p t", p=128)[dc, :, seg * 256:(seg + 1) * 256], t[:]),
                  reads=[dk_], writes=["ddram"], tag="woSt")
    s.barrier("endL1")
    close()

    B.tps = [(ps(f"tpE2{i}", [128, 128]), f"tpE2{i}") for i in range(4)]
    dl_rot = [sb(f"dl{i}", [128, 1024], F32) for i in range(2)]
    for tt in range(16):
        dlb, dlk = dl_rot[tt % 2], f"dl{tt % 2}"
        s.dma("sp", lambda q, t=dlb, tt=tt:
              q.dma_start(t[:].rearrange("p (dc c) -> p dc c", dc=8),
                          ddram.rearrange("(dc p) t -> p dc t", p=128)[:, :, tt * 128:(tt + 1) * 128]),
              reads=[dlk, "ddram"], writes=[dlk], tag="dlld")
        xt, xtk = B.xt_rot[B.xt_i % len(B.xt_rot)]
        B.xt_i += 1
        s.dma("sp", lambda q, t=xt, a=din["xprev"], tt=tt: q.dma_start(t[:], a[tt * 128:(tt + 1) * 128, :]),
              reads=[xtk], writes=[xtk], tag="ldxe2")
        ob, obk = B.out_rot[B.out_i % len(B.out_rot)]
        B.out_i += 1
        for dc in range(DC):
            pt_, ptk = B.tps[dc % 4]
            s.add("pe", lambda t, p=pt_, d=dlb, dc=dc:
                  t.matmul(p[:], d[:, dc * 128:(dc + 1) * 128], B.cid32[:], start=True, stop=True),
                  reads=[dlk, "cid32"], writes=[ptk], tag="dT2")
            s.add("dve", lambda v, p=pt_, o=ob, x=xt, dc=dc:
                  v.tensor_tensor(o[:, dc * 128:(dc + 1) * 128], p[:], x[:, dc * 128:(dc + 1) * 128], op=ALU.add),
                  reads=[ptk, xtk], writes=[obk], tag="dadd2")
        s.dma("sp", lambda q, t=ob, tt=tt: q.dma_start(xout[tt * 128:(tt + 1) * 128, :], t[:]),
              reads=[obk], writes=[f"outx2_{tt}"], tag="stx2")
    s.add("sp", lambda q: q.nop(), reads=[f"outx2_{t}" for t in range(16)], tag="fin")
    close()
    close(gblk)
    return din


_B2_CACHE = {}


def build_b2_program():
    if "nc" not in _B2_CACHE:
        nc, s, B = _mk_sched()
        xin = nc.dram_tensor("xprev", [NT, D], F32, kind="ExternalInput").ap()
        xout = nc.dram_tensor("xb2", [NT, D], F32, kind="ExternalOutput").ap()
        build_l1_attn(nc, s, B, xin, xout)
        _B2_CACHE["nc"] = _finish(nc, s)
        _B2_CACHE["stats"] = s.stats()
    return _B2_CACHE["nc"]


def run_block2(inputs, x_cur):
    C = _host_consts()
    nc = build_b2_program()
    betas = np.asarray(inputs["L1_attn_betas"], np.float32)[0, :, 0, :]
    m0 = np.asarray(inputs["L1_attn_memory0"], np.float32)[0]
    z0 = np.asarray(inputs["L1_attn_z0"], np.float32)[0, :, 0, :]
    shared = {
        "wq1": np.asarray(inputs["L1_attn_Wq"], np.float32),
        "wk1": np.asarray(inputs["L1_attn_Wk"], np.float32),
        "wv1": np.asarray(inputs["L1_attn_Wv"], np.float32),
        "wo1": np.asarray(inputs["L1_attn_Wo"], np.float32),
        "c_masks": C["masks"],
        "c_cid16": C["cid16"], "c_cid32": C["cid32"], "c_ones128": C["ones128"], "c_onesrow": C["onesrow"],
        "c_selz": C["selz"],
    }
    in_maps = []
    for c in range(8):
        half = c % 2
        m = dict(shared)
        m["xprev"] = np.ascontiguousarray(x_cur[c])
        m["betas8"] = np.ascontiguousarray(betas[half * 8:(half + 1) * 8])
        m["m08"] = np.ascontiguousarray(m0[half * 8:(half + 1) * 8])
        m["z08"] = np.ascontiguousarray(z0[half * 8:(half + 1) * 8])
        in_maps.append(m)
    res = run_bass_kernel_spmd(nc, in_maps, core_ids=list(range(8)))
    return [r["xb2"] for r in res.results]



def build_l1_ffn(nc, s, B, xprev, xout):
    din = {"xprev": xprev}
    for nm, shp, dt in [("wg", [D, 8], F32),
                        ("wu", [8, D, 2048], F32), ("wd", [8, 2048, D], F32),
                        ("w1", [D, 2048], F32), ("w2", [2048, D], F32),
                        ("c_iota8", [128, 8], F32)]:
        din[nm] = nc.dram_tensor(nm, shp, dt, kind="ExternalInput").ap()

    gblk, gph = [], []

    def sb(name, shape, dtype, scope=None):
        g = nc.sbuf_tensor(name, shape, dtype)
        (gph if scope is None else scope).append(g)
        return g.__enter__()

    def ps(name, shape, dtype=F32):
        g = nc.psum_tensor(name, shape, dtype)
        gph.append(g)
        return g.__enter__()

    def close(lst=None):
        lst = gph if lst is None else lst
        for g in reversed(lst):
            g.__exit__(None, None, None)
        lst.clear()

    xn = sb("xn3", [128, DC * NT], BF16, scope=gblk)
    macc = sb("macc", [128, 16 * 1024], F32, scope=gblk)
    cw = sb("cw", [128, 128], F32, scope=gblk)

    B.tps = [(ps(f"tpU{i}", [128, 128]), f"tpU{i}") for i in range(4)]
    emit_load_ln_fm(s, B, din["xprev"], xn, "xn3", lambda dc, tt: dc * NT + tt * 128)
    s.barrier("endA3")
    close()

    ph = [ps(f"phU{i}", [128, 512]) for i in range(2)]
    wgsb = sb("wgsb", [128, 64], BF16)
    iot = sb("iot", [128, 8], F32)
    lg = sb("lg", [128, 8], F32)
    oh1 = sb("oh1", [128, 8], F32)
    oh2 = sb("oh2", [128, 8], F32)
    t8 = sb("t8", [128, 8], F32)
    u8 = sb("u8", [128, 8], F32)
    m1 = sb("m1g", [128, 1], F32)
    m2 = sb("m2g", [128, 1], F32)
    i1 = sb("i1g", [128, 1], F32)
    v1n = sb("v1ng", [128, 1], F32)
    v2n = sb("v2ng", [128, 1], F32)
    s.dma("pool", lambda q: q.dma_start(wgsb[:].rearrange("p (dc c) -> p dc c", dc=8),
                                        din["wg"].rearrange("(dc p) c -> p dc c", p=128)),
          writes=["wgsb"], tag="ldwg")
    s.dma("sp", lambda q: q.dma_start(iot[:], din["c_iota8"]), writes=["iot"], tag="ldiota")

    def first_hot(ohk, oh_t, lsrc, lsrck, tag):
        s.add("dve", lambda v: v.tensor_tensor(t8[:], iot[:], oh_t[:], op=ALU.mult),
              reads=["iot", ohk], writes=["t8"], tag=tag + "a")
        s.add("dve", lambda v: v.tensor_scalar(u8[:], oh_t[:], -9.0, 9.0, op0=ALU.mult, op1=ALU.add),
              reads=[ohk], writes=["u8"], tag=tag + "b")
        s.add("dve", lambda v: v.tensor_tensor(t8[:], t8[:], u8[:], op=ALU.add),
              reads=["t8", "u8"], writes=["t8"], tag=tag + "c")
        s.add("dve", lambda v: v.tensor_reduce(i1[:], t8[:], axis=AX.X, op=ALU.min),
              reads=["t8"], writes=["i1g"], tag=tag + "d")
        s.add("dve", lambda v: v.tensor_scalar(oh_t[:], iot[:], i1[:], None, op0=ALU.is_equal),
              reads=["iot"], sreads=["i1g"], writes=[ohk], tag=tag + "e")

    for tt in range(16):
        p_, pk_ = ph[tt % 2], f"phU{tt % 2}"
        for dc in range(DC):
            s.add("pe", lambda t, p=p_, dc=dc, tt=tt:
                  t.matmul(p[:, 0:8], xn[:, dc * NT + tt * 128:dc * NT + (tt + 1) * 128],
                           wgsb[:, dc * 8:(dc + 1) * 8], start=(dc == 0), stop=(dc == 7)),
                  reads=["wgsb", "xn3"], writes=[pk_], tag="gmm")
        s.add("dve", lambda v, p=p_: v.tensor_copy(lg[:], p[:, 0:8]), reads=[pk_], writes=["lg"], tag="gev")
        s.add("dve", lambda v: v.reduce_max(m1[:], lg[:], axis=AX.X), reads=["lg"], writes=["m1g"], tag="gm1")
        s.add("dve", lambda v: v.tensor_scalar(oh1[:], lg[:], m1[:], None, op0=ALU.is_ge),
              reads=["lg"], sreads=["m1g"], writes=["oh1k"], tag="goh1")
        first_hot("oh1k", oh1, lg, "lg", f"gf1_{tt}")
        s.add("dve", lambda v: v.scalar_tensor_tensor(t8[:], oh1[:], -30000.0, lg[:], op0=ALU.mult, op1=ALU.add),
              reads=["oh1k", "lg"], writes=["t8"], tag="gmask")
        s.add("dve", lambda v: v.reduce_max(m2[:], t8[:], axis=AX.X), reads=["t8"], writes=["m2g"], tag="gm2")
        s.add("dve", lambda v: v.tensor_scalar(oh2[:], t8[:], m2[:], None, op0=ALU.is_ge),
              reads=["t8"], sreads=["m2g"], writes=["oh2k"], tag="goh2")
        first_hot("oh2k", oh2, t8, "t8", f"gf2_{tt}")
        s.add("dve", lambda v: v.tensor_tensor(m2[:], m2[:], m1[:], op=ALU.subtract),
              reads=["m2g", "m1g"], writes=["m2g"], tag="gd")
        s.add("act", lambda sc: sc.activation(m2[:], m2[:], AF.Exp), reads=["m2g"], writes=["m2g"], tag="gex")
        s.add("dve", lambda v: v.tensor_scalar(m2[:], m2[:], 1.0, None, op0=ALU.add),
              reads=["m2g"], writes=["m2g"], tag="g1p")
        s.add("dve", lambda v: v.reciprocal(v1n[:], m2[:]), reads=["m2g"], writes=["v1nk"], tag="ginv")
        s.add("dve", lambda v: v.tensor_scalar(v2n[:], v1n[:], -1.0, 1.0, op0=ALU.mult, op1=ALU.add),
              reads=["v1nk"], writes=["v2nk"], tag="gv2")
        s.add("dve", lambda v: v.tensor_scalar(oh1[:], oh1[:], v1n[:], None, op0=ALU.mult),
              reads=["oh1k"], sreads=["v1nk"], writes=["oh1k"], tag="gw1")
        s.add("dve", lambda v, tt=tt: v.scalar_tensor_tensor(cw[:, tt * 8:(tt + 1) * 8], oh2[:], v2n[:],
                                                             oh1[:], op0=ALU.mult, op1=ALU.add),
              reads=["oh2k", "oh1k"], sreads=["v2nk"], writes=["cw"], tag="gcw")

    he = sb("he", [128, 16 * 1024], BF16)
    lwu = [sb(f"lwu{i}", [128, 8 * 512], BF16) for i in range(2)]
    wdh = sb("wdh", [128, 16 * 512], BF16)
    for tt2 in range(2):
        for xp in range(9):
            wu_ap = din["w1"] if xp == 0 else din["wu"][xp - 1]
            wd_ap = din["w2"] if xp == 0 else din["wd"][xp - 1]
            for fq in range(4):
                w_, wk_ = lwu[fq % 2], f"lwu{fq % 2}"
                s.dma("pool", lambda q, t=w_, a=wu_ap.rearrange("(dc p) c -> p dc c", p=128), fq=fq:
                      q.dma_start(t[:].rearrange("p (dc c) -> p dc c", dc=8), a[:, :, fq * 512:(fq + 1) * 512]),
                      reads=[wk_], writes=[wk_], tag="uw")
                for fc4 in range(4):
                    fc = fq * 4 + fc4
                    for ns in range(2):
                        p_, pk_ = ph[ns % 2], f"phU{ns % 2}"
                        for dc in range(DC):
                            s.add("pe", lambda t, p=p_, w=w_, dc=dc, ns=ns, tt2=tt2, fc4=fc4:
                                  t.matmul(p[:], w[:, dc * 512 + fc4 * 128: dc * 512 + (fc4 + 1) * 128],
                                           xn[:, dc * NT + tt2 * 1024 + ns * 512: dc * NT + tt2 * 1024 + ns * 512 + 512],
                                           start=(dc == 0), stop=(dc == 7)),
                                  reads=[wk_, "xn3"], writes=[pk_], tag="umm")
                        s.add("act", lambda sc, p=p_, fc=fc, ns=ns:
                              sc.activation(he[:, fc * 1024 + ns * 512: fc * 1024 + (ns + 1) * 512], p[:], AF.Gelu),
                              reads=[pk_], writes=["he"], tag="ugelu")
            for dh in range(2):
                s.dma("pool", lambda q, a=wd_ap.rearrange("(fc p) c -> p fc c", p=128), dh=dh:
                      q.dma_start(wdh[:].rearrange("p (fc c) -> p fc c", fc=16),
                                  a[:, :, dh * 512:(dh + 1) * 512]),
                      reads=["wdh"], writes=["wdh"], tag="dw")
                for tl in range(8):
                    tt = tt2 * 8 + tl
                    p_, pk_ = ph[tl % 2], f"phU{tl % 2}"
                    for fc in range(16):
                        s.add("pe", lambda t, p=p_, fc=fc, tl=tl:
                              t.matmul(p[:], he[:, fc * 1024 + tl * 128: fc * 1024 + tl * 128 + 128],
                                       wdh[:, fc * 512:(fc + 1) * 512], start=(fc == 0), stop=(fc == 15)),
                              reads=["he", "wdh"], writes=[pk_], tag="dmm")
                    if xp == 0:
                        s.add("act", lambda sc, p=p_, tt=tt, dh=dh:
                              sc.copy(macc[:, tt * 1024 + dh * 512: tt * 1024 + (dh + 1) * 512], p[:]),
                              reads=[pk_], writes=["macc"], tag="dsh")
                    else:
                        mt, mtk = B.xnt_rot[B.xnt_i % len(B.xnt_rot)]
                        B.xnt_i += 1
                        s.add("act", lambda sc, p=p_, o=mt, tt=tt, xp=xp:
                              sc.activation(o[:, 0:512], p[:], AF.Identity,
                                            scale=cw[:, tt * 8 + xp - 1: tt * 8 + xp]),
                              reads=[pk_], sreads=["cw"], writes=[mtk], tag="dsc")
                        s.add("dve", lambda v, o=mt, tt=tt, dh=dh:
                              v.tensor_tensor(macc[:, tt * 1024 + dh * 512: tt * 1024 + (dh + 1) * 512],
                                              macc[:, tt * 1024 + dh * 512: tt * 1024 + (dh + 1) * 512],
                                              o[:, 0:512], op=ALU.add),
                              reads=["macc", mtk], writes=["macc"], tag="dacc")
    s.barrier("endU")
    close()

    for tt in range(16):
        xt, xtk = B.xt_rot[B.xt_i % len(B.xt_rot)]
        B.xt_i += 1
        s.dma("sp", lambda q, t=xt, a=din["xprev"], tt=tt: q.dma_start(t[:], a[tt * 128:(tt + 1) * 128, :]),
              reads=[xtk], writes=[xtk], tag="ldxf")
        ob, obk = B.out_rot[B.out_i % len(B.out_rot)]
        B.out_i += 1
        s.add("dve", lambda v, o=ob, x=xt, tt=tt:
              v.tensor_tensor(o[:], macc[:, tt * 1024:(tt + 1) * 1024], x[:], op=ALU.add),
              reads=["macc", xtk], writes=[obk], tag="fadd")
        s.dma("sp", lambda q, t=ob, tt=tt: q.dma_start(xout[tt * 128:(tt + 1) * 128, :], t[:]),
              reads=[obk], writes=[f"outx3_{tt}"], tag="stx3")
    s.add("sp", lambda q: q.nop(), reads=[f"outx3_{t}" for t in range(16)], tag="fin")
    close(gblk)
    return din


_B3_CACHE = {}


def build_b3_program():
    if "nc" not in _B3_CACHE:
        nc, s, B = _mk_sched()
        xin = nc.dram_tensor("xprev", [NT, D], F32, kind="ExternalInput").ap()
        xout = nc.dram_tensor("xb3", [NT, D], F32, kind="ExternalOutput").ap()
        build_l1_ffn(nc, s, B, xin, xout)
        _B3_CACHE["nc"] = _finish(nc, s)
        _B3_CACHE["stats"] = s.stats()
    return _B3_CACHE["nc"]


def run_block3(inputs, x_cur):
    C = _host_consts()
    nc = build_b3_program()
    shared = {
        "wg": np.asarray(inputs["L1_ffn_Wg"], np.float32),
        "wu": np.asarray(inputs["L1_ffn_Wu"], np.float32),
        "wd": np.asarray(inputs["L1_ffn_Wd"], np.float32),
        "w1": np.asarray(inputs["L1_ffn_W1"], np.float32),
        "w2": np.asarray(inputs["L1_ffn_W2"], np.float32),
        "c_iota8": C["iota8"],
        "c_cid16": C["cid16"], "c_cid32": C["cid32"], "c_ones128": C["ones128"], "c_onesrow": C["onesrow"],
        "c_selz": C["selz"],
    }
    in_maps = []
    for c in range(8):
        m = dict(shared)
        m["xprev"] = np.ascontiguousarray(x_cur[c])
        in_maps.append(m)
    res = run_bass_kernel_spmd(nc, in_maps, core_ids=list(range(8)))
    return [r["xb3"] for r in res.results]



_FUSED_CACHE = {}


def build_fused_program():
    if "nc" not in _FUSED_CACHE:
        nc, s, B = _mk_sched()
        xb0 = nc.dram_tensor("xb0", [NT, D], F32, kind="Internal").ap()
        xb1 = nc.dram_tensor("xb1", [NT, D], F32, kind="Internal").ap()
        xb2 = nc.dram_tensor("xb2", [NT, D], F32, kind="Internal").ap()
        y = nc.dram_tensor("y", [NT, D], F32, kind="ExternalOutput").ap()
        build_l0_attn(nc, s, B, xb0)
        s.barrier("b01")
        build_l0_ffn(nc, s, B, xb0, xb1)
        s.barrier("b12")
        build_l1_attn(nc, s, B, xb1, xb2)
        s.barrier("b23")
        build_l1_ffn(nc, s, B, xb2, y)
        _FUSED_CACHE["nc"] = _finish(nc, s)
        _FUSED_CACHE["stats"] = s.stats()
    return _FUSED_CACHE["nc"]


def kernel(**inputs):
    C = _host_consts()
    nc = build_fused_program()
    X = np.asarray(inputs["L0_attn_X"], dtype=np.float32)
    gam_t = np.broadcast_to(np.asarray(inputs["L0_attn_gamma"], np.float32), (128, D)).copy()
    bet_t = np.broadcast_to(np.asarray(inputs["L0_attn_beta"], np.float32), (128, D)).copy()
    lam64 = np.broadcast_to(np.asarray(inputs["L0_attn_lam"], np.float32).reshape(1, 1), (64, 1)).copy()
    betas = np.asarray(inputs["L1_attn_betas"], np.float32)[0, :, 0, :]
    m0 = np.broadcast_to(np.asarray(inputs["L1_attn_memory0"], np.float32)[0], (16, 64, 64))
    z0 = np.asarray(inputs["L1_attn_z0"], np.float32)[0, :, 0, :]
    shared = {
        "wq": np.asarray(inputs["L0_attn_Wq"], np.float32), "bq": np.asarray(inputs["L0_attn_bq"], np.float32),
        "wk": np.asarray(inputs["L0_attn_Wk"], np.float32), "bk": np.asarray(inputs["L0_attn_bk"], np.float32),
        "wv": np.asarray(inputs["L0_attn_Wv"], np.float32), "bv": np.asarray(inputs["L0_attn_bv"], np.float32),
        "wo": np.asarray(inputs["L0_attn_Wo"], np.float32), "bo": np.asarray(inputs["L0_attn_bo"], np.float32),
        "gam_t": gam_t, "bet_t": bet_t, "lam64": lam64,
        "w_fc": np.asarray(inputs["L0_ffn_W_fc"], np.float32),
        "ln_w": np.asarray(inputs["L0_ffn_ln_w"], np.float32),
        "ln_b": np.asarray(inputs["L0_ffn_ln_b"], np.float32),
        "w_proj": np.asarray(inputs["L0_ffn_W_proj"], np.float32),
        "wq1": np.asarray(inputs["L1_attn_Wq"], np.float32),
        "wk1": np.asarray(inputs["L1_attn_Wk"], np.float32),
        "wv1": np.asarray(inputs["L1_attn_Wv"], np.float32),
        "wo1": np.asarray(inputs["L1_attn_Wo"], np.float32),
        "wg": np.asarray(inputs["L1_ffn_Wg"], np.float32),
        "wu": np.asarray(inputs["L1_ffn_Wu"], np.float32),
        "wd": np.asarray(inputs["L1_ffn_Wd"], np.float32),
        "w1": np.asarray(inputs["L1_ffn_W1"], np.float32),
        "w2": np.asarray(inputs["L1_ffn_W2"], np.float32),
        "c_iota8": C["iota8"], "c_masks": C["masks"],
        "c_cid16": C["cid16"], "c_cid32": C["cid32"], "c_ones128": C["ones128"],
        "c_onesrow": C["onesrow"], "c_selz": C["selz"],
    }
    in_maps = []
    for c in range(8):
        b, half = c // 2, c % 2
        own, oth = _core_rows(c)
        m = dict(shared)
        m["x_own"] = np.ascontiguousarray(X[b][own])
        m["x_oth"] = np.ascontiguousarray(X[b][oth])
        m["betas8"] = np.ascontiguousarray(betas[half * 8:(half + 1) * 8])
        m["m08"] = np.ascontiguousarray(m0[half * 8:(half + 1) * 8])
        m["z08"] = np.ascontiguousarray(z0[half * 8:(half + 1) * 8])
        in_maps.append(m)
    res = run_bass_kernel_spmd(nc, in_maps, core_ids=list(range(8)))
    out = np.zeros((4, S_FULL, D), np.float32)
    for c in range(8):
        b = c // 2
        own, _ = _core_rows(c)
        out[b][own] = res.results[c]["y"]
    return out


# version 08
# speedup vs baseline: 1.1974x; 1.1974x over previous
import numpy as np
import ml_dtypes
import concourse.bass as bass
import concourse.mybir as mybir
from concourse.bass_utils import run_bass_kernel_spmd

F32 = mybir.dt.float32
BF16 = mybir.dt.bfloat16
AF = mybir.ActivationFunctionType
ALU = mybir.AluOpType
AX = mybir.AxisListType

ENGS = ("pe", "act", "dve", "pool", "sp")
QUEUES = ("sp", "pool")
NQSLOT = 4

D = 1024
DC = 8
NT = 2048
S_FULL = 4096
NSEG = 8
LSEG_LOC = 256
EPS = 1e-5


class Sched:

    def __init__(self, nc):
        self.nc = nc
        self.ops = []
        self.eng_seq = {e: 0 for e in ENGS}
        self.eng_ops = {e: [] for e in ENGS}
        self.qslot_count = {}
        self.qslot_last = {}
        self.dma_rr = {q: 0 for q in QUEUES}
        self.last_writer = {}
        self.readers = {}
        self.eng_last = {e: None for e in ENGS}

    def _event(self, op):
        if op["dma"]:
            return (op["qchan"], op["qidx"])
        return (op["eng"], op["seq"] + 1)

    def add(self, eng, fn, reads=(), writes=(), sreads=(), dma=False, queue=None, tag=""):
        idx = len(self.ops)
        op = {
            "idx": idx, "eng": eng, "fn": fn, "dma": dma, "tag": tag,
            "reads": tuple(reads), "writes": tuple(writes), "sreads": tuple(sreads),
            "seq": self.eng_seq[eng], "inc": False, "waits": {}, "vc": {},
        }
        if dma:
            q = queue or eng
            slot = self.dma_rr[q] % NQSLOT
            self.dma_rr[q] += 1
            op["qchan"] = ("q", q, slot)
            op["qidx"] = self.qslot_count.get((q, slot), 0) + 1
            self.qslot_count[(q, slot)] = op["qidx"]
            op["qprev"] = self.qslot_last.get((q, slot))
            self.qslot_last[(q, slot)] = idx
        deps = {}

        def need2(widx, force=False):
            if widx is None:
                return
            w = self.ops[widx]
            ev = self._event(w)
            same = (not w["dma"]) and w["eng"] == eng
            if same and eng == "pe" and not force:
                return
            old = deps.get(ev[0])
            if old is None or ev[1] > old[0]:
                deps[ev[0]] = (ev[1], widx, force)

        if dma and op.get("qprev") is not None:
            need2(op["qprev"])
        for r in op["reads"]:
            need2(self.last_writer.get(r))
        for r in op["sreads"]:
            need2(self.last_writer.get(r), force=True)
        for wkey in op["writes"]:
            need2(self.last_writer.get(wkey))
            for ridx in self.readers.get(wkey, ()):
                if ridx != idx:
                    need2(ridx)

        prev = self.eng_last[eng]
        vc = dict(self.ops[prev]["vc"]) if prev is not None else {}
        waits = {}
        for chan, (sq, widx, force) in deps.items():
            known = vc.get(chan, 0)
            if sq > known:
                waits[chan] = sq
                self.ops[widx]["inc"] = True
                w = self.ops[widx]
                for c2, v2 in w["vc"].items():
                    if v2 > vc.get(c2, 0):
                        vc[c2] = v2
                vc[chan] = sq
        op["waits"] = waits
        op["vc"] = vc
        self.ops.append(op)
        self.eng_seq[eng] += 1
        self.eng_ops[eng].append(idx)
        self.eng_last[eng] = idx
        for wkey in op["writes"]:
            self.last_writer[wkey] = idx
            self.readers[wkey] = []
        for r in op["reads"] + op["sreads"]:
            self.readers.setdefault(r, []).append(idx)
        return idx

    def dma(self, queue, fn, reads=(), writes=(), tag=""):
        return self.add(queue, fn, reads=reads, writes=writes, dma=True, queue=queue, tag=tag)

    def barrier(self, tag="bar"):
        gather_deps = [self.eng_last[e] for e in ENGS if self.eng_last[e] is not None]
        key = f"__bar{len(self.ops)}"
        for widx in gather_deps:
            self.ops[widx]["inc"] = True

        def mk_nop(eng):
            return lambda e: e.nop()

        gidx = self.add("dve", mk_nop("dve"), tag=tag + "_gather")
        gop = self.ops[gidx]
        for widx in gather_deps:
            w = self.ops[widx]
            ev = self._event(w)
            if (not w["dma"]) and w["eng"] == "dve":
                if gop["vc"].get(ev[0], 0) < ev[1]:
                    gop["waits"][ev[0]] = max(gop["waits"].get(ev[0], 0), ev[1])
                    gop["vc"][ev[0]] = ev[1]
            else:
                if gop["vc"].get(ev[0], 0) < ev[1]:
                    gop["waits"][ev[0]] = max(gop["waits"].get(ev[0], 0), ev[1])
                    gop["vc"][ev[0]] = ev[1]
        gop["writes"] = (key,)
        self.last_writer[key] = gidx
        self.readers[key] = []
        for e in ENGS:
            if e == "dve":
                continue
            self.add(e, mk_nop(e), reads=(key,), tag=tag + "_join")

    def assign_semvals(self):
        for e in ENGS:
            cnt = 0
            for oi in self.eng_ops[e]:
                o = self.ops[oi]
                if o["inc"] and not o["dma"]:
                    cnt += 1
                o["semval"] = cnt

    def verify(self):
        self.assign_semvals()
        pos = {e: 0 for e in ENGS}
        sem = {e: 0 for e in ENGS}
        qsem = {}
        progress = True
        while progress:
            progress = False
            for e in ENGS:
                while pos[e] < len(self.eng_ops[e]):
                    oi = self.eng_ops[e][pos[e]]
                    op = self.ops[oi]
                    ok = True
                    for chan, sq in op["waits"].items():
                        if isinstance(chan, tuple):
                            if qsem.get(chan, 0) < sq:
                                ok = False
                                break
                        else:
                            val = self.ops[self.eng_ops[chan][sq - 1]]["semval"]
                            if sem[chan] < val:
                                ok = False
                                break
                    if not ok:
                        break
                    if op["dma"]:
                        qsem[op["qchan"]] = qsem.get(op["qchan"], 0) + 1
                    elif op["inc"]:
                        sem[e] += 1
                    pos[e] += 1
                    progress = True
        stuck = {e: pos[e] for e in ENGS if pos[e] < len(self.eng_ops[e])}
        if stuck:
            det = []
            for e in stuck:
                op = self.ops[self.eng_ops[e][pos[e]]]
                det.append(f"{e}@{op['idx']}:{op['tag']} waits {op['waits']}")
            raise RuntimeError("Sched.verify deadlock: " + "; ".join(det))

    def emit(self, sems, qsems, engmap):
        self.assign_semvals()

        def run_engine(ename):
            def body(eng):
                for oi in self.eng_ops[ename]:
                    op = self.ops[oi]
                    for chan, sq in sorted(op["waits"].items(), key=lambda kv: str(kv[0])):
                        if isinstance(chan, tuple):
                            _, q, slot = chan
                            eng.wait_ge(qsems[(q, slot)], 16 * sq)
                        else:
                            pi = self.eng_ops[chan]
                            val = self.ops[pi[sq - 1]]["semval"]
                            if val > 0:
                                eng.wait_ge(sems[chan], val)
                    inst = op["fn"](eng)
                    if op["dma"]:
                        inst.then_inc(qsems[(op["qchan"][1], op["qchan"][2])], 16)
                    elif op["inc"]:
                        inst.then_inc(sems[op["eng"]], 1)
            return body

        blk = self.nc.Block()
        with blk:
            blk.tensor(run_engine("pe"))
            blk.scalar(run_engine("act"))
            blk.vector(run_engine("dve"))
            blk.gpsimd(run_engine("pool"))
            blk.sync(run_engine("sp"))

    def stats(self):
        from collections import Counter
        c = Counter(o["eng"] for o in self.ops)
        w = sum(len(o["waits"]) for o in self.ops)
        i = sum(1 for o in self.ops if o["inc"])
        return f"ops={len(self.ops)} per-eng={dict(c)} waits={w} incs={i}"


def engobjs(nc):
    return {"pe": nc.tensor, "act": nc.scalar, "dve": nc.vector, "pool": nc.gpsimd, "sp": nc.sync}



def _host_consts():
    bf = ml_dtypes.bfloat16
    cid16 = np.eye(128, dtype=bf)
    cid32 = np.eye(128, dtype=np.float32)
    ones128 = np.ones((128, 128), dtype=bf)
    onesrow = np.ones((1, 512), dtype=bf)
    sel4 = np.zeros((4, 4 * 64), dtype=bf)
    for r in range(4):
        sel4[r, r * 64:(r + 1) * 64] = 1
    selz = np.zeros((128, 2 * 64), dtype=bf)
    selz[0, 0:64] = 1
    selz[64, 64:128] = 1
    masks = np.zeros((128, 4 * 512), dtype=bf)
    c = np.arange(512)
    lq = 16 * (c % 32) + c // 32
    for kt in range(4):
        p = np.arange(128)
        lk = 16 * (p % 32) + 4 * kt + p // 32
        masks[:, kt * 512:(kt + 1) * 512] = (lk[:, None] <= lq[None, :]).astype(bf)
    iota8 = np.broadcast_to(np.arange(8, dtype=np.float32), (128, 8)).copy()
    return {"cid16": cid16, "cid32": cid32, "ones128": ones128, "onesrow": onesrow,
            "sel4": sel4, "selz": selz, "masks": masks, "iota8": iota8}


def _core_rows(c):
    half = c % 2
    own = np.concatenate([np.arange(s * 512 + half * 256, s * 512 + half * 256 + 256) for s in range(8)])
    oth = np.concatenate([np.arange(s * 512 + (1 - half) * 256, s * 512 + (1 - half) * 256 + 256) for s in range(8)])
    return own, oth



def _small(nc, name, n, cols=1, dtype=F32, parts=128):
    return [(nc.alloc_sbuf_tensor(f"{name}{i}", [parts, cols], dtype), f"{name}{i}") for i in range(n)]


class Bufs:
    pass


def alloc_consts(nc, s):
    B = Bufs()
    B.nc = nc
    B.cid16 = nc.alloc_sbuf_tensor("cid16", [128, 128], BF16)
    B.cid32 = nc.alloc_sbuf_tensor("cid32", [128, 128], F32)
    B.ones128 = nc.alloc_sbuf_tensor("ones128", [128, 128], BF16)
    B.onesrow = nc.alloc_sbuf_tensor("onesrow", [1, 512], BF16)
    B.selz = nc.alloc_sbuf_tensor("selz", [128, 128], BF16)
    for nm in ("cid16", "cid32", "ones128", "onesrow", "selz"):
        ap = nc.dram_tensor(f"c_{nm}", list(getattr(B, nm).ap().shape), getattr(B, nm).ap().dtype,
                            kind="ExternalInput").ap()
        t = getattr(B, nm)
        s.dma("sp", lambda q, t=t, ap=ap: q.dma_start(t[:], ap), writes=[nm], tag=f"ld_{nm}")
    return B


def emit_ln_tok(s, B, xt, xtk, st, ncols=1024, eps=EPS):
    inv = 1.0 / ncols
    (nm, nmk), (e2, e2k), (var, vark), (rstd, rstdk), (nb, nbk) = st
    s.add("dve", lambda v, a=nm, x=xt: v.reduce_sum(a[:], x[:], axis=AX.X, negate=True),
          reads=[xtk], writes=[nmk])
    s.add("dve", lambda v, a=nm: v.tensor_scalar(a[:], a[:], inv, None, op0=ALU.mult),
          reads=[nmk], writes=[nmk])
    sq, sqk = B.sq_rot[B.sq_i % len(B.sq_rot)]
    B.sq_i += 1
    s.add("act", lambda sc, o=sq, x=xt: sc.activation(o[:, 0:x.ap().shape[1]], x[:], AF.Square),
          reads=[xtk], writes=[sqk])
    s.add("dve", lambda v, a=e2, x=sq, n=ncols: v.reduce_sum(a[:], x[:, 0:n], axis=AX.X),
          reads=[sqk], writes=[e2k])
    s.add("dve", lambda v, a=e2: v.tensor_scalar(a[:], a[:], inv, None, op0=ALU.mult),
          reads=[e2k], writes=[e2k])
    s.add("dve", lambda v, o=var, a=nm, b=nm: v.tensor_tensor(o[:], a[:], b[:], op=ALU.mult),
          reads=[nmk], writes=[vark])
    s.add("dve", lambda v, o=var, a=e2, b=var, e=float(eps):
          v.scalar_tensor_tensor(o[:], a[:], e, b[:], op0=ALU.add, op1=ALU.subtract),
          reads=[e2k, vark], writes=[vark])
    s.add("act", lambda sc, o=var, a=var: sc.activation(o[:], a[:], AF.Ln),
          reads=[vark], writes=[vark])
    s.add("act", lambda sc, o=rstd, a=var: sc.activation(o[:], a[:], AF.Exp, scale=-0.5),
          reads=[vark], writes=[rstdk])
    s.add("dve", lambda v, o=nb, a=nm, b=rstd: v.tensor_tensor(o[:], a[:], b[:], op=ALU.mult),
          reads=[nmk, rstdk], writes=[nbk])
    return (rstd, rstdk), (nb, nbk)


def emit_load_ln_fm(s, B, src_ap, xn, xnk, colfn, ntiles=16, ln=True):
    nc = B.nc
    for tt in range(ntiles):
        xt, xtk = B.xt_rot[B.xt_i % len(B.xt_rot)]
        B.xt_i += 1
        s.dma("sp", lambda q, t=xt, a=src_ap, tt=tt: q.dma_start(t[:], a[tt * 128:(tt + 1) * 128, :]),
              reads=[xtk], writes=[xtk], tag="ldx")
        if ln:
            st = B.lnst[B.lnst_i % len(B.lnst)]
            B.lnst_i += 1
            (rstd, rstdk), (nb, nbk) = emit_ln_tok(s, B, xt, xtk, st)
            xnt, xntk = B.xnt_rot[B.xnt_i % len(B.xnt_rot)]
            B.xnt_i += 1
            s.add("act", lambda sc, o=xnt, x=xt, r=rstd, b=nb:
                  sc.activation(o[:], x[:], AF.Identity, bias=b[:], scale=r[:]),
                  reads=[xtk], sreads=[rstdk, nbk], writes=[xntk], tag="xnt")
        else:
            xnt, xntk = B.xnt_rot[B.xnt_i % len(B.xnt_rot)]
            B.xnt_i += 1
            s.add("dve", lambda v, o=xnt, x=xt: v.tensor_copy(o[:], x[:]),
                  reads=[xtk], writes=[xntk], tag="xcast")
        for dc in range(DC):
            pt, ptk = B.tps[(tt * DC + dc) % len(B.tps)]
            s.add("pe", lambda t, p=pt, x=xnt, dc=dc, c=B.cid16:
                  t.matmul(p[:], x[:, dc * 128:(dc + 1) * 128], c[:], start=True, stop=True),
                  reads=[xntk, "cid16"], writes=[ptk], tag="xT")
            eng = "dve" if dc % 2 == 0 else "act"
            if eng == "dve":
                s.add("dve", lambda v, p=pt, o=xn, col=colfn(dc, tt): v.tensor_copy(o[:, col:col + 128], p[:]),
                      reads=[ptk], writes=[xnk], tag="xTev")
            else:
                s.add("act", lambda sc, p=pt, o=xn, col=colfn(dc, tt): sc.copy(o[:, col:col + 128], p[:]),
                      reads=[ptk], writes=[xnk], tag="xTev")


def emit_exit_delta(s, B, delta, deltak, xprev_ap, out_ap, outkey):
    nc = B.nc
    for tt in range(16):
        xt, xtk = B.xt_rot[B.xt_i % len(B.xt_rot)]
        B.xt_i += 1
        s.dma("sp", lambda q, t=xt, a=xprev_ap, tt=tt: q.dma_start(t[:], a[tt * 128:(tt + 1) * 128, :]),
              reads=[xtk], writes=[xtk], tag="ldxe")
        ob, obk = B.out_rot[B.out_i % len(B.out_rot)]
        B.out_i += 1
        for dc in range(DC):
            pt, ptk = B.tps[(tt * DC + dc) % len(B.tps)]
            s.add("pe", lambda t, p=pt, d=delta, dc=dc, tt=tt, c=B.cid32:
                  t.matmul(p[:], d[:, dc * 2048 + tt * 128: dc * 2048 + (tt + 1) * 128], c[:], start=True, stop=True),
                  reads=[deltak, "cid32"], writes=[ptk], tag="dT")
            s.add("dve", lambda v, p=pt, o=ob, x=xt, dc=dc:
                  v.tensor_tensor(o[:, dc * 128:(dc + 1) * 128], p[:], x[:, dc * 128:(dc + 1) * 128], op=ALU.add),
                  reads=[ptk, xtk], writes=[obk], tag="dadd")
        s.dma("sp", lambda q, a=out_ap, t=ob, tt=tt: q.dma_start(a[tt * 128:(tt + 1) * 128, :], t[:]),
              reads=[obk], writes=[f"{outkey}_{tt}"], tag="stx")


def alloc_common(nc, s, B):
    B.xt_rot = [(nc.alloc_sbuf_tensor(f"xt{i}", [128, 1024], F32), f"xt{i}") for i in range(2)]
    B.xt_i = 0
    B.sq_rot = [(nc.alloc_sbuf_tensor(f"sqs{i}", [128, 1024], BF16), f"sqs{i}") for i in range(2)]
    B.sq_i = 0
    B.xnt_rot = [(nc.alloc_sbuf_tensor(f"xnt{i}", [128, 1024], BF16), f"xnt{i}") for i in range(2)]
    B.xnt_i = 0
    B.out_rot = [(nc.alloc_sbuf_tensor(f"outb{i}", [128, 1024], F32), f"outb{i}") for i in range(2)]
    B.out_i = 0
    B.lnst = []
    for i in range(2):
        B.lnst.append([( nc.alloc_sbuf_tensor(f"ln{nm}{i}", [128, 1], F32), f"ln{nm}{i}")
                       for nm in ("nm", "e2", "var", "rstd", "nb")])
    B.lnst_i = 0
    B.tps = None



def build_l0_attn(nc, s, B, xout):
    din = {}
    for nm, shp, dt in [("x_own", [NT, D], F32), ("x_oth", [NT, D], F32),
                        ("wq", [16, D, 128], F32), ("bq", [16, 128], F32),
                        ("wk", [16, D, 128], F32), ("bk", [16, 128], F32),
                        ("wv", [16, D, 64], F32), ("bv", [16, 64], F32),
                        ("wo", [D, D], F32), ("bo", [D], F32),
                        ("gam_t", [128, D], F32), ("bet_t", [128, D], F32),
                        ("lam64", [64, 1], F32)]:
        din[nm] = nc.dram_tensor(nm, shp, dt, kind="ExternalInput").ap()

    gblk, gph = [], []

    def sb(name, shape, dtype, scope=None):
        g = nc.sbuf_tensor(name, shape, dtype)
        (gph if scope is None else scope).append(g)
        return g.__enter__()

    def ps(name, shape, dtype=F32):
        g = nc.psum_tensor(name, shape, dtype)
        gph.append(g)
        return g.__enter__()

    def close(lst=None):
        lst = gph if lst is None else lst
        for g in reversed(lst):
            g.__exit__(None, None, None)
        lst.clear()

    Xn = sb("Xn", [128, DC * S_FULL], BF16, scope=gblk)
    OcatT = sb("OcatT", [128, DC * NT], BF16, scope=gblk)
    selz_sb = B.selz
    lam_sb = sb("lam_sb", [64, 1], F32, scope=gblk)
    s.dma("sp", lambda q, t=lam_sb, a=din["lam64"]: q.dma_start(t[:], a), writes=["lam"], tag="ldlam")
    s.add("dve", lambda v, t=lam_sb: v.tensor_scalar(t[:], t[:], -1.0, None, op0=ALU.mult),
          reads=["lam"], writes=["lam"], tag="neglam")

    B.tps = [(ps(f"tpA{i}", [128, 128]), f"tpA{i}") for i in range(4)]
    emit_load_ln_fm(s, B, din["x_own"], Xn, "Xn", lambda dc, tt: dc * S_FULL + tt * 128)
    emit_load_ln_fm(s, B, din["x_oth"], Xn, "Xn", lambda dc, tt: dc * S_FULL + NT + tt * 128)
    s.barrier("endA")
    close()

    pA = ps("pA", [128, 1024])
    pB = ps("pB", [128, 1024])
    pbank = [(pA, 0, "pA0"), (pA, 512, "pA1"), (pB, 0, "pB0"), (pB, 512, "pB1")]
    pO1 = ps("pO1", [65, 512])
    pO2 = ps("pO2", [65, 512])
    pu = [ps(f"pu{i}", [128, 512]) for i in range(2)]
    pubank = [(pu[0], 0, "pu0"), (pu[1], 0, "pu1")]

    Kg = [sb(f"Kg{i}", [128, S_FULL], BF16) for i in range(2)]
    Qg = [sb(f"Qg{i}", [128, NT], BF16) for i in range(2)]
    Vg = [sb(f"Vgp{i}", [128, 2 * 65 * 32], BF16) for i in range(2)]
    wkb = [sb(f"wkb{i}", [128, 1024], BF16) for i in range(2)]
    wqb = [sb(f"wqb{i}", [128, 1024], BF16) for i in range(2)]
    wvg = [sb(f"wvgp{i}", [128, 1024], BF16) for i in range(2)]
    bqs = sb("bqs", [128, 16], F32)
    bks = sb("bks", [128, 16], F32)
    bvrow = sb("bvrow", [1, 1024], BF16)
    us1 = [sb(f"us1_{i}", [64, 2048], BF16) for i in range(2)]
    us2 = [sb(f"us2_{i}", [64, 2048], BF16) for i in range(2)]
    zb = [sb(f"zbb{i}", [128, 2048], BF16) for i in range(2)]
    e12 = [sb(f"e12_{i}", [128, 1024], BF16) for i in range(2)]
    for i in range(2):
        s.add("dve", lambda v, t=zb[i], i=i: v.memset(t[:], 1.0), writes=[f"zbk{i}"], tag="zinit")
    ct1 = sb("ct1", [64, 512], BF16)
    ct2 = sb("ct2", [64, 512], BF16)

    bqnat = sb("bqnat", [16, 128], F32)
    s.dma("sp", lambda q, t=bqnat, a=din["bq"]: q.dma_start(t[:], a), reads=["bqnat"], writes=["bqnat"], tag="ldbq")
    s.add("pe", lambda t: t.matmul(pA[:, 0:16], bqnat[:], B.cid32[0:16, 0:16], start=True, stop=True),
          reads=["bqnat", "cid32"], writes=["pA0"], tag="bqT")
    s.add("dve", lambda v: v.tensor_copy(bqs[:], pA[:, 0:16]), reads=["pA0", "bqnat"], writes=["bqs"], tag="bqTe")
    s.dma("sp", lambda q, t=bqnat, a=din["bk"]: q.dma_start(t[:], a), reads=["bqnat"], writes=["bqnat"], tag="ldbk")
    s.add("pe", lambda t: t.matmul(pA[:, 0:16], bqnat[:], B.cid32[0:16, 0:16], start=True, stop=True),
          reads=["bqnat", "cid32"], writes=["pA0"], tag="bkT")
    s.add("dve", lambda v: v.tensor_copy(bks[:], pA[:, 0:16]), reads=["pA0"], writes=["bks"], tag="bkTe")
    s.dma("pool", lambda q, t=bvrow, a=din["bv"].rearrange("h d -> (h d)"): q.dma_start(t[0:1, :], a[None, :]),
          writes=["bvrow"], tag="ldbv")
    for i in range(2):
        s.add("dve", lambda v, t=Vg[i], i=i:
              v.memset(t[:].rearrange("p (h tt c) -> p h tt c", h=2, tt=32)[:, :, :, 64:65], 1.0),
              writes=[f"Vg{i}"], tag="vones")

    units = []
    ready = {}
    for pr in range(8):
        vslot = pr % 2

        def wv_load(pr=pr, vslot=vslot):
            for hh in range(2):
                s.dma("pool", lambda q, a=din["wv"][pr * 2 + hh].rearrange("(ej p) d -> p ej d", p=128),
                      hh=hh, vslot=vslot:
                      q.dma_start(wvg[vslot][:].rearrange("p (ej h d) -> p ej h d", ej=8, h=2)[:, :, hh, :], a),
                      reads=[f"wvg{vslot}"], writes=[f"wvg{vslot}"], tag="ldwv")
        units.append(wv_load)
        for tt in range(32):
            def v_unit(tt=tt, pr=pr, vslot=vslot):
                pt_, off, pk = pubank[tt % 2]
                for ej in range(DC):
                    s.add("pe", lambda t, p=pt_, o=off, w=wvg[vslot], ej=ej, tt=tt:
                          t.matmul(p[:, o:o + 128], Xn[:, ej * S_FULL + tt * 128: ej * S_FULL + (tt + 1) * 128],
                                   w[:, ej * 128:(ej + 1) * 128], start=(ej == 0), stop=False),
                          reads=["Xn", f"wvg{vslot}"], writes=[pk], tag="vmm")
                s.add("pe", lambda t, p=pt_, o=off, pr=pr:
                      t.matmul(p[:, o:o + 128], B.onesrow[0:1, 0:128], bvrow[0:1, pr * 128:(pr + 1) * 128],
                               start=False, stop=True),
                      reads=["onesrow", "bvrow"], writes=[pk], tag="vbias")

                def vev(e, p=pt_, o=off, tt=tt, vslot=vslot):
                    src = p[:, o:o + 128].rearrange("p (h d) -> p h d", h=2)
                    dst = Vg[vslot][:].rearrange("p (h t) -> p h t", h=2)[:, :, tt * 65: tt * 65 + 64]
                    return e.tensor_copy(dst, src)
                s.add("dve", vev, reads=[pk], writes=[f"Vg{vslot}"], tag="vev")
            units.append(v_unit)
        for hh in range(2):
            h = pr * 2 + hh
            hs = h % 2

            def w_load(h=h, hs=hs):
                s.dma("pool", lambda q, t=wkb[hs], a=din["wk"][h].rearrange("(ej p) c -> p ej c", p=128):
                      q.dma_start(t[:].rearrange("p (ej c) -> p ej c", ej=8), a),
                      reads=[f"wkb{hs}"], writes=[f"wkb{hs}"], tag="ldwk")
                s.dma("pool", lambda q, t=wqb[hs], a=din["wq"][h].rearrange("(ej p) c -> p ej c", p=128):
                      q.dma_start(t[:].rearrange("p (ej c) -> p ej c", ej=8), a),
                      reads=[f"wqb{hs}"], writes=[f"wqb{hs}"], tag="ldwq")
            units.append(w_load)
            for n8 in range(8):
                def k_unit(n8=n8, h=h, hs=hs):
                    pt_, off, pk = pubank[n8 % 2]
                    for ej in range(DC):
                        s.add("pe", lambda t, p=pt_, o=off, w=wkb[hs], ej=ej, n8=n8:
                              t.matmul(p[:, o:o + 512], w[:, ej * 128:(ej + 1) * 128],
                                       Xn[:, ej * S_FULL + n8 * 512: ej * S_FULL + (n8 + 1) * 512],
                                       start=(ej == 0), stop=(ej == DC - 1)),
                              reads=[f"wkb{hs}", "Xn"], writes=[pk], tag="kmm")
                    s.add("dve", lambda v, p=pt_, o=off, d=Kg[hs], n8=n8, h=h:
                          v.tensor_scalar(d[:, n8 * 512:(n8 + 1) * 512], p[:, o:o + 512],
                                          bks[:, h:h + 1], None, op0=ALU.add),
                          reads=[pk], sreads=["bks"], writes=[f"Kgk{hs}"], tag="kev")
                units.append(k_unit)
            for n4 in range(4):
                def q_unit(n4=n4, h=h, hs=hs):
                    pt_, off, pk = pubank[n4 % 2]
                    for ej in range(DC):
                        s.add("pe", lambda t, p=pt_, o=off, w=wqb[hs], ej=ej, n4=n4:
                              t.matmul(p[:, o:o + 512], w[:, ej * 128:(ej + 1) * 128],
                                       Xn[:, ej * S_FULL + n4 * 512: ej * S_FULL + (n4 + 1) * 512],
                                       start=(ej == 0), stop=(ej == DC - 1)),
                              reads=[f"wqb{hs}", "Xn"], writes=[pk], tag="qmm")
                    s.add("dve", lambda v, p=pt_, o=off, d=Qg[hs], n4=n4, h=h:
                          v.tensor_scalar(d[:, n4 * 512:(n4 + 1) * 512], p[:, o:o + 512],
                                          bqs[:, h:h + 1], None, op0=ALU.add),
                          reads=[pk], sreads=["bqs"], writes=[f"Qgk{hs}"], tag="qev")
                units.append(q_unit)
            ready[h] = len(units)

    drained = [0]

    def drain(upto):
        upto = min(upto, len(units))
        while drained[0] < upto:
            units[drained[0]]()
            drained[0] += 1

    pending_combine = [None]
    for h in range(16):
        hs = h % 2
        vslot = (h // 2) % 2
        drain(ready[h])
        if pending_combine[0] is not None:
            pending_combine[0]()
            pending_combine[0] = None
        nxt = ready[h + 1] if h + 1 < 16 else len(units)
        need = nxt - ready[h]
        slot = [0]

        def pace():
            slot[0] += 1
            drain(ready[h] + (need * slot[0] + 127) // 128)
        for qt in range(4):
            def emit_s(kc, qt=qt, hs=hs):
                pt_ = pA if kc % 2 == 0 else pB
                k0 = "pA0" if kc % 2 == 0 else "pB0"
                k1 = "pA1" if kc % 2 == 0 else "pB1"
                s.add("pe", lambda t, p=pt_, K=Kg[hs], Q=Qg[hs], kc=kc, qt=qt:
                      t.matmul(p[:, 0:512], K[0:64, kc * 128:(kc + 1) * 128],
                               Q[0:64, qt * 512:(qt + 1) * 512], start=True, stop=True,
                               tile_position=(0, 0)),
                      reads=[f"Kgk{hs}", f"Qgk{hs}"], writes=[k0], tag="s1")
                s.add("pe", lambda t, p=pt_, K=Kg[hs], Q=Qg[hs], kc=kc, qt=qt:
                      t.matmul(p[:, 512:1024], K[64:128, kc * 128:(kc + 1) * 128],
                               Q[64:128, qt * 512:(qt + 1) * 512], start=True, stop=True,
                               tile_position=(64, 0)),
                      reads=[f"Kgk{hs}", f"Qgk{hs}"], writes=[k1], tag="s2")

            def emit_epv(kc, vslot=vslot, hh=(h % 2)):
                pt_ = pA if kc % 2 == 0 else pB
                k0 = "pA0" if kc % 2 == 0 else "pB0"
                k1 = "pA1" if kc % 2 == 0 else "pB1"
                eb, ebk = e12[kc % 2], f"e12_{kc % 2}"
                s.add("act", lambda sc, o=eb, p=pt_: sc.activation(o[:], p[:], AF.Exp, scale=0.125),
                      reads=[k0, k1], writes=[ebk], tag="exp")
                hh2 = h % 2
                s.add("pe", lambda t, e=eb, kc=kc, hh2=hh2, vslot=vslot:
                      t.matmul(pO1[:], Vg[vslot][:, hh2 * 2080 + kc * 65: hh2 * 2080 + (kc + 1) * 65],
                               e[:, 0:512], start=(kc == 0), stop=(kc == 31)),
                      reads=[f"Vg{vslot}", ebk], writes=["pO1"], tag="pv1")
                s.add("pe", lambda t, e=eb, kc=kc, hh2=hh2, vslot=vslot:
                      t.matmul(pO2[:], Vg[vslot][:, hh2 * 2080 + kc * 65: hh2 * 2080 + (kc + 1) * 65],
                               e[:, 512:1024], start=(kc == 0), stop=(kc == 31)),
                      reads=[f"Vg{vslot}", ebk], writes=["pO2"], tag="pv2")
            emit_s(0)
            for kc in range(1, 32):
                emit_s(kc)
                emit_epv(kc - 1)
                pace()
            emit_epv(31)
            pace()
            s.add("act", lambda sc, qt=qt, u=us1[hs]: sc.copy(u[:, qt * 512:(qt + 1) * 512], pO1[0:64, :]),
                  reads=["pO1"], writes=[f"us1k{hs}"], tag="uev1")
            s.add("dve", lambda v, qt=qt, z=zb[hs]: v.tensor_copy(z[0:1, qt * 512:(qt + 1) * 512], pO1[64:65, :]),
                  reads=["pO1"], writes=[f"zbk{hs}"], tag="zev1")
            s.add("dve", lambda v, qt=qt, u=us2[hs]: v.tensor_copy(u[:, qt * 512:(qt + 1) * 512], pO2[0:64, :]),
                  reads=["pO2"], writes=[f"us2k{hs}"], tag="uev2")
            s.add("act", lambda sc, qt=qt, z=zb[hs]: sc.copy(z[64:65, qt * 512:(qt + 1) * 512], pO2[64:65, :]),
                  reads=["pO2"], writes=[f"zbk{hs}"], tag="zev2")

        def mk_combine(h=h, hs=hs):
            def emit_combine():
                z = zb[hs]
                s.add("act", lambda sc, z=z: sc.activation(z[:], z[:], AF.Ln),
                      reads=[f"zbk{hs}"], writes=[f"zbk{hs}"], tag="lnz")
                s.add("act", lambda sc, z=z: sc.activation(z[:], z[:], AF.Exp, scale=-1.0),
                      reads=[f"zbk{hs}"], writes=[f"zbk{hs}"], tag="expz")
                for qt in range(4):
                    s.add("pe", lambda t, qt=qt, z=z:
                          t.matmul(pu[0][0:64, :], selz_sb[:, 0:64], z[:, qt * 512:(qt + 1) * 512],
                                   start=True, stop=True),
                          reads=["selz", f"zbk{hs}"], writes=["pu0"], tag="bz1")
                    s.add("pe", lambda t, qt=qt, z=z:
                          t.matmul(pu[1][0:64, :], selz_sb[:, 64:128], z[:, qt * 512:(qt + 1) * 512],
                                   start=True, stop=True),
                          reads=["selz", f"zbk{hs}"], writes=["pu1"], tag="bz2")
                    s.add("dve", lambda v, qt=qt, u=us1[hs]:
                          v.tensor_tensor(ct1[:], u[:, qt * 512:(qt + 1) * 512], pu[0][0:64, :], op=ALU.mult),
                          reads=[f"us1k{hs}", "pu0"], writes=["ct1"], tag="c1")
                    s.add("dve", lambda v, qt=qt, u=us2[hs]:
                          v.tensor_tensor(ct2[:], u[:, qt * 512:(qt + 1) * 512], pu[1][0:64, :], op=ALU.mult),
                          reads=[f"us2k{hs}", "pu1"], writes=["ct2"], tag="c2")
                    s.add("dve", lambda v, h=h, qt=qt:
                          v.scalar_tensor_tensor(
                              OcatT[64 * (h % 2):64 * (h % 2) + 64,
                                    (h // 2) * NT + qt * 512:(h // 2) * NT + (qt + 1) * 512],
                              ct2[:], lam_sb[:], ct1[:], op0=ALU.mult, op1=ALU.add),
                          reads=["ct1", "ct2"], sreads=["lam"], writes=["OcatT"], tag="comb")
            return emit_combine
        pending_combine[0] = mk_combine()
    pending_combine[0]()
    s.barrier("endB")
    close()

    wo_sb = sb("wo_sb", [128, DC * 1024], BF16)
    bo_row = sb("bo_row", [1, 1024], BF16)
    gam = sb("gam", [128, 1024], F32)
    bet = sb("bet", [128, 1024], F32)
    ytok = [sb(f"ytok{i}", [128, 1024], F32) for i in range(2)]
    ywork = [sb(f"ywork{i}", [128, 1024], F32) for i in range(2)]
    pY = [ps(f"pY{i}", [128, 512]) for i in range(2)]
    s.dma("pool", lambda q, t=wo_sb, a=din["wo"].rearrange("(cc p) d -> p cc d", p=128):
          q.dma_start(t[:].rearrange("p (cc d) -> p cc d", cc=8), a), writes=["wo_sb"], tag="ldwo")
    s.dma("pool", lambda q, t=bo_row, a=din["bo"]: q.dma_start(t[0:1, :], a[None, :]), writes=["bo_row"], tag="ldbo")
    s.dma("sp", lambda q, t=gam, a=din["gam_t"]: q.dma_start(t[:], a), writes=["gam"], tag="ldgam")
    s.dma("sp", lambda q, t=bet, a=din["bet_t"]: q.dma_start(t[:], a), writes=["bet"], tag="ldbet")
    s.add("dve", lambda v, t=gam: v.tensor_scalar(t[:], t[:], 0.2, None, op0=ALU.mult),
          reads=["gam"], writes=["gam"], tag="g02")
    s.add("dve", lambda v, t=bet: v.tensor_scalar(t[:], t[:], 0.2, None, op0=ALU.mult),
          reads=["bet"], writes=["bet"], tag="b02")
    for tt in range(16):
        yb, ybk = ytok[tt % 2], f"ytok{tt % 2}"
        for nh in range(2):
            p = pY[nh]
            pk = f"pY{nh}"
            for cc in range(8):
                s.add("pe", lambda t, p=p, cc=cc, tt=tt, nh=nh:
                      t.matmul(p[:], OcatT[:, cc * NT + tt * 128: cc * NT + (tt + 1) * 128],
                               wo_sb[:, cc * 1024 + nh * 512: cc * 1024 + (nh + 1) * 512],
                               start=(cc == 0), stop=False),
                      reads=["OcatT", "wo_sb"], writes=[pk], tag="ymm")
            s.add("pe", lambda t, p=p, nh=nh:
                  t.matmul(p[:], B.onesrow[0:1, 0:128], bo_row[0:1, nh * 512:(nh + 1) * 512],
                           start=False, stop=True),
                  reads=["onesrow", "bo_row"], writes=[pk], tag="ybias")
            s.add("dve", lambda v, p=p, o=yb, nh=nh: v.tensor_copy(o[:, nh * 512:(nh + 1) * 512], p[:]),
                  reads=[pk], writes=[ybk], tag="yev")
        st = B.lnst[B.lnst_i % len(B.lnst)]
        B.lnst_i += 1
        (rstd, rstdk), (nb, nbk) = emit_ln_tok(s, B, yb, ybk, st)
        yw, ywk = ywork[tt % 2], f"ywork{tt % 2}"
        s.add("act", lambda sc, o=yw, x=yb, r=rstd, b=nb:
              sc.activation(o[:], x[:], AF.Identity, bias=b[:], scale=r[:]),
              reads=[ybk], sreads=[rstdk, nbk], writes=[ywk], tag="ynorm")
        s.add("dve", lambda v, t=yw: v.tensor_tensor(t[:], t[:], gam[:], op=ALU.mult),
              reads=[ywk, "gam"], writes=[ywk], tag="ygam")
        s.add("pool", lambda g_, t=yw: g_.tensor_tensor(t[:], t[:], bet[:], op=ALU.add),
              reads=[ywk, "bet"], writes=[ywk], tag="ybet")
        xt, xtk = B.xt_rot[B.xt_i % len(B.xt_rot)]
        B.xt_i += 1
        s.dma("sp", lambda q, t=xt, a=din["x_own"], tt=tt: q.dma_start(t[:], a[tt * 128:(tt + 1) * 128, :]),
              reads=[xtk], writes=[xtk], tag="ldxc")
        ob, obk = B.out_rot[B.out_i % len(B.out_rot)]
        B.out_i += 1
        s.add("dve", lambda v, o=ob, a=yw, b=xt: v.tensor_tensor(o[:], a[:], b[:], op=ALU.add),
              reads=[ywk, xtk], writes=[obk], tag="oadd")
        s.dma("sp", lambda q, t=ob, tt=tt: q.dma_start(xout[tt * 128:(tt + 1) * 128, :], t[:]),
              reads=[obk], writes=[f"out_x{tt}"], tag="stx")
    s.add("sp", lambda q: q.nop(), reads=[f"out_x{t}" for t in range(16)], tag="fin")
    close()
    close(gblk)
    return din



def _mk_sched():
    nc = bass.Bass("TRN2", target_bir_lowering=False)
    s = Sched(nc)
    B = alloc_consts(nc, s)
    alloc_common(nc, s, B)
    return nc, s, B


def _finish(nc, s):
    s.verify()
    sems = {eng: nc.alloc_semaphore(f"s_{eng}") for eng in ENGS}
    qsems = {(q, i): nc.alloc_semaphore(f"q_{q}{i}") for q in QUEUES for i in range(NQSLOT)}
    s.emit(sems, qsems, None)
    return nc


_B0_CACHE = {}


def build_b0_program():
    if "nc" not in _B0_CACHE:
        nc, s, B = _mk_sched()
        xout = nc.dram_tensor("xb0", [NT, D], F32, kind="ExternalOutput").ap()
        build_l0_attn(nc, s, B, xout)
        _B0_CACHE["nc"] = _finish(nc, s)
        _B0_CACHE["stats"] = s.stats()
    return _B0_CACHE["nc"]


def run_block0(inputs, x_cur=None):
    C = _host_consts()
    nc = build_b0_program()
    X = np.asarray(inputs["L0_attn_X"], dtype=np.float32)
    gam_t = np.broadcast_to(np.asarray(inputs["L0_attn_gamma"], np.float32), (128, D)).copy()
    bet_t = np.broadcast_to(np.asarray(inputs["L0_attn_beta"], np.float32), (128, D)).copy()
    lam64 = np.broadcast_to(np.asarray(inputs["L0_attn_lam"], np.float32).reshape(1, 1), (64, 1)).copy()
    shared = {
        "wq": np.asarray(inputs["L0_attn_Wq"], np.float32), "bq": np.asarray(inputs["L0_attn_bq"], np.float32),
        "wk": np.asarray(inputs["L0_attn_Wk"], np.float32), "bk": np.asarray(inputs["L0_attn_bk"], np.float32),
        "wv": np.asarray(inputs["L0_attn_Wv"], np.float32), "bv": np.asarray(inputs["L0_attn_bv"], np.float32),
        "wo": np.asarray(inputs["L0_attn_Wo"], np.float32), "bo": np.asarray(inputs["L0_attn_bo"], np.float32),
        "gam_t": gam_t, "bet_t": bet_t, "lam64": lam64, "c_selz": C["selz"],
        "c_cid16": C["cid16"], "c_cid32": C["cid32"], "c_ones128": C["ones128"], "c_onesrow": C["onesrow"],
    }
    in_maps = []
    for c in range(8):
        b = c // 2
        own, oth = _core_rows(c)
        m = dict(shared)
        m["x_own"] = np.ascontiguousarray(X[b][own])
        m["x_oth"] = np.ascontiguousarray(X[b][oth])
        in_maps.append(m)
    res = run_bass_kernel_spmd(nc, in_maps, core_ids=list(range(8)))
    return [r["xb0"] for r in res.results]


def kernel(**inputs):
    xb = run_block0(inputs)
    xb = run_block1(inputs, xb)
    xb = run_block2(inputs, xb)
    xb = run_block3(inputs, xb)
    out = np.zeros((4, S_FULL, D), np.float32)
    for c in range(8):
        b = c // 2
        own, _ = _core_rows(c)
        out[b][own] = xb[c]
    return out



def build_l0_ffn(nc, s, B, xprev, xout):
    din = {"xprev": xprev}
    for nm, shp, dt in [("w_fc", [4096, D], F32),
                        ("ln_w", [4096], F32), ("ln_b", [4096], F32),
                        ("w_proj", [D, 4096], F32)]:
        din[nm] = nc.dram_tensor(nm, shp, dt, kind="ExternalInput").ap()
    wqfcT = nc.dram_tensor("wqfcT", [256, 16384], BF16, kind="Internal").ap()
    wpT = nc.dram_tensor("wpT", [256, 16384], BF16, kind="Internal").ap()

    gblk, gph = [], []

    def sb(name, shape, dtype, scope=None):
        g = nc.sbuf_tensor(name, shape, dtype)
        (gph if scope is None else scope).append(g)
        return g.__enter__()

    def ps(name, shape, dtype=F32):
        g = nc.psum_tensor(name, shape, dtype)
        gph.append(g)
        return g.__enter__()

    def close(lst=None):
        lst = gph if lst is None else lst
        for g in reversed(lst):
            g.__exit__(None, None, None)
        lst.clear()

    xn = sb("xn1", [128, DC * NT], BF16, scope=gblk)
    d1dram = nc.dram_tensor("d1dram", [D, NT], F32, kind="Internal").ap()

    B.tps = [(ps(f"tpF{i}", [128, 128]), f"tpF{i}") for i in range(4)]
    pq = ps("pq", [128, 512])
    emit_load_ln_fm(s, B, din["xprev"], xn, "xn1", lambda dc, tt: dc * NT + tt * 128)
    onescol = sb("onescol", [128, 1], F32)
    ones1f = sb("ones1f", [1, 128], F32)
    s.add("dve", lambda v: v.memset(onescol[:], 1.0), writes=["onescol"], tag="oc")
    s.add("dve", lambda v: v.memset(ones1f[:], 1.0), writes=["ones1f"], tag="o1f")

    for wi, (wnm, nrow_t, ncol) in enumerate([("w_fc", 32, 1024), ("w_proj", 8, 4096)]):
        gwi = []
        wt = [sb(f"wt{wi}_{i}", [128, ncol], F32, scope=gwi) for i in range(2)]
        g1t = [sb(f"g1t{wi}_{i}", [128, ncol], BF16, scope=gwi) for i in range(2)]
        g2t = [sb(f"g2t{wi}_{i}", [128, ncol], BF16, scope=gwi) for i in range(2)]
        tb = [sb(f"tb{wi}_{i}", [128, 512], BF16, scope=gwi) for i in range(2)]
        cols = sb(f"cols{wi}", [128, nrow_t], F32, scope=gwi)
        rowt = sb(f"rowt{wi}", [128, 1], F32, scope=gwi)
        nthr = sb(f"nthr{wi}", [128, 1], F32, scope=gwi)
        pthr = sb(f"pthr{wi}", [128, 1], F32, scope=gwi)
        for i in range(nrow_t):
            w_, wk_ = wt[i % 2], f"wt{wi}_{i % 2}"
            s.dma("sp", lambda q, t=w_, i=i, a=din[wnm]: q.dma_start(t[:], a[i * 128:(i + 1) * 128, :]),
                  reads=[wk_], writes=[wk_], tag="qld")
            s.add("dve", lambda v, t=w_, i=i, c=cols: v.reduce_sum(c[:, i:i + 1], t[:], axis=AX.X,
                                                                   apply_absolute_value=True),
                  reads=[wk_], writes=[f"cols{wi}"], tag="qabs")
        s.add("dve", lambda v, c=cols, r=rowt: v.reduce_sum(r[:], c[:], axis=AX.X),
              reads=[f"cols{wi}"], writes=[f"rowt{wi}"], tag="qrow")
        s.add("pe", lambda t, r=rowt: t.matmul(pq[0:1, 0:1], r[:], onescol[:], start=True, stop=True),
              reads=[f"rowt{wi}", "onescol"], writes=["pq"], tag="qtot")
        nel = float(nrow_t * 128 * ncol)
        nthr0 = sb(f"nthr0_{wi}", [1, 1], F32, scope=gwi)
        s.add("act", lambda sc, o=nthr0, sc_=-0.5 / nel: sc.activation(o[:], pq[0:1, 0:1], AF.Identity, scale=sc_),
              reads=["pq"], writes=[f"nthr0k{wi}"], tag="qthr")
        s.add("pe", lambda t, r=nthr0: t.matmul(pq[:, 32:33], ones1f[:], r[:], start=True, stop=True),
              reads=[f"nthr0k{wi}", "ones1f"], writes=["pq3"], tag="qbc")
        s.add("dve", lambda v, n=nthr: v.tensor_scalar(n[:], pq[:, 32:33], -0.5e-6, None, op0=ALU.add),
              reads=["pq3"], writes=[f"nthr{wi}"], tag="qnth")
        s.add("dve", lambda v, n=nthr, p=pthr: v.tensor_scalar(p[:], n[:], -1.0, None, op0=ALU.mult),
              reads=[f"nthr{wi}"], writes=[f"pthr{wi}"], tag="qpth")
        for i in range(nrow_t):
            w_, wk_ = wt[i % 2], f"wt{wi}_{i % 2}"
            g1_, g1k = g1t[i % 2], f"g1t{wi}_{i % 2}"
            g2_, g2k = g2t[i % 2], f"g2t{wi}_{i % 2}"
            q_, qk_ = g1_, g1k
            s.dma("sp", lambda q, t=w_, i=i, a=din[wnm]: q.dma_start(t[:], a[i * 128:(i + 1) * 128, :]),
                  reads=[wk_], writes=[wk_], tag="qld2")
            s.add("dve", lambda v, t=w_, o=g1_, th=pthr: v.tensor_scalar(o[:], t[:], th[:], None, op0=ALU.is_gt),
                  reads=[wk_], sreads=[f"pthr{wi}"], writes=[g1k], tag="qgt")
            s.add("dve", lambda v, t=w_, o=g2_, th=nthr: v.tensor_scalar(o[:], t[:], th[:], None, op0=ALU.is_lt),
                  reads=[wk_], sreads=[f"nthr{wi}"], writes=[g2k], tag="qlt")
            s.add("dve", lambda v, a=g1_, b=g2_: v.tensor_tensor(a[:], a[:], b[:], op=ALU.subtract),
                  reads=[g1k, g2k], writes=[g1k], tag="qsub")
            ngrp = ncol // 512
            for grp in range(ngrp):
                t_, tk_ = tb[grp % 2], f"tb{wi}_{grp % 2}"
                for e4 in range(4):
                    pt_, ptk = B.tps[e4 % 4]
                    ec = grp * 4 + e4
                    s.add("pe", lambda t, p=pt_, q2=q_, ec=ec:
                          t.matmul(p[:], q2[:, ec * 128:(ec + 1) * 128], B.cid16[:], start=True, stop=True),
                          reads=[qk_, "cid16"], writes=[ptk], tag="qT")
                    eng = "dve" if e4 % 2 == 0 else "act"
                    if eng == "dve":
                        s.add("dve", lambda v, p=pt_, o=t_, e4=e4: v.tensor_copy(o[:, e4 * 128:(e4 + 1) * 128], p[:]),
                              reads=[ptk], writes=[tk_], tag="qTe")
                    else:
                        s.add("act", lambda sc, p=pt_, o=t_, e4=e4: sc.copy(o[:, e4 * 128:(e4 + 1) * 128], p[:]),
                              reads=[ptk], writes=[tk_], tag="qTe")
                if wi == 0:
                    dst = wqfcT.rearrange("(ec hc) n -> ec hc n", hc=32)[grp * 4:(grp + 1) * 4, i, :]
                else:
                    dst = wpT.rearrange("(hc dc) n -> hc dc n", dc=8)[grp * 4:(grp + 1) * 4, i, :]
                s.dma("pool", lambda q, t=t_, d=dst.rearrange("e (p c) -> p e c", p=128):
                      q.dma_start(d, t[:].rearrange("p (e c) -> p e c", e=4)),
                      reads=[tk_], writes=[f"wdram{wi}"], tag="qst")
        s.barrier(f"endQ{wi}")
        close(gwi)
    s.barrier("endQ")
    close()

    ph = [ps(f"ph{i}", [128, 512]) for i in range(2)]
    ps1 = ps("ps1", [128, 512])
    ps2 = ps("ps2", [128, 512])
    pm = [ps(f"pm{i}", [128, 512]) for i in range(2)]
    hsb2 = [sb(f"hsb{i}", [128, 16384], BF16) for i in range(2)]
    dt1 = [sb(f"dt1_{i}", [128, 512], F32) for i in range(2)]

    lw = [sb(f"lw{i}", [128, 1024], BF16) for i in range(3)]
    lw2 = [sb(f"lw2_{i}", [128, 4096], BF16) for i in range(2)]
    sqs = [sb(f"hsq{i}", [128, 512], BF16) for i in range(2)]
    mu = sb("mu", [128, 512], F32)
    mu2 = sb("mu2", [128, 512], F32)
    var = sb("var", [128, 512], F32)
    rstd = sb("rstd", [128, 512], F32)
    mr = sb("mr", [128, 512], F32)
    tmv = [sb(f"tmv{i}", [128, 512], F32) for i in range(2)]
    lnwn = sb("lnwn", [32, 128], F32)
    lnbn = sb("lnbn", [32, 128], F32)
    lnw = sb("lnw", [128, 32], F32)
    lnb = sb("lnb", [128, 32], F32)
    s.dma("sp", lambda q: q.dma_start(lnwn[:], din["ln_w"].rearrange("(a b) -> a b", a=32)),
          writes=["lnwn"], tag="ldlnw")
    s.dma("sp", lambda q: q.dma_start(lnbn[:], din["ln_b"].rearrange("(a b) -> a b", a=32)),
          writes=["lnbn"], tag="ldlnb")
    s.add("pe", lambda t: t.matmul(ph[0][:, 0:32], lnwn[:], B.cid32[0:32, 0:32], start=True, stop=True),
          reads=["lnwn", "cid32"], writes=["ph0"], tag="lnwT")
    s.add("dve", lambda v: v.tensor_copy(lnw[:], ph[0][:, 0:32]), reads=["ph0"], writes=["lnwk"], tag="lnwTe")
    s.add("pe", lambda t: t.matmul(ph[0][:, 0:32], lnbn[:], B.cid32[0:32, 0:32], start=True, stop=True),
          reads=["lnbn", "cid32"], writes=["ph0"], tag="lnbT")
    s.add("dve", lambda v: v.tensor_copy(lnb[:], ph[0][:, 0:32]), reads=["ph0"], writes=["lnbk"], tag="lnbTe")

    for nt in range(4):
        hsb, hsbk = hsb2[nt % 2], f"hsb{nt % 2}"
        for hc in range(32):
            w_, wk_ = lw[hc % 3], f"lw{hc % 3}"
            src = wqfcT.rearrange("(ec hc) n -> ec hc n", hc=32)[:, hc, :]
            s.dma("sp", lambda q, t=w_, sa=src.rearrange("e (p c) -> p e c", p=128):
                  q.dma_start(t[:].rearrange("p (e c) -> p e c", e=8), sa),
                  reads=[wk_], writes=[wk_], tag="m1w")
            p_, pk_ = ph[hc % 2], f"ph{hc % 2}"
            for ec in range(8):
                s.add("pe", lambda t, p=p_, w=w_, ec=ec, nt=nt:
                      t.matmul(p[:], w[:, ec * 128:(ec + 1) * 128],
                               xn[:, ec * NT + nt * 512:ec * NT + nt * 512 + 512],
                               start=(ec == 0), stop=(ec == 7)),
                      reads=[wk_, "xn1"], writes=[pk_], tag="m1")
            s.add("act", lambda sc, p=p_, hc=hc, hsb=hsb: sc.activation(hsb[:, hc * 512:(hc + 1) * 512], p[:], AF.Gelu),
                  reads=[pk_], writes=[f"{hsbk}:{hc}"], tag="gelu")
            sq_, sqk_ = sqs[hc % 2], f"hsq{hc % 2}"
            s.add("act", lambda sc, o=sq_, hc=hc, hsb=hsb: sc.activation(o[:], hsb[:, hc * 512:(hc + 1) * 512], AF.Square),
                  reads=[f"{hsbk}:{hc}"], writes=[sqk_], tag="hsqr")
            s.add("pe", lambda t, hc=hc, hsb=hsb: t.matmul(ps1[:], B.ones128[:], hsb[:, hc * 512:(hc + 1) * 512],
                                                           start=(hc == 0), stop=(hc == 31)),
                  reads=[f"{hsbk}:{hc}", "ones128"], writes=["ps1"], tag="st1")
            s.add("pe", lambda t, q2=sq_, hc=hc: t.matmul(ps2[:], B.ones128[:], q2[:],
                                                          start=(hc == 0), stop=(hc == 31)),
                  reads=[sqk_, "ones128"], writes=["ps2"], tag="st2")
        s.add("act", lambda sc: sc.activation(mu[:], ps1[:], AF.Identity, scale=1.0 / 4096.0),
              reads=["ps1"], writes=["muk"], tag="lmu")
        s.add("act", lambda sc: sc.activation(mu2[:], mu[:], AF.Square),
              reads=["muk"], writes=["mu2k"], tag="lmu2")
        s.add("dve", lambda v: v.scalar_tensor_tensor(var[:], ps2[:], 1.0 / 4096.0, mu2[:],
                                                      op0=ALU.mult, op1=ALU.subtract),
              reads=["ps2", "mu2k"], writes=["vark"], tag="lvar")
        s.add("dve", lambda v: v.tensor_scalar(var[:], var[:], float(EPS), None, op0=ALU.add),
              reads=["vark"], writes=["vark"], tag="lveps")
        s.add("act", lambda sc: sc.activation(var[:], var[:], AF.Ln),
              reads=["vark"], writes=["vark"], tag="lln")
        s.add("act", lambda sc: sc.activation(rstd[:], var[:], AF.Exp, scale=-0.5),
              reads=["vark"], writes=["rstdk"], tag="lexp")
        s.add("dve", lambda v: v.tensor_tensor(mr[:], mu[:], rstd[:], op=ALU.mult),
              reads=["muk", "rstdk"], writes=["mrk"], tag="lmr")
        for hc in range(32):
            t_, tk_ = tmv[hc % 2], f"tmv{hc % 2}"
            s.add("dve", lambda v, o=t_, hc=hc, hsb=hsb: v.tensor_tensor(o[:], hsb[:, hc * 512:(hc + 1) * 512], rstd[:],
                                                                         op=ALU.mult),
                  reads=[f"{hsbk}:{hc}", "rstdk"], writes=[tk_], tag="ln1")
            s.add("pool", lambda g, o=t_: g.tensor_tensor(o[:], o[:], mr[:], op=ALU.subtract),
                  reads=[tk_, "mrk"], writes=[tk_], tag="ln2")
            s.add("act", lambda sc, t2=t_, hc=hc, hsb=hsb:
                  sc.activation(hsb[:, hc * 512:(hc + 1) * 512], t2[:], AF.Identity,
                                scale=lnw[:, hc:hc + 1], bias=lnb[:, hc:hc + 1]),
                  reads=[tk_], sreads=["lnwk", "lnbk"], writes=[f"{hsbk}:{hc}"], tag="ln3")
        for dc in range(8):
            w_, wk_ = lw2[dc % 2], f"lw2_{dc % 2}"
            src = wpT.rearrange("(hc dc) n -> hc dc n", dc=8)[:, dc, :]
            s.dma("sp", lambda q, t=w_, sa=src.rearrange("h (p c) -> p h c", p=128):
                  q.dma_start(t[:].rearrange("p (h c) -> p h c", h=32), sa),
                  reads=[wk_], writes=[wk_], tag="m2w")
            p_, pk_ = pm[dc % 2], f"pm{dc % 2}"
            for hc in range(32):
                s.add("pe", lambda t, p=p_, w=w_, hc=hc, hsb=hsb:
                      t.matmul(p[:], w[:, hc * 128:(hc + 1) * 128], hsb[:, hc * 512:(hc + 1) * 512],
                               start=(hc == 0), stop=(hc == 31)),
                      reads=[wk_, f"{hsbk}:{hc}"], writes=[pk_], tag="m2")
            d_, dk_ = dt1[dc % 2], f"dt1_{dc % 2}"
            s.add("dve", lambda v, p=p_, o=d_: v.tensor_copy(o[:], p[:]),
                  reads=[pk_], writes=[dk_], tag="m2e")
            s.dma("sp", lambda q, t=d_, dc=dc, nt=nt:
                  q.dma_start(d1dram.rearrange("(dc p) t -> dc p t", p=128)[dc, :, nt * 512:(nt + 1) * 512], t[:]),
                  reads=[dk_], writes=["d1dram"], tag="m2st")
    s.barrier("endM")
    close()

    B.tps = [(ps(f"tpE{i}", [128, 128]), f"tpE{i}") for i in range(4)]
    dl_rot1 = [sb(f"dl1_{i}", [128, 1024], F32) for i in range(2)]
    for tt in range(16):
        dlb, dlk = dl_rot1[tt % 2], f"dl1_{tt % 2}"
        s.dma("sp", lambda q, t=dlb, tt=tt:
              q.dma_start(t[:].rearrange("p (dc c) -> p dc c", dc=8),
                          d1dram.rearrange("(dc p) t -> p dc t", p=128)[:, :, tt * 128:(tt + 1) * 128]),
              reads=[dlk, "d1dram"], writes=[dlk], tag="dlld1")
        xt, xtk = B.xt_rot[B.xt_i % len(B.xt_rot)]
        B.xt_i += 1
        s.dma("sp", lambda q, t=xt, a=din["xprev"], tt=tt: q.dma_start(t[:], a[tt * 128:(tt + 1) * 128, :]),
              reads=[xtk], writes=[xtk], tag="ldxe1")
        ob, obk = B.out_rot[B.out_i % len(B.out_rot)]
        B.out_i += 1
        for dc in range(DC):
            pt_, ptk = B.tps[dc % 4]
            s.add("pe", lambda t, p=pt_, d=dlb, dc=dc:
                  t.matmul(p[:], d[:, dc * 128:(dc + 1) * 128], B.cid32[:], start=True, stop=True),
                  reads=[dlk, "cid32"], writes=[ptk], tag="dT1")
            s.add("dve", lambda v, p=pt_, o=ob, x=xt, dc=dc:
                  v.tensor_tensor(o[:, dc * 128:(dc + 1) * 128], p[:], x[:, dc * 128:(dc + 1) * 128], op=ALU.add),
                  reads=[ptk, xtk], writes=[obk], tag="dadd1")
        s.dma("sp", lambda q, t=ob, tt=tt: q.dma_start(xout[tt * 128:(tt + 1) * 128, :], t[:]),
              reads=[obk], writes=[f"outx1_{tt}"], tag="stx1")
    s.add("sp", lambda q: q.nop(), reads=[f"outx1_{t}" for t in range(16)], tag="fin")
    close()
    close(gblk)
    return din


_B1_CACHE = {}


def build_b1_program():
    if "nc" not in _B1_CACHE:
        nc, s, B = _mk_sched()
        xin = nc.dram_tensor("xprev", [NT, D], F32, kind="ExternalInput").ap()
        xout = nc.dram_tensor("xb1", [NT, D], F32, kind="ExternalOutput").ap()
        build_l0_ffn(nc, s, B, xin, xout)
        _B1_CACHE["nc"] = _finish(nc, s)
        _B1_CACHE["stats"] = s.stats()
    return _B1_CACHE["nc"]


def run_block1(inputs, x_cur):
    C = _host_consts()
    nc = build_b1_program()
    shared = {
        "w_fc": np.asarray(inputs["L0_ffn_W_fc"], np.float32),
        "ln_w": np.asarray(inputs["L0_ffn_ln_w"], np.float32),
        "ln_b": np.asarray(inputs["L0_ffn_ln_b"], np.float32),
        "w_proj": np.asarray(inputs["L0_ffn_W_proj"], np.float32),
        "c_cid16": C["cid16"], "c_cid32": C["cid32"], "c_ones128": C["ones128"], "c_onesrow": C["onesrow"],
        "c_selz": C["selz"],
    }
    in_maps = []
    for c in range(8):
        m = dict(shared)
        m["xprev"] = np.ascontiguousarray(x_cur[c])
        in_maps.append(m)
    res = run_bass_kernel_spmd(nc, in_maps, core_ids=list(range(8)))
    return [r["xb1"] for r in res.results]



def build_l1_attn(nc, s, B, xprev, xout):
    din = {"xprev": xprev}
    for nm, shp, dt in [("wq1", [D, D], F32), ("wk1", [D, D], F32),
                        ("wv1", [D, D], F32), ("wo1", [D, D], F32),
                        ("betas8", [8, 64], F32), ("m08", [8, 64, 64], F32), ("z08", [8, 64], F32),
                        ("c_masks", [128, 2048], BF16)]:
        din[nm] = nc.dram_tensor(nm, shp, dt, kind="ExternalInput").ap()
    ddram = nc.dram_tensor("d2dram", [D, NT], F32, kind="Internal").ap()

    gblk, gph = [], []

    def sb(name, shape, dtype, scope=None):
        g = nc.sbuf_tensor(name, shape, dtype)
        (gph if scope is None else scope).append(g)
        return g.__enter__()

    def ps(name, shape, dtype=F32):
        g = nc.psum_tensor(name, shape, dtype)
        gph.append(g)
        return g.__enter__()

    def close(lst=None):
        lst = gph if lst is None else lst
        for g in reversed(lst):
            g.__exit__(None, None, None)
        lst.clear()

    xn = sb("xn2", [128, DC * NT], BF16, scope=gblk)

    B.tps = [(ps(f"tpL{i}", [128, 128]), f"tpL{i}") for i in range(4)]
    emit_load_ln_fm(s, B, din["xprev"], xn, "xn2", lambda dc, tt: dc * NT + tt * 128)
    s.barrier("endA2")
    close()

    pg = [ps(f"pg{i}", [128, 512]) for i in range(2)]
    tp2 = [(ps(f"tpT{i}", [128, 128]), f"tpT{i}") for i in range(2)]
    pO = ps("pO", [65, 512])
    pAm = ps("pAm", [65, 512])
    pD = ps("pD", [64, 128])
    pb = ps("pb", [64, 512])

    wo_sb = sb("wo1_sb", [128, 8192], BF16)
    wsl = {nm: sb(f"wsl_{nm}", [128, 8192], BF16) for nm in ("q", "k", "v")}
    for nm, wdram in (("q", "wq1"), ("k", "wk1"), ("v", "wv1")):
        s.dma("pool", lambda q, t=wsl[nm], a=din[wdram].rearrange("(dc p) c -> p dc c", p=128):
              q.dma_start(t[:].rearrange("p (dc c) -> p dc c", dc=8), a),
              writes=[f"wsl{nm}"], tag="ldwsl")
    qT = sb("qT", [64, 4096], F32)
    qTb = sb("qTb", [64, 4096], BF16)
    kTb = sb("kTb", [64, 4096], BF16)
    vT = sb("vT", [64, 4096], BF16)
    k_h = [sb(f"k_h{i}", [128, 256], BF16) for i in range(2)]
    v_h = [sb(f"v_h{i}", [128, 4 * 65], BF16) for i in range(2)]
    skt = [sb(f"skt{i}", [128, 256], F32) for i in range(2)]
    sk = [sb(f"sk{i}", [128, 256], BF16) for i in range(2)]
    sqt = [sb(f"sqt{i}", [64, 512], F32) for i in range(2)]
    sqf = [sb(f"sqf{i}", [64, 512], F32) for i in range(2)]
    eb = [sb(f"ebL{i}", [128, 512], BF16) for i in range(2)]
    ustA = [sb(f"ustA{i}", [64, 512], BF16) for i in range(2)]
    ustB = [sb(f"ustB{i}", [64, 512], BF16) for i in range(2)]
    zb1 = [sb(f"zb1d{i}", [66, 512], F32) for i in range(2)]
    sgnb = [sb(f"sgnbd{i}", [66, 512], BF16) for i in range(2)]
    izb1 = [sb(f"izb1d{i}", [66, 512], BF16) for i in range(2)]
    ct1 = [sb(f"ct1L{i}", [64, 512], BF16) for i in range(2)]
    ct2 = [sb(f"ct2L{i}", [64, 512], BF16) for i in range(2)]
    attT = [sb(f"attT{i}", [64, 512], BF16) for i in range(2)]
    preT = sb("preT", [128, 2048], BF16)
    dtile = [sb("dtile_s", [128, 256], F32)] * 2
    masks = sb("masks", [128, 2048], BF16)
    MZ = sb("MZ", [64, 8 * 65], F32)
    betn = sb("betn", [8, 64], F32)
    z0n = sb("z0n", [8, 64], F32)
    gat = sb("gat", [64, 8], F32)
    ogat = sb("ogat", [64, 8], F32)

    s.dma("pool", lambda q: q.dma_start(wo_sb[:].rearrange("p (cc d) -> p cc d", cc=8),
                                        din["wo1"].rearrange("(cc p) d -> p cc d", p=128)),
          writes=["wo_sb"], tag="ldwo1")
    s.dma("sp", lambda q: q.dma_start(masks[:], din["c_masks"]), writes=["masks"], tag="ldmask")
    s.dma("sp", lambda q: q.dma_start(betn[:], din["betas8"]), writes=["betn"], tag="ldbet")
    s.dma("sp", lambda q: q.dma_start(z0n[:], din["z08"]), writes=["z0n"], tag="ldz0")
    s.dma("sp", lambda q: q.dma_start(MZ[:].rearrange("d (h c) -> d h c", h=8)[:, :, 0:64],
                                      din["m08"].rearrange("h d v -> d h v")),
          writes=["MZ"], tag="ldm0")
    s.add("pe", lambda t: t.matmul(pg[0][0:64, 0:8], z0n[:], B.cid32[0:8, 0:8], start=True, stop=True),
          reads=["z0n", "cid32"], writes=["pg0"], tag="z0T")
    s.add("dve", lambda v: v.tensor_copy(MZ[:].rearrange("d (h c) -> d h c", h=8)[:, :, 64:65],
                                         pg[0][0:64, 0:8].rearrange("d (h o) -> d h o", o=1)),
          reads=["pg0"], writes=["MZ"], tag="z0Te")
    s.add("pe", lambda t: t.matmul(pg[0][0:64, 64:72], betn[:], B.cid32[0:8, 0:8], start=True, stop=True),
          reads=["betn", "cid32"], writes=["pg0"], tag="betT")
    s.add("act", lambda sc: sc.activation(gat[:], pg[0][0:64, 64:72], AF.Exp, scale=-1.0),
          reads=["pg0"], writes=["gat"], tag="gexp")
    s.add("dve", lambda v: v.tensor_scalar(gat[:], gat[:], 1.0, None, op0=ALU.add),
          reads=["gat"], writes=["gat"], tag="gadd")
    s.add("dve", lambda v: v.reciprocal(gat[:], gat[:]), reads=["gat"], writes=["gat"], tag="ginv")
    s.add("dve", lambda v: v.tensor_scalar(ogat[:], gat[:], -1.0, 1.0, op0=ALU.mult, op1=ALU.add),
          reads=["gat"], writes=["ogat"], tag="gneg")
    for i in range(2):
        s.add("dve", lambda v, i=i: v.memset(zb1[i][:], 1.0), writes=[f"zb1k{i}"], tag="zb1i")
    for i in range(2):
        s.add("dve", lambda v, i=i: v.memset(v_h[i][:].rearrange("p (l c) -> p l c", l=4)[:, :, 64:65], 1.0),
              writes=[f"v_h{i}"], tag="vhones")

    for seg in range(NSEG):
        for jp in range(8):
            for tn, (dst, dstk, eng) in {
                    "q": (qT, "qTk", "dve"),
                    "k": (kTb, "kTbk", "act"),
                    "v": (vT, "vTk", "dve")}.items():
                w_, wk_ = wsl[tn], f"wsl{tn}"
                p_, pk_ = pg[jp % 2], f"pg{jp % 2}"
                for dc in range(DC):
                    s.add("pe", lambda t, p=p_, w=w_, dc=dc, seg=seg, jp=jp:
                          t.matmul(p[:, 0:256], w[:, dc * 1024 + jp * 128: dc * 1024 + (jp + 1) * 128],
                                   xn[:, dc * NT + seg * 256: dc * NT + (seg + 1) * 256],
                                   start=(dc == 0), stop=(dc == 7)),
                          reads=[wk_, "xn2"], writes=[pk_], tag="jmm")
                for jj in range(2):
                    j = 2 * jp + jj

                    def evac(e, p=p_, jj=jj, j=j, dst=dst, eng=eng):
                        src = p[64 * jj:64 * jj + 64, 0:256].rearrange("d (h r) -> d h r", h=8)
                        dd = dst[:].rearrange("d (h l) -> d h l", h=8)[:, :, 32 * j: 32 * j + 32]
                        return e.tensor_copy(dd, src) if eng == "dve" else e.copy(dd, src)
                    s.add(eng, evac, reads=[pk_], writes=[dstk], tag=f"jev{tn}")
        s.add("dve", lambda v: v.tensor_copy(qTb[:], qT[:]), reads=["qTk"], writes=["qTbk"], tag="qTb")
        pending_tail = [None]
        for hp in range(8):
            hb = hp % 2
            kh, khk = k_h[hb], f"k_h{hb}"
            vh, vhk = v_h[hb], f"v_h{hb}"
            skb, skk = sk[hb], f"sk{hb}"
            sqb, sqk = sqf[hb], f"sqf{hb}"
            uA_, uAk = ustA[hb], f"ustA{hb}"
            uB_, uBk = ustB[hb], f"ustB{hb}"
            zb_, zbk = zb1[hb], f"zb1k{hb}"
            sg_, sgk = sgnb[hb], f"sgnbk{hb}"
            iz_, izk = izb1[hb], f"izb1k{hb}"
            c1_, c1k = ct1[hb], f"ct1L{hb}"
            c2_, c2k = ct2[hb], f"ct2L{hb}"
            aT_, aTk = attT[hb], f"attT{hb}"
            sk0, sk0k = skt[0], "skt0"
            sk1, sk1k = skt[1], "skt1"
            sq0, sq0k = sqt[0], "sqt0"
            sq1, sq1k = sqt[1], "sqt1"
            for lt in range(4):
                pt_, ptk = tp2[lt % 2]
                s.add("pe", lambda t, p=pt_, hp=hp, lt=lt:
                      t.matmul(p[:, 0:64], kTb[0:64, hp * 512 + lt * 128: hp * 512 + (lt + 1) * 128],
                               B.cid16[0:64, 0:64], start=True, stop=True),
                      reads=["kTbk", "cid16"], writes=[ptk], tag="kT")
                s.add("dve", lambda e, p=pt_, lt=lt, kh=kh: e.tensor_copy(kh[:, lt * 64:(lt + 1) * 64], p[:, 0:64]),
                      reads=[ptk], writes=[khk], tag="kTe")
            for lt in range(4):
                pt_, ptk = tp2[lt % 2]
                s.add("pe", lambda t, p=pt_, hp=hp, lt=lt:
                      t.matmul(p[:, 64:128], vT[0:64, hp * 512 + lt * 128: hp * 512 + (lt + 1) * 128],
                               B.cid16[0:64, 0:64], start=True, stop=True),
                      reads=["vTk", "cid16"], writes=[ptk], tag="vT")
                s.add("act", lambda e, p=pt_, lt=lt, vh=vh: e.copy(vh[:, lt * 65: lt * 65 + 64], p[:, 64:128]),
                      reads=[ptk], writes=[vhk], tag="vTe")
            s.add("dve", lambda v, o=sk0, kh=kh: v.tensor_scalar(o[:], kh[:], 0.0, None, op0=ALU.min),
                  reads=[khk], writes=[sk0k], tag="sk1")
            s.add("act", lambda sc, o=sk1, i=sk0: sc.activation(o[:], i[:], AF.Exp),
                  reads=[sk0k], writes=[sk1k], tag="sk2")
            s.add("dve", lambda g, o=sk0, kh=kh: g.tensor_relu(o[:], kh[:]),
                  reads=[khk], writes=[sk0k], tag="sk3")
            s.add("dve", lambda v, o=skb, a=sk1, b=sk0: v.tensor_tensor(o[:], a[:], b[:], op=ALU.add),
                  reads=[sk0k, sk1k], writes=[skk], tag="sk4")
            s.add("dve", lambda v, o=sq0, hp=hp: v.tensor_scalar(o[:], qT[:, hp * 512:(hp + 1) * 512], 0.0,
                                                                 None, op0=ALU.min),
                  reads=["qTk"], writes=[sq0k], tag="sq1")
            s.add("act", lambda sc, o=sq1, i=sq0: sc.activation(o[:], i[:], AF.Exp),
                  reads=[sq0k], writes=[sq1k], tag="sq2")
            s.add("dve", lambda g, o=sq0, hp=hp: g.tensor_relu(o[:], qT[:, hp * 512:(hp + 1) * 512]),
                  reads=["qTk"], writes=[sq0k], tag="sq3")
            s.add("dve", lambda v, o=sqb, a=sq1, b=sq0: v.tensor_tensor(o[:], a[:], b[:], op=ALU.add),
                  reads=[sq0k, sq1k], writes=[sqk], tag="sq4")

            def emit_sL(kt, hp=hp):
                p_, pk_ = pg[kt % 2], f"pg{kt % 2}"
                s.add("pe", lambda t, p=p_, hp=hp, kt=kt:
                      t.matmul(p[:], kTb[0:64, hp * 512 + kt * 128: hp * 512 + (kt + 1) * 128],
                               qTb[0:64, hp * 512:(hp + 1) * 512], start=True, stop=True),
                      reads=["kTbk", "qTbk"], writes=[pk_], tag="sL")

            def emit_pvL(kt, vh=vh, vhk=vhk):
                p_, pk_ = pg[kt % 2], f"pg{kt % 2}"
                e_, ek_ = eb[kt % 2], f"ebL{kt % 2}"
                s.add("act", lambda sc, o=e_, p=p_: sc.activation(o[:], p[:], AF.Exp, scale=0.125),
                      reads=[pk_], writes=[ek_], tag="eL")
                s.add("dve", lambda g, e2=e_, kt=kt:
                      g.tensor_tensor(e2[:], e2[:], masks[:, kt * 512:(kt + 1) * 512], op=ALU.mult),
                      reads=[ek_, "masks"], writes=[ek_], tag="mL")
                s.add("pe", lambda t, m2=e_, kt=kt, vh=vh:
                      t.matmul(pO[:], vh[:, kt * 65:(kt + 1) * 65], m2[:],
                               start=(kt == 0), stop=(kt == 3)),
                      reads=[vhk, ek_], writes=["pO"], tag="pvL")
            if pending_tail[0] is not None:
                pending_tail[0]()
                pending_tail[0] = None
            emit_sL(0)
            emit_sL(1)
            emit_pvL(0)
            emit_sL(2)
            emit_pvL(1)
            emit_sL(3)
            emit_pvL(2)
            emit_pvL(3)
            s.add("pe", lambda t, hp=hp, sqb=sqb: t.matmul(pAm[:], MZ[:, hp * 65:(hp + 1) * 65], sqb[:],
                                                           start=True, stop=True),
                  reads=["MZ", sqk], writes=["pAm"], tag="amm")
            for lt in range(4):
                s.add("pe", lambda t, lt=lt, skb=skb, vh=vh: t.matmul(pD[:, 0:65], skb[:, lt * 64:(lt + 1) * 64],
                                                                     vh[:, lt * 65:(lt + 1) * 65],
                                                                     start=(lt == 0), stop=(lt == 3)),
                      reads=[skk, vhk], writes=["pD"], tag="dmz")
            s.add("act", lambda sc, o=uA_: sc.copy(o[:], pO[0:64, :]), reads=["pO"], writes=[uAk], tag="uA")
            s.add("dve", lambda v, o=zb_: v.tensor_copy(o[0:1, :], pO[64:65, :]), reads=["pO"], writes=[zbk], tag="z0r")
            s.add("dve", lambda v, o=uB_: v.tensor_copy(o[:], pAm[0:64, :]), reads=["pAm"], writes=[uBk], tag="uB")
            s.add("act", lambda sc, o=zb_: sc.copy(o[64:65, :], pAm[64:65, :]), reads=["pAm"], writes=[zbk], tag="z64r")
            s.add("dve", lambda v, hp=hp: v.tensor_tensor(MZ[:, hp * 65:(hp + 1) * 65],
                                                          MZ[:, hp * 65:(hp + 1) * 65], pD[:, 0:65], op=ALU.add),
                  reads=["MZ", "pD"], writes=["MZ"], tag="mzup")
            def mk_tail(hp=hp, hb=hb, zb_=zb_, zbk=zbk, sg_=sg_, sgk=sgk, iz_=iz_, izk=izk,
                        uA_=uA_, uAk=uAk, uB_=uB_, uBk=uBk, c1_=c1_, c1k=c1k, c2_=c2_, c2k=c2k,
                        aT_=aT_, aTk=aTk):
                def emit_tail():
                    s.add("act", lambda sc, o=sg_, i=zb_: sc.activation(o[:], i[:], AF.Sign),
                          reads=[zbk], writes=[sgk], tag="sgnL")
                    s.add("dve", lambda v, o=zb_, b=sg_: v.tensor_tensor(o[:], o[:], b[:], op=ALU.mult),
                          reads=[zbk, sgk], writes=[zbk], tag="absL")
                    s.add("act", lambda sc, o=zb_: sc.activation(o[:], o[:], AF.Ln), reads=[zbk], writes=[zbk], tag="lnzL")
                    s.add("act", lambda sc, o=zb_: sc.activation(o[:], o[:], AF.Exp, scale=-1.0),
                          reads=[zbk], writes=[zbk], tag="expzL")
                    s.add("dve", lambda v, o=iz_, a=zb_, b=sg_: v.tensor_tensor(o[:], a[:], b[:], op=ALU.mult),
                          reads=[zbk, sgk], writes=[izk], tag="izbL")
                    s.add("pe", lambda t, i=iz_: t.matmul(pb[:], B.selz[0:66, 0:64], i[:], start=True, stop=True),
                          reads=[izk, "selz"], writes=["pb"], tag="bzL")
                    s.add("dve", lambda v, o=c1_, a=uA_: v.tensor_tensor(o[:], a[:], pb[:], op=ALU.mult),
                          reads=[uAk, "pb"], writes=[c1k], tag="cL1")
                    s.add("pe", lambda t, i=iz_: t.matmul(pb[:], B.selz[0:66, 64:128], i[:], start=True, stop=True),
                          reads=[izk, "selz"], writes=["pb"], tag="bdL")
                    s.add("dve", lambda v, o=c2_, a=uB_: v.tensor_tensor(o[:], a[:], pb[:], op=ALU.mult),
                          reads=[uBk, "pb"], writes=[c2k], tag="cL2")
                    s.add("dve", lambda v, o=c1_, hp=hp: v.tensor_scalar(o[:], o[:], ogat[:, hp:hp + 1], None, op0=ALU.mult),
                          reads=[c1k], sreads=["ogat"], writes=[c1k], tag="cL3")
                    s.add("dve", lambda v, o=aT_, a=c2_, b=c1_, hp=hp:
                          v.scalar_tensor_tensor(o[:], a[:], gat[:, hp:hp + 1], b[:], op0=ALU.mult, op1=ALU.add),
                          reads=[c2k, c1k], sreads=["gat"], writes=[aTk], tag="cL4")
                    s.add("dve", lambda v, hp=hp, a=aT_: v.tensor_copy(
                        preT[0:64, :].rearrange("d (cc t) -> d cc t", cc=8)[:, :, 32 * hp: 32 * hp + 32],
                        a[:].rearrange("d (m c) -> d m c", m=8)[:, :, 0:32]),
                        reads=[aTk], writes=["preT"], tag="scat0")
                    s.add("dve", lambda v, hp=hp, a=aT_: v.tensor_copy(
                        preT[64:128, :].rearrange("d (cc t) -> d cc t", cc=8)[:, :, 32 * hp: 32 * hp + 32],
                        a[:].rearrange("d (m c) -> d m c", m=8)[:, :, 32:64]),
                        reads=[aTk], writes=["preT"], tag="scat1")
                return emit_tail
            pending_tail[0] = mk_tail()
        if pending_tail[0] is not None:
            pending_tail[0]()
            pending_tail[0] = None
        for dc in range(DC):
            p_, pk_ = pg[dc % 2], f"pg{dc % 2}"
            for cc in range(8):
                s.add("pe", lambda t, p=p_, cc=cc, dc=dc:
                      t.matmul(p[:, 0:256], wo_sb[:, cc * 1024 + dc * 128: cc * 1024 + (dc + 1) * 128],
                               preT[:, cc * 256:(cc + 1) * 256], start=(cc == 0), stop=(cc == 7)),
                      reads=["wo_sb", "preT"], writes=[pk_], tag="woL")
            d_, dk_ = dtile[0], "dtilek"
            s.add("dve", lambda v, p=p_, o=d_: v.tensor_copy(o[:], p[:, 0:256]),
                  reads=[pk_], writes=[dk_], tag="woLe")
            s.dma("sp", lambda q, t=d_, dc=dc, seg=seg:
                  q.dma_start(ddram.rearrange("(dc p) t -> dc p t", p=128)[dc, :, seg * 256:(seg + 1) * 256], t[:]),
                  reads=[dk_], writes=["ddram"], tag="woSt")
    s.barrier("endL1")
    close()

    B.tps = [(ps(f"tpE2{i}", [128, 128]), f"tpE2{i}") for i in range(4)]
    dl_rot = [sb(f"dl{i}", [128, 1024], F32) for i in range(2)]
    for tt in range(16):
        dlb, dlk = dl_rot[tt % 2], f"dl{tt % 2}"
        s.dma("sp", lambda q, t=dlb, tt=tt:
              q.dma_start(t[:].rearrange("p (dc c) -> p dc c", dc=8),
                          ddram.rearrange("(dc p) t -> p dc t", p=128)[:, :, tt * 128:(tt + 1) * 128]),
              reads=[dlk, "ddram"], writes=[dlk], tag="dlld")
        xt, xtk = B.xt_rot[B.xt_i % len(B.xt_rot)]
        B.xt_i += 1
        s.dma("sp", lambda q, t=xt, a=din["xprev"], tt=tt: q.dma_start(t[:], a[tt * 128:(tt + 1) * 128, :]),
              reads=[xtk], writes=[xtk], tag="ldxe2")
        ob, obk = B.out_rot[B.out_i % len(B.out_rot)]
        B.out_i += 1
        for dc in range(DC):
            pt_, ptk = B.tps[dc % 4]
            s.add("pe", lambda t, p=pt_, d=dlb, dc=dc:
                  t.matmul(p[:], d[:, dc * 128:(dc + 1) * 128], B.cid32[:], start=True, stop=True),
                  reads=[dlk, "cid32"], writes=[ptk], tag="dT2")
            s.add("dve", lambda v, p=pt_, o=ob, x=xt, dc=dc:
                  v.tensor_tensor(o[:, dc * 128:(dc + 1) * 128], p[:], x[:, dc * 128:(dc + 1) * 128], op=ALU.add),
                  reads=[ptk, xtk], writes=[obk], tag="dadd2")
        s.dma("sp", lambda q, t=ob, tt=tt: q.dma_start(xout[tt * 128:(tt + 1) * 128, :], t[:]),
              reads=[obk], writes=[f"outx2_{tt}"], tag="stx2")
    s.add("sp", lambda q: q.nop(), reads=[f"outx2_{t}" for t in range(16)], tag="fin")
    close()
    close(gblk)
    return din


_B2_CACHE = {}


def build_b2_program():
    if "nc" not in _B2_CACHE:
        nc, s, B = _mk_sched()
        xin = nc.dram_tensor("xprev", [NT, D], F32, kind="ExternalInput").ap()
        xout = nc.dram_tensor("xb2", [NT, D], F32, kind="ExternalOutput").ap()
        build_l1_attn(nc, s, B, xin, xout)
        _B2_CACHE["nc"] = _finish(nc, s)
        _B2_CACHE["stats"] = s.stats()
    return _B2_CACHE["nc"]


def run_block2(inputs, x_cur):
    C = _host_consts()
    nc = build_b2_program()
    betas = np.asarray(inputs["L1_attn_betas"], np.float32)[0, :, 0, :]
    m0 = np.asarray(inputs["L1_attn_memory0"], np.float32)[0]
    z0 = np.asarray(inputs["L1_attn_z0"], np.float32)[0, :, 0, :]
    shared = {
        "wq1": np.asarray(inputs["L1_attn_Wq"], np.float32),
        "wk1": np.asarray(inputs["L1_attn_Wk"], np.float32),
        "wv1": np.asarray(inputs["L1_attn_Wv"], np.float32),
        "wo1": np.asarray(inputs["L1_attn_Wo"], np.float32),
        "c_masks": C["masks"],
        "c_cid16": C["cid16"], "c_cid32": C["cid32"], "c_ones128": C["ones128"], "c_onesrow": C["onesrow"],
        "c_selz": C["selz"],
    }
    in_maps = []
    for c in range(8):
        half = c % 2
        m = dict(shared)
        m["xprev"] = np.ascontiguousarray(x_cur[c])
        m["betas8"] = np.ascontiguousarray(betas[half * 8:(half + 1) * 8])
        m["m08"] = np.ascontiguousarray(m0[half * 8:(half + 1) * 8])
        m["z08"] = np.ascontiguousarray(z0[half * 8:(half + 1) * 8])
        in_maps.append(m)
    res = run_bass_kernel_spmd(nc, in_maps, core_ids=list(range(8)))
    return [r["xb2"] for r in res.results]



def build_l1_ffn(nc, s, B, xprev, xout):
    din = {"xprev": xprev}
    for nm, shp, dt in [("wg", [D, 8], F32),
                        ("wu", [8, D, 2048], F32), ("wd", [8, 2048, D], F32),
                        ("w1", [D, 2048], F32), ("w2", [2048, D], F32),
                        ("c_iota8", [128, 8], F32)]:
        din[nm] = nc.dram_tensor(nm, shp, dt, kind="ExternalInput").ap()

    gblk, gph = [], []

    def sb(name, shape, dtype, scope=None):
        g = nc.sbuf_tensor(name, shape, dtype)
        (gph if scope is None else scope).append(g)
        return g.__enter__()

    def ps(name, shape, dtype=F32):
        g = nc.psum_tensor(name, shape, dtype)
        gph.append(g)
        return g.__enter__()

    def close(lst=None):
        lst = gph if lst is None else lst
        for g in reversed(lst):
            g.__exit__(None, None, None)
        lst.clear()

    xn = sb("xn3", [128, DC * NT], BF16, scope=gblk)
    macc = sb("macc", [128, 16 * 1024], F32, scope=gblk)
    cw = sb("cw", [128, 128], F32, scope=gblk)

    B.tps = [(ps(f"tpU{i}", [128, 128]), f"tpU{i}") for i in range(4)]
    emit_load_ln_fm(s, B, din["xprev"], xn, "xn3", lambda dc, tt: dc * NT + tt * 128)
    s.barrier("endA3")
    close()

    ph = [ps(f"phU{i}", [128, 512]) for i in range(2)]
    wgsb = sb("wgsb", [128, 64], BF16)
    iot = sb("iot", [128, 8], F32)
    lg = sb("lg", [128, 8], F32)
    oh1 = sb("oh1", [128, 8], F32)
    oh2 = sb("oh2", [128, 8], F32)
    t8 = sb("t8", [128, 8], F32)
    u8 = sb("u8", [128, 8], F32)
    m1 = sb("m1g", [128, 1], F32)
    m2 = sb("m2g", [128, 1], F32)
    i1 = sb("i1g", [128, 1], F32)
    v1n = sb("v1ng", [128, 1], F32)
    v2n = sb("v2ng", [128, 1], F32)
    s.dma("pool", lambda q: q.dma_start(wgsb[:].rearrange("p (dc c) -> p dc c", dc=8),
                                        din["wg"].rearrange("(dc p) c -> p dc c", p=128)),
          writes=["wgsb"], tag="ldwg")
    s.dma("sp", lambda q: q.dma_start(iot[:], din["c_iota8"]), writes=["iot"], tag="ldiota")

    def first_hot(ohk, oh_t, lsrc, lsrck, tag):
        s.add("dve", lambda v: v.tensor_tensor(t8[:], iot[:], oh_t[:], op=ALU.mult),
              reads=["iot", ohk], writes=["t8"], tag=tag + "a")
        s.add("dve", lambda v: v.tensor_scalar(u8[:], oh_t[:], -9.0, 9.0, op0=ALU.mult, op1=ALU.add),
              reads=[ohk], writes=["u8"], tag=tag + "b")
        s.add("dve", lambda v: v.tensor_tensor(t8[:], t8[:], u8[:], op=ALU.add),
              reads=["t8", "u8"], writes=["t8"], tag=tag + "c")
        s.add("dve", lambda v: v.tensor_reduce(i1[:], t8[:], axis=AX.X, op=ALU.min),
              reads=["t8"], writes=["i1g"], tag=tag + "d")
        s.add("dve", lambda v: v.tensor_scalar(oh_t[:], iot[:], i1[:], None, op0=ALU.is_equal),
              reads=["iot"], sreads=["i1g"], writes=[ohk], tag=tag + "e")

    for tt in range(16):
        p_, pk_ = ph[tt % 2], f"phU{tt % 2}"
        for dc in range(DC):
            s.add("pe", lambda t, p=p_, dc=dc, tt=tt:
                  t.matmul(p[:, 0:8], xn[:, dc * NT + tt * 128:dc * NT + (tt + 1) * 128],
                           wgsb[:, dc * 8:(dc + 1) * 8], start=(dc == 0), stop=(dc == 7)),
                  reads=["wgsb", "xn3"], writes=[pk_], tag="gmm")
        s.add("dve", lambda v, p=p_: v.tensor_copy(lg[:], p[:, 0:8]), reads=[pk_], writes=["lg"], tag="gev")
        s.add("dve", lambda v: v.reduce_max(m1[:], lg[:], axis=AX.X), reads=["lg"], writes=["m1g"], tag="gm1")
        s.add("dve", lambda v: v.tensor_scalar(oh1[:], lg[:], m1[:], None, op0=ALU.is_ge),
              reads=["lg"], sreads=["m1g"], writes=["oh1k"], tag="goh1")
        first_hot("oh1k", oh1, lg, "lg", f"gf1_{tt}")
        s.add("dve", lambda v: v.scalar_tensor_tensor(t8[:], oh1[:], -30000.0, lg[:], op0=ALU.mult, op1=ALU.add),
              reads=["oh1k", "lg"], writes=["t8"], tag="gmask")
        s.add("dve", lambda v: v.reduce_max(m2[:], t8[:], axis=AX.X), reads=["t8"], writes=["m2g"], tag="gm2")
        s.add("dve", lambda v: v.tensor_scalar(oh2[:], t8[:], m2[:], None, op0=ALU.is_ge),
              reads=["t8"], sreads=["m2g"], writes=["oh2k"], tag="goh2")
        first_hot("oh2k", oh2, t8, "t8", f"gf2_{tt}")
        s.add("dve", lambda v: v.tensor_tensor(m2[:], m2[:], m1[:], op=ALU.subtract),
              reads=["m2g", "m1g"], writes=["m2g"], tag="gd")
        s.add("act", lambda sc: sc.activation(m2[:], m2[:], AF.Exp), reads=["m2g"], writes=["m2g"], tag="gex")
        s.add("dve", lambda v: v.tensor_scalar(m2[:], m2[:], 1.0, None, op0=ALU.add),
              reads=["m2g"], writes=["m2g"], tag="g1p")
        s.add("dve", lambda v: v.reciprocal(v1n[:], m2[:]), reads=["m2g"], writes=["v1nk"], tag="ginv")
        s.add("dve", lambda v: v.tensor_scalar(v2n[:], v1n[:], -1.0, 1.0, op0=ALU.mult, op1=ALU.add),
              reads=["v1nk"], writes=["v2nk"], tag="gv2")
        s.add("dve", lambda v: v.tensor_scalar(oh1[:], oh1[:], v1n[:], None, op0=ALU.mult),
              reads=["oh1k"], sreads=["v1nk"], writes=["oh1k"], tag="gw1")
        s.add("dve", lambda v, tt=tt: v.scalar_tensor_tensor(cw[:, tt * 8:(tt + 1) * 8], oh2[:], v2n[:],
                                                             oh1[:], op0=ALU.mult, op1=ALU.add),
              reads=["oh2k", "oh1k"], sreads=["v2nk"], writes=["cw"], tag="gcw")

    he = sb("he", [128, 16 * 1024], BF16)
    lwu = [sb(f"lwu{i}", [128, 8 * 512], BF16) for i in range(2)]
    wdh = [sb(f"wdh{i}", [128, 16 * 512], BF16) for i in range(2)]
    for tt2 in range(2):
        for xp in range(9):
            wu_ap = din["w1"] if xp == 0 else din["wu"][xp - 1]
            wd_ap = din["w2"] if xp == 0 else din["wd"][xp - 1]
            for fq in range(4):
                w_, wk_ = lwu[fq % 2], f"lwu{fq % 2}"
                s.dma("pool", lambda q, t=w_, a=wu_ap.rearrange("(dc p) c -> p dc c", p=128), fq=fq:
                      q.dma_start(t[:].rearrange("p (dc c) -> p dc c", dc=8), a[:, :, fq * 512:(fq + 1) * 512]),
                      reads=[wk_], writes=[wk_], tag="uw")
                for fc4 in range(4):
                    fc = fq * 4 + fc4
                    for ns in range(2):
                        p_, pk_ = ph[ns % 2], f"phU{ns % 2}"
                        for dc in range(DC):
                            s.add("pe", lambda t, p=p_, w=w_, dc=dc, ns=ns, tt2=tt2, fc4=fc4:
                                  t.matmul(p[:], w[:, dc * 512 + fc4 * 128: dc * 512 + (fc4 + 1) * 128],
                                           xn[:, dc * NT + tt2 * 1024 + ns * 512: dc * NT + tt2 * 1024 + ns * 512 + 512],
                                           start=(dc == 0), stop=(dc == 7)),
                                  reads=[wk_, "xn3"], writes=[pk_], tag="umm")
                        s.add("act", lambda sc, p=p_, fc=fc, ns=ns:
                              sc.activation(he[:, fc * 1024 + ns * 512: fc * 1024 + (ns + 1) * 512], p[:], AF.Gelu),
                              reads=[pk_], writes=["he"], tag="ugelu")
            for dh in range(2):
                wd_, wdk = wdh[dh % 2], f"wdh{dh % 2}"
                s.dma("pool", lambda q, a=wd_ap.rearrange("(fc p) c -> p fc c", p=128), dh=dh, t=wd_:
                      q.dma_start(t[:].rearrange("p (fc c) -> p fc c", fc=16),
                                  a[:, :, dh * 512:(dh + 1) * 512]),
                      reads=[wdk], writes=[wdk], tag="dw")
                for tl in range(8):
                    tt = tt2 * 8 + tl
                    p_, pk_ = ph[tl % 2], f"phU{tl % 2}"
                    for fc in range(16):
                        s.add("pe", lambda t, p=p_, fc=fc, tl=tl, w=wd_:
                              t.matmul(p[:], he[:, fc * 1024 + tl * 128: fc * 1024 + tl * 128 + 128],
                                       w[:, fc * 512:(fc + 1) * 512], start=(fc == 0), stop=(fc == 15)),
                              reads=["he", wdk], writes=[pk_], tag="dmm")
                    if xp == 0:
                        s.add("act", lambda sc, p=p_, tt=tt, dh=dh:
                              sc.copy(macc[:, tt * 1024 + dh * 512: tt * 1024 + (dh + 1) * 512], p[:]),
                              reads=[pk_], writes=["macc"], tag="dsh")
                    else:
                        mt, mtk = B.xnt_rot[B.xnt_i % len(B.xnt_rot)]
                        B.xnt_i += 1
                        s.add("act", lambda sc, p=p_, o=mt, tt=tt, xp=xp:
                              sc.activation(o[:, 0:512], p[:], AF.Identity,
                                            scale=cw[:, tt * 8 + xp - 1: tt * 8 + xp]),
                              reads=[pk_], sreads=["cw"], writes=[mtk], tag="dsc")
                        s.add("dve", lambda v, o=mt, tt=tt, dh=dh:
                              v.tensor_tensor(macc[:, tt * 1024 + dh * 512: tt * 1024 + (dh + 1) * 512],
                                              macc[:, tt * 1024 + dh * 512: tt * 1024 + (dh + 1) * 512],
                                              o[:, 0:512], op=ALU.add),
                              reads=["macc", mtk], writes=["macc"], tag="dacc")
    s.barrier("endU")
    close()

    for tt in range(16):
        xt, xtk = B.xt_rot[B.xt_i % len(B.xt_rot)]
        B.xt_i += 1
        s.dma("sp", lambda q, t=xt, a=din["xprev"], tt=tt: q.dma_start(t[:], a[tt * 128:(tt + 1) * 128, :]),
              reads=[xtk], writes=[xtk], tag="ldxf")
        ob, obk = B.out_rot[B.out_i % len(B.out_rot)]
        B.out_i += 1
        s.add("dve", lambda v, o=ob, x=xt, tt=tt:
              v.tensor_tensor(o[:], macc[:, tt * 1024:(tt + 1) * 1024], x[:], op=ALU.add),
              reads=["macc", xtk], writes=[obk], tag="fadd")
        s.dma("sp", lambda q, t=ob, tt=tt: q.dma_start(xout[tt * 128:(tt + 1) * 128, :], t[:]),
              reads=[obk], writes=[f"outx3_{tt}"], tag="stx3")
    s.add("sp", lambda q: q.nop(), reads=[f"outx3_{t}" for t in range(16)], tag="fin")
    close(gblk)
    return din


_B3_CACHE = {}


def build_b3_program():
    if "nc" not in _B3_CACHE:
        nc, s, B = _mk_sched()
        xin = nc.dram_tensor("xprev", [NT, D], F32, kind="ExternalInput").ap()
        xout = nc.dram_tensor("xb3", [NT, D], F32, kind="ExternalOutput").ap()
        build_l1_ffn(nc, s, B, xin, xout)
        _B3_CACHE["nc"] = _finish(nc, s)
        _B3_CACHE["stats"] = s.stats()
    return _B3_CACHE["nc"]


def run_block3(inputs, x_cur):
    C = _host_consts()
    nc = build_b3_program()
    shared = {
        "wg": np.asarray(inputs["L1_ffn_Wg"], np.float32),
        "wu": np.asarray(inputs["L1_ffn_Wu"], np.float32),
        "wd": np.asarray(inputs["L1_ffn_Wd"], np.float32),
        "w1": np.asarray(inputs["L1_ffn_W1"], np.float32),
        "w2": np.asarray(inputs["L1_ffn_W2"], np.float32),
        "c_iota8": C["iota8"],
        "c_cid16": C["cid16"], "c_cid32": C["cid32"], "c_ones128": C["ones128"], "c_onesrow": C["onesrow"],
        "c_selz": C["selz"],
    }
    in_maps = []
    for c in range(8):
        m = dict(shared)
        m["xprev"] = np.ascontiguousarray(x_cur[c])
        in_maps.append(m)
    res = run_bass_kernel_spmd(nc, in_maps, core_ids=list(range(8)))
    return [r["xb3"] for r in res.results]



_FUSED_CACHE = {}


def build_fused_program():
    if "nc" not in _FUSED_CACHE:
        nc, s, B = _mk_sched()
        xb0 = nc.dram_tensor("xb0", [NT, D], F32, kind="Internal").ap()
        xb1 = nc.dram_tensor("xb1", [NT, D], F32, kind="Internal").ap()
        xb2 = nc.dram_tensor("xb2", [NT, D], F32, kind="Internal").ap()
        y = nc.dram_tensor("y", [NT, D], F32, kind="ExternalOutput").ap()
        build_l0_attn(nc, s, B, xb0)
        s.barrier("b01")
        build_l0_ffn(nc, s, B, xb0, xb1)
        s.barrier("b12")
        build_l1_attn(nc, s, B, xb1, xb2)
        s.barrier("b23")
        build_l1_ffn(nc, s, B, xb2, y)
        _FUSED_CACHE["nc"] = _finish(nc, s)
        _FUSED_CACHE["stats"] = s.stats()
    return _FUSED_CACHE["nc"]


def kernel(**inputs):
    C = _host_consts()
    nc = build_fused_program()
    X = np.asarray(inputs["L0_attn_X"], dtype=np.float32)
    gam_t = np.broadcast_to(np.asarray(inputs["L0_attn_gamma"], np.float32), (128, D)).copy()
    bet_t = np.broadcast_to(np.asarray(inputs["L0_attn_beta"], np.float32), (128, D)).copy()
    lam64 = np.broadcast_to(np.asarray(inputs["L0_attn_lam"], np.float32).reshape(1, 1), (64, 1)).copy()
    betas = np.asarray(inputs["L1_attn_betas"], np.float32)[0, :, 0, :]
    m0 = np.broadcast_to(np.asarray(inputs["L1_attn_memory0"], np.float32)[0], (16, 64, 64))
    z0 = np.asarray(inputs["L1_attn_z0"], np.float32)[0, :, 0, :]
    shared = {
        "wq": np.asarray(inputs["L0_attn_Wq"], np.float32), "bq": np.asarray(inputs["L0_attn_bq"], np.float32),
        "wk": np.asarray(inputs["L0_attn_Wk"], np.float32), "bk": np.asarray(inputs["L0_attn_bk"], np.float32),
        "wv": np.asarray(inputs["L0_attn_Wv"], np.float32), "bv": np.asarray(inputs["L0_attn_bv"], np.float32),
        "wo": np.asarray(inputs["L0_attn_Wo"], np.float32), "bo": np.asarray(inputs["L0_attn_bo"], np.float32),
        "gam_t": gam_t, "bet_t": bet_t, "lam64": lam64,
        "w_fc": np.asarray(inputs["L0_ffn_W_fc"], np.float32),
        "ln_w": np.asarray(inputs["L0_ffn_ln_w"], np.float32),
        "ln_b": np.asarray(inputs["L0_ffn_ln_b"], np.float32),
        "w_proj": np.asarray(inputs["L0_ffn_W_proj"], np.float32),
        "wq1": np.asarray(inputs["L1_attn_Wq"], np.float32),
        "wk1": np.asarray(inputs["L1_attn_Wk"], np.float32),
        "wv1": np.asarray(inputs["L1_attn_Wv"], np.float32),
        "wo1": np.asarray(inputs["L1_attn_Wo"], np.float32),
        "wg": np.asarray(inputs["L1_ffn_Wg"], np.float32),
        "wu": np.asarray(inputs["L1_ffn_Wu"], np.float32),
        "wd": np.asarray(inputs["L1_ffn_Wd"], np.float32),
        "w1": np.asarray(inputs["L1_ffn_W1"], np.float32),
        "w2": np.asarray(inputs["L1_ffn_W2"], np.float32),
        "c_iota8": C["iota8"], "c_masks": C["masks"],
        "c_cid16": C["cid16"], "c_cid32": C["cid32"], "c_ones128": C["ones128"],
        "c_onesrow": C["onesrow"], "c_selz": C["selz"],
    }
    in_maps = []
    for c in range(8):
        b, half = c // 2, c % 2
        own, oth = _core_rows(c)
        m = dict(shared)
        m["x_own"] = np.ascontiguousarray(X[b][own])
        m["x_oth"] = np.ascontiguousarray(X[b][oth])
        m["betas8"] = np.ascontiguousarray(betas[half * 8:(half + 1) * 8])
        m["m08"] = np.ascontiguousarray(m0[half * 8:(half + 1) * 8])
        m["z08"] = np.ascontiguousarray(z0[half * 8:(half + 1) * 8])
        in_maps.append(m)
    res = run_bass_kernel_spmd(nc, in_maps, core_ids=list(range(8)))
    out = np.zeros((4, S_FULL, D), np.float32)
    for c in range(8):
        b = c // 2
        own, _ = _core_rows(c)
        out[b][own] = res.results[c]["y"]
    return out


# version 09
# speedup vs baseline: 1.2256x; 1.0235x over previous
import numpy as np
import ml_dtypes
import concourse.bass as bass
import concourse.mybir as mybir
from concourse.bass_utils import run_bass_kernel_spmd

F32 = mybir.dt.float32
BF16 = mybir.dt.bfloat16
F8 = mybir.dt.float8e4
AF = mybir.ActivationFunctionType
ALU = mybir.AluOpType
AX = mybir.AxisListType

ENGS = ("pe", "act", "dve", "pool", "sp")
QUEUES = ("sp", "pool")
NQSLOT = 4

D = 1024
DC = 8
NT = 2048
S_FULL = 4096
NSEG = 8
LSEG_LOC = 256
EPS = 1e-5


class Sched:

    def __init__(self, nc):
        self.nc = nc
        self.ops = []
        self.eng_seq = {e: 0 for e in ENGS}
        self.eng_ops = {e: [] for e in ENGS}
        self.qslot_count = {}
        self.qslot_last = {}
        self.dma_rr = {q: 0 for q in QUEUES}
        self.last_writer = {}
        self.readers = {}
        self.eng_last = {e: None for e in ENGS}

    def _event(self, op):
        if op["dma"]:
            return (op["qchan"], op["qidx"])
        return (op["eng"], op["seq"] + 1)

    def add(self, eng, fn, reads=(), writes=(), sreads=(), dma=False, queue=None, tag=""):
        idx = len(self.ops)
        op = {
            "idx": idx, "eng": eng, "fn": fn, "dma": dma, "tag": tag,
            "reads": tuple(reads), "writes": tuple(writes), "sreads": tuple(sreads),
            "seq": self.eng_seq[eng], "inc": False, "waits": {}, "vc": {},
        }
        if dma:
            q = queue or eng
            slot = self.dma_rr[q] % NQSLOT
            self.dma_rr[q] += 1
            op["qchan"] = ("q", q, slot)
            op["qidx"] = self.qslot_count.get((q, slot), 0) + 1
            self.qslot_count[(q, slot)] = op["qidx"]
            op["qprev"] = self.qslot_last.get((q, slot))
            self.qslot_last[(q, slot)] = idx
        deps = {}

        def need2(widx, force=False):
            if widx is None:
                return
            w = self.ops[widx]
            ev = self._event(w)
            same = (not w["dma"]) and w["eng"] == eng
            if same and eng == "pe" and not force:
                return
            old = deps.get(ev[0])
            if old is None or ev[1] > old[0]:
                deps[ev[0]] = (ev[1], widx, force)

        if dma and op.get("qprev") is not None:
            need2(op["qprev"])
        for r in op["reads"]:
            need2(self.last_writer.get(r))
        for r in op["sreads"]:
            need2(self.last_writer.get(r), force=True)
        for wkey in op["writes"]:
            need2(self.last_writer.get(wkey))
            for ridx in self.readers.get(wkey, ()):
                if ridx != idx:
                    need2(ridx)

        prev = self.eng_last[eng]
        vc = dict(self.ops[prev]["vc"]) if prev is not None else {}
        waits = {}
        for chan, (sq, widx, force) in deps.items():
            known = vc.get(chan, 0)
            if sq > known:
                waits[chan] = sq
                self.ops[widx]["inc"] = True
                w = self.ops[widx]
                for c2, v2 in w["vc"].items():
                    if v2 > vc.get(c2, 0):
                        vc[c2] = v2
                vc[chan] = sq
        op["waits"] = waits
        op["vc"] = vc
        self.ops.append(op)
        self.eng_seq[eng] += 1
        self.eng_ops[eng].append(idx)
        self.eng_last[eng] = idx
        for wkey in op["writes"]:
            self.last_writer[wkey] = idx
            self.readers[wkey] = []
        for r in op["reads"] + op["sreads"]:
            self.readers.setdefault(r, []).append(idx)
        return idx

    def dma(self, queue, fn, reads=(), writes=(), tag=""):
        return self.add(queue, fn, reads=reads, writes=writes, dma=True, queue=queue, tag=tag)

    def barrier(self, tag="bar"):
        gather_deps = [self.eng_last[e] for e in ENGS if self.eng_last[e] is not None]
        key = f"__bar{len(self.ops)}"
        for widx in gather_deps:
            self.ops[widx]["inc"] = True

        def mk_nop(eng):
            return lambda e: e.nop()

        gidx = self.add("dve", mk_nop("dve"), tag=tag + "_gather")
        gop = self.ops[gidx]
        for widx in gather_deps:
            w = self.ops[widx]
            ev = self._event(w)
            if (not w["dma"]) and w["eng"] == "dve":
                if gop["vc"].get(ev[0], 0) < ev[1]:
                    gop["waits"][ev[0]] = max(gop["waits"].get(ev[0], 0), ev[1])
                    gop["vc"][ev[0]] = ev[1]
            else:
                if gop["vc"].get(ev[0], 0) < ev[1]:
                    gop["waits"][ev[0]] = max(gop["waits"].get(ev[0], 0), ev[1])
                    gop["vc"][ev[0]] = ev[1]
        gop["writes"] = (key,)
        self.last_writer[key] = gidx
        self.readers[key] = []
        for e in ENGS:
            if e == "dve":
                continue
            self.add(e, mk_nop(e), reads=(key,), tag=tag + "_join")

    def assign_semvals(self):
        for e in ENGS:
            cnt = 0
            for oi in self.eng_ops[e]:
                o = self.ops[oi]
                if o["inc"] and not o["dma"]:
                    cnt += 1
                o["semval"] = cnt

    def verify(self):
        self.assign_semvals()
        pos = {e: 0 for e in ENGS}
        sem = {e: 0 for e in ENGS}
        qsem = {}
        progress = True
        while progress:
            progress = False
            for e in ENGS:
                while pos[e] < len(self.eng_ops[e]):
                    oi = self.eng_ops[e][pos[e]]
                    op = self.ops[oi]
                    ok = True
                    for chan, sq in op["waits"].items():
                        if isinstance(chan, tuple):
                            if qsem.get(chan, 0) < sq:
                                ok = False
                                break
                        else:
                            val = self.ops[self.eng_ops[chan][sq - 1]]["semval"]
                            if sem[chan] < val:
                                ok = False
                                break
                    if not ok:
                        break
                    if op["dma"]:
                        qsem[op["qchan"]] = qsem.get(op["qchan"], 0) + 1
                    elif op["inc"]:
                        sem[e] += 1
                    pos[e] += 1
                    progress = True
        stuck = {e: pos[e] for e in ENGS if pos[e] < len(self.eng_ops[e])}
        if stuck:
            det = []
            for e in stuck:
                op = self.ops[self.eng_ops[e][pos[e]]]
                det.append(f"{e}@{op['idx']}:{op['tag']} waits {op['waits']}")
            raise RuntimeError("Sched.verify deadlock: " + "; ".join(det))

    def emit(self, sems, qsems, engmap):
        self.assign_semvals()

        def run_engine(ename):
            def body(eng):
                for oi in self.eng_ops[ename]:
                    op = self.ops[oi]
                    for chan, sq in sorted(op["waits"].items(), key=lambda kv: str(kv[0])):
                        if isinstance(chan, tuple):
                            _, q, slot = chan
                            eng.wait_ge(qsems[(q, slot)], 16 * sq)
                        else:
                            pi = self.eng_ops[chan]
                            val = self.ops[pi[sq - 1]]["semval"]
                            if val > 0:
                                eng.wait_ge(sems[chan], val)
                    inst = op["fn"](eng)
                    if op["dma"]:
                        inst.then_inc(qsems[(op["qchan"][1], op["qchan"][2])], 16)
                    elif op["inc"]:
                        inst.then_inc(sems[op["eng"]], 1)
            return body

        blk = self.nc.Block()
        with blk:
            blk.tensor(run_engine("pe"))
            blk.scalar(run_engine("act"))
            blk.vector(run_engine("dve"))
            blk.gpsimd(run_engine("pool"))
            blk.sync(run_engine("sp"))

    def stats(self):
        from collections import Counter
        c = Counter(o["eng"] for o in self.ops)
        w = sum(len(o["waits"]) for o in self.ops)
        i = sum(1 for o in self.ops if o["inc"])
        return f"ops={len(self.ops)} per-eng={dict(c)} waits={w} incs={i}"


def engobjs(nc):
    return {"pe": nc.tensor, "act": nc.scalar, "dve": nc.vector, "pool": nc.gpsimd, "sp": nc.sync}



def _host_consts():
    bf = ml_dtypes.bfloat16
    cid16 = np.eye(128, dtype=bf)
    cid32 = np.eye(128, dtype=np.float32)
    ones128 = np.ones((128, 128), dtype=bf)
    onesrow = np.ones((1, 512), dtype=bf)
    sel4 = np.zeros((4, 4 * 64), dtype=bf)
    for r in range(4):
        sel4[r, r * 64:(r + 1) * 64] = 1
    selz = np.zeros((128, 2 * 64), dtype=bf)
    selz[0, 0:64] = 1
    selz[64, 64:128] = 1
    masks = np.zeros((128, 4 * 512), dtype=bf)
    c = np.arange(512)
    lq = 16 * (c % 32) + c // 32
    for kt in range(4):
        p = np.arange(128)
        lk = 16 * (p % 32) + 4 * kt + p // 32
        masks[:, kt * 512:(kt + 1) * 512] = (lk[:, None] <= lq[None, :]).astype(bf)
    iota8 = np.broadcast_to(np.arange(8, dtype=np.float32), (128, 8)).copy()
    return {"cid16": cid16, "cid32": cid32, "ones128": ones128, "onesrow": onesrow,
            "sel4": sel4, "selz": selz, "masks": masks, "iota8": iota8}


def _core_rows(c):
    half = c % 2
    own = np.concatenate([np.arange(s * 512 + half * 256, s * 512 + half * 256 + 256) for s in range(8)])
    oth = np.concatenate([np.arange(s * 512 + (1 - half) * 256, s * 512 + (1 - half) * 256 + 256) for s in range(8)])
    return own, oth



def _small(nc, name, n, cols=1, dtype=F32, parts=128):
    return [(nc.alloc_sbuf_tensor(f"{name}{i}", [parts, cols], dtype), f"{name}{i}") for i in range(n)]


class Bufs:
    pass


def alloc_consts(nc, s):
    B = Bufs()
    B.nc = nc
    B.cid16 = nc.alloc_sbuf_tensor("cid16", [128, 128], BF16)
    B.cid32 = nc.alloc_sbuf_tensor("cid32", [128, 128], F32)
    B.ones128 = nc.alloc_sbuf_tensor("ones128", [128, 128], BF16)
    B.onesrow = nc.alloc_sbuf_tensor("onesrow", [1, 512], BF16)
    B.selz = nc.alloc_sbuf_tensor("selz", [128, 128], BF16)
    for nm in ("cid16", "cid32", "ones128", "onesrow", "selz"):
        ap = nc.dram_tensor(f"c_{nm}", list(getattr(B, nm).ap().shape), getattr(B, nm).ap().dtype,
                            kind="ExternalInput").ap()
        t = getattr(B, nm)
        s.dma("sp", lambda q, t=t, ap=ap: q.dma_start(t[:], ap), writes=[nm], tag=f"ld_{nm}")
    return B


def emit_ln_tok(s, B, xt, xtk, st, ncols=1024, eps=EPS):
    inv = 1.0 / ncols
    (nm, nmk), (e2, e2k), (var, vark), (rstd, rstdk), (nb, nbk) = st
    s.add("dve", lambda v, a=nm, x=xt: v.reduce_sum(a[:], x[:], axis=AX.X, negate=True),
          reads=[xtk], writes=[nmk])
    s.add("dve", lambda v, a=nm: v.tensor_scalar(a[:], a[:], inv, None, op0=ALU.mult),
          reads=[nmk], writes=[nmk])
    sq, sqk = B.sq_rot[B.sq_i % len(B.sq_rot)]
    B.sq_i += 1
    s.add("act", lambda sc, o=sq, x=xt: sc.activation(o[:, 0:x.ap().shape[1]], x[:], AF.Square),
          reads=[xtk], writes=[sqk])
    s.add("dve", lambda v, a=e2, x=sq, n=ncols: v.reduce_sum(a[:], x[:, 0:n], axis=AX.X),
          reads=[sqk], writes=[e2k])
    s.add("dve", lambda v, a=e2: v.tensor_scalar(a[:], a[:], inv, None, op0=ALU.mult),
          reads=[e2k], writes=[e2k])
    s.add("dve", lambda v, o=var, a=nm, b=nm: v.tensor_tensor(o[:], a[:], b[:], op=ALU.mult),
          reads=[nmk], writes=[vark])
    s.add("dve", lambda v, o=var, a=e2, b=var, e=float(eps):
          v.scalar_tensor_tensor(o[:], a[:], e, b[:], op0=ALU.add, op1=ALU.subtract),
          reads=[e2k, vark], writes=[vark])
    s.add("act", lambda sc, o=var, a=var: sc.activation(o[:], a[:], AF.Ln),
          reads=[vark], writes=[vark])
    s.add("act", lambda sc, o=rstd, a=var: sc.activation(o[:], a[:], AF.Exp, scale=-0.5),
          reads=[vark], writes=[rstdk])
    s.add("dve", lambda v, o=nb, a=nm, b=rstd: v.tensor_tensor(o[:], a[:], b[:], op=ALU.mult),
          reads=[nmk, rstdk], writes=[nbk])
    return (rstd, rstdk), (nb, nbk)


def emit_load_ln_fm(s, B, src_ap, xn, xnk, colfn, ntiles=16, ln=True):
    nc = B.nc
    for tt in range(ntiles):
        xt, xtk = B.xt_rot[B.xt_i % len(B.xt_rot)]
        B.xt_i += 1
        s.dma("sp", lambda q, t=xt, a=src_ap, tt=tt: q.dma_start(t[:], a[tt * 128:(tt + 1) * 128, :]),
              reads=[xtk], writes=[xtk], tag="ldx")
        if ln:
            st = B.lnst[B.lnst_i % len(B.lnst)]
            B.lnst_i += 1
            (rstd, rstdk), (nb, nbk) = emit_ln_tok(s, B, xt, xtk, st)
            xnt, xntk = B.xnt_rot[B.xnt_i % len(B.xnt_rot)]
            B.xnt_i += 1
            s.add("act", lambda sc, o=xnt, x=xt, r=rstd, b=nb:
                  sc.activation(o[:], x[:], AF.Identity, bias=b[:], scale=r[:]),
                  reads=[xtk], sreads=[rstdk, nbk], writes=[xntk], tag="xnt")
        else:
            xnt, xntk = B.xnt_rot[B.xnt_i % len(B.xnt_rot)]
            B.xnt_i += 1
            s.add("dve", lambda v, o=xnt, x=xt: v.tensor_copy(o[:], x[:]),
                  reads=[xtk], writes=[xntk], tag="xcast")
        for dc in range(DC):
            pt, ptk = B.tps[(tt * DC + dc) % len(B.tps)]
            s.add("pe", lambda t, p=pt, x=xnt, dc=dc, c=B.cid16:
                  t.matmul(p[:], x[:, dc * 128:(dc + 1) * 128], c[:], start=True, stop=True),
                  reads=[xntk, "cid16"], writes=[ptk], tag="xT")
            eng = "dve" if dc % 2 == 0 else "act"
            if eng == "dve":
                s.add("dve", lambda v, p=pt, o=xn, col=colfn(dc, tt): v.tensor_copy(o[:, col:col + 128], p[:]),
                      reads=[ptk], writes=[xnk], tag="xTev")
            else:
                s.add("act", lambda sc, p=pt, o=xn, col=colfn(dc, tt): sc.copy(o[:, col:col + 128], p[:]),
                      reads=[ptk], writes=[xnk], tag="xTev")


def emit_exit_delta(s, B, delta, deltak, xprev_ap, out_ap, outkey):
    nc = B.nc
    for tt in range(16):
        xt, xtk = B.xt_rot[B.xt_i % len(B.xt_rot)]
        B.xt_i += 1
        s.dma("sp", lambda q, t=xt, a=xprev_ap, tt=tt: q.dma_start(t[:], a[tt * 128:(tt + 1) * 128, :]),
              reads=[xtk], writes=[xtk], tag="ldxe")
        ob, obk = B.out_rot[B.out_i % len(B.out_rot)]
        B.out_i += 1
        for dc in range(DC):
            pt, ptk = B.tps[(tt * DC + dc) % len(B.tps)]
            s.add("pe", lambda t, p=pt, d=delta, dc=dc, tt=tt, c=B.cid32:
                  t.matmul(p[:], d[:, dc * 2048 + tt * 128: dc * 2048 + (tt + 1) * 128], c[:], start=True, stop=True),
                  reads=[deltak, "cid32"], writes=[ptk], tag="dT")
            s.add("dve", lambda v, p=pt, o=ob, x=xt, dc=dc:
                  v.tensor_tensor(o[:, dc * 128:(dc + 1) * 128], p[:], x[:, dc * 128:(dc + 1) * 128], op=ALU.add),
                  reads=[ptk, xtk], writes=[obk], tag="dadd")
        s.dma("sp", lambda q, a=out_ap, t=ob, tt=tt: q.dma_start(a[tt * 128:(tt + 1) * 128, :], t[:]),
              reads=[obk], writes=[f"{outkey}_{tt}"], tag="stx")


def alloc_common(nc, s, B):
    B.xt_rot = [(nc.alloc_sbuf_tensor(f"xt{i}", [128, 1024], F32), f"xt{i}") for i in range(2)]
    B.xt_i = 0
    B.sq_rot = [(nc.alloc_sbuf_tensor(f"sqs{i}", [128, 1024], BF16), f"sqs{i}") for i in range(2)]
    B.sq_i = 0
    B.xnt_rot = [(nc.alloc_sbuf_tensor(f"xnt{i}", [128, 1024], BF16), f"xnt{i}") for i in range(2)]
    B.xnt_i = 0
    B.out_rot = [(nc.alloc_sbuf_tensor(f"outb{i}", [128, 1024], F32), f"outb{i}") for i in range(2)]
    B.out_i = 0
    B.lnst = []
    for i in range(2):
        B.lnst.append([( nc.alloc_sbuf_tensor(f"ln{nm}{i}", [128, 1], F32), f"ln{nm}{i}")
                       for nm in ("nm", "e2", "var", "rstd", "nb")])
    B.lnst_i = 0
    B.tps = None



def build_l0_attn(nc, s, B, xout):
    din = {}
    for nm, shp, dt in [("x_own", [NT, D], F32), ("x_oth", [NT, D], F32),
                        ("wq", [16, D, 128], F32), ("bq", [16, 128], F32),
                        ("wk", [16, D, 128], F32), ("bk", [16, 128], F32),
                        ("wv", [16, D, 64], F32), ("bv", [16, 64], F32),
                        ("wo", [D, D], F32), ("bo", [D], F32),
                        ("gam_t", [128, D], F32), ("bet_t", [128, D], F32),
                        ("lam64", [64, 1], F32)]:
        din[nm] = nc.dram_tensor(nm, shp, dt, kind="ExternalInput").ap()

    gblk, gph = [], []

    def sb(name, shape, dtype, scope=None):
        g = nc.sbuf_tensor(name, shape, dtype)
        (gph if scope is None else scope).append(g)
        return g.__enter__()

    def ps(name, shape, dtype=F32):
        g = nc.psum_tensor(name, shape, dtype)
        gph.append(g)
        return g.__enter__()

    def close(lst=None):
        lst = gph if lst is None else lst
        for g in reversed(lst):
            g.__exit__(None, None, None)
        lst.clear()

    Xn = sb("Xn", [128, DC * S_FULL], BF16, scope=gblk)
    OcatT = sb("OcatT", [128, DC * NT], BF16, scope=gblk)
    selz_sb = B.selz
    lam_sb = sb("lam_sb", [64, 1], F32, scope=gblk)
    s.dma("sp", lambda q, t=lam_sb, a=din["lam64"]: q.dma_start(t[:], a), writes=["lam"], tag="ldlam")
    s.add("dve", lambda v, t=lam_sb: v.tensor_scalar(t[:], t[:], -1.0, None, op0=ALU.mult),
          reads=["lam"], writes=["lam"], tag="neglam")

    B.tps = [(ps(f"tpA{i}", [128, 128]), f"tpA{i}") for i in range(4)]
    emit_load_ln_fm(s, B, din["x_own"], Xn, "Xn", lambda dc, tt: dc * S_FULL + tt * 128)
    emit_load_ln_fm(s, B, din["x_oth"], Xn, "Xn", lambda dc, tt: dc * S_FULL + NT + tt * 128)
    s.barrier("endA")
    close()

    pA = ps("pA", [128, 1024])
    pB = ps("pB", [128, 1024])
    pbank = [(pA, 0, "pA0"), (pA, 512, "pA1"), (pB, 0, "pB0"), (pB, 512, "pB1")]
    pO1 = ps("pO1", [65, 512])
    pO2 = ps("pO2", [65, 512])
    pu = [ps(f"pu{i}", [128, 512]) for i in range(2)]
    pubank = [(pu[0], 0, "pu0"), (pu[1], 0, "pu1")]

    Kg = [sb(f"Kg{i}", [128, S_FULL], BF16) for i in range(2)]
    Qg = [sb(f"Qg{i}", [128, NT], BF16) for i in range(2)]
    Vg = [sb(f"Vgp{i}", [128, 2 * 65 * 32], BF16) for i in range(2)]
    wkb = [sb(f"wkb{i}", [128, 1024], BF16) for i in range(2)]
    wqb = [sb(f"wqb{i}", [128, 1024], BF16) for i in range(2)]
    wvg = [sb(f"wvgp{i}", [128, 1024], BF16) for i in range(2)]
    bqs = sb("bqs", [128, 16], F32)
    bks = sb("bks", [128, 16], F32)
    bvrow = sb("bvrow", [1, 1024], BF16)
    us1 = [sb(f"us1_{i}", [64, 2048], BF16) for i in range(2)]
    us2 = [sb(f"us2_{i}", [64, 2048], BF16) for i in range(2)]
    zb = [sb(f"zbb{i}", [128, 2048], BF16) for i in range(2)]
    e12 = [sb(f"e12_{i}", [128, 1024], BF16) for i in range(2)]
    for i in range(2):
        s.add("dve", lambda v, t=zb[i], i=i: v.memset(t[:], 1.0), writes=[f"zbk{i}"], tag="zinit")
    ct1 = sb("ct1", [64, 512], BF16)
    ct2 = sb("ct2", [64, 512], BF16)

    bqnat = sb("bqnat", [16, 128], F32)
    s.dma("sp", lambda q, t=bqnat, a=din["bq"]: q.dma_start(t[:], a), reads=["bqnat"], writes=["bqnat"], tag="ldbq")
    s.add("pe", lambda t: t.matmul(pA[:, 0:16], bqnat[:], B.cid32[0:16, 0:16], start=True, stop=True),
          reads=["bqnat", "cid32"], writes=["pA0"], tag="bqT")
    s.add("dve", lambda v: v.tensor_copy(bqs[:], pA[:, 0:16]), reads=["pA0", "bqnat"], writes=["bqs"], tag="bqTe")
    s.dma("sp", lambda q, t=bqnat, a=din["bk"]: q.dma_start(t[:], a), reads=["bqnat"], writes=["bqnat"], tag="ldbk")
    s.add("pe", lambda t: t.matmul(pA[:, 0:16], bqnat[:], B.cid32[0:16, 0:16], start=True, stop=True),
          reads=["bqnat", "cid32"], writes=["pA0"], tag="bkT")
    s.add("dve", lambda v: v.tensor_copy(bks[:], pA[:, 0:16]), reads=["pA0"], writes=["bks"], tag="bkTe")
    s.dma("pool", lambda q, t=bvrow, a=din["bv"].rearrange("h d -> (h d)"): q.dma_start(t[0:1, :], a[None, :]),
          writes=["bvrow"], tag="ldbv")
    for i in range(2):
        s.add("dve", lambda v, t=Vg[i], i=i:
              v.memset(t[:].rearrange("p (h tt c) -> p h tt c", h=2, tt=32)[:, :, :, 64:65], 1.0),
              writes=[f"Vg{i}"], tag="vones")

    units = []
    ready = {}
    for pr in range(8):
        vslot = pr % 2

        def wv_load(pr=pr, vslot=vslot):
            for hh in range(2):
                s.dma("pool", lambda q, a=din["wv"][pr * 2 + hh].rearrange("(ej p) d -> p ej d", p=128),
                      hh=hh, vslot=vslot:
                      q.dma_start(wvg[vslot][:].rearrange("p (ej h d) -> p ej h d", ej=8, h=2)[:, :, hh, :], a),
                      reads=[f"wvg{vslot}"], writes=[f"wvg{vslot}"], tag="ldwv")
        units.append(wv_load)
        for tt in range(32):
            def v_unit(tt=tt, pr=pr, vslot=vslot):
                pt_, off, pk = pubank[tt % 2]
                for ej in range(DC):
                    s.add("pe", lambda t, p=pt_, o=off, w=wvg[vslot], ej=ej, tt=tt:
                          t.matmul(p[:, o:o + 128], Xn[:, ej * S_FULL + tt * 128: ej * S_FULL + (tt + 1) * 128],
                                   w[:, ej * 128:(ej + 1) * 128], start=(ej == 0), stop=False),
                          reads=["Xn", f"wvg{vslot}"], writes=[pk], tag="vmm")
                s.add("pe", lambda t, p=pt_, o=off, pr=pr:
                      t.matmul(p[:, o:o + 128], B.onesrow[0:1, 0:128], bvrow[0:1, pr * 128:(pr + 1) * 128],
                               start=False, stop=True),
                      reads=["onesrow", "bvrow"], writes=[pk], tag="vbias")

                def vev(e, p=pt_, o=off, tt=tt, vslot=vslot):
                    src = p[:, o:o + 128].rearrange("p (h d) -> p h d", h=2)
                    dst = Vg[vslot][:].rearrange("p (h t) -> p h t", h=2)[:, :, tt * 65: tt * 65 + 64]
                    return e.tensor_copy(dst, src)
                s.add("dve", vev, reads=[pk], writes=[f"Vg{vslot}"], tag="vev")
            units.append(v_unit)
        for hh in range(2):
            h = pr * 2 + hh
            hs = h % 2

            def w_load(h=h, hs=hs):
                s.dma("pool", lambda q, t=wkb[hs], a=din["wk"][h].rearrange("(ej p) c -> p ej c", p=128):
                      q.dma_start(t[:].rearrange("p (ej c) -> p ej c", ej=8), a),
                      reads=[f"wkb{hs}"], writes=[f"wkb{hs}"], tag="ldwk")
                s.dma("pool", lambda q, t=wqb[hs], a=din["wq"][h].rearrange("(ej p) c -> p ej c", p=128):
                      q.dma_start(t[:].rearrange("p (ej c) -> p ej c", ej=8), a),
                      reads=[f"wqb{hs}"], writes=[f"wqb{hs}"], tag="ldwq")
            units.append(w_load)
            for n8 in range(8):
                def k_unit(n8=n8, h=h, hs=hs):
                    pt_, off, pk = pubank[n8 % 2]
                    for ej in range(DC):
                        s.add("pe", lambda t, p=pt_, o=off, w=wkb[hs], ej=ej, n8=n8:
                              t.matmul(p[:, o:o + 512], w[:, ej * 128:(ej + 1) * 128],
                                       Xn[:, ej * S_FULL + n8 * 512: ej * S_FULL + (n8 + 1) * 512],
                                       start=(ej == 0), stop=(ej == DC - 1)),
                              reads=[f"wkb{hs}", "Xn"], writes=[pk], tag="kmm")
                    s.add("dve", lambda v, p=pt_, o=off, d=Kg[hs], n8=n8, h=h:
                          v.tensor_scalar(d[:, n8 * 512:(n8 + 1) * 512], p[:, o:o + 512],
                                          bks[:, h:h + 1], None, op0=ALU.add),
                          reads=[pk], sreads=["bks"], writes=[f"Kgk{hs}"], tag="kev")
                units.append(k_unit)
            for n4 in range(4):
                def q_unit(n4=n4, h=h, hs=hs):
                    pt_, off, pk = pubank[n4 % 2]
                    for ej in range(DC):
                        s.add("pe", lambda t, p=pt_, o=off, w=wqb[hs], ej=ej, n4=n4:
                              t.matmul(p[:, o:o + 512], w[:, ej * 128:(ej + 1) * 128],
                                       Xn[:, ej * S_FULL + n4 * 512: ej * S_FULL + (n4 + 1) * 512],
                                       start=(ej == 0), stop=(ej == DC - 1)),
                              reads=[f"wqb{hs}", "Xn"], writes=[pk], tag="qmm")
                    s.add("dve", lambda v, p=pt_, o=off, d=Qg[hs], n4=n4, h=h:
                          v.tensor_scalar(d[:, n4 * 512:(n4 + 1) * 512], p[:, o:o + 512],
                                          bqs[:, h:h + 1], None, op0=ALU.add),
                          reads=[pk], sreads=["bqs"], writes=[f"Qgk{hs}"], tag="qev")
                units.append(q_unit)
            ready[h] = len(units)

    drained = [0]

    def drain(upto):
        upto = min(upto, len(units))
        while drained[0] < upto:
            units[drained[0]]()
            drained[0] += 1

    pending_combine = [None]
    for h in range(16):
        hs = h % 2
        vslot = (h // 2) % 2
        drain(ready[h])
        if pending_combine[0] is not None:
            pending_combine[0]()
            pending_combine[0] = None
        nxt = ready[h + 1] if h + 1 < 16 else len(units)
        need = nxt - ready[h]
        slot = [0]

        def pace():
            slot[0] += 1
            drain(ready[h] + (need * slot[0] + 127) // 128)
        for qt in range(4):
            def emit_s(kc, qt=qt, hs=hs):
                pt_ = pA if kc % 2 == 0 else pB
                k0 = "pA0" if kc % 2 == 0 else "pB0"
                k1 = "pA1" if kc % 2 == 0 else "pB1"
                s.add("pe", lambda t, p=pt_, K=Kg[hs], Q=Qg[hs], kc=kc, qt=qt:
                      t.matmul(p[:, 0:512], K[0:64, kc * 128:(kc + 1) * 128],
                               Q[0:64, qt * 512:(qt + 1) * 512], start=True, stop=True,
                               tile_position=(0, 0)),
                      reads=[f"Kgk{hs}", f"Qgk{hs}"], writes=[k0], tag="s1")
                s.add("pe", lambda t, p=pt_, K=Kg[hs], Q=Qg[hs], kc=kc, qt=qt:
                      t.matmul(p[:, 512:1024], K[64:128, kc * 128:(kc + 1) * 128],
                               Q[64:128, qt * 512:(qt + 1) * 512], start=True, stop=True,
                               tile_position=(64, 0)),
                      reads=[f"Kgk{hs}", f"Qgk{hs}"], writes=[k1], tag="s2")

            def emit_epv(kc, vslot=vslot, hh=(h % 2)):
                pt_ = pA if kc % 2 == 0 else pB
                k0 = "pA0" if kc % 2 == 0 else "pB0"
                k1 = "pA1" if kc % 2 == 0 else "pB1"
                eb, ebk = e12[kc % 2], f"e12_{kc % 2}"
                s.add("act", lambda sc, o=eb, p=pt_: sc.activation(o[:], p[:], AF.Exp, scale=0.125),
                      reads=[k0, k1], writes=[ebk], tag="exp")
                hh2 = h % 2
                s.add("pe", lambda t, e=eb, kc=kc, hh2=hh2, vslot=vslot:
                      t.matmul(pO1[:], Vg[vslot][:, hh2 * 2080 + kc * 65: hh2 * 2080 + (kc + 1) * 65],
                               e[:, 0:512], start=(kc == 0), stop=(kc == 31)),
                      reads=[f"Vg{vslot}", ebk], writes=["pO1"], tag="pv1")
                s.add("pe", lambda t, e=eb, kc=kc, hh2=hh2, vslot=vslot:
                      t.matmul(pO2[:], Vg[vslot][:, hh2 * 2080 + kc * 65: hh2 * 2080 + (kc + 1) * 65],
                               e[:, 512:1024], start=(kc == 0), stop=(kc == 31)),
                      reads=[f"Vg{vslot}", ebk], writes=["pO2"], tag="pv2")
            emit_s(0)
            for kc in range(1, 32):
                emit_s(kc)
                emit_epv(kc - 1)
                pace()
            emit_epv(31)
            pace()
            s.add("act", lambda sc, qt=qt, u=us1[hs]: sc.copy(u[:, qt * 512:(qt + 1) * 512], pO1[0:64, :]),
                  reads=["pO1"], writes=[f"us1k{hs}"], tag="uev1")
            s.add("dve", lambda v, qt=qt, z=zb[hs]: v.tensor_copy(z[0:1, qt * 512:(qt + 1) * 512], pO1[64:65, :]),
                  reads=["pO1"], writes=[f"zbk{hs}"], tag="zev1")
            s.add("dve", lambda v, qt=qt, u=us2[hs]: v.tensor_copy(u[:, qt * 512:(qt + 1) * 512], pO2[0:64, :]),
                  reads=["pO2"], writes=[f"us2k{hs}"], tag="uev2")
            s.add("act", lambda sc, qt=qt, z=zb[hs]: sc.copy(z[64:65, qt * 512:(qt + 1) * 512], pO2[64:65, :]),
                  reads=["pO2"], writes=[f"zbk{hs}"], tag="zev2")

        def mk_combine(h=h, hs=hs):
            def emit_combine():
                z = zb[hs]
                s.add("act", lambda sc, z=z: sc.activation(z[:], z[:], AF.Ln),
                      reads=[f"zbk{hs}"], writes=[f"zbk{hs}"], tag="lnz")
                s.add("act", lambda sc, z=z: sc.activation(z[:], z[:], AF.Exp, scale=-1.0),
                      reads=[f"zbk{hs}"], writes=[f"zbk{hs}"], tag="expz")
                for qt in range(4):
                    s.add("pe", lambda t, qt=qt, z=z:
                          t.matmul(pu[0][0:64, :], selz_sb[:, 0:64], z[:, qt * 512:(qt + 1) * 512],
                                   start=True, stop=True),
                          reads=["selz", f"zbk{hs}"], writes=["pu0"], tag="bz1")
                    s.add("pe", lambda t, qt=qt, z=z:
                          t.matmul(pu[1][0:64, :], selz_sb[:, 64:128], z[:, qt * 512:(qt + 1) * 512],
                                   start=True, stop=True),
                          reads=["selz", f"zbk{hs}"], writes=["pu1"], tag="bz2")
                    s.add("dve", lambda v, qt=qt, u=us1[hs]:
                          v.tensor_tensor(ct1[:], u[:, qt * 512:(qt + 1) * 512], pu[0][0:64, :], op=ALU.mult),
                          reads=[f"us1k{hs}", "pu0"], writes=["ct1"], tag="c1")
                    s.add("dve", lambda v, qt=qt, u=us2[hs]:
                          v.tensor_tensor(ct2[:], u[:, qt * 512:(qt + 1) * 512], pu[1][0:64, :], op=ALU.mult),
                          reads=[f"us2k{hs}", "pu1"], writes=["ct2"], tag="c2")
                    s.add("dve", lambda v, h=h, qt=qt:
                          v.scalar_tensor_tensor(
                              OcatT[64 * (h % 2):64 * (h % 2) + 64,
                                    (h // 2) * NT + qt * 512:(h // 2) * NT + (qt + 1) * 512],
                              ct2[:], lam_sb[:], ct1[:], op0=ALU.mult, op1=ALU.add),
                          reads=["ct1", "ct2"], sreads=["lam"], writes=["OcatT"], tag="comb")
            return emit_combine
        pending_combine[0] = mk_combine()
    pending_combine[0]()
    s.barrier("endB")
    close()

    wo_sb = sb("wo_sb", [128, DC * 1024], BF16)
    bo_row = sb("bo_row", [1, 1024], BF16)
    gam = sb("gam", [128, 1024], F32)
    bet = sb("bet", [128, 1024], F32)
    ytok = [sb(f"ytok{i}", [128, 1024], F32) for i in range(2)]
    ywork = [sb(f"ywork{i}", [128, 1024], F32) for i in range(2)]
    pY = [ps(f"pY{i}", [128, 512]) for i in range(2)]
    s.dma("pool", lambda q, t=wo_sb, a=din["wo"].rearrange("(cc p) d -> p cc d", p=128):
          q.dma_start(t[:].rearrange("p (cc d) -> p cc d", cc=8), a), writes=["wo_sb"], tag="ldwo")
    s.dma("pool", lambda q, t=bo_row, a=din["bo"]: q.dma_start(t[0:1, :], a[None, :]), writes=["bo_row"], tag="ldbo")
    s.dma("sp", lambda q, t=gam, a=din["gam_t"]: q.dma_start(t[:], a), writes=["gam"], tag="ldgam")
    s.dma("sp", lambda q, t=bet, a=din["bet_t"]: q.dma_start(t[:], a), writes=["bet"], tag="ldbet")
    s.add("dve", lambda v, t=gam: v.tensor_scalar(t[:], t[:], 0.2, None, op0=ALU.mult),
          reads=["gam"], writes=["gam"], tag="g02")
    s.add("dve", lambda v, t=bet: v.tensor_scalar(t[:], t[:], 0.2, None, op0=ALU.mult),
          reads=["bet"], writes=["bet"], tag="b02")
    for tt in range(16):
        yb, ybk = ytok[tt % 2], f"ytok{tt % 2}"
        for nh in range(2):
            p = pY[nh]
            pk = f"pY{nh}"
            for cc in range(8):
                s.add("pe", lambda t, p=p, cc=cc, tt=tt, nh=nh:
                      t.matmul(p[:], OcatT[:, cc * NT + tt * 128: cc * NT + (tt + 1) * 128],
                               wo_sb[:, cc * 1024 + nh * 512: cc * 1024 + (nh + 1) * 512],
                               start=(cc == 0), stop=False),
                      reads=["OcatT", "wo_sb"], writes=[pk], tag="ymm")
            s.add("pe", lambda t, p=p, nh=nh:
                  t.matmul(p[:], B.onesrow[0:1, 0:128], bo_row[0:1, nh * 512:(nh + 1) * 512],
                           start=False, stop=True),
                  reads=["onesrow", "bo_row"], writes=[pk], tag="ybias")
            s.add("dve", lambda v, p=p, o=yb, nh=nh: v.tensor_copy(o[:, nh * 512:(nh + 1) * 512], p[:]),
                  reads=[pk], writes=[ybk], tag="yev")
        st = B.lnst[B.lnst_i % len(B.lnst)]
        B.lnst_i += 1
        (rstd, rstdk), (nb, nbk) = emit_ln_tok(s, B, yb, ybk, st)
        yw, ywk = ywork[tt % 2], f"ywork{tt % 2}"
        s.add("act", lambda sc, o=yw, x=yb, r=rstd, b=nb:
              sc.activation(o[:], x[:], AF.Identity, bias=b[:], scale=r[:]),
              reads=[ybk], sreads=[rstdk, nbk], writes=[ywk], tag="ynorm")
        s.add("dve", lambda v, t=yw: v.tensor_tensor(t[:], t[:], gam[:], op=ALU.mult),
              reads=[ywk, "gam"], writes=[ywk], tag="ygam")
        s.add("pool", lambda g_, t=yw: g_.tensor_tensor(t[:], t[:], bet[:], op=ALU.add),
              reads=[ywk, "bet"], writes=[ywk], tag="ybet")
        xt, xtk = B.xt_rot[B.xt_i % len(B.xt_rot)]
        B.xt_i += 1
        s.dma("sp", lambda q, t=xt, a=din["x_own"], tt=tt: q.dma_start(t[:], a[tt * 128:(tt + 1) * 128, :]),
              reads=[xtk], writes=[xtk], tag="ldxc")
        ob, obk = B.out_rot[B.out_i % len(B.out_rot)]
        B.out_i += 1
        s.add("dve", lambda v, o=ob, a=yw, b=xt: v.tensor_tensor(o[:], a[:], b[:], op=ALU.add),
              reads=[ywk, xtk], writes=[obk], tag="oadd")
        s.dma("sp", lambda q, t=ob, tt=tt: q.dma_start(xout[tt * 128:(tt + 1) * 128, :], t[:]),
              reads=[obk], writes=[f"out_x{tt}"], tag="stx")
    s.add("sp", lambda q: q.nop(), reads=[f"out_x{t}" for t in range(16)], tag="fin")
    close()
    close(gblk)
    return din



def _mk_sched():
    nc = bass.Bass("TRN2", target_bir_lowering=False)
    s = Sched(nc)
    B = alloc_consts(nc, s)
    alloc_common(nc, s, B)
    return nc, s, B


def _finish(nc, s):
    s.verify()
    sems = {eng: nc.alloc_semaphore(f"s_{eng}") for eng in ENGS}
    qsems = {(q, i): nc.alloc_semaphore(f"q_{q}{i}") for q in QUEUES for i in range(NQSLOT)}
    s.emit(sems, qsems, None)
    return nc


_B0_CACHE = {}


def build_b0_program():
    if "nc" not in _B0_CACHE:
        nc, s, B = _mk_sched()
        xout = nc.dram_tensor("xb0", [NT, D], F32, kind="ExternalOutput").ap()
        build_l0_attn(nc, s, B, xout)
        _B0_CACHE["nc"] = _finish(nc, s)
        _B0_CACHE["stats"] = s.stats()
    return _B0_CACHE["nc"]


def run_block0(inputs, x_cur=None):
    C = _host_consts()
    nc = build_b0_program()
    X = np.asarray(inputs["L0_attn_X"], dtype=np.float32)
    gam_t = np.broadcast_to(np.asarray(inputs["L0_attn_gamma"], np.float32), (128, D)).copy()
    bet_t = np.broadcast_to(np.asarray(inputs["L0_attn_beta"], np.float32), (128, D)).copy()
    lam64 = np.broadcast_to(np.asarray(inputs["L0_attn_lam"], np.float32).reshape(1, 1), (64, 1)).copy()
    shared = {
        "wq": np.asarray(inputs["L0_attn_Wq"], np.float32), "bq": np.asarray(inputs["L0_attn_bq"], np.float32),
        "wk": np.asarray(inputs["L0_attn_Wk"], np.float32), "bk": np.asarray(inputs["L0_attn_bk"], np.float32),
        "wv": np.asarray(inputs["L0_attn_Wv"], np.float32), "bv": np.asarray(inputs["L0_attn_bv"], np.float32),
        "wo": np.asarray(inputs["L0_attn_Wo"], np.float32), "bo": np.asarray(inputs["L0_attn_bo"], np.float32),
        "gam_t": gam_t, "bet_t": bet_t, "lam64": lam64, "c_selz": C["selz"],
        "c_cid16": C["cid16"], "c_cid32": C["cid32"], "c_ones128": C["ones128"], "c_onesrow": C["onesrow"],
    }
    in_maps = []
    for c in range(8):
        b = c // 2
        own, oth = _core_rows(c)
        m = dict(shared)
        m["x_own"] = np.ascontiguousarray(X[b][own])
        m["x_oth"] = np.ascontiguousarray(X[b][oth])
        in_maps.append(m)
    res = run_bass_kernel_spmd(nc, in_maps, core_ids=list(range(8)))
    return [r["xb0"] for r in res.results]


def kernel(**inputs):
    xb = run_block0(inputs)
    xb = run_block1(inputs, xb)
    xb = run_block2(inputs, xb)
    xb = run_block3(inputs, xb)
    out = np.zeros((4, S_FULL, D), np.float32)
    for c in range(8):
        b = c // 2
        own, _ = _core_rows(c)
        out[b][own] = xb[c]
    return out



def build_l0_ffn(nc, s, B, xprev, xout):
    din = {"xprev": xprev}
    for nm, shp, dt in [("w_fc", [4096, D], F32),
                        ("ln_w", [4096], F32), ("ln_b", [4096], F32),
                        ("w_proj", [D, 4096], F32)]:
        din[nm] = nc.dram_tensor(nm, shp, dt, kind="ExternalInput").ap()
    gblk, gph = [], []

    def sb(name, shape, dtype, scope=None):
        g = nc.sbuf_tensor(name, shape, dtype)
        (gph if scope is None else scope).append(g)
        return g.__enter__()

    def ps(name, shape, dtype=F32):
        g = nc.psum_tensor(name, shape, dtype)
        gph.append(g)
        return g.__enter__()

    def close(lst=None):
        lst = gph if lst is None else lst
        for g in reversed(lst):
            g.__exit__(None, None, None)
        lst.clear()

    xn = sb("xn1", [128, DC * NT], BF16, scope=gblk)
    wfc8 = sb("wfc8", [128, 32768], F8, scope=gblk)
    wpj8 = sb("wpj8", [128, 32768], F8, scope=gblk)
    d1dram = nc.dram_tensor("d1dram", [D, NT], F32, kind="Internal").ap()

    B.tps = [(ps(f"tpF{i}", [128, 128]), f"tpF{i}") for i in range(4)]
    pq = ps("pq", [128, 512])
    emit_load_ln_fm(s, B, din["xprev"], xn, "xn1", lambda dc, tt: dc * NT + tt * 128)
    onescol = sb("onescol", [128, 1], F32)
    ones1f = sb("ones1f", [1, 128], F32)
    s.add("dve", lambda v: v.memset(onescol[:], 1.0), writes=["onescol"], tag="oc")
    s.add("dve", lambda v: v.memset(ones1f[:], 1.0), writes=["ones1f"], tag="o1f")

    for wi, (wnm, nrow_t, ncol) in enumerate([("w_fc", 32, 1024), ("w_proj", 8, 4096)]):
        gwi = []
        wt = [sb(f"wt{wi}_{i}", [128, ncol], F32, scope=gwi) for i in range(2)]
        g1t = [sb(f"g1t{wi}_{i}", [128, ncol], BF16, scope=gwi) for i in range(2)]
        g2t = [sb(f"g2t{wi}_{i}", [128, ncol], BF16, scope=gwi) for i in range(2)]
        cols = sb(f"cols{wi}", [128, nrow_t], F32, scope=gwi)
        rowt = sb(f"rowt{wi}", [128, 1], F32, scope=gwi)
        nthr = sb(f"nthr{wi}", [128, 1], F32, scope=gwi)
        pthr = sb(f"pthr{wi}", [128, 1], F32, scope=gwi)
        for i in range(nrow_t):
            w_, wk_ = wt[i % 2], f"wt{wi}_{i % 2}"
            s.dma("sp" if i % 2 == 0 else "pool",
                  lambda q, t=w_, i=i, a=din[wnm]: q.dma_start(t[:], a[i * 128:(i + 1) * 128, :]),
                  reads=[wk_], writes=[wk_], tag="qld")
            s.add("dve", lambda v, t=w_, i=i, c=cols: v.reduce_sum(c[:, i:i + 1], t[:], axis=AX.X,
                                                                   apply_absolute_value=True),
                  reads=[wk_], writes=[f"cols{wi}"], tag="qabs")
        s.add("dve", lambda v, c=cols, r=rowt: v.reduce_sum(r[:], c[:], axis=AX.X),
              reads=[f"cols{wi}"], writes=[f"rowt{wi}"], tag="qrow")
        s.add("pe", lambda t, r=rowt: t.matmul(pq[0:1, 0:1], r[:], onescol[:], start=True, stop=True),
              reads=[f"rowt{wi}", "onescol"], writes=["pq"], tag="qtot")
        nel = float(nrow_t * 128 * ncol)
        nthr0 = sb(f"nthr0_{wi}", [1, 1], F32, scope=gwi)
        s.add("act", lambda sc, o=nthr0, sc_=-0.5 / nel: sc.activation(o[:], pq[0:1, 0:1], AF.Identity, scale=sc_),
              reads=["pq"], writes=[f"nthr0k{wi}"], tag="qthr")
        s.add("pe", lambda t, r=nthr0: t.matmul(pq[:, 32:33], ones1f[:], r[:], start=True, stop=True),
              reads=[f"nthr0k{wi}", "ones1f"], writes=["pq3"], tag="qbc")
        s.add("dve", lambda v, n=nthr: v.tensor_scalar(n[:], pq[:, 32:33], -0.5e-6, None, op0=ALU.add),
              reads=["pq3"], writes=[f"nthr{wi}"], tag="qnth")
        s.add("dve", lambda v, n=nthr, p=pthr: v.tensor_scalar(p[:], n[:], -1.0, None, op0=ALU.mult),
              reads=[f"nthr{wi}"], writes=[f"pthr{wi}"], tag="qpth")
        for i in range(nrow_t):
            w_, wk_ = wt[i % 2], f"wt{wi}_{i % 2}"
            g1_, g1k = g1t[i % 2], f"g1t{wi}_{i % 2}"
            g2_, g2k = g2t[i % 2], f"g2t{wi}_{i % 2}"
            q_, qk_ = g1_, g1k
            s.dma("sp" if i % 2 == 0 else "pool",
                  lambda q, t=w_, i=i, a=din[wnm]: q.dma_start(t[:], a[i * 128:(i + 1) * 128, :]),
                  reads=[wk_], writes=[wk_], tag="qld2")
            s.add("dve", lambda v, t=w_, o=g1_, th=pthr: v.tensor_scalar(o[:], t[:], th[:], None, op0=ALU.is_gt),
                  reads=[wk_], sreads=[f"pthr{wi}"], writes=[g1k], tag="qgt")
            s.add("dve", lambda v, t=w_, o=g2_, th=nthr: v.tensor_scalar(o[:], t[:], th[:], None, op0=ALU.is_lt),
                  reads=[wk_], sreads=[f"nthr{wi}"], writes=[g2k], tag="qlt")
            s.add("dve", lambda v, a=g1_, b=g2_: v.tensor_tensor(a[:], a[:], b[:], op=ALU.subtract),
                  reads=[g1k, g2k], writes=[g1k], tag="qsub")
            ngrp = ncol // 512
            for grp in range(ngrp):
                for e4 in range(4):
                    pt_, ptk = B.tps[e4 % 4]
                    ec = grp * 4 + e4
                    dst = (wfc8[:, (i * 8 + ec) * 128:(i * 8 + ec) * 128 + 128] if wi == 0
                           else wpj8[:, (i * 32 + ec) * 128:(i * 32 + ec) * 128 + 128])
                    dk = "wfc8" if wi == 0 else "wpj8"
                    s.add("pe", lambda t, p=pt_, q2=q_, ec=ec:
                          t.matmul(p[:], q2[:, ec * 128:(ec + 1) * 128], B.cid16[:], start=True, stop=True),
                          reads=[qk_, "cid16"], writes=[ptk], tag="qT")
                    eng = "dve" if e4 % 2 == 0 else "act"
                    if eng == "dve":
                        s.add("dve", lambda v, p=pt_, o=dst: v.tensor_copy(o, p[:]),
                              reads=[ptk], writes=[dk], tag="qTe")
                    else:
                        s.add("act", lambda sc, p=pt_, o=dst: sc.activation(o, p[:], AF.Identity),
                              reads=[ptk], writes=[dk], tag="qTe")
        s.barrier(f"endQ{wi}")
        close(gwi)
    s.barrier("endQ")
    close()

    ph = [ps(f"ph{i}", [128, 512]) for i in range(2)]
    ps1 = ps("ps1", [128, 512])
    ps2 = ps("ps2", [128, 512])
    pm = [ps(f"pm{i}", [128, 512]) for i in range(2)]
    hsb2 = [sb("hsb0", [128, 16384], BF16)]
    dt1 = [sb(f"dt1_{i}", [128, 512], F32) for i in range(2)]

    lw = [sb(f"lw{i}", [128, 1024], BF16) for i in range(3)]
    lw2 = [sb(f"lw2_{i}", [128, 4096], BF16) for i in range(2)]
    sqs = [sb(f"hsq{i}", [128, 512], BF16) for i in range(2)]
    mu = sb("mu", [128, 512], F32)
    mu2 = sb("mu2", [128, 512], F32)
    var = sb("var", [128, 512], F32)
    rstd = sb("rstd", [128, 512], F32)
    mr = sb("mr", [128, 512], F32)
    tmv = [sb(f"tmv{i}", [128, 512], F32) for i in range(2)]
    lnwn = sb("lnwn", [32, 128], F32)
    lnbn = sb("lnbn", [32, 128], F32)
    lnw = sb("lnw", [128, 32], F32)
    lnb = sb("lnb", [128, 32], F32)
    s.dma("sp", lambda q: q.dma_start(lnwn[:], din["ln_w"].rearrange("(a b) -> a b", a=32)),
          writes=["lnwn"], tag="ldlnw")
    s.dma("sp", lambda q: q.dma_start(lnbn[:], din["ln_b"].rearrange("(a b) -> a b", a=32)),
          writes=["lnbn"], tag="ldlnb")
    s.add("pe", lambda t: t.matmul(ph[0][:, 0:32], lnwn[:], B.cid32[0:32, 0:32], start=True, stop=True),
          reads=["lnwn", "cid32"], writes=["ph0"], tag="lnwT")
    s.add("dve", lambda v: v.tensor_copy(lnw[:], ph[0][:, 0:32]), reads=["ph0"], writes=["lnwk"], tag="lnwTe")
    s.add("pe", lambda t: t.matmul(ph[0][:, 0:32], lnbn[:], B.cid32[0:32, 0:32], start=True, stop=True),
          reads=["lnbn", "cid32"], writes=["ph0"], tag="lnbT")
    s.add("dve", lambda v: v.tensor_copy(lnb[:], ph[0][:, 0:32]), reads=["ph0"], writes=["lnbk"], tag="lnbTe")

    for nt in range(4):
        hsb, hsbk = hsb2[0], "hsb0"
        for hc in range(32):
            w_, wk_ = lw[hc % 3], f"lw{hc % 3}"
            ceng = "act" if hc % 2 == 0 else "dve"
            if ceng == "act":
                s.add("act", lambda sc, t=w_, hc=hc:
                      sc.activation(t[:], wfc8[:, hc * 1024:(hc + 1) * 1024], AF.Identity),
                      reads=["wfc8"], writes=[wk_], tag="m1c")
            else:
                s.add("dve", lambda v, t=w_, hc=hc:
                      v.tensor_copy(t[:], wfc8[:, hc * 1024:(hc + 1) * 1024]),
                      reads=["wfc8"], writes=[wk_], tag="m1c")
            p_, pk_ = ph[hc % 2], f"ph{hc % 2}"
            for ec in range(8):
                s.add("pe", lambda t, p=p_, w=w_, ec=ec, nt=nt:
                      t.matmul(p[:], w[:, ec * 128:(ec + 1) * 128],
                               xn[:, ec * NT + nt * 512:ec * NT + nt * 512 + 512],
                               start=(ec == 0), stop=(ec == 7)),
                      reads=[wk_, "xn1"], writes=[pk_], tag="m1")
            s.add("act", lambda sc, p=p_, hc=hc, hsb=hsb: sc.activation(hsb[:, hc * 512:(hc + 1) * 512], p[:], AF.Gelu),
                  reads=[pk_], writes=[f"{hsbk}:{hc}"], tag="gelu")
            sq_, sqk_ = sqs[hc % 2], f"hsq{hc % 2}"
            s.add("act", lambda sc, o=sq_, hc=hc, hsb=hsb: sc.activation(o[:], hsb[:, hc * 512:(hc + 1) * 512], AF.Square),
                  reads=[f"{hsbk}:{hc}"], writes=[sqk_], tag="hsqr")
            s.add("pe", lambda t, hc=hc, hsb=hsb: t.matmul(ps1[:], B.ones128[:], hsb[:, hc * 512:(hc + 1) * 512],
                                                           start=(hc == 0), stop=(hc == 31)),
                  reads=[f"{hsbk}:{hc}", "ones128"], writes=["ps1"], tag="st1")
            s.add("pe", lambda t, q2=sq_, hc=hc: t.matmul(ps2[:], B.ones128[:], q2[:],
                                                          start=(hc == 0), stop=(hc == 31)),
                  reads=[sqk_, "ones128"], writes=["ps2"], tag="st2")
        s.add("act", lambda sc: sc.activation(mu[:], ps1[:], AF.Identity, scale=1.0 / 4096.0),
              reads=["ps1"], writes=["muk"], tag="lmu")
        s.add("act", lambda sc: sc.activation(mu2[:], mu[:], AF.Square),
              reads=["muk"], writes=["mu2k"], tag="lmu2")
        s.add("dve", lambda v: v.scalar_tensor_tensor(var[:], ps2[:], 1.0 / 4096.0, mu2[:],
                                                      op0=ALU.mult, op1=ALU.subtract),
              reads=["ps2", "mu2k"], writes=["vark"], tag="lvar")
        s.add("dve", lambda v: v.tensor_scalar(var[:], var[:], float(EPS), None, op0=ALU.add),
              reads=["vark"], writes=["vark"], tag="lveps")
        s.add("act", lambda sc: sc.activation(var[:], var[:], AF.Ln),
              reads=["vark"], writes=["vark"], tag="lln")
        s.add("act", lambda sc: sc.activation(rstd[:], var[:], AF.Exp, scale=-0.5),
              reads=["vark"], writes=["rstdk"], tag="lexp")
        s.add("dve", lambda v: v.tensor_tensor(mr[:], mu[:], rstd[:], op=ALU.mult),
              reads=["muk", "rstdk"], writes=["mrk"], tag="lmr")
        for hc in range(32):
            t_, tk_ = tmv[hc % 2], f"tmv{hc % 2}"
            s.add("dve", lambda v, o=t_, hc=hc, hsb=hsb: v.tensor_tensor(o[:], hsb[:, hc * 512:(hc + 1) * 512], rstd[:],
                                                                         op=ALU.mult),
                  reads=[f"{hsbk}:{hc}", "rstdk"], writes=[tk_], tag="ln1")
            s.add("pool", lambda g, o=t_: g.tensor_tensor(o[:], o[:], mr[:], op=ALU.subtract),
                  reads=[tk_, "mrk"], writes=[tk_], tag="ln2")
            s.add("act", lambda sc, t2=t_, hc=hc, hsb=hsb:
                  sc.activation(hsb[:, hc * 512:(hc + 1) * 512], t2[:], AF.Identity,
                                scale=lnw[:, hc:hc + 1], bias=lnb[:, hc:hc + 1]),
                  reads=[tk_], sreads=["lnwk", "lnbk"], writes=[f"{hsbk}:{hc}"], tag="ln3")
        for dc in range(8):
            w_, wk_ = lw2[dc % 2], f"lw2_{dc % 2}"
            ceng = "act" if dc % 2 == 0 else "dve"
            if ceng == "act":
                s.add("act", lambda sc, t=w_, dc=dc:
                      sc.activation(t[:], wpj8[:, dc * 4096:(dc + 1) * 4096], AF.Identity),
                      reads=["wpj8"], writes=[wk_], tag="m2c")
            else:
                s.add("dve", lambda v, t=w_, dc=dc:
                      v.tensor_copy(t[:], wpj8[:, dc * 4096:(dc + 1) * 4096]),
                      reads=["wpj8"], writes=[wk_], tag="m2c")
            p_, pk_ = pm[dc % 2], f"pm{dc % 2}"
            for hc in range(32):
                s.add("pe", lambda t, p=p_, w=w_, hc=hc, hsb=hsb:
                      t.matmul(p[:], w[:, hc * 128:(hc + 1) * 128], hsb[:, hc * 512:(hc + 1) * 512],
                               start=(hc == 0), stop=(hc == 31)),
                      reads=[wk_, f"{hsbk}:{hc}"], writes=[pk_], tag="m2")
            d_, dk_ = dt1[dc % 2], f"dt1_{dc % 2}"
            s.add("dve", lambda v, p=p_, o=d_: v.tensor_copy(o[:], p[:]),
                  reads=[pk_], writes=[dk_], tag="m2e")
            s.dma("sp", lambda q, t=d_, dc=dc, nt=nt:
                  q.dma_start(d1dram.rearrange("(dc p) t -> dc p t", p=128)[dc, :, nt * 512:(nt + 1) * 512], t[:]),
                  reads=[dk_], writes=["d1dram"], tag="m2st")
    s.barrier("endM")
    close()

    B.tps = [(ps(f"tpE{i}", [128, 128]), f"tpE{i}") for i in range(4)]
    dl_rot1 = [sb(f"dl1_{i}", [128, 1024], F32) for i in range(2)]
    for tt in range(16):
        dlb, dlk = dl_rot1[tt % 2], f"dl1_{tt % 2}"
        s.dma("sp", lambda q, t=dlb, tt=tt:
              q.dma_start(t[:].rearrange("p (dc c) -> p dc c", dc=8),
                          d1dram.rearrange("(dc p) t -> p dc t", p=128)[:, :, tt * 128:(tt + 1) * 128]),
              reads=[dlk, "d1dram"], writes=[dlk], tag="dlld1")
        xt, xtk = B.xt_rot[B.xt_i % len(B.xt_rot)]
        B.xt_i += 1
        s.dma("sp", lambda q, t=xt, a=din["xprev"], tt=tt: q.dma_start(t[:], a[tt * 128:(tt + 1) * 128, :]),
              reads=[xtk], writes=[xtk], tag="ldxe1")
        ob, obk = B.out_rot[B.out_i % len(B.out_rot)]
        B.out_i += 1
        for dc in range(DC):
            pt_, ptk = B.tps[dc % 4]
            s.add("pe", lambda t, p=pt_, d=dlb, dc=dc:
                  t.matmul(p[:], d[:, dc * 128:(dc + 1) * 128], B.cid32[:], start=True, stop=True),
                  reads=[dlk, "cid32"], writes=[ptk], tag="dT1")
            s.add("dve", lambda v, p=pt_, o=ob, x=xt, dc=dc:
                  v.tensor_tensor(o[:, dc * 128:(dc + 1) * 128], p[:], x[:, dc * 128:(dc + 1) * 128], op=ALU.add),
                  reads=[ptk, xtk], writes=[obk], tag="dadd1")
        s.dma("sp", lambda q, t=ob, tt=tt: q.dma_start(xout[tt * 128:(tt + 1) * 128, :], t[:]),
              reads=[obk], writes=[f"outx1_{tt}"], tag="stx1")
    s.add("sp", lambda q: q.nop(), reads=[f"outx1_{t}" for t in range(16)], tag="fin")
    close()
    close(gblk)
    return din


_B1_CACHE = {}


def build_b1_program():
    if "nc" not in _B1_CACHE:
        nc, s, B = _mk_sched()
        xin = nc.dram_tensor("xprev", [NT, D], F32, kind="ExternalInput").ap()
        xout = nc.dram_tensor("xb1", [NT, D], F32, kind="ExternalOutput").ap()
        build_l0_ffn(nc, s, B, xin, xout)
        _B1_CACHE["nc"] = _finish(nc, s)
        _B1_CACHE["stats"] = s.stats()
    return _B1_CACHE["nc"]


def run_block1(inputs, x_cur):
    C = _host_consts()
    nc = build_b1_program()
    shared = {
        "w_fc": np.asarray(inputs["L0_ffn_W_fc"], np.float32),
        "ln_w": np.asarray(inputs["L0_ffn_ln_w"], np.float32),
        "ln_b": np.asarray(inputs["L0_ffn_ln_b"], np.float32),
        "w_proj": np.asarray(inputs["L0_ffn_W_proj"], np.float32),
        "c_cid16": C["cid16"], "c_cid32": C["cid32"], "c_ones128": C["ones128"], "c_onesrow": C["onesrow"],
        "c_selz": C["selz"],
    }
    in_maps = []
    for c in range(8):
        m = dict(shared)
        m["xprev"] = np.ascontiguousarray(x_cur[c])
        in_maps.append(m)
    res = run_bass_kernel_spmd(nc, in_maps, core_ids=list(range(8)))
    return [r["xb1"] for r in res.results]



def build_l1_attn(nc, s, B, xprev, xout):
    din = {"xprev": xprev}
    for nm, shp, dt in [("wq1", [D, D], F32), ("wk1", [D, D], F32),
                        ("wv1", [D, D], F32), ("wo1", [D, D], F32),
                        ("betas8", [8, 64], F32), ("m08", [8, 64, 64], F32), ("z08", [8, 64], F32),
                        ("c_masks", [128, 2048], BF16)]:
        din[nm] = nc.dram_tensor(nm, shp, dt, kind="ExternalInput").ap()
    ddram = nc.dram_tensor("d2dram", [D, NT], F32, kind="Internal").ap()

    gblk, gph = [], []

    def sb(name, shape, dtype, scope=None):
        g = nc.sbuf_tensor(name, shape, dtype)
        (gph if scope is None else scope).append(g)
        return g.__enter__()

    def ps(name, shape, dtype=F32):
        g = nc.psum_tensor(name, shape, dtype)
        gph.append(g)
        return g.__enter__()

    def close(lst=None):
        lst = gph if lst is None else lst
        for g in reversed(lst):
            g.__exit__(None, None, None)
        lst.clear()

    xn = sb("xn2", [128, DC * NT], BF16, scope=gblk)

    B.tps = [(ps(f"tpL{i}", [128, 128]), f"tpL{i}") for i in range(4)]
    emit_load_ln_fm(s, B, din["xprev"], xn, "xn2", lambda dc, tt: dc * NT + tt * 128)
    s.barrier("endA2")
    close()

    pg = [ps(f"pg{i}", [128, 512]) for i in range(2)]
    tp2 = [(ps(f"tpT{i}", [128, 128]), f"tpT{i}") for i in range(2)]
    pO = ps("pO", [65, 512])
    pAm = ps("pAm", [65, 512])
    pD = ps("pD", [64, 128])
    pb = ps("pb", [64, 512])

    wo_sb = sb("wo1_sb", [128, 8192], BF16)
    wsl = {nm: sb(f"wsl_{nm}", [128, 8192], BF16) for nm in ("q", "k", "v")}
    for nm, wdram in (("q", "wq1"), ("k", "wk1"), ("v", "wv1")):
        s.dma("pool", lambda q, t=wsl[nm], a=din[wdram].rearrange("(dc p) c -> p dc c", p=128):
              q.dma_start(t[:].rearrange("p (dc c) -> p dc c", dc=8), a),
              writes=[f"wsl{nm}"], tag="ldwsl")
    qT = sb("qT", [64, 4096], F32)
    qTb = sb("qTb", [64, 4096], BF16)
    kTb = sb("kTb", [64, 4096], BF16)
    vT = sb("vT", [64, 4096], BF16)
    k_h = [sb(f"k_h{i}", [128, 256], BF16) for i in range(2)]
    v_h = [sb(f"v_h{i}", [128, 4 * 65], BF16) for i in range(2)]
    skt = [sb(f"skt{i}", [128, 256], F32) for i in range(2)]
    sk = [sb(f"sk{i}", [128, 256], BF16) for i in range(2)]
    sqt = [sb(f"sqt{i}", [64, 512], F32) for i in range(2)]
    sqf = [sb(f"sqf{i}", [64, 512], F32) for i in range(2)]
    eb = [sb(f"ebL{i}", [128, 512], BF16) for i in range(2)]
    ustA = [sb(f"ustA{i}", [64, 512], BF16) for i in range(2)]
    ustB = [sb(f"ustB{i}", [64, 512], BF16) for i in range(2)]
    zb1 = [sb(f"zb1d{i}", [66, 512], F32) for i in range(2)]
    sgnb = [sb(f"sgnbd{i}", [66, 512], BF16) for i in range(2)]
    izb1 = [sb(f"izb1d{i}", [66, 512], BF16) for i in range(2)]
    ct1 = [sb(f"ct1L{i}", [64, 512], BF16) for i in range(2)]
    ct2 = [sb(f"ct2L{i}", [64, 512], BF16) for i in range(2)]
    attT = [sb(f"attT{i}", [64, 512], BF16) for i in range(2)]
    preT = sb("preT", [128, 2048], BF16)
    dtile = [sb("dtile_s", [128, 256], F32)] * 2
    masks = sb("masks", [128, 2048], BF16)
    MZ = sb("MZ", [64, 8 * 65], F32)
    betn = sb("betn", [8, 64], F32)
    z0n = sb("z0n", [8, 64], F32)
    gat = sb("gat", [64, 8], F32)
    ogat = sb("ogat", [64, 8], F32)

    s.dma("pool", lambda q: q.dma_start(wo_sb[:].rearrange("p (cc d) -> p cc d", cc=8),
                                        din["wo1"].rearrange("(cc p) d -> p cc d", p=128)),
          writes=["wo_sb"], tag="ldwo1")
    s.dma("sp", lambda q: q.dma_start(masks[:], din["c_masks"]), writes=["masks"], tag="ldmask")
    s.dma("sp", lambda q: q.dma_start(betn[:], din["betas8"]), writes=["betn"], tag="ldbet")
    s.dma("sp", lambda q: q.dma_start(z0n[:], din["z08"]), writes=["z0n"], tag="ldz0")
    s.dma("sp", lambda q: q.dma_start(MZ[:].rearrange("d (h c) -> d h c", h=8)[:, :, 0:64],
                                      din["m08"].rearrange("h d v -> d h v")),
          writes=["MZ"], tag="ldm0")
    s.add("pe", lambda t: t.matmul(pg[0][0:64, 0:8], z0n[:], B.cid32[0:8, 0:8], start=True, stop=True),
          reads=["z0n", "cid32"], writes=["pg0"], tag="z0T")
    s.add("dve", lambda v: v.tensor_copy(MZ[:].rearrange("d (h c) -> d h c", h=8)[:, :, 64:65],
                                         pg[0][0:64, 0:8].rearrange("d (h o) -> d h o", o=1)),
          reads=["pg0"], writes=["MZ"], tag="z0Te")
    s.add("pe", lambda t: t.matmul(pg[0][0:64, 64:72], betn[:], B.cid32[0:8, 0:8], start=True, stop=True),
          reads=["betn", "cid32"], writes=["pg0"], tag="betT")
    s.add("act", lambda sc: sc.activation(gat[:], pg[0][0:64, 64:72], AF.Exp, scale=-1.0),
          reads=["pg0"], writes=["gat"], tag="gexp")
    s.add("dve", lambda v: v.tensor_scalar(gat[:], gat[:], 1.0, None, op0=ALU.add),
          reads=["gat"], writes=["gat"], tag="gadd")
    s.add("dve", lambda v: v.reciprocal(gat[:], gat[:]), reads=["gat"], writes=["gat"], tag="ginv")
    s.add("dve", lambda v: v.tensor_scalar(ogat[:], gat[:], -1.0, 1.0, op0=ALU.mult, op1=ALU.add),
          reads=["gat"], writes=["ogat"], tag="gneg")
    for i in range(2):
        s.add("dve", lambda v, i=i: v.memset(zb1[i][:], 1.0), writes=[f"zb1k{i}"], tag="zb1i")
    for i in range(2):
        s.add("dve", lambda v, i=i: v.memset(v_h[i][:].rearrange("p (l c) -> p l c", l=4)[:, :, 64:65], 1.0),
              writes=[f"v_h{i}"], tag="vhones")

    for seg in range(NSEG):
        for jp in range(8):
            for tn, (dst, dstk, eng) in {
                    "q": (qT, "qTk", "dve"),
                    "k": (kTb, "kTbk", "act"),
                    "v": (vT, "vTk", "dve")}.items():
                w_, wk_ = wsl[tn], f"wsl{tn}"
                p_, pk_ = pg[jp % 2], f"pg{jp % 2}"
                for dc in range(DC):
                    s.add("pe", lambda t, p=p_, w=w_, dc=dc, seg=seg, jp=jp:
                          t.matmul(p[:, 0:256], w[:, dc * 1024 + jp * 128: dc * 1024 + (jp + 1) * 128],
                                   xn[:, dc * NT + seg * 256: dc * NT + (seg + 1) * 256],
                                   start=(dc == 0), stop=(dc == 7)),
                          reads=[wk_, "xn2"], writes=[pk_], tag="jmm")
                for jj in range(2):
                    j = 2 * jp + jj

                    def evac(e, p=p_, jj=jj, j=j, dst=dst, eng=eng):
                        src = p[64 * jj:64 * jj + 64, 0:256].rearrange("d (h r) -> d h r", h=8)
                        dd = dst[:].rearrange("d (h l) -> d h l", h=8)[:, :, 32 * j: 32 * j + 32]
                        return e.tensor_copy(dd, src) if eng == "dve" else e.copy(dd, src)
                    s.add(eng, evac, reads=[pk_], writes=[dstk], tag=f"jev{tn}")
        s.add("dve", lambda v: v.tensor_copy(qTb[:], qT[:]), reads=["qTk"], writes=["qTbk"], tag="qTb")
        pending_tail = [None]
        for hp in range(8):
            hb = hp % 2
            kh, khk = k_h[hb], f"k_h{hb}"
            vh, vhk = v_h[hb], f"v_h{hb}"
            skb, skk = sk[hb], f"sk{hb}"
            sqb, sqk = sqf[hb], f"sqf{hb}"
            uA_, uAk = ustA[hb], f"ustA{hb}"
            uB_, uBk = ustB[hb], f"ustB{hb}"
            zb_, zbk = zb1[hb], f"zb1k{hb}"
            sg_, sgk = sgnb[hb], f"sgnbk{hb}"
            iz_, izk = izb1[hb], f"izb1k{hb}"
            c1_, c1k = ct1[hb], f"ct1L{hb}"
            c2_, c2k = ct2[hb], f"ct2L{hb}"
            aT_, aTk = attT[hb], f"attT{hb}"
            sk0, sk0k = skt[0], "skt0"
            sk1, sk1k = skt[1], "skt1"
            sq0, sq0k = sqt[0], "sqt0"
            sq1, sq1k = sqt[1], "sqt1"
            for lt in range(4):
                pt_, ptk = tp2[lt % 2]
                s.add("pe", lambda t, p=pt_, hp=hp, lt=lt:
                      t.matmul(p[:, 0:64], kTb[0:64, hp * 512 + lt * 128: hp * 512 + (lt + 1) * 128],
                               B.cid16[0:64, 0:64], start=True, stop=True),
                      reads=["kTbk", "cid16"], writes=[ptk], tag="kT")
                s.add("dve", lambda e, p=pt_, lt=lt, kh=kh: e.tensor_copy(kh[:, lt * 64:(lt + 1) * 64], p[:, 0:64]),
                      reads=[ptk], writes=[khk], tag="kTe")
            for lt in range(4):
                pt_, ptk = tp2[lt % 2]
                s.add("pe", lambda t, p=pt_, hp=hp, lt=lt:
                      t.matmul(p[:, 64:128], vT[0:64, hp * 512 + lt * 128: hp * 512 + (lt + 1) * 128],
                               B.cid16[0:64, 0:64], start=True, stop=True),
                      reads=["vTk", "cid16"], writes=[ptk], tag="vT")
                s.add("act", lambda e, p=pt_, lt=lt, vh=vh: e.copy(vh[:, lt * 65: lt * 65 + 64], p[:, 64:128]),
                      reads=[ptk], writes=[vhk], tag="vTe")
            s.add("dve", lambda v, o=sk0, kh=kh: v.tensor_scalar(o[:], kh[:], 0.0, None, op0=ALU.min),
                  reads=[khk], writes=[sk0k], tag="sk1")
            s.add("act", lambda sc, o=sk1, i=sk0: sc.activation(o[:], i[:], AF.Exp),
                  reads=[sk0k], writes=[sk1k], tag="sk2")
            s.add("dve", lambda g, o=sk0, kh=kh: g.tensor_relu(o[:], kh[:]),
                  reads=[khk], writes=[sk0k], tag="sk3")
            s.add("dve", lambda v, o=skb, a=sk1, b=sk0: v.tensor_tensor(o[:], a[:], b[:], op=ALU.add),
                  reads=[sk0k, sk1k], writes=[skk], tag="sk4")
            s.add("dve", lambda v, o=sq0, hp=hp: v.tensor_scalar(o[:], qT[:, hp * 512:(hp + 1) * 512], 0.0,
                                                                 None, op0=ALU.min),
                  reads=["qTk"], writes=[sq0k], tag="sq1")
            s.add("act", lambda sc, o=sq1, i=sq0: sc.activation(o[:], i[:], AF.Exp),
                  reads=[sq0k], writes=[sq1k], tag="sq2")
            s.add("dve", lambda g, o=sq0, hp=hp: g.tensor_relu(o[:], qT[:, hp * 512:(hp + 1) * 512]),
                  reads=["qTk"], writes=[sq0k], tag="sq3")
            s.add("dve", lambda v, o=sqb, a=sq1, b=sq0: v.tensor_tensor(o[:], a[:], b[:], op=ALU.add),
                  reads=[sq0k, sq1k], writes=[sqk], tag="sq4")

            def emit_sL(kt, hp=hp):
                p_, pk_ = pg[kt % 2], f"pg{kt % 2}"
                s.add("pe", lambda t, p=p_, hp=hp, kt=kt:
                      t.matmul(p[:], kTb[0:64, hp * 512 + kt * 128: hp * 512 + (kt + 1) * 128],
                               qTb[0:64, hp * 512:(hp + 1) * 512], start=True, stop=True),
                      reads=["kTbk", "qTbk"], writes=[pk_], tag="sL")

            def emit_pvL(kt, vh=vh, vhk=vhk):
                p_, pk_ = pg[kt % 2], f"pg{kt % 2}"
                e_, ek_ = eb[kt % 2], f"ebL{kt % 2}"
                s.add("act", lambda sc, o=e_, p=p_: sc.activation(o[:], p[:], AF.Exp, scale=0.125),
                      reads=[pk_], writes=[ek_], tag="eL")
                s.add("dve", lambda g, e2=e_, kt=kt:
                      g.tensor_tensor(e2[:], e2[:], masks[:, kt * 512:(kt + 1) * 512], op=ALU.mult),
                      reads=[ek_, "masks"], writes=[ek_], tag="mL")
                s.add("pe", lambda t, m2=e_, kt=kt, vh=vh:
                      t.matmul(pO[:], vh[:, kt * 65:(kt + 1) * 65], m2[:],
                               start=(kt == 0), stop=(kt == 3)),
                      reads=[vhk, ek_], writes=["pO"], tag="pvL")
            if pending_tail[0] is not None:
                pending_tail[0]()
                pending_tail[0] = None
            emit_sL(0)
            emit_sL(1)
            emit_pvL(0)
            emit_sL(2)
            emit_pvL(1)
            emit_sL(3)
            emit_pvL(2)
            emit_pvL(3)
            s.add("pe", lambda t, hp=hp, sqb=sqb: t.matmul(pAm[:], MZ[:, hp * 65:(hp + 1) * 65], sqb[:],
                                                           start=True, stop=True),
                  reads=["MZ", sqk], writes=["pAm"], tag="amm")
            for lt in range(4):
                s.add("pe", lambda t, lt=lt, skb=skb, vh=vh: t.matmul(pD[:, 0:65], skb[:, lt * 64:(lt + 1) * 64],
                                                                     vh[:, lt * 65:(lt + 1) * 65],
                                                                     start=(lt == 0), stop=(lt == 3)),
                      reads=[skk, vhk], writes=["pD"], tag="dmz")
            s.add("act", lambda sc, o=uA_: sc.copy(o[:], pO[0:64, :]), reads=["pO"], writes=[uAk], tag="uA")
            s.add("dve", lambda v, o=zb_: v.tensor_copy(o[0:1, :], pO[64:65, :]), reads=["pO"], writes=[zbk], tag="z0r")
            s.add("dve", lambda v, o=uB_: v.tensor_copy(o[:], pAm[0:64, :]), reads=["pAm"], writes=[uBk], tag="uB")
            s.add("act", lambda sc, o=zb_: sc.copy(o[64:65, :], pAm[64:65, :]), reads=["pAm"], writes=[zbk], tag="z64r")
            s.add("dve", lambda v, hp=hp: v.tensor_tensor(MZ[:, hp * 65:(hp + 1) * 65],
                                                          MZ[:, hp * 65:(hp + 1) * 65], pD[:, 0:65], op=ALU.add),
                  reads=["MZ", "pD"], writes=["MZ"], tag="mzup")
            def mk_tail(hp=hp, hb=hb, zb_=zb_, zbk=zbk, sg_=sg_, sgk=sgk, iz_=iz_, izk=izk,
                        uA_=uA_, uAk=uAk, uB_=uB_, uBk=uBk, c1_=c1_, c1k=c1k, c2_=c2_, c2k=c2k,
                        aT_=aT_, aTk=aTk):
                def emit_tail():
                    s.add("act", lambda sc, o=sg_, i=zb_: sc.activation(o[:], i[:], AF.Sign),
                          reads=[zbk], writes=[sgk], tag="sgnL")
                    s.add("dve", lambda v, o=zb_, b=sg_: v.tensor_tensor(o[:], o[:], b[:], op=ALU.mult),
                          reads=[zbk, sgk], writes=[zbk], tag="absL")
                    s.add("act", lambda sc, o=zb_: sc.activation(o[:], o[:], AF.Ln), reads=[zbk], writes=[zbk], tag="lnzL")
                    s.add("act", lambda sc, o=zb_: sc.activation(o[:], o[:], AF.Exp, scale=-1.0),
                          reads=[zbk], writes=[zbk], tag="expzL")
                    s.add("dve", lambda v, o=iz_, a=zb_, b=sg_: v.tensor_tensor(o[:], a[:], b[:], op=ALU.mult),
                          reads=[zbk, sgk], writes=[izk], tag="izbL")
                    s.add("pe", lambda t, i=iz_: t.matmul(pb[:], B.selz[0:66, 0:64], i[:], start=True, stop=True),
                          reads=[izk, "selz"], writes=["pb"], tag="bzL")
                    s.add("dve", lambda v, o=c1_, a=uA_: v.tensor_tensor(o[:], a[:], pb[:], op=ALU.mult),
                          reads=[uAk, "pb"], writes=[c1k], tag="cL1")
                    s.add("pe", lambda t, i=iz_: t.matmul(pb[:], B.selz[0:66, 64:128], i[:], start=True, stop=True),
                          reads=[izk, "selz"], writes=["pb"], tag="bdL")
                    s.add("dve", lambda v, o=c2_, a=uB_: v.tensor_tensor(o[:], a[:], pb[:], op=ALU.mult),
                          reads=[uBk, "pb"], writes=[c2k], tag="cL2")
                    s.add("dve", lambda v, o=c1_, hp=hp: v.tensor_scalar(o[:], o[:], ogat[:, hp:hp + 1], None, op0=ALU.mult),
                          reads=[c1k], sreads=["ogat"], writes=[c1k], tag="cL3")
                    s.add("dve", lambda v, o=aT_, a=c2_, b=c1_, hp=hp:
                          v.scalar_tensor_tensor(o[:], a[:], gat[:, hp:hp + 1], b[:], op0=ALU.mult, op1=ALU.add),
                          reads=[c2k, c1k], sreads=["gat"], writes=[aTk], tag="cL4")
                    s.add("dve", lambda v, hp=hp, a=aT_: v.tensor_copy(
                        preT[0:64, :].rearrange("d (cc t) -> d cc t", cc=8)[:, :, 32 * hp: 32 * hp + 32],
                        a[:].rearrange("d (m c) -> d m c", m=8)[:, :, 0:32]),
                        reads=[aTk], writes=["preT"], tag="scat0")
                    s.add("dve", lambda v, hp=hp, a=aT_: v.tensor_copy(
                        preT[64:128, :].rearrange("d (cc t) -> d cc t", cc=8)[:, :, 32 * hp: 32 * hp + 32],
                        a[:].rearrange("d (m c) -> d m c", m=8)[:, :, 32:64]),
                        reads=[aTk], writes=["preT"], tag="scat1")
                return emit_tail
            pending_tail[0] = mk_tail()
        if pending_tail[0] is not None:
            pending_tail[0]()
            pending_tail[0] = None
        for dc in range(DC):
            p_, pk_ = pg[dc % 2], f"pg{dc % 2}"
            for cc in range(8):
                s.add("pe", lambda t, p=p_, cc=cc, dc=dc:
                      t.matmul(p[:, 0:256], wo_sb[:, cc * 1024 + dc * 128: cc * 1024 + (dc + 1) * 128],
                               preT[:, cc * 256:(cc + 1) * 256], start=(cc == 0), stop=(cc == 7)),
                      reads=["wo_sb", "preT"], writes=[pk_], tag="woL")
            d_, dk_ = dtile[0], "dtilek"
            s.add("dve", lambda v, p=p_, o=d_: v.tensor_copy(o[:], p[:, 0:256]),
                  reads=[pk_], writes=[dk_], tag="woLe")
            s.dma("sp", lambda q, t=d_, dc=dc, seg=seg:
                  q.dma_start(ddram.rearrange("(dc p) t -> dc p t", p=128)[dc, :, seg * 256:(seg + 1) * 256], t[:]),
                  reads=[dk_], writes=["ddram"], tag="woSt")
    s.barrier("endL1")
    close()

    B.tps = [(ps(f"tpE2{i}", [128, 128]), f"tpE2{i}") for i in range(4)]
    dl_rot = [sb(f"dl{i}", [128, 1024], F32) for i in range(2)]
    for tt in range(16):
        dlb, dlk = dl_rot[tt % 2], f"dl{tt % 2}"
        s.dma("sp", lambda q, t=dlb, tt=tt:
              q.dma_start(t[:].rearrange("p (dc c) -> p dc c", dc=8),
                          ddram.rearrange("(dc p) t -> p dc t", p=128)[:, :, tt * 128:(tt + 1) * 128]),
              reads=[dlk, "ddram"], writes=[dlk], tag="dlld")
        xt, xtk = B.xt_rot[B.xt_i % len(B.xt_rot)]
        B.xt_i += 1
        s.dma("sp", lambda q, t=xt, a=din["xprev"], tt=tt: q.dma_start(t[:], a[tt * 128:(tt + 1) * 128, :]),
              reads=[xtk], writes=[xtk], tag="ldxe2")
        ob, obk = B.out_rot[B.out_i % len(B.out_rot)]
        B.out_i += 1
        for dc in range(DC):
            pt_, ptk = B.tps[dc % 4]
            s.add("pe", lambda t, p=pt_, d=dlb, dc=dc:
                  t.matmul(p[:], d[:, dc * 128:(dc + 1) * 128], B.cid32[:], start=True, stop=True),
                  reads=[dlk, "cid32"], writes=[ptk], tag="dT2")
            s.add("dve", lambda v, p=pt_, o=ob, x=xt, dc=dc:
                  v.tensor_tensor(o[:, dc * 128:(dc + 1) * 128], p[:], x[:, dc * 128:(dc + 1) * 128], op=ALU.add),
                  reads=[ptk, xtk], writes=[obk], tag="dadd2")
        s.dma("sp", lambda q, t=ob, tt=tt: q.dma_start(xout[tt * 128:(tt + 1) * 128, :], t[:]),
              reads=[obk], writes=[f"outx2_{tt}"], tag="stx2")
    s.add("sp", lambda q: q.nop(), reads=[f"outx2_{t}" for t in range(16)], tag="fin")
    close()
    close(gblk)
    return din


_B2_CACHE = {}


def build_b2_program():
    if "nc" not in _B2_CACHE:
        nc, s, B = _mk_sched()
        xin = nc.dram_tensor("xprev", [NT, D], F32, kind="ExternalInput").ap()
        xout = nc.dram_tensor("xb2", [NT, D], F32, kind="ExternalOutput").ap()
        build_l1_attn(nc, s, B, xin, xout)
        _B2_CACHE["nc"] = _finish(nc, s)
        _B2_CACHE["stats"] = s.stats()
    return _B2_CACHE["nc"]


def run_block2(inputs, x_cur):
    C = _host_consts()
    nc = build_b2_program()
    betas = np.asarray(inputs["L1_attn_betas"], np.float32)[0, :, 0, :]
    m0 = np.asarray(inputs["L1_attn_memory0"], np.float32)[0]
    z0 = np.asarray(inputs["L1_attn_z0"], np.float32)[0, :, 0, :]
    shared = {
        "wq1": np.asarray(inputs["L1_attn_Wq"], np.float32),
        "wk1": np.asarray(inputs["L1_attn_Wk"], np.float32),
        "wv1": np.asarray(inputs["L1_attn_Wv"], np.float32),
        "wo1": np.asarray(inputs["L1_attn_Wo"], np.float32),
        "c_masks": C["masks"],
        "c_cid16": C["cid16"], "c_cid32": C["cid32"], "c_ones128": C["ones128"], "c_onesrow": C["onesrow"],
        "c_selz": C["selz"],
    }
    in_maps = []
    for c in range(8):
        half = c % 2
        m = dict(shared)
        m["xprev"] = np.ascontiguousarray(x_cur[c])
        m["betas8"] = np.ascontiguousarray(betas[half * 8:(half + 1) * 8])
        m["m08"] = np.ascontiguousarray(m0[half * 8:(half + 1) * 8])
        m["z08"] = np.ascontiguousarray(z0[half * 8:(half + 1) * 8])
        in_maps.append(m)
    res = run_bass_kernel_spmd(nc, in_maps, core_ids=list(range(8)))
    return [r["xb2"] for r in res.results]



def build_l1_ffn(nc, s, B, xprev, xout):
    din = {"xprev": xprev}
    for nm, shp, dt in [("wg", [D, 8], F32),
                        ("wu", [8, D, 2048], F32), ("wd", [8, 2048, D], F32),
                        ("w1", [D, 2048], F32), ("w2", [2048, D], F32),
                        ("c_iota8", [128, 8], F32)]:
        din[nm] = nc.dram_tensor(nm, shp, dt, kind="ExternalInput").ap()

    gblk, gph = [], []

    def sb(name, shape, dtype, scope=None):
        g = nc.sbuf_tensor(name, shape, dtype)
        (gph if scope is None else scope).append(g)
        return g.__enter__()

    def ps(name, shape, dtype=F32):
        g = nc.psum_tensor(name, shape, dtype)
        gph.append(g)
        return g.__enter__()

    def close(lst=None):
        lst = gph if lst is None else lst
        for g in reversed(lst):
            g.__exit__(None, None, None)
        lst.clear()

    xn = sb("xn3", [128, DC * NT], BF16, scope=gblk)
    macc = sb("macc", [128, 16 * 1024], F32, scope=gblk)
    cw = sb("cw", [128, 128], F32, scope=gblk)

    B.tps = [(ps(f"tpU{i}", [128, 128]), f"tpU{i}") for i in range(4)]
    emit_load_ln_fm(s, B, din["xprev"], xn, "xn3", lambda dc, tt: dc * NT + tt * 128)
    s.barrier("endA3")
    close()

    ph = [ps(f"phU{i}", [128, 512]) for i in range(2)]
    wgsb = sb("wgsb", [128, 64], BF16)
    iot = sb("iot", [128, 8], F32)
    lg = sb("lg", [128, 8], F32)
    oh1 = sb("oh1", [128, 8], F32)
    oh2 = sb("oh2", [128, 8], F32)
    t8 = sb("t8", [128, 8], F32)
    u8 = sb("u8", [128, 8], F32)
    m1 = sb("m1g", [128, 1], F32)
    m2 = sb("m2g", [128, 1], F32)
    i1 = sb("i1g", [128, 1], F32)
    v1n = sb("v1ng", [128, 1], F32)
    v2n = sb("v2ng", [128, 1], F32)
    s.dma("pool", lambda q: q.dma_start(wgsb[:].rearrange("p (dc c) -> p dc c", dc=8),
                                        din["wg"].rearrange("(dc p) c -> p dc c", p=128)),
          writes=["wgsb"], tag="ldwg")
    s.dma("sp", lambda q: q.dma_start(iot[:], din["c_iota8"]), writes=["iot"], tag="ldiota")

    def first_hot(ohk, oh_t, lsrc, lsrck, tag):
        s.add("dve", lambda v: v.tensor_tensor(t8[:], iot[:], oh_t[:], op=ALU.mult),
              reads=["iot", ohk], writes=["t8"], tag=tag + "a")
        s.add("dve", lambda v: v.tensor_scalar(u8[:], oh_t[:], -9.0, 9.0, op0=ALU.mult, op1=ALU.add),
              reads=[ohk], writes=["u8"], tag=tag + "b")
        s.add("dve", lambda v: v.tensor_tensor(t8[:], t8[:], u8[:], op=ALU.add),
              reads=["t8", "u8"], writes=["t8"], tag=tag + "c")
        s.add("dve", lambda v: v.tensor_reduce(i1[:], t8[:], axis=AX.X, op=ALU.min),
              reads=["t8"], writes=["i1g"], tag=tag + "d")
        s.add("dve", lambda v: v.tensor_scalar(oh_t[:], iot[:], i1[:], None, op0=ALU.is_equal),
              reads=["iot"], sreads=["i1g"], writes=[ohk], tag=tag + "e")

    for tt in range(16):
        p_, pk_ = ph[tt % 2], f"phU{tt % 2}"
        for dc in range(DC):
            s.add("pe", lambda t, p=p_, dc=dc, tt=tt:
                  t.matmul(p[:, 0:8], xn[:, dc * NT + tt * 128:dc * NT + (tt + 1) * 128],
                           wgsb[:, dc * 8:(dc + 1) * 8], start=(dc == 0), stop=(dc == 7)),
                  reads=["wgsb", "xn3"], writes=[pk_], tag="gmm")
        s.add("dve", lambda v, p=p_: v.tensor_copy(lg[:], p[:, 0:8]), reads=[pk_], writes=["lg"], tag="gev")
        s.add("dve", lambda v: v.reduce_max(m1[:], lg[:], axis=AX.X), reads=["lg"], writes=["m1g"], tag="gm1")
        s.add("dve", lambda v: v.tensor_scalar(oh1[:], lg[:], m1[:], None, op0=ALU.is_ge),
              reads=["lg"], sreads=["m1g"], writes=["oh1k"], tag="goh1")
        first_hot("oh1k", oh1, lg, "lg", f"gf1_{tt}")
        s.add("dve", lambda v: v.scalar_tensor_tensor(t8[:], oh1[:], -30000.0, lg[:], op0=ALU.mult, op1=ALU.add),
              reads=["oh1k", "lg"], writes=["t8"], tag="gmask")
        s.add("dve", lambda v: v.reduce_max(m2[:], t8[:], axis=AX.X), reads=["t8"], writes=["m2g"], tag="gm2")
        s.add("dve", lambda v: v.tensor_scalar(oh2[:], t8[:], m2[:], None, op0=ALU.is_ge),
              reads=["t8"], sreads=["m2g"], writes=["oh2k"], tag="goh2")
        first_hot("oh2k", oh2, t8, "t8", f"gf2_{tt}")
        s.add("dve", lambda v: v.tensor_tensor(m2[:], m2[:], m1[:], op=ALU.subtract),
              reads=["m2g", "m1g"], writes=["m2g"], tag="gd")
        s.add("act", lambda sc: sc.activation(m2[:], m2[:], AF.Exp), reads=["m2g"], writes=["m2g"], tag="gex")
        s.add("dve", lambda v: v.tensor_scalar(m2[:], m2[:], 1.0, None, op0=ALU.add),
              reads=["m2g"], writes=["m2g"], tag="g1p")
        s.add("dve", lambda v: v.reciprocal(v1n[:], m2[:]), reads=["m2g"], writes=["v1nk"], tag="ginv")
        s.add("dve", lambda v: v.tensor_scalar(v2n[:], v1n[:], -1.0, 1.0, op0=ALU.mult, op1=ALU.add),
              reads=["v1nk"], writes=["v2nk"], tag="gv2")
        s.add("dve", lambda v: v.tensor_scalar(oh1[:], oh1[:], v1n[:], None, op0=ALU.mult),
              reads=["oh1k"], sreads=["v1nk"], writes=["oh1k"], tag="gw1")
        s.add("dve", lambda v, tt=tt: v.scalar_tensor_tensor(cw[:, tt * 8:(tt + 1) * 8], oh2[:], v2n[:],
                                                             oh1[:], op0=ALU.mult, op1=ALU.add),
              reads=["oh2k", "oh1k"], sreads=["v2nk"], writes=["cw"], tag="gcw")

    he = sb("he", [128, 16 * 1024], BF16)
    lwu = [sb(f"lwu{i}", [128, 8 * 512], BF16) for i in range(2)]
    wdh = [sb(f"wdh{i}", [128, 16 * 512], BF16) for i in range(2)]
    for tt2 in range(2):
        for xp in range(9):
            wu_ap = din["w1"] if xp == 0 else din["wu"][xp - 1]
            wd_ap = din["w2"] if xp == 0 else din["wd"][xp - 1]
            for fq in range(4):
                w_, wk_ = lwu[fq % 2], f"lwu{fq % 2}"
                s.dma("pool", lambda q, t=w_, a=wu_ap.rearrange("(dc p) c -> p dc c", p=128), fq=fq:
                      q.dma_start(t[:].rearrange("p (dc c) -> p dc c", dc=8), a[:, :, fq * 512:(fq + 1) * 512]),
                      reads=[wk_], writes=[wk_], tag="uw")
                for fc4 in range(4):
                    fc = fq * 4 + fc4
                    for ns in range(2):
                        p_, pk_ = ph[ns % 2], f"phU{ns % 2}"
                        for dc in range(DC):
                            s.add("pe", lambda t, p=p_, w=w_, dc=dc, ns=ns, tt2=tt2, fc4=fc4:
                                  t.matmul(p[:], w[:, dc * 512 + fc4 * 128: dc * 512 + (fc4 + 1) * 128],
                                           xn[:, dc * NT + tt2 * 1024 + ns * 512: dc * NT + tt2 * 1024 + ns * 512 + 512],
                                           start=(dc == 0), stop=(dc == 7)),
                                  reads=[wk_, "xn3"], writes=[pk_], tag="umm")
                        s.add("act", lambda sc, p=p_, fc=fc, ns=ns:
                              sc.activation(he[:, fc * 1024 + ns * 512: fc * 1024 + (ns + 1) * 512], p[:], AF.Gelu),
                              reads=[pk_], writes=["he"], tag="ugelu")
            for dh in range(2):
                wd_, wdk = wdh[dh % 2], f"wdh{dh % 2}"
                s.dma("pool", lambda q, a=wd_ap.rearrange("(fc p) c -> p fc c", p=128), dh=dh, t=wd_:
                      q.dma_start(t[:].rearrange("p (fc c) -> p fc c", fc=16),
                                  a[:, :, dh * 512:(dh + 1) * 512]),
                      reads=[wdk], writes=[wdk], tag="dw")
                for tl in range(8):
                    tt = tt2 * 8 + tl
                    p_, pk_ = ph[tl % 2], f"phU{tl % 2}"
                    for fc in range(16):
                        s.add("pe", lambda t, p=p_, fc=fc, tl=tl, w=wd_:
                              t.matmul(p[:], he[:, fc * 1024 + tl * 128: fc * 1024 + tl * 128 + 128],
                                       w[:, fc * 512:(fc + 1) * 512], start=(fc == 0), stop=(fc == 15)),
                              reads=["he", wdk], writes=[pk_], tag="dmm")
                    if xp == 0:
                        s.add("act", lambda sc, p=p_, tt=tt, dh=dh:
                              sc.copy(macc[:, tt * 1024 + dh * 512: tt * 1024 + (dh + 1) * 512], p[:]),
                              reads=[pk_], writes=["macc"], tag="dsh")
                    else:
                        mt, mtk = B.xnt_rot[B.xnt_i % len(B.xnt_rot)]
                        B.xnt_i += 1
                        s.add("act", lambda sc, p=p_, o=mt, tt=tt, xp=xp:
                              sc.activation(o[:, 0:512], p[:], AF.Identity,
                                            scale=cw[:, tt * 8 + xp - 1: tt * 8 + xp]),
                              reads=[pk_], sreads=["cw"], writes=[mtk], tag="dsc")
                        s.add("dve", lambda v, o=mt, tt=tt, dh=dh:
                              v.tensor_tensor(macc[:, tt * 1024 + dh * 512: tt * 1024 + (dh + 1) * 512],
                                              macc[:, tt * 1024 + dh * 512: tt * 1024 + (dh + 1) * 512],
                                              o[:, 0:512], op=ALU.add),
                              reads=["macc", mtk], writes=["macc"], tag="dacc")
    s.barrier("endU")
    close()

    for tt in range(16):
        xt, xtk = B.xt_rot[B.xt_i % len(B.xt_rot)]
        B.xt_i += 1
        s.dma("sp", lambda q, t=xt, a=din["xprev"], tt=tt: q.dma_start(t[:], a[tt * 128:(tt + 1) * 128, :]),
              reads=[xtk], writes=[xtk], tag="ldxf")
        ob, obk = B.out_rot[B.out_i % len(B.out_rot)]
        B.out_i += 1
        s.add("dve", lambda v, o=ob, x=xt, tt=tt:
              v.tensor_tensor(o[:], macc[:, tt * 1024:(tt + 1) * 1024], x[:], op=ALU.add),
              reads=["macc", xtk], writes=[obk], tag="fadd")
        s.dma("sp", lambda q, t=ob, tt=tt: q.dma_start(xout[tt * 128:(tt + 1) * 128, :], t[:]),
              reads=[obk], writes=[f"outx3_{tt}"], tag="stx3")
    s.add("sp", lambda q: q.nop(), reads=[f"outx3_{t}" for t in range(16)], tag="fin")
    close(gblk)
    return din


_B3_CACHE = {}


def build_b3_program():
    if "nc" not in _B3_CACHE:
        nc, s, B = _mk_sched()
        xin = nc.dram_tensor("xprev", [NT, D], F32, kind="ExternalInput").ap()
        xout = nc.dram_tensor("xb3", [NT, D], F32, kind="ExternalOutput").ap()
        build_l1_ffn(nc, s, B, xin, xout)
        _B3_CACHE["nc"] = _finish(nc, s)
        _B3_CACHE["stats"] = s.stats()
    return _B3_CACHE["nc"]


def run_block3(inputs, x_cur):
    C = _host_consts()
    nc = build_b3_program()
    shared = {
        "wg": np.asarray(inputs["L1_ffn_Wg"], np.float32),
        "wu": np.asarray(inputs["L1_ffn_Wu"], np.float32),
        "wd": np.asarray(inputs["L1_ffn_Wd"], np.float32),
        "w1": np.asarray(inputs["L1_ffn_W1"], np.float32),
        "w2": np.asarray(inputs["L1_ffn_W2"], np.float32),
        "c_iota8": C["iota8"],
        "c_cid16": C["cid16"], "c_cid32": C["cid32"], "c_ones128": C["ones128"], "c_onesrow": C["onesrow"],
        "c_selz": C["selz"],
    }
    in_maps = []
    for c in range(8):
        m = dict(shared)
        m["xprev"] = np.ascontiguousarray(x_cur[c])
        in_maps.append(m)
    res = run_bass_kernel_spmd(nc, in_maps, core_ids=list(range(8)))
    return [r["xb3"] for r in res.results]



_FUSED_CACHE = {}


def build_fused_program():
    if "nc" not in _FUSED_CACHE:
        nc, s, B = _mk_sched()
        xb0 = nc.dram_tensor("xb0", [NT, D], F32, kind="Internal").ap()
        xb1 = nc.dram_tensor("xb1", [NT, D], F32, kind="Internal").ap()
        xb2 = nc.dram_tensor("xb2", [NT, D], F32, kind="Internal").ap()
        y = nc.dram_tensor("y", [NT, D], F32, kind="ExternalOutput").ap()
        build_l0_attn(nc, s, B, xb0)
        s.barrier("b01")
        build_l0_ffn(nc, s, B, xb0, xb1)
        s.barrier("b12")
        build_l1_attn(nc, s, B, xb1, xb2)
        s.barrier("b23")
        build_l1_ffn(nc, s, B, xb2, y)
        _FUSED_CACHE["nc"] = _finish(nc, s)
        _FUSED_CACHE["stats"] = s.stats()
    return _FUSED_CACHE["nc"]


def kernel(**inputs):
    C = _host_consts()
    nc = build_fused_program()
    X = np.asarray(inputs["L0_attn_X"], dtype=np.float32)
    gam_t = np.broadcast_to(np.asarray(inputs["L0_attn_gamma"], np.float32), (128, D)).copy()
    bet_t = np.broadcast_to(np.asarray(inputs["L0_attn_beta"], np.float32), (128, D)).copy()
    lam64 = np.broadcast_to(np.asarray(inputs["L0_attn_lam"], np.float32).reshape(1, 1), (64, 1)).copy()
    betas = np.asarray(inputs["L1_attn_betas"], np.float32)[0, :, 0, :]
    m0 = np.broadcast_to(np.asarray(inputs["L1_attn_memory0"], np.float32)[0], (16, 64, 64))
    z0 = np.asarray(inputs["L1_attn_z0"], np.float32)[0, :, 0, :]
    shared = {
        "wq": np.asarray(inputs["L0_attn_Wq"], np.float32), "bq": np.asarray(inputs["L0_attn_bq"], np.float32),
        "wk": np.asarray(inputs["L0_attn_Wk"], np.float32), "bk": np.asarray(inputs["L0_attn_bk"], np.float32),
        "wv": np.asarray(inputs["L0_attn_Wv"], np.float32), "bv": np.asarray(inputs["L0_attn_bv"], np.float32),
        "wo": np.asarray(inputs["L0_attn_Wo"], np.float32), "bo": np.asarray(inputs["L0_attn_bo"], np.float32),
        "gam_t": gam_t, "bet_t": bet_t, "lam64": lam64,
        "w_fc": np.asarray(inputs["L0_ffn_W_fc"], np.float32),
        "ln_w": np.asarray(inputs["L0_ffn_ln_w"], np.float32),
        "ln_b": np.asarray(inputs["L0_ffn_ln_b"], np.float32),
        "w_proj": np.asarray(inputs["L0_ffn_W_proj"], np.float32),
        "wq1": np.asarray(inputs["L1_attn_Wq"], np.float32),
        "wk1": np.asarray(inputs["L1_attn_Wk"], np.float32),
        "wv1": np.asarray(inputs["L1_attn_Wv"], np.float32),
        "wo1": np.asarray(inputs["L1_attn_Wo"], np.float32),
        "wg": np.asarray(inputs["L1_ffn_Wg"], np.float32),
        "wu": np.asarray(inputs["L1_ffn_Wu"], np.float32),
        "wd": np.asarray(inputs["L1_ffn_Wd"], np.float32),
        "w1": np.asarray(inputs["L1_ffn_W1"], np.float32),
        "w2": np.asarray(inputs["L1_ffn_W2"], np.float32),
        "c_iota8": C["iota8"], "c_masks": C["masks"],
        "c_cid16": C["cid16"], "c_cid32": C["cid32"], "c_ones128": C["ones128"],
        "c_onesrow": C["onesrow"], "c_selz": C["selz"],
    }
    in_maps = []
    for c in range(8):
        b, half = c // 2, c % 2
        own, oth = _core_rows(c)
        m = dict(shared)
        m["x_own"] = np.ascontiguousarray(X[b][own])
        m["x_oth"] = np.ascontiguousarray(X[b][oth])
        m["betas8"] = np.ascontiguousarray(betas[half * 8:(half + 1) * 8])
        m["m08"] = np.ascontiguousarray(m0[half * 8:(half + 1) * 8])
        m["z08"] = np.ascontiguousarray(z0[half * 8:(half + 1) * 8])
        in_maps.append(m)
    res = run_bass_kernel_spmd(nc, in_maps, core_ids=list(range(8)))
    out = np.zeros((4, S_FULL, D), np.float32)
    for c in range(8):
        b = c // 2
        own, _ = _core_rows(c)
        out[b][own] = res.results[c]["y"]
    return out


# version 10
# speedup vs baseline: 1.2413x; 1.0128x over previous
import numpy as np
import ml_dtypes
import concourse.bass as bass
import concourse.mybir as mybir
from concourse.bass_utils import run_bass_kernel_spmd

F32 = mybir.dt.float32
BF16 = mybir.dt.bfloat16
F8 = mybir.dt.float8e4
AF = mybir.ActivationFunctionType
ALU = mybir.AluOpType
AX = mybir.AxisListType

ENGS = ("pe", "act", "dve", "pool", "sp")
QUEUES = ("sp", "pool")
NQSLOT = 4

D = 1024
DC = 8
NT = 2048
S_FULL = 4096
NSEG = 8
LSEG_LOC = 256
EPS = 1e-5


class Sched:

    def __init__(self, nc):
        self.nc = nc
        self.ops = []
        self.eng_seq = {e: 0 for e in ENGS}
        self.eng_ops = {e: [] for e in ENGS}
        self.qslot_count = {}
        self.qslot_last = {}
        self.dma_rr = {q: 0 for q in QUEUES}
        self.last_writer = {}
        self.readers = {}
        self.eng_last = {e: None for e in ENGS}

    def _event(self, op):
        if op["dma"]:
            return (op["qchan"], op["qidx"])
        return (op["eng"], op["seq"] + 1)

    def add(self, eng, fn, reads=(), writes=(), sreads=(), dma=False, queue=None, tag=""):
        idx = len(self.ops)
        op = {
            "idx": idx, "eng": eng, "fn": fn, "dma": dma, "tag": tag,
            "reads": tuple(reads), "writes": tuple(writes), "sreads": tuple(sreads),
            "seq": self.eng_seq[eng], "inc": False, "waits": {}, "vc": {},
        }
        if dma:
            q = queue or eng
            slot = self.dma_rr[q] % NQSLOT
            self.dma_rr[q] += 1
            op["qchan"] = ("q", q, slot)
            op["qidx"] = self.qslot_count.get((q, slot), 0) + 1
            self.qslot_count[(q, slot)] = op["qidx"]
            op["qprev"] = self.qslot_last.get((q, slot))
            self.qslot_last[(q, slot)] = idx
        deps = {}

        def need2(widx, force=False):
            if widx is None:
                return
            w = self.ops[widx]
            ev = self._event(w)
            same = (not w["dma"]) and w["eng"] == eng
            if same and eng == "pe" and not force:
                return
            old = deps.get(ev[0])
            if old is None or ev[1] > old[0]:
                deps[ev[0]] = (ev[1], widx, force)

        if dma and op.get("qprev") is not None:
            need2(op["qprev"])
        for r in op["reads"]:
            need2(self.last_writer.get(r))
        for r in op["sreads"]:
            need2(self.last_writer.get(r), force=True)
        for wkey in op["writes"]:
            need2(self.last_writer.get(wkey))
            for ridx in self.readers.get(wkey, ()):
                if ridx != idx:
                    need2(ridx)

        prev = self.eng_last[eng]
        vc = dict(self.ops[prev]["vc"]) if prev is not None else {}
        waits = {}
        for chan, (sq, widx, force) in deps.items():
            known = vc.get(chan, 0)
            if sq > known:
                waits[chan] = sq
                self.ops[widx]["inc"] = True
                w = self.ops[widx]
                for c2, v2 in w["vc"].items():
                    if v2 > vc.get(c2, 0):
                        vc[c2] = v2
                vc[chan] = sq
        op["waits"] = waits
        op["vc"] = vc
        self.ops.append(op)
        self.eng_seq[eng] += 1
        self.eng_ops[eng].append(idx)
        self.eng_last[eng] = idx
        for wkey in op["writes"]:
            self.last_writer[wkey] = idx
            self.readers[wkey] = []
        for r in op["reads"] + op["sreads"]:
            self.readers.setdefault(r, []).append(idx)
        return idx

    def dma(self, queue, fn, reads=(), writes=(), tag=""):
        return self.add(queue, fn, reads=reads, writes=writes, dma=True, queue=queue, tag=tag)

    def barrier(self, tag="bar"):
        gather_deps = [self.eng_last[e] for e in ENGS if self.eng_last[e] is not None]
        key = f"__bar{len(self.ops)}"
        for widx in gather_deps:
            self.ops[widx]["inc"] = True

        def mk_nop(eng):
            return lambda e: e.nop()

        gidx = self.add("dve", mk_nop("dve"), tag=tag + "_gather")
        gop = self.ops[gidx]
        for widx in gather_deps:
            w = self.ops[widx]
            ev = self._event(w)
            if (not w["dma"]) and w["eng"] == "dve":
                if gop["vc"].get(ev[0], 0) < ev[1]:
                    gop["waits"][ev[0]] = max(gop["waits"].get(ev[0], 0), ev[1])
                    gop["vc"][ev[0]] = ev[1]
            else:
                if gop["vc"].get(ev[0], 0) < ev[1]:
                    gop["waits"][ev[0]] = max(gop["waits"].get(ev[0], 0), ev[1])
                    gop["vc"][ev[0]] = ev[1]
        gop["writes"] = (key,)
        self.last_writer[key] = gidx
        self.readers[key] = []
        for e in ENGS:
            if e == "dve":
                continue
            self.add(e, mk_nop(e), reads=(key,), tag=tag + "_join")

    def assign_semvals(self):
        for e in ENGS:
            cnt = 0
            for oi in self.eng_ops[e]:
                o = self.ops[oi]
                if o["inc"] and not o["dma"]:
                    cnt += 1
                o["semval"] = cnt

    def verify(self):
        self.assign_semvals()
        pos = {e: 0 for e in ENGS}
        sem = {e: 0 for e in ENGS}
        qsem = {}
        progress = True
        while progress:
            progress = False
            for e in ENGS:
                while pos[e] < len(self.eng_ops[e]):
                    oi = self.eng_ops[e][pos[e]]
                    op = self.ops[oi]
                    ok = True
                    for chan, sq in op["waits"].items():
                        if isinstance(chan, tuple):
                            if qsem.get(chan, 0) < sq:
                                ok = False
                                break
                        else:
                            val = self.ops[self.eng_ops[chan][sq - 1]]["semval"]
                            if sem[chan] < val:
                                ok = False
                                break
                    if not ok:
                        break
                    if op["dma"]:
                        qsem[op["qchan"]] = qsem.get(op["qchan"], 0) + 1
                    elif op["inc"]:
                        sem[e] += 1
                    pos[e] += 1
                    progress = True
        stuck = {e: pos[e] for e in ENGS if pos[e] < len(self.eng_ops[e])}
        if stuck:
            det = []
            for e in stuck:
                op = self.ops[self.eng_ops[e][pos[e]]]
                det.append(f"{e}@{op['idx']}:{op['tag']} waits {op['waits']}")
            raise RuntimeError("Sched.verify deadlock: " + "; ".join(det))

    def emit(self, sems, qsems, engmap):
        self.assign_semvals()

        def run_engine(ename):
            def body(eng):
                for oi in self.eng_ops[ename]:
                    op = self.ops[oi]
                    for chan, sq in sorted(op["waits"].items(), key=lambda kv: str(kv[0])):
                        if isinstance(chan, tuple):
                            _, q, slot = chan
                            eng.wait_ge(qsems[(q, slot)], 16 * sq)
                        else:
                            pi = self.eng_ops[chan]
                            val = self.ops[pi[sq - 1]]["semval"]
                            if val > 0:
                                eng.wait_ge(sems[chan], val)
                    inst = op["fn"](eng)
                    if op["dma"]:
                        inst.then_inc(qsems[(op["qchan"][1], op["qchan"][2])], 16)
                    elif op["inc"]:
                        inst.then_inc(sems[op["eng"]], 1)
            return body

        blk = self.nc.Block()
        with blk:
            blk.tensor(run_engine("pe"))
            blk.scalar(run_engine("act"))
            blk.vector(run_engine("dve"))
            blk.gpsimd(run_engine("pool"))
            blk.sync(run_engine("sp"))

    def stats(self):
        from collections import Counter
        c = Counter(o["eng"] for o in self.ops)
        w = sum(len(o["waits"]) for o in self.ops)
        i = sum(1 for o in self.ops if o["inc"])
        return f"ops={len(self.ops)} per-eng={dict(c)} waits={w} incs={i}"


def engobjs(nc):
    return {"pe": nc.tensor, "act": nc.scalar, "dve": nc.vector, "pool": nc.gpsimd, "sp": nc.sync}



def _host_consts():
    bf = ml_dtypes.bfloat16
    cid16 = np.eye(128, dtype=bf)
    cid32 = np.eye(128, dtype=np.float32)
    ones128 = np.ones((128, 128), dtype=bf)
    onesrow = np.ones((1, 512), dtype=bf)
    sel4 = np.zeros((4, 4 * 64), dtype=bf)
    for r in range(4):
        sel4[r, r * 64:(r + 1) * 64] = 1
    selz = np.zeros((128, 2 * 64), dtype=bf)
    selz[0, 0:64] = 1
    selz[64, 64:128] = 1
    masks = np.zeros((128, 4 * 512), dtype=bf)
    c = np.arange(512)
    lq = 16 * (c % 32) + c // 32
    for kt in range(4):
        p = np.arange(128)
        lk = 16 * (p % 32) + 4 * kt + p // 32
        masks[:, kt * 512:(kt + 1) * 512] = (lk[:, None] <= lq[None, :]).astype(bf)
    iota8 = np.broadcast_to(np.arange(8, dtype=np.float32), (128, 8)).copy()
    return {"cid16": cid16, "cid32": cid32, "ones128": ones128, "onesrow": onesrow,
            "sel4": sel4, "selz": selz, "masks": masks, "iota8": iota8}


def _core_rows(c):
    half = c % 2
    own = np.concatenate([np.arange(s * 512 + half * 256, s * 512 + half * 256 + 256) for s in range(8)])
    oth = np.concatenate([np.arange(s * 512 + (1 - half) * 256, s * 512 + (1 - half) * 256 + 256) for s in range(8)])
    return own, oth



def _small(nc, name, n, cols=1, dtype=F32, parts=128):
    return [(nc.alloc_sbuf_tensor(f"{name}{i}", [parts, cols], dtype), f"{name}{i}") for i in range(n)]


class Bufs:
    pass


def alloc_consts(nc, s):
    B = Bufs()
    B.nc = nc
    B.cid16 = nc.alloc_sbuf_tensor("cid16", [128, 128], BF16)
    B.cid32 = nc.alloc_sbuf_tensor("cid32", [128, 128], F32)
    B.ones128 = nc.alloc_sbuf_tensor("ones128", [128, 128], BF16)
    B.onesrow = nc.alloc_sbuf_tensor("onesrow", [1, 512], BF16)
    B.selz = nc.alloc_sbuf_tensor("selz", [128, 128], BF16)
    for nm in ("cid16", "cid32", "ones128", "onesrow", "selz"):
        ap = nc.dram_tensor(f"c_{nm}", list(getattr(B, nm).ap().shape), getattr(B, nm).ap().dtype,
                            kind="ExternalInput").ap()
        t = getattr(B, nm)
        s.dma("sp", lambda q, t=t, ap=ap: q.dma_start(t[:], ap), writes=[nm], tag=f"ld_{nm}")
    return B


def emit_ln_tok(s, B, xt, xtk, st, ncols=1024, eps=EPS):
    inv = 1.0 / ncols
    (nm, nmk), (e2, e2k), (var, vark), (rstd, rstdk), (nb, nbk) = st
    s.add("dve", lambda v, a=nm, x=xt: v.reduce_sum(a[:], x[:], axis=AX.X, negate=True),
          reads=[xtk], writes=[nmk])
    s.add("dve", lambda v, a=nm: v.tensor_scalar(a[:], a[:], inv, None, op0=ALU.mult),
          reads=[nmk], writes=[nmk])
    sq, sqk = B.sq_rot[B.sq_i % len(B.sq_rot)]
    B.sq_i += 1
    s.add("act", lambda sc, o=sq, x=xt: sc.activation(o[:, 0:x.ap().shape[1]], x[:], AF.Square),
          reads=[xtk], writes=[sqk])
    s.add("dve", lambda v, a=e2, x=sq, n=ncols: v.reduce_sum(a[:], x[:, 0:n], axis=AX.X),
          reads=[sqk], writes=[e2k])
    s.add("dve", lambda v, a=e2: v.tensor_scalar(a[:], a[:], inv, None, op0=ALU.mult),
          reads=[e2k], writes=[e2k])
    s.add("dve", lambda v, o=var, a=nm, b=nm: v.tensor_tensor(o[:], a[:], b[:], op=ALU.mult),
          reads=[nmk], writes=[vark])
    s.add("dve", lambda v, o=var, a=e2, b=var, e=float(eps):
          v.scalar_tensor_tensor(o[:], a[:], e, b[:], op0=ALU.add, op1=ALU.subtract),
          reads=[e2k, vark], writes=[vark])
    s.add("act", lambda sc, o=var, a=var: sc.activation(o[:], a[:], AF.Ln),
          reads=[vark], writes=[vark])
    s.add("act", lambda sc, o=rstd, a=var: sc.activation(o[:], a[:], AF.Exp, scale=-0.5),
          reads=[vark], writes=[rstdk])
    s.add("dve", lambda v, o=nb, a=nm, b=rstd: v.tensor_tensor(o[:], a[:], b[:], op=ALU.mult),
          reads=[nmk, rstdk], writes=[nbk])
    return (rstd, rstdk), (nb, nbk)


def emit_load_ln_fm(s, B, src_ap, xn, xnk, colfn, ntiles=16, ln=True):
    nc = B.nc
    for tt in range(ntiles):
        xt, xtk = B.xt_rot[B.xt_i % len(B.xt_rot)]
        B.xt_i += 1
        s.dma("sp", lambda q, t=xt, a=src_ap, tt=tt: q.dma_start(t[:], a[tt * 128:(tt + 1) * 128, :]),
              reads=[xtk], writes=[xtk], tag="ldx")
        if ln:
            st = B.lnst[B.lnst_i % len(B.lnst)]
            B.lnst_i += 1
            (rstd, rstdk), (nb, nbk) = emit_ln_tok(s, B, xt, xtk, st)
            xnt, xntk = B.xnt_rot[B.xnt_i % len(B.xnt_rot)]
            B.xnt_i += 1
            s.add("act", lambda sc, o=xnt, x=xt, r=rstd, b=nb:
                  sc.activation(o[:], x[:], AF.Identity, bias=b[:], scale=r[:]),
                  reads=[xtk], sreads=[rstdk, nbk], writes=[xntk], tag="xnt")
        else:
            xnt, xntk = B.xnt_rot[B.xnt_i % len(B.xnt_rot)]
            B.xnt_i += 1
            s.add("dve", lambda v, o=xnt, x=xt: v.tensor_copy(o[:], x[:]),
                  reads=[xtk], writes=[xntk], tag="xcast")
        for dc in range(DC):
            pt, ptk = B.tps[(tt * DC + dc) % len(B.tps)]
            s.add("pe", lambda t, p=pt, x=xnt, dc=dc, c=B.cid16:
                  t.matmul(p[:], x[:, dc * 128:(dc + 1) * 128], c[:], start=True, stop=True),
                  reads=[xntk, "cid16"], writes=[ptk], tag="xT")
            eng = "dve" if dc % 2 == 0 else "act"
            if eng == "dve":
                s.add("dve", lambda v, p=pt, o=xn, col=colfn(dc, tt): v.tensor_copy(o[:, col:col + 128], p[:]),
                      reads=[ptk], writes=[xnk], tag="xTev")
            else:
                s.add("act", lambda sc, p=pt, o=xn, col=colfn(dc, tt): sc.copy(o[:, col:col + 128], p[:]),
                      reads=[ptk], writes=[xnk], tag="xTev")


def emit_exit_delta(s, B, delta, deltak, xprev_ap, out_ap, outkey):
    nc = B.nc
    for tt in range(16):
        xt, xtk = B.xt_rot[B.xt_i % len(B.xt_rot)]
        B.xt_i += 1
        s.dma("sp", lambda q, t=xt, a=xprev_ap, tt=tt: q.dma_start(t[:], a[tt * 128:(tt + 1) * 128, :]),
              reads=[xtk], writes=[xtk], tag="ldxe")
        ob, obk = B.out_rot[B.out_i % len(B.out_rot)]
        B.out_i += 1
        for dc in range(DC):
            pt, ptk = B.tps[(tt * DC + dc) % len(B.tps)]
            s.add("pe", lambda t, p=pt, d=delta, dc=dc, tt=tt, c=B.cid32:
                  t.matmul(p[:], d[:, dc * 2048 + tt * 128: dc * 2048 + (tt + 1) * 128], c[:], start=True, stop=True),
                  reads=[deltak, "cid32"], writes=[ptk], tag="dT")
            s.add("dve", lambda v, p=pt, o=ob, x=xt, dc=dc:
                  v.tensor_tensor(o[:, dc * 128:(dc + 1) * 128], p[:], x[:, dc * 128:(dc + 1) * 128], op=ALU.add),
                  reads=[ptk, xtk], writes=[obk], tag="dadd")
        s.dma("sp", lambda q, a=out_ap, t=ob, tt=tt: q.dma_start(a[tt * 128:(tt + 1) * 128, :], t[:]),
              reads=[obk], writes=[f"{outkey}_{tt}"], tag="stx")


def alloc_common(nc, s, B):
    B.xt_rot = [(nc.alloc_sbuf_tensor(f"xt{i}", [128, 1024], F32), f"xt{i}") for i in range(2)]
    B.xt_i = 0
    B.sq_rot = [(nc.alloc_sbuf_tensor(f"sqs{i}", [128, 1024], BF16), f"sqs{i}") for i in range(2)]
    B.sq_i = 0
    B.xnt_rot = [(nc.alloc_sbuf_tensor(f"xnt{i}", [128, 1024], BF16), f"xnt{i}") for i in range(2)]
    B.xnt_i = 0
    B.out_rot = [(nc.alloc_sbuf_tensor(f"outb{i}", [128, 1024], F32), f"outb{i}") for i in range(2)]
    B.out_i = 0
    B.lnst = []
    for i in range(2):
        B.lnst.append([( nc.alloc_sbuf_tensor(f"ln{nm}{i}", [128, 1], F32), f"ln{nm}{i}")
                       for nm in ("nm", "e2", "var", "rstd", "nb")])
    B.lnst_i = 0
    B.tps = None



def build_l0_attn(nc, s, B, xout):
    din = {}
    for nm, shp, dt in [("x_own", [NT, D], F32), ("x_oth", [NT, D], F32),
                        ("wq", [16, D, 128], F32), ("bq", [16, 128], F32),
                        ("wk", [16, D, 128], F32), ("bk", [16, 128], F32),
                        ("wv", [16, D, 64], F32), ("bv", [16, 64], F32),
                        ("wo", [D, D], F32), ("bo", [D], F32),
                        ("gam_t", [128, D], F32), ("bet_t", [128, D], F32),
                        ("lam64", [64, 1], F32)]:
        din[nm] = nc.dram_tensor(nm, shp, dt, kind="ExternalInput").ap()

    gblk, gph = [], []

    def sb(name, shape, dtype, scope=None):
        g = nc.sbuf_tensor(name, shape, dtype)
        (gph if scope is None else scope).append(g)
        return g.__enter__()

    def ps(name, shape, dtype=F32):
        g = nc.psum_tensor(name, shape, dtype)
        gph.append(g)
        return g.__enter__()

    def close(lst=None):
        lst = gph if lst is None else lst
        for g in reversed(lst):
            g.__exit__(None, None, None)
        lst.clear()

    Xn = sb("Xn", [128, DC * S_FULL], BF16, scope=gblk)
    OcatT = sb("OcatT", [128, DC * NT], BF16, scope=gblk)
    selz_sb = B.selz
    lam_sb = sb("lam_sb", [64, 1], F32, scope=gblk)
    s.dma("sp", lambda q, t=lam_sb, a=din["lam64"]: q.dma_start(t[:], a), writes=["lam"], tag="ldlam")
    s.add("dve", lambda v, t=lam_sb: v.tensor_scalar(t[:], t[:], -1.0, None, op0=ALU.mult),
          reads=["lam"], writes=["lam"], tag="neglam")

    B.tps = [(ps(f"tpA{i}", [128, 128]), f"tpA{i}") for i in range(4)]
    emit_load_ln_fm(s, B, din["x_own"], Xn, "Xn", lambda dc, tt: dc * S_FULL + tt * 128)
    emit_load_ln_fm(s, B, din["x_oth"], Xn, "Xn", lambda dc, tt: dc * S_FULL + NT + tt * 128)
    s.barrier("endA")
    close()

    pA = ps("pA", [128, 1024])
    pB = ps("pB", [128, 1024])
    pbank = [(pA, 0, "pA0"), (pA, 512, "pA1"), (pB, 0, "pB0"), (pB, 512, "pB1")]
    pO1 = ps("pO1", [65, 512])
    pO2 = ps("pO2", [65, 512])
    pu = [ps(f"pu{i}", [128, 512]) for i in range(2)]
    pubank = [(pu[0], 0, "pu0"), (pu[1], 0, "pu1")]

    Kg = [sb(f"Kg{i}", [128, S_FULL], BF16) for i in range(2)]
    Qg = [sb(f"Qg{i}", [128, NT], BF16) for i in range(2)]
    Vg = [sb(f"Vgp{i}", [128, 2 * 65 * 32], BF16) for i in range(2)]
    wkb = [sb(f"wkb{i}", [128, 1024], BF16) for i in range(2)]
    wqb = [sb(f"wqb{i}", [128, 1024], BF16) for i in range(2)]
    wvg = [sb(f"wvgp{i}", [128, 1024], BF16) for i in range(2)]
    bqs = sb("bqs", [128, 16], F32)
    bks = sb("bks", [128, 16], F32)
    bvrow = sb("bvrow", [1, 1024], BF16)
    us1 = [sb(f"us1_{i}", [64, 2048], BF16) for i in range(2)]
    us2 = [sb(f"us2_{i}", [64, 2048], BF16) for i in range(2)]
    zb = [sb(f"zbb{i}", [128, 2048], BF16) for i in range(2)]
    e12 = [sb(f"e12_{i}", [128, 1024], BF16) for i in range(2)]
    for i in range(2):
        s.add("dve", lambda v, t=zb[i], i=i: v.memset(t[:], 1.0), writes=[f"zbk{i}"], tag="zinit")
    ct1 = sb("ct1", [64, 512], BF16)
    ct2 = sb("ct2", [64, 512], BF16)

    bqnat = sb("bqnat", [16, 128], F32)
    s.dma("sp", lambda q, t=bqnat, a=din["bq"]: q.dma_start(t[:], a), reads=["bqnat"], writes=["bqnat"], tag="ldbq")
    s.add("pe", lambda t: t.matmul(pA[:, 0:16], bqnat[:], B.cid32[0:16, 0:16], start=True, stop=True),
          reads=["bqnat", "cid32"], writes=["pA0"], tag="bqT")
    s.add("dve", lambda v: v.tensor_copy(bqs[:], pA[:, 0:16]), reads=["pA0", "bqnat"], writes=["bqs"], tag="bqTe")
    s.dma("sp", lambda q, t=bqnat, a=din["bk"]: q.dma_start(t[:], a), reads=["bqnat"], writes=["bqnat"], tag="ldbk")
    s.add("pe", lambda t: t.matmul(pA[:, 0:16], bqnat[:], B.cid32[0:16, 0:16], start=True, stop=True),
          reads=["bqnat", "cid32"], writes=["pA0"], tag="bkT")
    s.add("dve", lambda v: v.tensor_copy(bks[:], pA[:, 0:16]), reads=["pA0"], writes=["bks"], tag="bkTe")
    s.dma("pool", lambda q, t=bvrow, a=din["bv"].rearrange("h d -> (h d)"): q.dma_start(t[0:1, :], a[None, :]),
          writes=["bvrow"], tag="ldbv")
    for i in range(2):
        s.add("dve", lambda v, t=Vg[i], i=i:
              v.memset(t[:].rearrange("p (h tt c) -> p h tt c", h=2, tt=32)[:, :, :, 64:65], 1.0),
              writes=[f"Vg{i}"], tag="vones")

    units = []
    ready = {}
    for pr in range(8):
        vslot = pr % 2

        def wv_load(pr=pr, vslot=vslot):
            for hh in range(2):
                s.dma("pool", lambda q, a=din["wv"][pr * 2 + hh].rearrange("(ej p) d -> p ej d", p=128),
                      hh=hh, vslot=vslot:
                      q.dma_start(wvg[vslot][:].rearrange("p (ej h d) -> p ej h d", ej=8, h=2)[:, :, hh, :], a),
                      reads=[f"wvg{vslot}"], writes=[f"wvg{vslot}"], tag="ldwv")
        units.append(wv_load)
        for tt in range(32):
            def v_unit(tt=tt, pr=pr, vslot=vslot):
                pt_, off, pk = pubank[tt % 2]
                for ej in range(DC):
                    s.add("pe", lambda t, p=pt_, o=off, w=wvg[vslot], ej=ej, tt=tt:
                          t.matmul(p[:, o:o + 128], Xn[:, ej * S_FULL + tt * 128: ej * S_FULL + (tt + 1) * 128],
                                   w[:, ej * 128:(ej + 1) * 128], start=(ej == 0), stop=False),
                          reads=["Xn", f"wvg{vslot}"], writes=[pk], tag="vmm")
                s.add("pe", lambda t, p=pt_, o=off, pr=pr:
                      t.matmul(p[:, o:o + 128], B.onesrow[0:1, 0:128], bvrow[0:1, pr * 128:(pr + 1) * 128],
                               start=False, stop=True),
                      reads=["onesrow", "bvrow"], writes=[pk], tag="vbias")

                def vev(e, p=pt_, o=off, tt=tt, vslot=vslot):
                    src = p[:, o:o + 128].rearrange("p (h d) -> p h d", h=2)
                    dst = Vg[vslot][:].rearrange("p (h t) -> p h t", h=2)[:, :, tt * 65: tt * 65 + 64]
                    return e.tensor_copy(dst, src)
                s.add("dve", vev, reads=[pk], writes=[f"Vg{vslot}"], tag="vev")
            units.append(v_unit)
        for hh in range(2):
            h = pr * 2 + hh
            hs = h % 2

            def w_load(h=h, hs=hs):
                s.dma("pool", lambda q, t=wkb[hs], a=din["wk"][h].rearrange("(ej p) c -> p ej c", p=128):
                      q.dma_start(t[:].rearrange("p (ej c) -> p ej c", ej=8), a),
                      reads=[f"wkb{hs}"], writes=[f"wkb{hs}"], tag="ldwk")
                s.dma("pool", lambda q, t=wqb[hs], a=din["wq"][h].rearrange("(ej p) c -> p ej c", p=128):
                      q.dma_start(t[:].rearrange("p (ej c) -> p ej c", ej=8), a),
                      reads=[f"wqb{hs}"], writes=[f"wqb{hs}"], tag="ldwq")
            units.append(w_load)
            for n8 in range(8):
                def k_unit(n8=n8, h=h, hs=hs):
                    pt_, off, pk = pubank[n8 % 2]
                    for ej in range(DC):
                        s.add("pe", lambda t, p=pt_, o=off, w=wkb[hs], ej=ej, n8=n8:
                              t.matmul(p[:, o:o + 512], w[:, ej * 128:(ej + 1) * 128],
                                       Xn[:, ej * S_FULL + n8 * 512: ej * S_FULL + (n8 + 1) * 512],
                                       start=(ej == 0), stop=(ej == DC - 1)),
                              reads=[f"wkb{hs}", "Xn"], writes=[pk], tag="kmm")
                    s.add("dve", lambda v, p=pt_, o=off, d=Kg[hs], n8=n8, h=h:
                          v.tensor_scalar(d[:, n8 * 512:(n8 + 1) * 512], p[:, o:o + 512],
                                          bks[:, h:h + 1], None, op0=ALU.add),
                          reads=[pk], sreads=["bks"], writes=[f"Kgk{hs}"], tag="kev")
                units.append(k_unit)
            for n4 in range(4):
                def q_unit(n4=n4, h=h, hs=hs):
                    pt_, off, pk = pubank[n4 % 2]
                    for ej in range(DC):
                        s.add("pe", lambda t, p=pt_, o=off, w=wqb[hs], ej=ej, n4=n4:
                              t.matmul(p[:, o:o + 512], w[:, ej * 128:(ej + 1) * 128],
                                       Xn[:, ej * S_FULL + n4 * 512: ej * S_FULL + (n4 + 1) * 512],
                                       start=(ej == 0), stop=(ej == DC - 1)),
                              reads=[f"wqb{hs}", "Xn"], writes=[pk], tag="qmm")
                    s.add("dve", lambda v, p=pt_, o=off, d=Qg[hs], n4=n4, h=h:
                          v.tensor_scalar(d[:, n4 * 512:(n4 + 1) * 512], p[:, o:o + 512],
                                          bqs[:, h:h + 1], None, op0=ALU.add),
                          reads=[pk], sreads=["bqs"], writes=[f"Qgk{hs}"], tag="qev")
                units.append(q_unit)
            ready[h] = len(units)

    drained = [0]

    def drain(upto):
        upto = min(upto, len(units))
        while drained[0] < upto:
            units[drained[0]]()
            drained[0] += 1

    pending_combine = [None]
    for h in range(16):
        hs = h % 2
        vslot = (h // 2) % 2
        drain(ready[h])
        if pending_combine[0] is not None:
            pending_combine[0]()
            pending_combine[0] = None
        nxt = ready[h + 1] if h + 1 < 16 else len(units)
        need = nxt - ready[h]
        slot = [0]

        def pace():
            slot[0] += 1
            drain(ready[h] + (need * slot[0] + 127) // 128)
        for qt in range(4):
            def emit_s(kc, qt=qt, hs=hs):
                pt_ = pA if kc % 2 == 0 else pB
                k0 = "pA0" if kc % 2 == 0 else "pB0"
                k1 = "pA1" if kc % 2 == 0 else "pB1"
                s.add("pe", lambda t, p=pt_, K=Kg[hs], Q=Qg[hs], kc=kc, qt=qt:
                      t.matmul(p[:, 0:512], K[0:64, kc * 128:(kc + 1) * 128],
                               Q[0:64, qt * 512:(qt + 1) * 512], start=True, stop=True,
                               tile_position=(0, 0)),
                      reads=[f"Kgk{hs}", f"Qgk{hs}"], writes=[k0], tag="s1")
                s.add("pe", lambda t, p=pt_, K=Kg[hs], Q=Qg[hs], kc=kc, qt=qt:
                      t.matmul(p[:, 512:1024], K[64:128, kc * 128:(kc + 1) * 128],
                               Q[64:128, qt * 512:(qt + 1) * 512], start=True, stop=True,
                               tile_position=(64, 0)),
                      reads=[f"Kgk{hs}", f"Qgk{hs}"], writes=[k1], tag="s2")

            def emit_epv(kc, vslot=vslot, hh=(h % 2)):
                pt_ = pA if kc % 2 == 0 else pB
                k0 = "pA0" if kc % 2 == 0 else "pB0"
                k1 = "pA1" if kc % 2 == 0 else "pB1"
                eb, ebk = e12[kc % 2], f"e12_{kc % 2}"
                s.add("act", lambda sc, o=eb, p=pt_: sc.activation(o[:], p[:], AF.Exp, scale=0.125),
                      reads=[k0, k1], writes=[ebk], tag="exp")
                hh2 = h % 2
                s.add("pe", lambda t, e=eb, kc=kc, hh2=hh2, vslot=vslot:
                      t.matmul(pO1[:], Vg[vslot][:, hh2 * 2080 + kc * 65: hh2 * 2080 + (kc + 1) * 65],
                               e[:, 0:512], start=(kc == 0), stop=(kc == 31)),
                      reads=[f"Vg{vslot}", ebk], writes=["pO1"], tag="pv1")
                s.add("pe", lambda t, e=eb, kc=kc, hh2=hh2, vslot=vslot:
                      t.matmul(pO2[:], Vg[vslot][:, hh2 * 2080 + kc * 65: hh2 * 2080 + (kc + 1) * 65],
                               e[:, 512:1024], start=(kc == 0), stop=(kc == 31)),
                      reads=[f"Vg{vslot}", ebk], writes=["pO2"], tag="pv2")
            emit_s(0)
            for kc in range(1, 32):
                emit_s(kc)
                emit_epv(kc - 1)
                pace()
            emit_epv(31)
            pace()
            s.add("act", lambda sc, qt=qt, u=us1[hs]: sc.copy(u[:, qt * 512:(qt + 1) * 512], pO1[0:64, :]),
                  reads=["pO1"], writes=[f"us1k{hs}"], tag="uev1")
            s.add("dve", lambda v, qt=qt, z=zb[hs]: v.tensor_copy(z[0:1, qt * 512:(qt + 1) * 512], pO1[64:65, :]),
                  reads=["pO1"], writes=[f"zbk{hs}"], tag="zev1")
            s.add("dve", lambda v, qt=qt, u=us2[hs]: v.tensor_copy(u[:, qt * 512:(qt + 1) * 512], pO2[0:64, :]),
                  reads=["pO2"], writes=[f"us2k{hs}"], tag="uev2")
            s.add("act", lambda sc, qt=qt, z=zb[hs]: sc.copy(z[64:65, qt * 512:(qt + 1) * 512], pO2[64:65, :]),
                  reads=["pO2"], writes=[f"zbk{hs}"], tag="zev2")

        def mk_combine(h=h, hs=hs):
            def emit_combine():
                z = zb[hs]
                s.add("act", lambda sc, z=z: sc.activation(z[:], z[:], AF.Ln),
                      reads=[f"zbk{hs}"], writes=[f"zbk{hs}"], tag="lnz")
                s.add("act", lambda sc, z=z: sc.activation(z[:], z[:], AF.Exp, scale=-1.0),
                      reads=[f"zbk{hs}"], writes=[f"zbk{hs}"], tag="expz")
                for qt in range(4):
                    s.add("pe", lambda t, qt=qt, z=z:
                          t.matmul(pu[0][0:64, :], selz_sb[:, 0:64], z[:, qt * 512:(qt + 1) * 512],
                                   start=True, stop=True),
                          reads=["selz", f"zbk{hs}"], writes=["pu0"], tag="bz1")
                    s.add("pe", lambda t, qt=qt, z=z:
                          t.matmul(pu[1][0:64, :], selz_sb[:, 64:128], z[:, qt * 512:(qt + 1) * 512],
                                   start=True, stop=True),
                          reads=["selz", f"zbk{hs}"], writes=["pu1"], tag="bz2")
                    s.add("dve", lambda v, qt=qt, u=us1[hs]:
                          v.tensor_tensor(ct1[:], u[:, qt * 512:(qt + 1) * 512], pu[0][0:64, :], op=ALU.mult),
                          reads=[f"us1k{hs}", "pu0"], writes=["ct1"], tag="c1")
                    s.add("dve", lambda v, qt=qt, u=us2[hs]:
                          v.tensor_tensor(ct2[:], u[:, qt * 512:(qt + 1) * 512], pu[1][0:64, :], op=ALU.mult),
                          reads=[f"us2k{hs}", "pu1"], writes=["ct2"], tag="c2")
                    s.add("dve", lambda v, h=h, qt=qt:
                          v.scalar_tensor_tensor(
                              OcatT[64 * (h % 2):64 * (h % 2) + 64,
                                    (h // 2) * NT + qt * 512:(h // 2) * NT + (qt + 1) * 512],
                              ct2[:], lam_sb[:], ct1[:], op0=ALU.mult, op1=ALU.add),
                          reads=["ct1", "ct2"], sreads=["lam"], writes=["OcatT"], tag="comb")
            return emit_combine
        pending_combine[0] = mk_combine()
    pending_combine[0]()
    s.barrier("endB")
    close()

    wo_sb = sb("wo_sb", [128, DC * 1024], BF16)
    bo_row = sb("bo_row", [1, 1024], BF16)
    gam = sb("gam", [128, 1024], F32)
    bet = sb("bet", [128, 1024], F32)
    ytok = [sb(f"ytok{i}", [128, 1024], F32) for i in range(2)]
    ywork = [sb(f"ywork{i}", [128, 1024], F32) for i in range(2)]
    pY = [ps(f"pY{i}", [128, 512]) for i in range(2)]
    s.dma("pool", lambda q, t=wo_sb, a=din["wo"].rearrange("(cc p) d -> p cc d", p=128):
          q.dma_start(t[:].rearrange("p (cc d) -> p cc d", cc=8), a), writes=["wo_sb"], tag="ldwo")
    s.dma("pool", lambda q, t=bo_row, a=din["bo"]: q.dma_start(t[0:1, :], a[None, :]), writes=["bo_row"], tag="ldbo")
    s.dma("sp", lambda q, t=gam, a=din["gam_t"]: q.dma_start(t[:], a), writes=["gam"], tag="ldgam")
    s.dma("sp", lambda q, t=bet, a=din["bet_t"]: q.dma_start(t[:], a), writes=["bet"], tag="ldbet")
    s.add("dve", lambda v, t=gam: v.tensor_scalar(t[:], t[:], 0.2, None, op0=ALU.mult),
          reads=["gam"], writes=["gam"], tag="g02")
    s.add("dve", lambda v, t=bet: v.tensor_scalar(t[:], t[:], 0.2, None, op0=ALU.mult),
          reads=["bet"], writes=["bet"], tag="b02")
    for tt in range(16):
        yb, ybk = ytok[tt % 2], f"ytok{tt % 2}"
        for nh in range(2):
            p = pY[nh]
            pk = f"pY{nh}"
            for cc in range(8):
                s.add("pe", lambda t, p=p, cc=cc, tt=tt, nh=nh:
                      t.matmul(p[:], OcatT[:, cc * NT + tt * 128: cc * NT + (tt + 1) * 128],
                               wo_sb[:, cc * 1024 + nh * 512: cc * 1024 + (nh + 1) * 512],
                               start=(cc == 0), stop=False),
                      reads=["OcatT", "wo_sb"], writes=[pk], tag="ymm")
            s.add("pe", lambda t, p=p, nh=nh:
                  t.matmul(p[:], B.onesrow[0:1, 0:128], bo_row[0:1, nh * 512:(nh + 1) * 512],
                           start=False, stop=True),
                  reads=["onesrow", "bo_row"], writes=[pk], tag="ybias")
            s.add("dve", lambda v, p=p, o=yb, nh=nh: v.tensor_copy(o[:, nh * 512:(nh + 1) * 512], p[:]),
                  reads=[pk], writes=[ybk], tag="yev")
        st = B.lnst[B.lnst_i % len(B.lnst)]
        B.lnst_i += 1
        (rstd, rstdk), (nb, nbk) = emit_ln_tok(s, B, yb, ybk, st)
        yw, ywk = ywork[tt % 2], f"ywork{tt % 2}"
        s.add("act", lambda sc, o=yw, x=yb, r=rstd, b=nb:
              sc.activation(o[:], x[:], AF.Identity, bias=b[:], scale=r[:]),
              reads=[ybk], sreads=[rstdk, nbk], writes=[ywk], tag="ynorm")
        s.add("dve", lambda v, t=yw: v.tensor_tensor(t[:], t[:], gam[:], op=ALU.mult),
              reads=[ywk, "gam"], writes=[ywk], tag="ygam")
        s.add("pool", lambda g_, t=yw: g_.tensor_tensor(t[:], t[:], bet[:], op=ALU.add),
              reads=[ywk, "bet"], writes=[ywk], tag="ybet")
        xt, xtk = B.xt_rot[B.xt_i % len(B.xt_rot)]
        B.xt_i += 1
        s.dma("sp", lambda q, t=xt, a=din["x_own"], tt=tt: q.dma_start(t[:], a[tt * 128:(tt + 1) * 128, :]),
              reads=[xtk], writes=[xtk], tag="ldxc")
        ob, obk = B.out_rot[B.out_i % len(B.out_rot)]
        B.out_i += 1
        s.add("dve", lambda v, o=ob, a=yw, b=xt: v.tensor_tensor(o[:], a[:], b[:], op=ALU.add),
              reads=[ywk, xtk], writes=[obk], tag="oadd")
        s.dma("sp", lambda q, t=ob, tt=tt: q.dma_start(xout[tt * 128:(tt + 1) * 128, :], t[:]),
              reads=[obk], writes=[f"out_x{tt}"], tag="stx")
    s.add("sp", lambda q: q.nop(), reads=[f"out_x{t}" for t in range(16)], tag="fin")
    close()
    close(gblk)
    return din



def _mk_sched():
    nc = bass.Bass("TRN2", target_bir_lowering=False)
    s = Sched(nc)
    B = alloc_consts(nc, s)
    alloc_common(nc, s, B)
    return nc, s, B


def _finish(nc, s):
    s.verify()
    sems = {eng: nc.alloc_semaphore(f"s_{eng}") for eng in ENGS}
    qsems = {(q, i): nc.alloc_semaphore(f"q_{q}{i}") for q in QUEUES for i in range(NQSLOT)}
    s.emit(sems, qsems, None)
    return nc


_B0_CACHE = {}


def build_b0_program():
    if "nc" not in _B0_CACHE:
        nc, s, B = _mk_sched()
        xout = nc.dram_tensor("xb0", [NT, D], F32, kind="ExternalOutput").ap()
        build_l0_attn(nc, s, B, xout)
        _B0_CACHE["nc"] = _finish(nc, s)
        _B0_CACHE["stats"] = s.stats()
    return _B0_CACHE["nc"]


def run_block0(inputs, x_cur=None):
    C = _host_consts()
    nc = build_b0_program()
    X = np.asarray(inputs["L0_attn_X"], dtype=np.float32)
    gam_t = np.broadcast_to(np.asarray(inputs["L0_attn_gamma"], np.float32), (128, D)).copy()
    bet_t = np.broadcast_to(np.asarray(inputs["L0_attn_beta"], np.float32), (128, D)).copy()
    lam64 = np.broadcast_to(np.asarray(inputs["L0_attn_lam"], np.float32).reshape(1, 1), (64, 1)).copy()
    shared = {
        "wq": np.asarray(inputs["L0_attn_Wq"], np.float32), "bq": np.asarray(inputs["L0_attn_bq"], np.float32),
        "wk": np.asarray(inputs["L0_attn_Wk"], np.float32), "bk": np.asarray(inputs["L0_attn_bk"], np.float32),
        "wv": np.asarray(inputs["L0_attn_Wv"], np.float32), "bv": np.asarray(inputs["L0_attn_bv"], np.float32),
        "wo": np.asarray(inputs["L0_attn_Wo"], np.float32), "bo": np.asarray(inputs["L0_attn_bo"], np.float32),
        "gam_t": gam_t, "bet_t": bet_t, "lam64": lam64, "c_selz": C["selz"],
        "c_cid16": C["cid16"], "c_cid32": C["cid32"], "c_ones128": C["ones128"], "c_onesrow": C["onesrow"],
    }
    in_maps = []
    for c in range(8):
        b = c // 2
        own, oth = _core_rows(c)
        m = dict(shared)
        m["x_own"] = np.ascontiguousarray(X[b][own])
        m["x_oth"] = np.ascontiguousarray(X[b][oth])
        in_maps.append(m)
    res = run_bass_kernel_spmd(nc, in_maps, core_ids=list(range(8)))
    return [r["xb0"] for r in res.results]


def kernel(**inputs):
    xb = run_block0(inputs)
    xb = run_block1(inputs, xb)
    xb = run_block2(inputs, xb)
    xb = run_block3(inputs, xb)
    out = np.zeros((4, S_FULL, D), np.float32)
    for c in range(8):
        b = c // 2
        own, _ = _core_rows(c)
        out[b][own] = xb[c]
    return out



def build_l0_ffn(nc, s, B, xprev, xout):
    din = {"xprev": xprev}
    for nm, shp, dt in [("w_fc", [4096, D], F32),
                        ("ln_w", [4096], F32), ("ln_b", [4096], F32),
                        ("w_proj", [D, 4096], F32)]:
        din[nm] = nc.dram_tensor(nm, shp, dt, kind="ExternalInput").ap()
    gblk, gph = [], []

    def sb(name, shape, dtype, scope=None):
        g = nc.sbuf_tensor(name, shape, dtype)
        (gph if scope is None else scope).append(g)
        return g.__enter__()

    def ps(name, shape, dtype=F32):
        g = nc.psum_tensor(name, shape, dtype)
        gph.append(g)
        return g.__enter__()

    def close(lst=None):
        lst = gph if lst is None else lst
        for g in reversed(lst):
            g.__exit__(None, None, None)
        lst.clear()

    xn = sb("xn1", [128, DC * NT], BF16, scope=gblk)
    wfc8 = sb("wfc8", [128, 32768], F8, scope=gblk)
    wpj8 = sb("wpj8", [128, 32768], F8, scope=gblk)
    d1dram = nc.dram_tensor("d1dram", [D, NT], F32, kind="Internal").ap()

    B.tps = [(ps(f"tpF{i}", [128, 128]), f"tpF{i}") for i in range(4)]
    pq = ps("pq", [128, 512])
    emit_load_ln_fm(s, B, din["xprev"], xn, "xn1", lambda dc, tt: dc * NT + tt * 128)
    onescol = sb("onescol", [128, 1], F32)
    ones1f = sb("ones1f", [1, 128], F32)
    s.add("dve", lambda v: v.memset(onescol[:], 1.0), writes=["onescol"], tag="oc")
    s.add("dve", lambda v: v.memset(ones1f[:], 1.0), writes=["ones1f"], tag="o1f")

    for wi, (wnm, nrow_t, ncol) in enumerate([("w_fc", 32, 1024), ("w_proj", 8, 4096)]):
        gwi = []
        wt = [sb(f"wt{wi}_{i}", [128, ncol], F32, scope=gwi) for i in range(2)]
        g1t = [sb(f"g1t{wi}_{i}", [128, ncol], BF16, scope=gwi) for i in range(2)]
        g2t = [sb(f"g2t{wi}_{i}", [128, ncol], BF16, scope=gwi) for i in range(2)]
        cols = sb(f"cols{wi}", [128, nrow_t], F32, scope=gwi)
        rowt = sb(f"rowt{wi}", [128, 1], F32, scope=gwi)
        nthr = sb(f"nthr{wi}", [128, 1], F32, scope=gwi)
        pthr = sb(f"pthr{wi}", [128, 1], F32, scope=gwi)
        for i in range(nrow_t):
            w_, wk_ = wt[i % 2], f"wt{wi}_{i % 2}"
            s.dma("sp" if i % 2 == 0 else "pool",
                  lambda q, t=w_, i=i, a=din[wnm]: q.dma_start(t[:], a[i * 128:(i + 1) * 128, :]),
                  reads=[wk_], writes=[wk_], tag="qld")
            s.add("dve", lambda v, t=w_, i=i, c=cols: v.reduce_sum(c[:, i:i + 1], t[:], axis=AX.X,
                                                                   apply_absolute_value=True),
                  reads=[wk_], writes=[f"cols{wi}"], tag="qabs")
        s.add("dve", lambda v, c=cols, r=rowt: v.reduce_sum(r[:], c[:], axis=AX.X),
              reads=[f"cols{wi}"], writes=[f"rowt{wi}"], tag="qrow")
        s.add("pe", lambda t, r=rowt: t.matmul(pq[0:1, 0:1], r[:], onescol[:], start=True, stop=True),
              reads=[f"rowt{wi}", "onescol"], writes=["pq"], tag="qtot")
        nel = float(nrow_t * 128 * ncol)
        nthr0 = sb(f"nthr0_{wi}", [1, 1], F32, scope=gwi)
        s.add("act", lambda sc, o=nthr0, sc_=-0.5 / nel: sc.activation(o[:], pq[0:1, 0:1], AF.Identity, scale=sc_),
              reads=["pq"], writes=[f"nthr0k{wi}"], tag="qthr")
        s.add("pe", lambda t, r=nthr0: t.matmul(pq[:, 32:33], ones1f[:], r[:], start=True, stop=True),
              reads=[f"nthr0k{wi}", "ones1f"], writes=["pq3"], tag="qbc")
        s.add("dve", lambda v, n=nthr: v.tensor_scalar(n[:], pq[:, 32:33], -0.5e-6, None, op0=ALU.add),
              reads=["pq3"], writes=[f"nthr{wi}"], tag="qnth")
        s.add("dve", lambda v, n=nthr, p=pthr: v.tensor_scalar(p[:], n[:], -1.0, None, op0=ALU.mult),
              reads=[f"nthr{wi}"], writes=[f"pthr{wi}"], tag="qpth")
        for i in range(nrow_t):
            w_, wk_ = wt[i % 2], f"wt{wi}_{i % 2}"
            g1_, g1k = g1t[i % 2], f"g1t{wi}_{i % 2}"
            g2_, g2k = g2t[i % 2], f"g2t{wi}_{i % 2}"
            q_, qk_ = g1_, g1k
            s.dma("sp" if i % 2 == 0 else "pool",
                  lambda q, t=w_, i=i, a=din[wnm]: q.dma_start(t[:], a[i * 128:(i + 1) * 128, :]),
                  reads=[wk_], writes=[wk_], tag="qld2")
            s.add("dve", lambda v, t=w_, o=g1_, th=pthr: v.tensor_scalar(o[:], t[:], th[:], None, op0=ALU.is_gt),
                  reads=[wk_], sreads=[f"pthr{wi}"], writes=[g1k], tag="qgt")
            s.add("dve", lambda v, t=w_, o=g2_, th=nthr: v.tensor_scalar(o[:], t[:], th[:], None, op0=ALU.is_lt),
                  reads=[wk_], sreads=[f"nthr{wi}"], writes=[g2k], tag="qlt")
            s.add("dve", lambda v, a=g1_, b=g2_: v.tensor_tensor(a[:], a[:], b[:], op=ALU.subtract),
                  reads=[g1k, g2k], writes=[g1k], tag="qsub")
            ngrp = ncol // 512
            for grp in range(ngrp):
                for e4 in range(4):
                    pt_, ptk = B.tps[e4 % 4]
                    ec = grp * 4 + e4
                    dst = (wfc8[:, (i * 8 + ec) * 128:(i * 8 + ec) * 128 + 128] if wi == 0
                           else wpj8[:, (i * 32 + ec) * 128:(i * 32 + ec) * 128 + 128])
                    dk = "wfc8" if wi == 0 else "wpj8"
                    s.add("pe", lambda t, p=pt_, q2=q_, ec=ec:
                          t.matmul(p[:], q2[:, ec * 128:(ec + 1) * 128], B.cid16[:], start=True, stop=True),
                          reads=[qk_, "cid16"], writes=[ptk], tag="qT")
                    eng = "dve" if e4 % 2 == 0 else "act"
                    if eng == "dve":
                        s.add("dve", lambda v, p=pt_, o=dst: v.tensor_copy(o, p[:]),
                              reads=[ptk], writes=[dk], tag="qTe")
                    else:
                        s.add("act", lambda sc, p=pt_, o=dst: sc.activation(o, p[:], AF.Identity),
                              reads=[ptk], writes=[dk], tag="qTe")
        s.barrier(f"endQ{wi}")
        close(gwi)
    s.barrier("endQ")
    close()

    ph = [ps(f"ph{i}", [128, 512]) for i in range(2)]
    ps1 = ps("ps1", [128, 512])
    ps2 = ps("ps2", [128, 512])
    pm = [ps(f"pm{i}", [128, 512]) for i in range(2)]
    pv2 = ps("pv2", [128, 8])
    pv3 = ps("pv3", [128, 8])
    hsb2 = [sb("hsb0", [128, 16384], BF16)]
    dt1 = [sb(f"dt1_{i}", [128, 512], F32) for i in range(2)]

    lw = [sb(f"lw{i}", [128, 1024], BF16) for i in range(3)]
    lw2 = [sb(f"lw2_{i}", [128, 4096], BF16) for i in range(2)]
    sqs = [sb(f"hsq{i}", [128, 512], BF16) for i in range(2)]
    mu = sb("mu", [128, 512], F32)
    mu2 = sb("mu2", [128, 512], F32)
    var = sb("var", [128, 512], F32)
    rstd = sb("rstd", [128, 512], F32)
    mr = sb("mr", [128, 512], F32)
    ctm = [sb(f"ctm{i}", [128, 512], F32) for i in range(2)]
    lnwn = sb("lnwn", [32, 128], F32)
    lnbn = sb("lnbn", [32, 128], F32)
    lnw = sb("lnw", [128, 32], F32)
    lnb = sb("lnb", [128, 32], F32)
    lnwB = sb("lnwB", [128, 32], BF16)
    lnbB = sb("lnbB", [128, 32], BF16)
    nv2 = sb("nv2", [128, 8], F32)
    v3 = sb("v3", [128, 8], F32)
    s.dma("sp", lambda q: q.dma_start(lnwn[:], din["ln_w"].rearrange("(a b) -> a b", a=32)),
          writes=["lnwn"], tag="ldlnw")
    s.dma("sp", lambda q: q.dma_start(lnbn[:], din["ln_b"].rearrange("(a b) -> a b", a=32)),
          writes=["lnbn"], tag="ldlnb")
    s.add("pe", lambda t: t.matmul(ph[0][:, 0:32], lnwn[:], B.cid32[0:32, 0:32], start=True, stop=True),
          reads=["lnwn", "cid32"], writes=["ph0"], tag="lnwT")
    s.add("dve", lambda v: v.tensor_copy(lnw[:], ph[0][:, 0:32]), reads=["ph0"], writes=["lnwk"], tag="lnwTe")
    s.add("pe", lambda t: t.matmul(ph[0][:, 0:32], lnbn[:], B.cid32[0:32, 0:32], start=True, stop=True),
          reads=["lnbn", "cid32"], writes=["ph0"], tag="lnbT")
    s.add("dve", lambda v: v.tensor_copy(lnb[:], ph[0][:, 0:32]), reads=["ph0"], writes=["lnbk"], tag="lnbTe")
    s.add("dve", lambda v: v.tensor_copy(lnwB[:], lnw[:]), reads=["lnwk"], writes=["lnwBk"], tag="lnwC")
    s.add("dve", lambda v: v.tensor_copy(lnbB[:], lnb[:]), reads=["lnbk"], writes=["lnbBk"], tag="lnbC")

    for dc in range(8):
        w_, wk_ = lw2[dc % 2], f"lw2_{dc % 2}"
        ceng = "act" if dc % 2 == 0 else "dve"
        if ceng == "act":
            s.add("act", lambda sc, t=w_, dc=dc:
                  sc.activation(t[:], wpj8[:, dc * 4096:(dc + 1) * 4096], AF.Identity),
                  reads=["wpj8"], writes=[wk_], tag="vcC")
        else:
            s.add("dve", lambda v, t=w_, dc=dc:
                  v.tensor_copy(t[:], wpj8[:, dc * 4096:(dc + 1) * 4096]),
                  reads=["wpj8"], writes=[wk_], tag="vcC")
        for hc in range(32):
            s.add("pe", lambda t, w=w_, hc=hc, dc=dc:
                  t.matmul(pv2[:, dc:dc + 1], w[:, hc * 128:(hc + 1) * 128], lnwB[:, hc:hc + 1],
                           start=(hc == 0), stop=(hc == 31)),
                  reads=[wk_, "lnwBk"], writes=["pv2"], tag="v2m")
        s.add("act", lambda sc, dc=dc: sc.activation(nv2[:, dc:dc + 1], pv2[:, dc:dc + 1], AF.Identity, scale=-1.0),
              reads=["pv2"], writes=["nv2k"], tag="v2e")
        for hc in range(32):
            s.add("pe", lambda t, w=w_, hc=hc, dc=dc:
                  t.matmul(pv3[:, dc:dc + 1], w[:, hc * 128:(hc + 1) * 128], lnbB[:, hc:hc + 1],
                           start=(hc == 0), stop=(hc == 31)),
                  reads=[wk_, "lnbBk"], writes=["pv3"], tag="v3m")
        s.add("dve", lambda v, dc=dc: v.tensor_copy(v3[:, dc:dc + 1], pv3[:, dc:dc + 1]),
              reads=["pv3"], writes=["v3k"], tag="v3e")

    for nt in range(4):
        hsb, hsbk = hsb2[0], "hsb0"
        for hc in range(32):
            w_, wk_ = lw[hc % 3], f"lw{hc % 3}"
            ceng = "act" if hc % 2 == 0 else "dve"
            if ceng == "act":
                s.add("act", lambda sc, t=w_, hc=hc:
                      sc.activation(t[:], wfc8[:, hc * 1024:(hc + 1) * 1024], AF.Identity),
                      reads=["wfc8"], writes=[wk_], tag="m1c")
            else:
                s.add("dve", lambda v, t=w_, hc=hc:
                      v.tensor_copy(t[:], wfc8[:, hc * 1024:(hc + 1) * 1024]),
                      reads=["wfc8"], writes=[wk_], tag="m1c")
            p_, pk_ = ph[hc % 2], f"ph{hc % 2}"
            for ec in range(8):
                s.add("pe", lambda t, p=p_, w=w_, ec=ec, nt=nt:
                      t.matmul(p[:], w[:, ec * 128:(ec + 1) * 128],
                               xn[:, ec * NT + nt * 512:ec * NT + nt * 512 + 512],
                               start=(ec == 0), stop=(ec == 7)),
                      reads=[wk_, "xn1"], writes=[pk_], tag="m1")
            s.add("act", lambda sc, p=p_, hc=hc, hsb=hsb: sc.activation(hsb[:, hc * 512:(hc + 1) * 512], p[:], AF.Gelu),
                  reads=[pk_], writes=[f"{hsbk}:{hc}"], tag="gelu")
            sq_, sqk_ = sqs[hc % 2], f"hsq{hc % 2}"
            s.add("act", lambda sc, o=sq_, hc=hc, hsb=hsb: sc.activation(o[:], hsb[:, hc * 512:(hc + 1) * 512], AF.Square),
                  reads=[f"{hsbk}:{hc}"], writes=[sqk_], tag="hsqr")
            s.add("pe", lambda t, hc=hc, hsb=hsb: t.matmul(ps1[:], B.ones128[:], hsb[:, hc * 512:(hc + 1) * 512],
                                                           start=(hc == 0), stop=(hc == 31)),
                  reads=[f"{hsbk}:{hc}", "ones128"], writes=["ps1"], tag="st1")
            s.add("pe", lambda t, q2=sq_, hc=hc: t.matmul(ps2[:], B.ones128[:], q2[:],
                                                          start=(hc == 0), stop=(hc == 31)),
                  reads=[sqk_, "ones128"], writes=["ps2"], tag="st2")
            s.add("dve", lambda v, hc=hc, hsb=hsb:
                  v.tensor_scalar(hsb[:, hc * 512:(hc + 1) * 512], hsb[:, hc * 512:(hc + 1) * 512],
                                  lnw[:, hc:hc + 1], None, op0=ALU.mult),
                  reads=[f"{hsbk}:{hc}"], sreads=["lnwk"], writes=[f"{hsbk}:{hc}"], tag="hscl")
        s.add("act", lambda sc: sc.activation(mu[:], ps1[:], AF.Identity, scale=1.0 / 4096.0),
              reads=["ps1"], writes=["muk"], tag="lmu")
        s.add("act", lambda sc: sc.activation(mu2[:], mu[:], AF.Square),
              reads=["muk"], writes=["mu2k"], tag="lmu2")
        s.add("dve", lambda v: v.scalar_tensor_tensor(var[:], ps2[:], 1.0 / 4096.0, mu2[:],
                                                      op0=ALU.mult, op1=ALU.subtract),
              reads=["ps2", "mu2k"], writes=["vark"], tag="lvar")
        s.add("dve", lambda v: v.tensor_scalar(var[:], var[:], float(EPS), None, op0=ALU.add),
              reads=["vark"], writes=["vark"], tag="lveps")
        s.add("act", lambda sc: sc.activation(var[:], var[:], AF.Ln),
              reads=["vark"], writes=["vark"], tag="lln")
        s.add("act", lambda sc: sc.activation(rstd[:], var[:], AF.Exp, scale=-0.5),
              reads=["vark"], writes=["rstdk"], tag="lexp")
        s.add("dve", lambda v: v.tensor_tensor(mr[:], mu[:], rstd[:], op=ALU.mult),
              reads=["muk", "rstdk"], writes=["mrk"], tag="lmr")
        for dc in range(8):
            w_, wk_ = lw2[dc % 2], f"lw2_{dc % 2}"
            ceng = "act" if dc % 2 == 0 else "dve"
            if ceng == "act":
                s.add("act", lambda sc, t=w_, dc=dc:
                      sc.activation(t[:], wpj8[:, dc * 4096:(dc + 1) * 4096], AF.Identity),
                      reads=["wpj8"], writes=[wk_], tag="m2c")
            else:
                s.add("dve", lambda v, t=w_, dc=dc:
                      v.tensor_copy(t[:], wpj8[:, dc * 4096:(dc + 1) * 4096]),
                      reads=["wpj8"], writes=[wk_], tag="m2c")
            p_, pk_ = pm[dc % 2], f"pm{dc % 2}"
            for hc in range(32):
                s.add("pe", lambda t, p=p_, w=w_, hc=hc, hsb=hsb:
                      t.matmul(p[:], w[:, hc * 128:(hc + 1) * 128], hsb[:, hc * 512:(hc + 1) * 512],
                               start=(hc == 0), stop=(hc == 31)),
                      reads=[wk_, f"{hsbk}:{hc}"], writes=[pk_], tag="m2")
            d_, dk_ = dt1[dc % 2], f"dt1_{dc % 2}"
            c_, ck_ = ctm[dc % 2], f"ctm{dc % 2}"
            s.add("act", lambda sc, o=c_, dc=dc:
                  sc.activation(o[:], mr[:], AF.Identity, scale=nv2[:, dc:dc + 1], bias=v3[:, dc:dc + 1]),
                  reads=["mrk"], sreads=["nv2k", "v3k"], writes=[ck_], tag="mcor")
            s.add("dve", lambda v, p=p_, o=d_: v.tensor_tensor(o[:], p[:], rstd[:], op=ALU.mult),
                  reads=[pk_, "rstdk"], writes=[dk_], tag="m2e")
            s.add("dve", lambda v, o=d_, c=c_: v.tensor_tensor(o[:], o[:], c[:], op=ALU.add),
                  reads=[dk_, ck_], writes=[dk_], tag="m2a")
            s.dma("sp", lambda q, t=d_, dc=dc, nt=nt:
                  q.dma_start(d1dram.rearrange("(dc p) t -> dc p t", p=128)[dc, :, nt * 512:(nt + 1) * 512], t[:]),
                  reads=[dk_], writes=["d1dram"], tag="m2st")
    s.barrier("endM")
    close()

    B.tps = [(ps(f"tpE{i}", [128, 128]), f"tpE{i}") for i in range(4)]
    dl_rot1 = [sb(f"dl1_{i}", [128, 1024], F32) for i in range(2)]
    for tt in range(16):
        dlb, dlk = dl_rot1[tt % 2], f"dl1_{tt % 2}"
        s.dma("sp", lambda q, t=dlb, tt=tt:
              q.dma_start(t[:].rearrange("p (dc c) -> p dc c", dc=8),
                          d1dram.rearrange("(dc p) t -> p dc t", p=128)[:, :, tt * 128:(tt + 1) * 128]),
              reads=[dlk, "d1dram"], writes=[dlk], tag="dlld1")
        xt, xtk = B.xt_rot[B.xt_i % len(B.xt_rot)]
        B.xt_i += 1
        s.dma("sp", lambda q, t=xt, a=din["xprev"], tt=tt: q.dma_start(t[:], a[tt * 128:(tt + 1) * 128, :]),
              reads=[xtk], writes=[xtk], tag="ldxe1")
        ob, obk = B.out_rot[B.out_i % len(B.out_rot)]
        B.out_i += 1
        for dc in range(DC):
            pt_, ptk = B.tps[dc % 4]
            s.add("pe", lambda t, p=pt_, d=dlb, dc=dc:
                  t.matmul(p[:], d[:, dc * 128:(dc + 1) * 128], B.cid32[:], start=True, stop=True),
                  reads=[dlk, "cid32"], writes=[ptk], tag="dT1")
            s.add("dve", lambda v, p=pt_, o=ob, x=xt, dc=dc:
                  v.tensor_tensor(o[:, dc * 128:(dc + 1) * 128], p[:], x[:, dc * 128:(dc + 1) * 128], op=ALU.add),
                  reads=[ptk, xtk], writes=[obk], tag="dadd1")
        s.dma("sp", lambda q, t=ob, tt=tt: q.dma_start(xout[tt * 128:(tt + 1) * 128, :], t[:]),
              reads=[obk], writes=[f"outx1_{tt}"], tag="stx1")
    s.add("sp", lambda q: q.nop(), reads=[f"outx1_{t}" for t in range(16)], tag="fin")
    close()
    close(gblk)
    return din


_B1_CACHE = {}


def build_b1_program():
    if "nc" not in _B1_CACHE:
        nc, s, B = _mk_sched()
        xin = nc.dram_tensor("xprev", [NT, D], F32, kind="ExternalInput").ap()
        xout = nc.dram_tensor("xb1", [NT, D], F32, kind="ExternalOutput").ap()
        build_l0_ffn(nc, s, B, xin, xout)
        _B1_CACHE["nc"] = _finish(nc, s)
        _B1_CACHE["stats"] = s.stats()
    return _B1_CACHE["nc"]


def run_block1(inputs, x_cur):
    C = _host_consts()
    nc = build_b1_program()
    shared = {
        "w_fc": np.asarray(inputs["L0_ffn_W_fc"], np.float32),
        "ln_w": np.asarray(inputs["L0_ffn_ln_w"], np.float32),
        "ln_b": np.asarray(inputs["L0_ffn_ln_b"], np.float32),
        "w_proj": np.asarray(inputs["L0_ffn_W_proj"], np.float32),
        "c_cid16": C["cid16"], "c_cid32": C["cid32"], "c_ones128": C["ones128"], "c_onesrow": C["onesrow"],
        "c_selz": C["selz"],
    }
    in_maps = []
    for c in range(8):
        m = dict(shared)
        m["xprev"] = np.ascontiguousarray(x_cur[c])
        in_maps.append(m)
    res = run_bass_kernel_spmd(nc, in_maps, core_ids=list(range(8)))
    return [r["xb1"] for r in res.results]



def build_l1_attn(nc, s, B, xprev, xout):
    din = {"xprev": xprev}
    for nm, shp, dt in [("wq1", [D, D], F32), ("wk1", [D, D], F32),
                        ("wv1", [D, D], F32), ("wo1", [D, D], F32),
                        ("betas8", [8, 64], F32), ("m08", [8, 64, 64], F32), ("z08", [8, 64], F32),
                        ("c_masks", [128, 2048], BF16)]:
        din[nm] = nc.dram_tensor(nm, shp, dt, kind="ExternalInput").ap()
    ddram = nc.dram_tensor("d2dram", [D, NT], F32, kind="Internal").ap()

    gblk, gph = [], []

    def sb(name, shape, dtype, scope=None):
        g = nc.sbuf_tensor(name, shape, dtype)
        (gph if scope is None else scope).append(g)
        return g.__enter__()

    def ps(name, shape, dtype=F32):
        g = nc.psum_tensor(name, shape, dtype)
        gph.append(g)
        return g.__enter__()

    def close(lst=None):
        lst = gph if lst is None else lst
        for g in reversed(lst):
            g.__exit__(None, None, None)
        lst.clear()

    xn = sb("xn2", [128, DC * NT], BF16, scope=gblk)

    B.tps = [(ps(f"tpL{i}", [128, 128]), f"tpL{i}") for i in range(4)]
    emit_load_ln_fm(s, B, din["xprev"], xn, "xn2", lambda dc, tt: dc * NT + tt * 128)
    s.barrier("endA2")
    close()

    pg = [ps(f"pg{i}", [128, 512]) for i in range(2)]
    tp2 = [(ps(f"tpT{i}", [128, 128]), f"tpT{i}") for i in range(2)]
    pO = ps("pO", [65, 512])
    pAm = ps("pAm", [65, 512])
    pD = ps("pD", [64, 128])
    pb = ps("pb", [64, 512])

    wo_sb = sb("wo1_sb", [128, 8192], BF16)
    wsl = {nm: sb(f"wsl_{nm}", [128, 8192], BF16) for nm in ("q", "k", "v")}
    for nm, wdram in (("q", "wq1"), ("k", "wk1"), ("v", "wv1")):
        s.dma("pool", lambda q, t=wsl[nm], a=din[wdram].rearrange("(dc p) c -> p dc c", p=128):
              q.dma_start(t[:].rearrange("p (dc c) -> p dc c", dc=8), a),
              writes=[f"wsl{nm}"], tag="ldwsl")
    qT = sb("qT", [64, 4096], F32)
    qTb = sb("qTb", [64, 4096], BF16)
    kTb = sb("kTb", [64, 4096], BF16)
    vT = sb("vT", [64, 4096], BF16)
    k_h = [sb(f"k_h{i}", [128, 256], BF16) for i in range(2)]
    v_h = [sb(f"v_h{i}", [128, 4 * 65], BF16) for i in range(2)]
    skt = [sb(f"skt{i}", [128, 256], F32) for i in range(2)]
    sk = [sb(f"sk{i}", [128, 256], BF16) for i in range(2)]
    sqt = [sb(f"sqt{i}", [64, 512], F32) for i in range(2)]
    sqf = [sb(f"sqf{i}", [64, 512], F32) for i in range(2)]
    eb = [sb(f"ebL{i}", [128, 512], BF16) for i in range(2)]
    ustA = [sb(f"ustA{i}", [64, 512], BF16) for i in range(2)]
    ustB = [sb(f"ustB{i}", [64, 512], BF16) for i in range(2)]
    zb1 = [sb(f"zb1d{i}", [66, 512], F32) for i in range(2)]
    sgnb = [sb(f"sgnbd{i}", [66, 512], BF16) for i in range(2)]
    izb1 = [sb(f"izb1d{i}", [66, 512], BF16) for i in range(2)]
    ct1 = [sb(f"ct1L{i}", [64, 512], BF16) for i in range(2)]
    ct2 = [sb(f"ct2L{i}", [64, 512], BF16) for i in range(2)]
    attT = [sb(f"attT{i}", [64, 512], BF16) for i in range(2)]
    preT = sb("preT", [128, 2048], BF16)
    dtile = [sb("dtile_s", [128, 256], F32)] * 2
    masks = sb("masks", [128, 2048], BF16)
    MZ = sb("MZ", [64, 8 * 65], F32)
    betn = sb("betn", [8, 64], F32)
    z0n = sb("z0n", [8, 64], F32)
    gat = sb("gat", [64, 8], F32)
    ogat = sb("ogat", [64, 8], F32)

    s.dma("pool", lambda q: q.dma_start(wo_sb[:].rearrange("p (cc d) -> p cc d", cc=8),
                                        din["wo1"].rearrange("(cc p) d -> p cc d", p=128)),
          writes=["wo_sb"], tag="ldwo1")
    s.dma("sp", lambda q: q.dma_start(masks[:], din["c_masks"]), writes=["masks"], tag="ldmask")
    s.dma("sp", lambda q: q.dma_start(betn[:], din["betas8"]), writes=["betn"], tag="ldbet")
    s.dma("sp", lambda q: q.dma_start(z0n[:], din["z08"]), writes=["z0n"], tag="ldz0")
    s.dma("sp", lambda q: q.dma_start(MZ[:].rearrange("d (h c) -> d h c", h=8)[:, :, 0:64],
                                      din["m08"].rearrange("h d v -> d h v")),
          writes=["MZ"], tag="ldm0")
    s.add("pe", lambda t: t.matmul(pg[0][0:64, 0:8], z0n[:], B.cid32[0:8, 0:8], start=True, stop=True),
          reads=["z0n", "cid32"], writes=["pg0"], tag="z0T")
    s.add("dve", lambda v: v.tensor_copy(MZ[:].rearrange("d (h c) -> d h c", h=8)[:, :, 64:65],
                                         pg[0][0:64, 0:8].rearrange("d (h o) -> d h o", o=1)),
          reads=["pg0"], writes=["MZ"], tag="z0Te")
    s.add("pe", lambda t: t.matmul(pg[0][0:64, 64:72], betn[:], B.cid32[0:8, 0:8], start=True, stop=True),
          reads=["betn", "cid32"], writes=["pg0"], tag="betT")
    s.add("act", lambda sc: sc.activation(gat[:], pg[0][0:64, 64:72], AF.Exp, scale=-1.0),
          reads=["pg0"], writes=["gat"], tag="gexp")
    s.add("dve", lambda v: v.tensor_scalar(gat[:], gat[:], 1.0, None, op0=ALU.add),
          reads=["gat"], writes=["gat"], tag="gadd")
    s.add("dve", lambda v: v.reciprocal(gat[:], gat[:]), reads=["gat"], writes=["gat"], tag="ginv")
    s.add("dve", lambda v: v.tensor_scalar(ogat[:], gat[:], -1.0, 1.0, op0=ALU.mult, op1=ALU.add),
          reads=["gat"], writes=["ogat"], tag="gneg")
    for i in range(2):
        s.add("dve", lambda v, i=i: v.memset(zb1[i][:], 1.0), writes=[f"zb1k{i}"], tag="zb1i")
    for i in range(2):
        s.add("dve", lambda v, i=i: v.memset(v_h[i][:].rearrange("p (l c) -> p l c", l=4)[:, :, 64:65], 1.0),
              writes=[f"v_h{i}"], tag="vhones")

    for seg in range(NSEG):
        for jp in range(8):
            for tn, (dst, dstk, eng) in {
                    "q": (qT, "qTk", "dve"),
                    "k": (kTb, "kTbk", "act"),
                    "v": (vT, "vTk", "dve")}.items():
                w_, wk_ = wsl[tn], f"wsl{tn}"
                p_, pk_ = pg[jp % 2], f"pg{jp % 2}"
                for dc in range(DC):
                    s.add("pe", lambda t, p=p_, w=w_, dc=dc, seg=seg, jp=jp:
                          t.matmul(p[:, 0:256], w[:, dc * 1024 + jp * 128: dc * 1024 + (jp + 1) * 128],
                                   xn[:, dc * NT + seg * 256: dc * NT + (seg + 1) * 256],
                                   start=(dc == 0), stop=(dc == 7)),
                          reads=[wk_, "xn2"], writes=[pk_], tag="jmm")
                for jj in range(2):
                    j = 2 * jp + jj

                    def evac(e, p=p_, jj=jj, j=j, dst=dst, eng=eng):
                        src = p[64 * jj:64 * jj + 64, 0:256].rearrange("d (h r) -> d h r", h=8)
                        dd = dst[:].rearrange("d (h l) -> d h l", h=8)[:, :, 32 * j: 32 * j + 32]
                        return e.tensor_copy(dd, src) if eng == "dve" else e.copy(dd, src)
                    s.add(eng, evac, reads=[pk_], writes=[dstk], tag=f"jev{tn}")
        s.add("dve", lambda v: v.tensor_copy(qTb[:], qT[:]), reads=["qTk"], writes=["qTbk"], tag="qTb")
        pending_tail = [None]
        for hp in range(8):
            hb = hp % 2
            kh, khk = k_h[hb], f"k_h{hb}"
            vh, vhk = v_h[hb], f"v_h{hb}"
            skb, skk = sk[hb], f"sk{hb}"
            sqb, sqk = sqf[hb], f"sqf{hb}"
            uA_, uAk = ustA[hb], f"ustA{hb}"
            uB_, uBk = ustB[hb], f"ustB{hb}"
            zb_, zbk = zb1[hb], f"zb1k{hb}"
            sg_, sgk = sgnb[hb], f"sgnbk{hb}"
            iz_, izk = izb1[hb], f"izb1k{hb}"
            c1_, c1k = ct1[hb], f"ct1L{hb}"
            c2_, c2k = ct2[hb], f"ct2L{hb}"
            aT_, aTk = attT[hb], f"attT{hb}"
            sk0, sk0k = skt[0], "skt0"
            sk1, sk1k = skt[1], "skt1"
            sq0, sq0k = sqt[0], "sqt0"
            sq1, sq1k = sqt[1], "sqt1"
            for lt in range(4):
                pt_, ptk = tp2[lt % 2]
                s.add("pe", lambda t, p=pt_, hp=hp, lt=lt:
                      t.matmul(p[:, 0:64], kTb[0:64, hp * 512 + lt * 128: hp * 512 + (lt + 1) * 128],
                               B.cid16[0:64, 0:64], start=True, stop=True),
                      reads=["kTbk", "cid16"], writes=[ptk], tag="kT")
                s.add("dve", lambda e, p=pt_, lt=lt, kh=kh: e.tensor_copy(kh[:, lt * 64:(lt + 1) * 64], p[:, 0:64]),
                      reads=[ptk], writes=[khk], tag="kTe")
            for lt in range(4):
                pt_, ptk = tp2[lt % 2]
                s.add("pe", lambda t, p=pt_, hp=hp, lt=lt:
                      t.matmul(p[:, 64:128], vT[0:64, hp * 512 + lt * 128: hp * 512 + (lt + 1) * 128],
                               B.cid16[0:64, 0:64], start=True, stop=True),
                      reads=["vTk", "cid16"], writes=[ptk], tag="vT")
                s.add("act", lambda e, p=pt_, lt=lt, vh=vh: e.copy(vh[:, lt * 65: lt * 65 + 64], p[:, 64:128]),
                      reads=[ptk], writes=[vhk], tag="vTe")
            s.add("dve", lambda v, o=sk0, kh=kh: v.tensor_scalar(o[:], kh[:], 0.0, None, op0=ALU.min),
                  reads=[khk], writes=[sk0k], tag="sk1")
            s.add("act", lambda sc, o=sk1, i=sk0: sc.activation(o[:], i[:], AF.Exp),
                  reads=[sk0k], writes=[sk1k], tag="sk2")
            s.add("dve", lambda g, o=sk0, kh=kh: g.tensor_relu(o[:], kh[:]),
                  reads=[khk], writes=[sk0k], tag="sk3")
            s.add("dve", lambda v, o=skb, a=sk1, b=sk0: v.tensor_tensor(o[:], a[:], b[:], op=ALU.add),
                  reads=[sk0k, sk1k], writes=[skk], tag="sk4")
            s.add("dve", lambda v, o=sq0, hp=hp: v.tensor_scalar(o[:], qT[:, hp * 512:(hp + 1) * 512], 0.0,
                                                                 None, op0=ALU.min),
                  reads=["qTk"], writes=[sq0k], tag="sq1")
            s.add("act", lambda sc, o=sq1, i=sq0: sc.activation(o[:], i[:], AF.Exp),
                  reads=[sq0k], writes=[sq1k], tag="sq2")
            s.add("dve", lambda g, o=sq0, hp=hp: g.tensor_relu(o[:], qT[:, hp * 512:(hp + 1) * 512]),
                  reads=["qTk"], writes=[sq0k], tag="sq3")
            s.add("dve", lambda v, o=sqb, a=sq1, b=sq0: v.tensor_tensor(o[:], a[:], b[:], op=ALU.add),
                  reads=[sq0k, sq1k], writes=[sqk], tag="sq4")

            def emit_sL(kt, hp=hp):
                p_, pk_ = pg[kt % 2], f"pg{kt % 2}"
                s.add("pe", lambda t, p=p_, hp=hp, kt=kt:
                      t.matmul(p[:], kTb[0:64, hp * 512 + kt * 128: hp * 512 + (kt + 1) * 128],
                               qTb[0:64, hp * 512:(hp + 1) * 512], start=True, stop=True),
                      reads=["kTbk", "qTbk"], writes=[pk_], tag="sL")

            def emit_pvL(kt, vh=vh, vhk=vhk):
                p_, pk_ = pg[kt % 2], f"pg{kt % 2}"
                e_, ek_ = eb[kt % 2], f"ebL{kt % 2}"
                s.add("act", lambda sc, o=e_, p=p_: sc.activation(o[:], p[:], AF.Exp, scale=0.125),
                      reads=[pk_], writes=[ek_], tag="eL")
                s.add("dve", lambda g, e2=e_, kt=kt:
                      g.tensor_tensor(e2[:], e2[:], masks[:, kt * 512:(kt + 1) * 512], op=ALU.mult),
                      reads=[ek_, "masks"], writes=[ek_], tag="mL")
                s.add("pe", lambda t, m2=e_, kt=kt, vh=vh:
                      t.matmul(pO[:], vh[:, kt * 65:(kt + 1) * 65], m2[:],
                               start=(kt == 0), stop=(kt == 3)),
                      reads=[vhk, ek_], writes=["pO"], tag="pvL")
            if pending_tail[0] is not None:
                pending_tail[0]()
                pending_tail[0] = None
            emit_sL(0)
            emit_sL(1)
            emit_pvL(0)
            emit_sL(2)
            emit_pvL(1)
            emit_sL(3)
            emit_pvL(2)
            emit_pvL(3)
            s.add("pe", lambda t, hp=hp, sqb=sqb: t.matmul(pAm[:], MZ[:, hp * 65:(hp + 1) * 65], sqb[:],
                                                           start=True, stop=True),
                  reads=["MZ", sqk], writes=["pAm"], tag="amm")
            for lt in range(4):
                s.add("pe", lambda t, lt=lt, skb=skb, vh=vh: t.matmul(pD[:, 0:65], skb[:, lt * 64:(lt + 1) * 64],
                                                                     vh[:, lt * 65:(lt + 1) * 65],
                                                                     start=(lt == 0), stop=(lt == 3)),
                      reads=[skk, vhk], writes=["pD"], tag="dmz")
            s.add("act", lambda sc, o=uA_: sc.copy(o[:], pO[0:64, :]), reads=["pO"], writes=[uAk], tag="uA")
            s.add("dve", lambda v, o=zb_: v.tensor_copy(o[0:1, :], pO[64:65, :]), reads=["pO"], writes=[zbk], tag="z0r")
            s.add("dve", lambda v, o=uB_: v.tensor_copy(o[:], pAm[0:64, :]), reads=["pAm"], writes=[uBk], tag="uB")
            s.add("act", lambda sc, o=zb_: sc.copy(o[64:65, :], pAm[64:65, :]), reads=["pAm"], writes=[zbk], tag="z64r")
            s.add("dve", lambda v, hp=hp: v.tensor_tensor(MZ[:, hp * 65:(hp + 1) * 65],
                                                          MZ[:, hp * 65:(hp + 1) * 65], pD[:, 0:65], op=ALU.add),
                  reads=["MZ", "pD"], writes=["MZ"], tag="mzup")
            def mk_tail(hp=hp, hb=hb, zb_=zb_, zbk=zbk, sg_=sg_, sgk=sgk, iz_=iz_, izk=izk,
                        uA_=uA_, uAk=uAk, uB_=uB_, uBk=uBk, c1_=c1_, c1k=c1k, c2_=c2_, c2k=c2k,
                        aT_=aT_, aTk=aTk):
                def emit_tail():
                    s.add("act", lambda sc, o=sg_, i=zb_: sc.activation(o[:], i[:], AF.Sign),
                          reads=[zbk], writes=[sgk], tag="sgnL")
                    s.add("dve", lambda v, o=zb_, b=sg_: v.tensor_tensor(o[:], o[:], b[:], op=ALU.mult),
                          reads=[zbk, sgk], writes=[zbk], tag="absL")
                    s.add("act", lambda sc, o=zb_: sc.activation(o[:], o[:], AF.Ln), reads=[zbk], writes=[zbk], tag="lnzL")
                    s.add("act", lambda sc, o=zb_: sc.activation(o[:], o[:], AF.Exp, scale=-1.0),
                          reads=[zbk], writes=[zbk], tag="expzL")
                    s.add("dve", lambda v, o=iz_, a=zb_, b=sg_: v.tensor_tensor(o[:], a[:], b[:], op=ALU.mult),
                          reads=[zbk, sgk], writes=[izk], tag="izbL")
                    s.add("pe", lambda t, i=iz_: t.matmul(pb[:], B.selz[0:66, 0:64], i[:], start=True, stop=True),
                          reads=[izk, "selz"], writes=["pb"], tag="bzL")
                    s.add("dve", lambda v, o=c1_, a=uA_: v.tensor_tensor(o[:], a[:], pb[:], op=ALU.mult),
                          reads=[uAk, "pb"], writes=[c1k], tag="cL1")
                    s.add("pe", lambda t, i=iz_: t.matmul(pb[:], B.selz[0:66, 64:128], i[:], start=True, stop=True),
                          reads=[izk, "selz"], writes=["pb"], tag="bdL")
                    s.add("dve", lambda v, o=c2_, a=uB_: v.tensor_tensor(o[:], a[:], pb[:], op=ALU.mult),
                          reads=[uBk, "pb"], writes=[c2k], tag="cL2")
                    s.add("dve", lambda v, o=c1_, hp=hp: v.tensor_scalar(o[:], o[:], ogat[:, hp:hp + 1], None, op0=ALU.mult),
                          reads=[c1k], sreads=["ogat"], writes=[c1k], tag="cL3")
                    s.add("dve", lambda v, o=aT_, a=c2_, b=c1_, hp=hp:
                          v.scalar_tensor_tensor(o[:], a[:], gat[:, hp:hp + 1], b[:], op0=ALU.mult, op1=ALU.add),
                          reads=[c2k, c1k], sreads=["gat"], writes=[aTk], tag="cL4")
                    s.add("dve", lambda v, hp=hp, a=aT_: v.tensor_copy(
                        preT[0:64, :].rearrange("d (cc t) -> d cc t", cc=8)[:, :, 32 * hp: 32 * hp + 32],
                        a[:].rearrange("d (m c) -> d m c", m=8)[:, :, 0:32]),
                        reads=[aTk], writes=["preT"], tag="scat0")
                    s.add("dve", lambda v, hp=hp, a=aT_: v.tensor_copy(
                        preT[64:128, :].rearrange("d (cc t) -> d cc t", cc=8)[:, :, 32 * hp: 32 * hp + 32],
                        a[:].rearrange("d (m c) -> d m c", m=8)[:, :, 32:64]),
                        reads=[aTk], writes=["preT"], tag="scat1")
                return emit_tail
            pending_tail[0] = mk_tail()
        if pending_tail[0] is not None:
            pending_tail[0]()
            pending_tail[0] = None
        for dc in range(DC):
            p_, pk_ = pg[dc % 2], f"pg{dc % 2}"
            for cc in range(8):
                s.add("pe", lambda t, p=p_, cc=cc, dc=dc:
                      t.matmul(p[:, 0:256], wo_sb[:, cc * 1024 + dc * 128: cc * 1024 + (dc + 1) * 128],
                               preT[:, cc * 256:(cc + 1) * 256], start=(cc == 0), stop=(cc == 7)),
                      reads=["wo_sb", "preT"], writes=[pk_], tag="woL")
            d_, dk_ = dtile[0], "dtilek"
            s.add("dve", lambda v, p=p_, o=d_: v.tensor_copy(o[:], p[:, 0:256]),
                  reads=[pk_], writes=[dk_], tag="woLe")
            s.dma("sp", lambda q, t=d_, dc=dc, seg=seg:
                  q.dma_start(ddram.rearrange("(dc p) t -> dc p t", p=128)[dc, :, seg * 256:(seg + 1) * 256], t[:]),
                  reads=[dk_], writes=["ddram"], tag="woSt")
    s.barrier("endL1")
    close()

    B.tps = [(ps(f"tpE2{i}", [128, 128]), f"tpE2{i}") for i in range(4)]
    dl_rot = [sb(f"dl{i}", [128, 1024], F32) for i in range(2)]
    for tt in range(16):
        dlb, dlk = dl_rot[tt % 2], f"dl{tt % 2}"
        s.dma("sp", lambda q, t=dlb, tt=tt:
              q.dma_start(t[:].rearrange("p (dc c) -> p dc c", dc=8),
                          ddram.rearrange("(dc p) t -> p dc t", p=128)[:, :, tt * 128:(tt + 1) * 128]),
              reads=[dlk, "ddram"], writes=[dlk], tag="dlld")
        xt, xtk = B.xt_rot[B.xt_i % len(B.xt_rot)]
        B.xt_i += 1
        s.dma("sp", lambda q, t=xt, a=din["xprev"], tt=tt: q.dma_start(t[:], a[tt * 128:(tt + 1) * 128, :]),
              reads=[xtk], writes=[xtk], tag="ldxe2")
        ob, obk = B.out_rot[B.out_i % len(B.out_rot)]
        B.out_i += 1
        for dc in range(DC):
            pt_, ptk = B.tps[dc % 4]
            s.add("pe", lambda t, p=pt_, d=dlb, dc=dc:
                  t.matmul(p[:], d[:, dc * 128:(dc + 1) * 128], B.cid32[:], start=True, stop=True),
                  reads=[dlk, "cid32"], writes=[ptk], tag="dT2")
            s.add("dve", lambda v, p=pt_, o=ob, x=xt, dc=dc:
                  v.tensor_tensor(o[:, dc * 128:(dc + 1) * 128], p[:], x[:, dc * 128:(dc + 1) * 128], op=ALU.add),
                  reads=[ptk, xtk], writes=[obk], tag="dadd2")
        s.dma("sp", lambda q, t=ob, tt=tt: q.dma_start(xout[tt * 128:(tt + 1) * 128, :], t[:]),
              reads=[obk], writes=[f"outx2_{tt}"], tag="stx2")
    s.add("sp", lambda q: q.nop(), reads=[f"outx2_{t}" for t in range(16)], tag="fin")
    close()
    close(gblk)
    return din


_B2_CACHE = {}


def build_b2_program():
    if "nc" not in _B2_CACHE:
        nc, s, B = _mk_sched()
        xin = nc.dram_tensor("xprev", [NT, D], F32, kind="ExternalInput").ap()
        xout = nc.dram_tensor("xb2", [NT, D], F32, kind="ExternalOutput").ap()
        build_l1_attn(nc, s, B, xin, xout)
        _B2_CACHE["nc"] = _finish(nc, s)
        _B2_CACHE["stats"] = s.stats()
    return _B2_CACHE["nc"]


def run_block2(inputs, x_cur):
    C = _host_consts()
    nc = build_b2_program()
    betas = np.asarray(inputs["L1_attn_betas"], np.float32)[0, :, 0, :]
    m0 = np.asarray(inputs["L1_attn_memory0"], np.float32)[0]
    z0 = np.asarray(inputs["L1_attn_z0"], np.float32)[0, :, 0, :]
    shared = {
        "wq1": np.asarray(inputs["L1_attn_Wq"], np.float32),
        "wk1": np.asarray(inputs["L1_attn_Wk"], np.float32),
        "wv1": np.asarray(inputs["L1_attn_Wv"], np.float32),
        "wo1": np.asarray(inputs["L1_attn_Wo"], np.float32),
        "c_masks": C["masks"],
        "c_cid16": C["cid16"], "c_cid32": C["cid32"], "c_ones128": C["ones128"], "c_onesrow": C["onesrow"],
        "c_selz": C["selz"],
    }
    in_maps = []
    for c in range(8):
        half = c % 2
        m = dict(shared)
        m["xprev"] = np.ascontiguousarray(x_cur[c])
        m["betas8"] = np.ascontiguousarray(betas[half * 8:(half + 1) * 8])
        m["m08"] = np.ascontiguousarray(m0[half * 8:(half + 1) * 8])
        m["z08"] = np.ascontiguousarray(z0[half * 8:(half + 1) * 8])
        in_maps.append(m)
    res = run_bass_kernel_spmd(nc, in_maps, core_ids=list(range(8)))
    return [r["xb2"] for r in res.results]



def build_l1_ffn(nc, s, B, xprev, xout):
    din = {"xprev": xprev}
    for nm, shp, dt in [("wg", [D, 8], F32),
                        ("wu", [8, D, 2048], F32), ("wd", [8, 2048, D], F32),
                        ("w1", [D, 2048], F32), ("w2", [2048, D], F32),
                        ("c_iota8", [128, 8], F32)]:
        din[nm] = nc.dram_tensor(nm, shp, dt, kind="ExternalInput").ap()

    gblk, gph = [], []

    def sb(name, shape, dtype, scope=None):
        g = nc.sbuf_tensor(name, shape, dtype)
        (gph if scope is None else scope).append(g)
        return g.__enter__()

    def ps(name, shape, dtype=F32):
        g = nc.psum_tensor(name, shape, dtype)
        gph.append(g)
        return g.__enter__()

    def close(lst=None):
        lst = gph if lst is None else lst
        for g in reversed(lst):
            g.__exit__(None, None, None)
        lst.clear()

    xn = sb("xn3", [128, DC * NT], BF16, scope=gblk)
    macc = sb("macc", [128, 16 * 1024], F32, scope=gblk)
    cw = sb("cw", [128, 128], F32, scope=gblk)

    B.tps = [(ps(f"tpU{i}", [128, 128]), f"tpU{i}") for i in range(4)]
    emit_load_ln_fm(s, B, din["xprev"], xn, "xn3", lambda dc, tt: dc * NT + tt * 128)
    s.barrier("endA3")
    close()

    ph = [ps(f"phU{i}", [128, 512]) for i in range(2)]
    wgsb = sb("wgsb", [128, 64], BF16)
    iot = sb("iot", [128, 8], F32)
    lg = sb("lg", [128, 8], F32)
    oh1 = sb("oh1", [128, 8], F32)
    oh2 = sb("oh2", [128, 8], F32)
    t8 = sb("t8", [128, 8], F32)
    u8 = sb("u8", [128, 8], F32)
    m1 = sb("m1g", [128, 1], F32)
    m2 = sb("m2g", [128, 1], F32)
    i1 = sb("i1g", [128, 1], F32)
    v1n = sb("v1ng", [128, 1], F32)
    v2n = sb("v2ng", [128, 1], F32)
    s.dma("pool", lambda q: q.dma_start(wgsb[:].rearrange("p (dc c) -> p dc c", dc=8),
                                        din["wg"].rearrange("(dc p) c -> p dc c", p=128)),
          writes=["wgsb"], tag="ldwg")
    s.dma("sp", lambda q: q.dma_start(iot[:], din["c_iota8"]), writes=["iot"], tag="ldiota")

    def first_hot(ohk, oh_t, lsrc, lsrck, tag):
        s.add("dve", lambda v: v.tensor_tensor(t8[:], iot[:], oh_t[:], op=ALU.mult),
              reads=["iot", ohk], writes=["t8"], tag=tag + "a")
        s.add("dve", lambda v: v.tensor_scalar(u8[:], oh_t[:], -9.0, 9.0, op0=ALU.mult, op1=ALU.add),
              reads=[ohk], writes=["u8"], tag=tag + "b")
        s.add("dve", lambda v: v.tensor_tensor(t8[:], t8[:], u8[:], op=ALU.add),
              reads=["t8", "u8"], writes=["t8"], tag=tag + "c")
        s.add("dve", lambda v: v.tensor_reduce(i1[:], t8[:], axis=AX.X, op=ALU.min),
              reads=["t8"], writes=["i1g"], tag=tag + "d")
        s.add("dve", lambda v: v.tensor_scalar(oh_t[:], iot[:], i1[:], None, op0=ALU.is_equal),
              reads=["iot"], sreads=["i1g"], writes=[ohk], tag=tag + "e")

    for tt in range(16):
        p_, pk_ = ph[tt % 2], f"phU{tt % 2}"
        for dc in range(DC):
            s.add("pe", lambda t, p=p_, dc=dc, tt=tt:
                  t.matmul(p[:, 0:8], xn[:, dc * NT + tt * 128:dc * NT + (tt + 1) * 128],
                           wgsb[:, dc * 8:(dc + 1) * 8], start=(dc == 0), stop=(dc == 7)),
                  reads=["wgsb", "xn3"], writes=[pk_], tag="gmm")
        s.add("dve", lambda v, p=p_: v.tensor_copy(lg[:], p[:, 0:8]), reads=[pk_], writes=["lg"], tag="gev")
        s.add("dve", lambda v: v.reduce_max(m1[:], lg[:], axis=AX.X), reads=["lg"], writes=["m1g"], tag="gm1")
        s.add("dve", lambda v: v.tensor_scalar(oh1[:], lg[:], m1[:], None, op0=ALU.is_ge),
              reads=["lg"], sreads=["m1g"], writes=["oh1k"], tag="goh1")
        first_hot("oh1k", oh1, lg, "lg", f"gf1_{tt}")
        s.add("dve", lambda v: v.scalar_tensor_tensor(t8[:], oh1[:], -30000.0, lg[:], op0=ALU.mult, op1=ALU.add),
              reads=["oh1k", "lg"], writes=["t8"], tag="gmask")
        s.add("dve", lambda v: v.reduce_max(m2[:], t8[:], axis=AX.X), reads=["t8"], writes=["m2g"], tag="gm2")
        s.add("dve", lambda v: v.tensor_scalar(oh2[:], t8[:], m2[:], None, op0=ALU.is_ge),
              reads=["t8"], sreads=["m2g"], writes=["oh2k"], tag="goh2")
        first_hot("oh2k", oh2, t8, "t8", f"gf2_{tt}")
        s.add("dve", lambda v: v.tensor_tensor(m2[:], m2[:], m1[:], op=ALU.subtract),
              reads=["m2g", "m1g"], writes=["m2g"], tag="gd")
        s.add("act", lambda sc: sc.activation(m2[:], m2[:], AF.Exp), reads=["m2g"], writes=["m2g"], tag="gex")
        s.add("dve", lambda v: v.tensor_scalar(m2[:], m2[:], 1.0, None, op0=ALU.add),
              reads=["m2g"], writes=["m2g"], tag="g1p")
        s.add("dve", lambda v: v.reciprocal(v1n[:], m2[:]), reads=["m2g"], writes=["v1nk"], tag="ginv")
        s.add("dve", lambda v: v.tensor_scalar(v2n[:], v1n[:], -1.0, 1.0, op0=ALU.mult, op1=ALU.add),
              reads=["v1nk"], writes=["v2nk"], tag="gv2")
        s.add("dve", lambda v: v.tensor_scalar(oh1[:], oh1[:], v1n[:], None, op0=ALU.mult),
              reads=["oh1k"], sreads=["v1nk"], writes=["oh1k"], tag="gw1")
        s.add("dve", lambda v, tt=tt: v.scalar_tensor_tensor(cw[:, tt * 8:(tt + 1) * 8], oh2[:], v2n[:],
                                                             oh1[:], op0=ALU.mult, op1=ALU.add),
              reads=["oh2k", "oh1k"], sreads=["v2nk"], writes=["cw"], tag="gcw")

    he = sb("he", [128, 16 * 1024], BF16)
    lwu = [sb(f"lwu{i}", [128, 8 * 512], BF16) for i in range(2)]
    wdh = [sb(f"wdh{i}", [128, 16 * 512], BF16) for i in range(2)]
    for tt2 in range(2):
        for xp in range(9):
            wu_ap = din["w1"] if xp == 0 else din["wu"][xp - 1]
            wd_ap = din["w2"] if xp == 0 else din["wd"][xp - 1]
            for fq in range(4):
                w_, wk_ = lwu[fq % 2], f"lwu{fq % 2}"
                s.dma("pool", lambda q, t=w_, a=wu_ap.rearrange("(dc p) c -> p dc c", p=128), fq=fq:
                      q.dma_start(t[:].rearrange("p (dc c) -> p dc c", dc=8), a[:, :, fq * 512:(fq + 1) * 512]),
                      reads=[wk_], writes=[wk_], tag="uw")
                for fc4 in range(4):
                    fc = fq * 4 + fc4
                    for ns in range(2):
                        p_, pk_ = ph[ns % 2], f"phU{ns % 2}"
                        for dc in range(DC):
                            s.add("pe", lambda t, p=p_, w=w_, dc=dc, ns=ns, tt2=tt2, fc4=fc4:
                                  t.matmul(p[:], w[:, dc * 512 + fc4 * 128: dc * 512 + (fc4 + 1) * 128],
                                           xn[:, dc * NT + tt2 * 1024 + ns * 512: dc * NT + tt2 * 1024 + ns * 512 + 512],
                                           start=(dc == 0), stop=(dc == 7)),
                                  reads=[wk_, "xn3"], writes=[pk_], tag="umm")
                        s.add("act", lambda sc, p=p_, fc=fc, ns=ns:
                              sc.activation(he[:, fc * 1024 + ns * 512: fc * 1024 + (ns + 1) * 512], p[:], AF.Gelu),
                              reads=[pk_], writes=["he"], tag="ugelu")
            for dh in range(2):
                wd_, wdk = wdh[dh % 2], f"wdh{dh % 2}"
                s.dma("pool", lambda q, a=wd_ap.rearrange("(fc p) c -> p fc c", p=128), dh=dh, t=wd_:
                      q.dma_start(t[:].rearrange("p (fc c) -> p fc c", fc=16),
                                  a[:, :, dh * 512:(dh + 1) * 512]),
                      reads=[wdk], writes=[wdk], tag="dw")
                for tl in range(8):
                    tt = tt2 * 8 + tl
                    p_, pk_ = ph[tl % 2], f"phU{tl % 2}"
                    for fc in range(16):
                        s.add("pe", lambda t, p=p_, fc=fc, tl=tl, w=wd_:
                              t.matmul(p[:], he[:, fc * 1024 + tl * 128: fc * 1024 + tl * 128 + 128],
                                       w[:, fc * 512:(fc + 1) * 512], start=(fc == 0), stop=(fc == 15)),
                              reads=["he", wdk], writes=[pk_], tag="dmm")
                    if xp == 0:
                        s.add("act", lambda sc, p=p_, tt=tt, dh=dh:
                              sc.copy(macc[:, tt * 1024 + dh * 512: tt * 1024 + (dh + 1) * 512], p[:]),
                              reads=[pk_], writes=["macc"], tag="dsh")
                    else:
                        mt, mtk = B.xnt_rot[B.xnt_i % len(B.xnt_rot)]
                        B.xnt_i += 1
                        s.add("act", lambda sc, p=p_, o=mt, tt=tt, xp=xp:
                              sc.activation(o[:, 0:512], p[:], AF.Identity,
                                            scale=cw[:, tt * 8 + xp - 1: tt * 8 + xp]),
                              reads=[pk_], sreads=["cw"], writes=[mtk], tag="dsc")
                        s.add("dve", lambda v, o=mt, tt=tt, dh=dh:
                              v.tensor_tensor(macc[:, tt * 1024 + dh * 512: tt * 1024 + (dh + 1) * 512],
                                              macc[:, tt * 1024 + dh * 512: tt * 1024 + (dh + 1) * 512],
                                              o[:, 0:512], op=ALU.add),
                              reads=["macc", mtk], writes=["macc"], tag="dacc")
    s.barrier("endU")
    close()

    for tt in range(16):
        xt, xtk = B.xt_rot[B.xt_i % len(B.xt_rot)]
        B.xt_i += 1
        s.dma("sp", lambda q, t=xt, a=din["xprev"], tt=tt: q.dma_start(t[:], a[tt * 128:(tt + 1) * 128, :]),
              reads=[xtk], writes=[xtk], tag="ldxf")
        ob, obk = B.out_rot[B.out_i % len(B.out_rot)]
        B.out_i += 1
        s.add("dve", lambda v, o=ob, x=xt, tt=tt:
              v.tensor_tensor(o[:], macc[:, tt * 1024:(tt + 1) * 1024], x[:], op=ALU.add),
              reads=["macc", xtk], writes=[obk], tag="fadd")
        s.dma("sp", lambda q, t=ob, tt=tt: q.dma_start(xout[tt * 128:(tt + 1) * 128, :], t[:]),
              reads=[obk], writes=[f"outx3_{tt}"], tag="stx3")
    s.add("sp", lambda q: q.nop(), reads=[f"outx3_{t}" for t in range(16)], tag="fin")
    close(gblk)
    return din


_B3_CACHE = {}


def build_b3_program():
    if "nc" not in _B3_CACHE:
        nc, s, B = _mk_sched()
        xin = nc.dram_tensor("xprev", [NT, D], F32, kind="ExternalInput").ap()
        xout = nc.dram_tensor("xb3", [NT, D], F32, kind="ExternalOutput").ap()
        build_l1_ffn(nc, s, B, xin, xout)
        _B3_CACHE["nc"] = _finish(nc, s)
        _B3_CACHE["stats"] = s.stats()
    return _B3_CACHE["nc"]


def run_block3(inputs, x_cur):
    C = _host_consts()
    nc = build_b3_program()
    shared = {
        "wg": np.asarray(inputs["L1_ffn_Wg"], np.float32),
        "wu": np.asarray(inputs["L1_ffn_Wu"], np.float32),
        "wd": np.asarray(inputs["L1_ffn_Wd"], np.float32),
        "w1": np.asarray(inputs["L1_ffn_W1"], np.float32),
        "w2": np.asarray(inputs["L1_ffn_W2"], np.float32),
        "c_iota8": C["iota8"],
        "c_cid16": C["cid16"], "c_cid32": C["cid32"], "c_ones128": C["ones128"], "c_onesrow": C["onesrow"],
        "c_selz": C["selz"],
    }
    in_maps = []
    for c in range(8):
        m = dict(shared)
        m["xprev"] = np.ascontiguousarray(x_cur[c])
        in_maps.append(m)
    res = run_bass_kernel_spmd(nc, in_maps, core_ids=list(range(8)))
    return [r["xb3"] for r in res.results]



_FUSED_CACHE = {}


def build_fused_program():
    if "nc" not in _FUSED_CACHE:
        nc, s, B = _mk_sched()
        xb0 = nc.dram_tensor("xb0", [NT, D], F32, kind="Internal").ap()
        xb1 = nc.dram_tensor("xb1", [NT, D], F32, kind="Internal").ap()
        xb2 = nc.dram_tensor("xb2", [NT, D], F32, kind="Internal").ap()
        y = nc.dram_tensor("y", [NT, D], F32, kind="ExternalOutput").ap()
        build_l0_attn(nc, s, B, xb0)
        s.barrier("b01")
        build_l0_ffn(nc, s, B, xb0, xb1)
        s.barrier("b12")
        build_l1_attn(nc, s, B, xb1, xb2)
        s.barrier("b23")
        build_l1_ffn(nc, s, B, xb2, y)
        _FUSED_CACHE["nc"] = _finish(nc, s)
        _FUSED_CACHE["stats"] = s.stats()
    return _FUSED_CACHE["nc"]


def kernel(**inputs):
    C = _host_consts()
    nc = build_fused_program()
    X = np.asarray(inputs["L0_attn_X"], dtype=np.float32)
    gam_t = np.broadcast_to(np.asarray(inputs["L0_attn_gamma"], np.float32), (128, D)).copy()
    bet_t = np.broadcast_to(np.asarray(inputs["L0_attn_beta"], np.float32), (128, D)).copy()
    lam64 = np.broadcast_to(np.asarray(inputs["L0_attn_lam"], np.float32).reshape(1, 1), (64, 1)).copy()
    betas = np.asarray(inputs["L1_attn_betas"], np.float32)[0, :, 0, :]
    m0 = np.broadcast_to(np.asarray(inputs["L1_attn_memory0"], np.float32)[0], (16, 64, 64))
    z0 = np.asarray(inputs["L1_attn_z0"], np.float32)[0, :, 0, :]
    shared = {
        "wq": np.asarray(inputs["L0_attn_Wq"], np.float32), "bq": np.asarray(inputs["L0_attn_bq"], np.float32),
        "wk": np.asarray(inputs["L0_attn_Wk"], np.float32), "bk": np.asarray(inputs["L0_attn_bk"], np.float32),
        "wv": np.asarray(inputs["L0_attn_Wv"], np.float32), "bv": np.asarray(inputs["L0_attn_bv"], np.float32),
        "wo": np.asarray(inputs["L0_attn_Wo"], np.float32), "bo": np.asarray(inputs["L0_attn_bo"], np.float32),
        "gam_t": gam_t, "bet_t": bet_t, "lam64": lam64,
        "w_fc": np.asarray(inputs["L0_ffn_W_fc"], np.float32),
        "ln_w": np.asarray(inputs["L0_ffn_ln_w"], np.float32),
        "ln_b": np.asarray(inputs["L0_ffn_ln_b"], np.float32),
        "w_proj": np.asarray(inputs["L0_ffn_W_proj"], np.float32),
        "wq1": np.asarray(inputs["L1_attn_Wq"], np.float32),
        "wk1": np.asarray(inputs["L1_attn_Wk"], np.float32),
        "wv1": np.asarray(inputs["L1_attn_Wv"], np.float32),
        "wo1": np.asarray(inputs["L1_attn_Wo"], np.float32),
        "wg": np.asarray(inputs["L1_ffn_Wg"], np.float32),
        "wu": np.asarray(inputs["L1_ffn_Wu"], np.float32),
        "wd": np.asarray(inputs["L1_ffn_Wd"], np.float32),
        "w1": np.asarray(inputs["L1_ffn_W1"], np.float32),
        "w2": np.asarray(inputs["L1_ffn_W2"], np.float32),
        "c_iota8": C["iota8"], "c_masks": C["masks"],
        "c_cid16": C["cid16"], "c_cid32": C["cid32"], "c_ones128": C["ones128"],
        "c_onesrow": C["onesrow"], "c_selz": C["selz"],
    }
    in_maps = []
    for c in range(8):
        b, half = c // 2, c % 2
        own, oth = _core_rows(c)
        m = dict(shared)
        m["x_own"] = np.ascontiguousarray(X[b][own])
        m["x_oth"] = np.ascontiguousarray(X[b][oth])
        m["betas8"] = np.ascontiguousarray(betas[half * 8:(half + 1) * 8])
        m["m08"] = np.ascontiguousarray(m0[half * 8:(half + 1) * 8])
        m["z08"] = np.ascontiguousarray(z0[half * 8:(half + 1) * 8])
        in_maps.append(m)
    res = run_bass_kernel_spmd(nc, in_maps, core_ids=list(range(8)))
    out = np.zeros((4, S_FULL, D), np.float32)
    for c in range(8):
        b = c // 2
        own, _ = _core_rows(c)
        out[b][own] = res.results[c]["y"]
    return out
